# version 01
# baseline (speedup 1.0000x reference)
import numpy as np
from contextlib import ExitStack

import concourse.bass as bass
import concourse.mybir as mybir
from concourse.bass_utils import run_bass_kernel_spmd

F32 = mybir.dt.float32
BF16 = mybir.dt.bfloat16
I32 = mybir.dt.int32
Alu = mybir.AluOpType
Act = mybir.ActivationFunctionType

B, T, C = 4, 4096, 1024
T2 = T // 2
NH, HD, KVR = 16, 64, 128
EF, EO, NST, DTR = 2048, 1024, 16, 64
FH = 3072
QSCALE = 0.125
FSC = 1.0 / 127.0
EPS = 1e-5
GROUPS = [[0, 1], [2, 3], [4, 5], [6, 7]]
ENGS = ("tensor", "vector", "scalar", "sync", "gpsimd")


class Ctr:
    __slots__ = ("name", "sem", "step", "n", "is_dma", "watermark", "settled")

    def __init__(self, name, sem, step):
        self.name, self.sem, self.step, self.n = name, sem, step, 0
        self.is_dma = step == 16
        self.watermark = 0
        self.settled = 0


class Prog:

    def __init__(self, nc, E):
        self.nc = nc
        self.parts = {e: [] for e in ENGS}
        self.ctrs = {}
        for name, eng, step in [("ts", "tensor", 1), ("vs", "vector", 1),
                                ("ss", "scalar", 1), ("gc", "gpsimd", 1),
                                ("qs", "sync", 16), ("qg", "gpsimd", 16)]:
            self.ctrs[name] = Ctr(name, E(nc.semaphore(name)), step)
        self.home = {"ts": "tensor", "vs": "vector", "ss": "scalar",
                     "gc": "gpsimd", "qs": "sync", "qg": "gpsimd"}
        self.waited = {e: {} for e in ENGS}

    def wait(self, eng, cname, val=None):
        c = self.ctrs[cname]
        val = c.n if val is None else val
        if val <= 0 or self.waited[eng].get(cname, -1) >= val:
            return
        if c.is_dma and val < c.n:
            val = c.n
            if self.waited[eng].get(cname, -1) >= val:
                return
        self.waited[eng][cname] = val
        if c.is_dma and eng != self.home[cname]:
            c.watermark = max(c.watermark, val)
        self.parts[eng].append(lambda e, s=c.sem, v=val: e.wait_ge(s, v))

    def op(self, cname, fn, sw=True):
        c = self.ctrs[cname]
        eng = self.home[cname]
        if sw and eng in ("vector", "scalar"):
            self.wait(eng, cname, c.n)
        if c.is_dma and c.watermark > c.settled and c.n > c.settled:
            self.waited[eng][cname] = max(self.waited[eng].get(cname, -1), c.n)
            self.parts[eng].append(lambda e, s=c.sem, v=c.n: e.wait_ge(s, v))
            c.settled = c.n
        c.n += c.step
        self.parts[eng].append(
            lambda e, fn=fn, s=c.sem, k=c.step: fn(e).then_inc(s, k))
        return c.n

    def raw(self, eng, fn):
        self.parts[eng].append(fn)

    def bar(self):
        vals = {k: c.n for k, c in self.ctrs.items()}
        for k, c in self.ctrs.items():
            if c.is_dma:
                self.wait(self.home[k], k, c.n)
                c.settled = c.n
        for e in ENGS:
            for k, v in vals.items():
                self.wait(e, k, v)

    def finalize(self, block):
        getattr(block, "tensor")(lambda e: [f(e) for f in self.parts["tensor"]])
        getattr(block, "vector")(lambda e: [f(e) for f in self.parts["vector"]])
        getattr(block, "scalar")(lambda e: [f(e) for f in self.parts["scalar"]])
        getattr(block, "sync")(lambda e: [f(e) for f in self.parts["sync"]])
        getattr(block, "gpsimd")(lambda e: [f(e) for f in self.parts["gpsimd"]])


def build_nc(dbg=(), groups=None):
    nc = bass.Bass("TRN2", target_bir_lowering=False)
    dbg = set(dbg)
    global GROUPS
    GROUPS = groups if groups is not None else [[0, 1], [2, 3], [4, 5], [6, 7]]
    ctx = ExitStack()
    E = ctx.enter_context

    def DI(name, shape, dtype=F32):
        return nc.dram_tensor(name, list(shape), dtype, kind="ExternalInput")

    x_own = DI("x_own", [8, 128, T2])
    x_full = DI("x_full", [8, 128, T])
    wq = DI("wq", [8, 128, C])
    wkvd = DI("wkvd", [8, 128, KVR])
    wkvu_k = DI("wkvu_k", [128, C])
    wkvu_v = DI("wkvu_v", [128, C])
    wout = DI("wout", [8, 128, C])
    wgT = DI("wgT", [8, 128, FH], I32)
    wuT = DI("wuT", [8, 128, FH], I32)
    wdT = DI("wdT", [24, 128, C], I32)
    inpT = DI("inpT", [8, 128, EF])
    xpT = DI("xpT", [8, 128, 96])
    dtT = DI("dtT", [64, EO])
    outT = DI("outT", [8, 128, C])
    a_log = DI("a_log", [128, 128])
    convw = DI("convw", [128, 32])
    convb = DI("convb", [128, 8])
    dtb = DI("dtb", [128, 8])
    dpar = DI("dpar", [128, 8])
    ones128 = DI("ones128", [128, 128], BF16)
    eye128 = DI("eye128", [128, 128], BF16)
    sel2 = DI("sel2", [2, 128], BF16)
    mask_a = DI("mask_a", [128, 128], BF16)
    mask_b = DI("mask_b", [128, 128], BF16)

    y_own = nc.dram_tensor("y_own", [8, 128, T2], F32, kind="ExternalOutput")

    dbg_t = {}
    for name, shape, dt in [
        ("d_xhat1", [8, 128, T], BF16), ("d_q", [8, 128, T2], BF16),
        ("d_k", [8, 128, T], BF16), ("d_x1", [8, 128, T2], F32),
        ("d_x2", [8, 128, T2], F32), ("d_dbc", [128, T], F32),
        ("d_delta", [8, 128, T], F32), ("d_xcs", [8, 128, T], BF16),
        ("d_x3", [8, 128, T2], F32), ("d_ymul", [8, 128, T], BF16),
    ]:
        if name in dbg:
            dbg_t[name] = nc.dram_tensor(name, shape, dt, kind="ExternalOutput")

    kT_scr = nc.dram_tensor("kT_scr", [8, 128, T], BF16)
    v_scr = nc.dram_tensor("v_scr", [32, 128, C], BF16)
    q_scr = nc.dram_tensor("q_scr", [8, 128, T2], BF16)
    wg_scr = nc.dram_tensor("wg_scr", [24, 8, 128, 128], BF16)
    wu_scr = nc.dram_tensor("wu_scr", [24, 8, 128, 128], BF16)
    wd_scr = nc.dram_tensor("wd_scr", [8, 24, 128, 128], BF16)
    xc_scr = nc.dram_tensor("xc_scr", [8, 128, T], BF16)
    zs_scr = nc.dram_tensor("zs_scr", [8, 128, T], BF16)
    dl_scr = nc.dram_tensor("dl_scr", [8, 128, T], F32)
    ym_scr = nc.dram_tensor("ym_scr", [8, 128, T], BF16)
    park_scr = nc.dram_tensor("park_scr", [8, 128, T2], F32)
    ln3_in = nc.dram_tensor("ln3_in", [1025, T2], BF16)
    ln3_out = nc.dram_tensor("ln3_out", [2, 1025, T2], BF16)
    dbc_in = nc.dram_tensor("dbc_in", [96, T], F32)
    dbc_out = nc.dram_tensor("dbc_out", [96, T], F32)
    dbc_bf = nc.dram_tensor("dbc_bf", [32, T], BF16)
    rs_in = nc.dram_tensor("rs_in", [2, 8, 128, T2], F32)
    rs_out = nc.dram_tensor("rs_out", [8, 128, T2], F32)

    P = Prog(nc, E)
    sb_frees = {}
    sb_order = []

    def SA(name, shape, dtype):
        cm = nc.sbuf_tensor(name, list(shape), dtype)
        t = cm.__enter__()
        sb_frees[name] = cm
        sb_order.append(name)
        return t

    def SF(*names):
        for n in sorted(names, key=lambda n: -sb_order.index(n)):
            assert n == sb_order[-1], f"non-LIFO free {n} vs stack {sb_order[-6:]}"
            sb_order.pop()
            sb_frees.pop(n).__exit__(None, None, None)

    PS = nc.psum_tensor("PS", [128, 4096], F32).__enter__()

    ones_sb = SA("ones_sb", [128, 128], BF16)
    eye_sb = SA("eye_sb", [128, 128], BF16)
    sel2_sb = SA("sel2_sb", [2, 128], BF16)
    maska_sb = SA("maska_sb", [128, 128], BF16)
    maskb_sb = SA("maskb_sb", [128, 128], BF16)
    class _RowMap:
        def __init__(self, raw, width, slots):
            self.raw, self.w, self.slots = raw, width, slots
        def __getitem__(self, key):
            rs, cs = key
            off = self.slots[rs.start]
            a = cs.start or 0
            b = cs.stop if cs.stop is not None else self.w
            return self.raw[0:1, off + a:off + b]

    rows_raw = SA("rows_raw", [1, 3 * T], BF16)
    rows_sb = _RowMap(rows_raw, T, {0: 0, 2: 0, 4: 0, 6: 0, 5: T, 1: 2 * T, 3: 2 * T})
    csum_raw = SA("csum_raw", [1, 8192], BF16)
    csum_sb = _RowMap(csum_raw, 8192, {0: 0, 1: 3072, 2: 6144, 3: 7168, 4: 6144})
    srow = SA("srow", [1, 2048], F32)
    scol = SA("scol", [128, 160], F32)
    sbcol = SA("sbcol", [128, 64], BF16)
    c0s = SA("c0s", [1, 8], F32)
    c0b = SA("c0b", [1, 2], BF16)
    cvec_sb = SA("cvec_sb", [128, 8], F32)
    amat_sb = SA("amat_sb", [128, 128], F32)
    convw_sb = SA("convw_sb", [128, 32], F32)
    convb_sb = SA("convb_sb", [128, 8], F32)
    dtb_sb = SA("dtb_sb", [128, 8], F32)
    dpar_sb = SA("dpar_sb", [128, 8], F32)
    state_sb = SA("state_sb", [128, 128], F32)
    xfm_h = [SA("x_fm", [128, 8 * T2], F32)]

    def xfm(c):
        return xfm_h[0][:, c * T2:(c + 1) * T2]

    op, wait, bar = P.op, P.wait, P.bar

    def lp(fn):
        def go(e):
            with nc.allow_low_precision(reason="intentional bf16 elementwise"):
                return fn(e)
        return go

    al_sb = SA("al_sb", [128, 128], F32)
    for dst, src in [(ones_sb, ones128), (eye_sb, eye128), (sel2_sb, sel2),
                     (maska_sb, mask_a), (maskb_sb, mask_b), (convw_sb, convw),
                     (convb_sb, convb), (dtb_sb, dtb), (dpar_sb, dpar),
                     (al_sb, a_log)]:
        op("qs", lambda e, d=dst, s=src: e.dma_start(out=d[:], in_=s.ap()))
    wait("scalar", "qs")
    op("ss", lambda e: e.activation(amat_sb[:], al_sb[:], Act.Exp))
    wait("vector", "ss")
    op("vs", lambda e: e.tensor_scalar_mul(amat_sb[:], amat_sb[:], -1.0))
    op("vs", lambda e: e.memset(state_sb[:], 0.0))
    op("vs", lambda e: e.memset(cvec_sb[:, 1:2], EPS))

    wi = [SA("p0_wi0", [128, FH], I32), SA("p0_wi1", [128, FH], I32)]
    wb = [SA("p0_wb0", [128, FH], BF16), SA("p0_wb1", [128, FH], BF16)]
    conv_sets = [(wgT, wg_scr, 0, 8, FH), (wuT, wu_scr, 1, 8, FH),
                 (wdT, wd_scr, None, 24, C)]
    conv_v = {}
    store_g = {}
    colsum_t = {}
    it = 0
    for src, dst, crow, nchunk, w in conv_sets:
        for c in range(nchunk):
            b = it % 2
            if it >= 2:
                wait("sync", "vs", conv_v[it - 2])
            op("qs", lambda e, b=b, src=src, c=c, w=w:
               e.dma_start(out=wi[b][:, 0:w], in_=src.ap()[c]))
            wait("vector", "qs")
            if it >= 2:
                wait("vector", "qg", store_g[it - 2])
                if it - 2 in colsum_t:
                    wait("vector", "ts", colsum_t[it - 2])
            conv_v[it] = op("vs", lambda e, b=b, w=w:
                            e.tensor_copy(wb[b][:, 0:w], wi[b][:, 0:w]))
            wait("gpsimd", "vs", conv_v[it])
            if crow is None:
                store_g[it] = op("qg", lambda e, b=b, dst=dst, c=c:
                                 e.dma_start(out=dst.ap()[:, c].transpose([1, 0, 2]),
                                             in_=wb[b][:, 0:C].rearrange("p (f j) -> p f j", j=128)))
            else:
                store_g[it] = op("qg", lambda e, b=b, dst=dst, c=c:
                                 e.dma_start(out=dst.ap()[:, c].transpose([1, 0, 2]),
                                             in_=wb[b][:, 0:FH].rearrange("p (h j) -> p h j", j=128)))
            if crow is not None:
                wait("tensor", "vs", conv_v[it])
                for h in range(6):
                    op("ts", lambda e, b=b, h=h, c=c, nchunk=nchunk:
                       e.matmul(PS[0:1, h * 512:(h + 1) * 512], ones_sb[:, 0:1],
                                wb[b][:, h * 512:(h + 1) * 512],
                                start=(c == 0), stop=(c == nchunk - 1),
                                skip_group_check=True))
                colsum_t[it] = P.ctrs["ts"].n
            it += 1
        if crow is not None:
            wait("vector", "ts")
            op("vs", lambda e, crow=crow: e.tensor_copy(
                csum_sb[crow:crow + 1, 0:FH], PS[0:1, 0:FH]))
            wait("tensor", "vs")
    SF("al_sb", "p0_wi0", "p0_wi1", "p0_wb0", "p0_wb1")
    bar()

    def ln_stats(xb_chunks, ntok, row_idx, nchunks=8, extra_sq=None):
        xsq = extra_sq
        nc_cols = ntok // 128
        nhalf = max(1, ntok // 2048)
        for half in range(nhalf):
            h0 = half * 2048
            hw = min(2048, ntok - h0)
            hc = hw // 128
            wait("tensor", "vs")
            for c in range(nchunks):
                wait("scalar", "vs")
                wait("scalar", "ts")
                op("ss", lambda e, c=c, h0=h0, hw=hw:
                   e.activation(xsq[:, 0:hw], xb_chunks(c)[:, h0:h0 + hw], Act.Square))
                wait("tensor", "ss")
                for t0 in range(0, hw, 512):
                    op("ts", lambda e, c=c, h0=h0, t0=t0:
                       e.matmul(PS[0:1, t0:t0 + 512], ones_sb[:, 0:1],
                                xb_chunks(c)[:, h0 + t0:h0 + t0 + 512],
                                start=(c == 0), stop=(c == nchunks - 1),
                                skip_group_check=True))
                    op("ts", lambda e, c=c, t0=t0:
                       e.matmul(PS[0:1, 2048 + t0:2048 + t0 + 512], ones_sb[:, 0:1],
                                xsq[:, t0:t0 + 512],
                                start=(c == 0), stop=(c == nchunks - 1),
                                skip_group_check=True))
            wait("vector", "ts")
            wait("vector", "qs")
            op("vs", lambda e, hw=hw: e.tensor_copy(srow[0:1, 0:hw], PS[0:1, 0:hw]))
            wait("sync", "vs")
            op("qs", lambda e, half=half, hc=hc, hw=hw: e.dma_start(
                out=scol[:, half * 16:half * 16 + hc], in_=srow[0:1, 0:hw]))
            qsx = P.ctrs["qs"].n
            wait("vector", "qs", qsx)
            op("vs", lambda e, hw=hw: e.tensor_copy(srow[0:1, 0:hw],
                                                    PS[0:1, 2048:2048 + hw]))
            wait("sync", "vs")
            op("qs", lambda e, half=half, hc=hc, hw=hw: e.dma_start(
                out=scol[:, 32 + half * 16:32 + half * 16 + hc], in_=srow[0:1, 0:hw]))
        wait("vector", "qs")
        ncc = nc_cols
        op("vs", lambda e: e.tensor_scalar_mul(
            scol[:, 64:64 + ncc], scol[:, 0:ncc], 1.0 / 1024.0))
        op("vs", lambda e: e.tensor_mul(
            scol[:, 96:96 + ncc], scol[:, 64:64 + ncc], scol[:, 64:64 + ncc]))
        op("vs", lambda e: e.scalar_tensor_tensor(
            scol[:, 96:96 + ncc], scol[:, 32:32 + ncc], 1.0 / 1024.0,
            scol[:, 96:96 + ncc], Alu.mult, Alu.subtract))
        wait("scalar", "vs")
        op("ss", lambda e: e.activation(
            scol[:, 96:96 + ncc], scol[:, 96:96 + ncc], Act.Ln,
            bias=cvec_sb[:, 1:2]))
        op("ss", lambda e: e.activation(
            scol[:, 96:96 + ncc], scol[:, 96:96 + ncc], Act.Exp, scale=-0.5))
        wait("vector", "ss")
        op("vs", lambda e: e.scalar_tensor_tensor(
            scol[:, 128:128 + ncc], scol[:, 64:64 + ncc], -1.0,
            scol[:, 96:96 + ncc], Alu.mult, Alu.mult))
        op("vs", lambda e: e.tensor_copy(sbcol[:, 0:ncc], scol[:, 96:96 + ncc]))
        op("vs", lambda e: e.tensor_copy(sbcol[:, 32:32 + ncc], scol[:, 128:128 + ncc]))
        wait("sync", "vs")
        for half in range(nhalf):
            h0 = half * 2048
            hw = min(2048, ntok - h0)
            hc = hw // 128
            op("qs", lambda e, h0=h0, hw=hw, half=half, hc=hc: e.dma_start(
                out=rows_sb[5:6, h0:h0 + hw],
                in_=sbcol[:, half * 16:half * 16 + hc]))
            op("qs", lambda e, h0=h0, hw=hw, half=half, hc=hc, r=row_idx: e.dma_start(
                out=rows_sb[r:r + 1, h0:h0 + hw],
                in_=sbcol[:, 32 + half * 16:32 + half * 16 + hc]))
        wait("tensor", "qs")
        wait("vector", "qs")

    def bcast_row(row_ap, ntok, dst_sb, pbank=0):
        for t0 in range(0, ntok, 512):
            bk = pbank + (t0 // 512) % 2
            wait("tensor", "vs")
            op("ts", lambda e, t0=t0, bk=bk: e.matmul(
                PS[:, bk * 512:(bk + 1) * 512], ones_sb[0:1, :],
                row_ap[:, t0:t0 + 512], start=True, stop=True))
            wait("vector", "ts")
            op("vs", lambda e, t0=t0, bk=bk: e.tensor_copy(
                dst_sb[:, t0:t0 + 512], PS[:, bk * 512:(bk + 1) * 512]))

    for c in range(8):
        op("qs", lambda e, c=c: e.dma_start(out=xfm(c), in_=x_own.ap()[c]))
    q_xown = P.ctrs["qs"].n

    xb1 = SA("xb1", [128, 8 * T2], BF16)
    xsq_a = SA("xsq_a", [128, T2], BF16)
    rbc_a = SA("rbc_a", [128, T], BF16)
    wait("vector", "qs", q_xown)
    for c in range(8):
        op("vs", lambda e, c=c: e.tensor_copy(xb1[:, c * T2:(c + 1) * T2], xfm(c)))
    ln_stats(lambda c: xb1[:, c * T2:(c + 1) * T2], T2, 0, extra_sq=xsq_a)
    bcast_row(rows_sb[5:6, :], T2, rbc_a)
    wait("vector", "vs")
    for c in range(8):
        op("vs", lambda e, c=c: e.tensor_mul(
            xb1[:, c * T2:(c + 1) * T2], xb1[:, c * T2:(c + 1) * T2], rbc_a[:, 0:T2]))
    v_xhat1 = P.ctrs["vs"].n

    wqf = [SA("wqf0", [128, C], F32), SA("wqf1", [128, C], F32)]
    wqb = SA("wqb", [128, 8 * 128], BF16)
    wqb8 = SA("wqb8", [8, 1], F32)
    qout = SA("qout", [128, T2], BF16)
    qsq = SA("qsq", [128, T2], BF16)
    wq_all = SA("wq_all", [128, 8 * C], BF16)
    conv_done = []
    for c in range(8):
        b = c % 2
        if c >= 2:
            wait("sync", "vs", conv_done[c - 2])
        op("qs", lambda e, c=c, b=b: e.dma_start(out=wqf[b][:], in_=wq.ap()[c]))
        wait("vector", "qs")
        conv_done.append(op("vs", lambda e, c=c, b=b: e.tensor_copy(
            wq_all[:, c * C:(c + 1) * C], wqf[b][:])))
        wait("tensor", "vs", conv_done[-1])
        for h in range(2):
            op("ts", lambda e, c=c, h=h: e.matmul(
                PS[0:1, h * 512:(h + 1) * 512], ones_sb[:, 0:1],
                wq_all[:, c * C + h * 512:c * C + (h + 1) * 512],
                start=(c == 0), stop=(c == 7), skip_group_check=True))
    wait("vector", "ts")
    op("vs", lambda e: e.tensor_copy(csum_sb[2:3, 0:C], PS[0:1, 0:C]))
    wait("tensor", "vs")
    wait("tensor", "vs", v_xhat1)
    qevs = []
    for fc in range(8):
        wait("scalar", "qg")
        for t0 in range(0, T2, 512):
            bk = 2 + ((fc * 4 + t0 // 512) % 2)
            if len(qevs) >= 2:
                wait("tensor", "ss", qevs[-2])
            for cc in range(8):
                op("ts", lambda e, fc=fc, t0=t0, cc=cc, bk=bk: e.matmul(
                    PS[:, bk * 512:(bk + 1) * 512],
                    wq_all[:, cc * C + fc * 128:cc * C + fc * 128 + 128],
                    xb1[:, cc * T2 + t0:cc * T2 + t0 + 512],
                    start=(cc == 0), stop=False, skip_group_check=True))
            op("ts", lambda e, fc=fc, t0=t0, bk=bk: e.matmul(
                PS[:, bk * 512:(bk + 1) * 512],
                csum_sb[2:3, fc * 128:fc * 128 + 128],
                rows_sb[0:1, t0:t0 + 512],
                start=False, stop=True, skip_group_check=True))
            wait("scalar", "ts")
            qevs.append(op("ss", lambda e, t0=t0, bk=bk: e.activation(
                qout[:, t0:t0 + 512], PS[:, bk * 512:(bk + 1) * 512], Act.Copy)))
        wait("gpsimd", "ss")
        op("qg", lambda e, fc=fc: e.dma_start(out=q_scr.ap()[fc], in_=qout[:]))
        wait("scalar", "ts")
        op("ss", lambda e: e.activation(qsq[:], qout[:], Act.Square))
        wait("tensor", "ss")
        for t0 in range(0, T2, 512):
            op("ts", lambda e, fc=fc, t0=t0: e.matmul(
                PS[0:1, 2048 + t0:2048 + t0 + 512], ones_sb[:, 0:1],
                qsq[:, t0:t0 + 512], start=(fc == 0), stop=(fc == 7),
                skip_group_check=True))
        if "d_q" in dbg_t:
            op("qg", lambda e, fc=fc: e.dma_start(out=dbg_t["d_q"].ap()[fc], in_=qout[:]))
    wait("vector", "ts")
    op("vs", lambda e: e.reduce_max(c0s[0:1, 0:1], PS[0:1, 2048:4096],
                                    axis=mybir.AxisListType.X))
    SF("xb1", "wqf0", "wqf1", "wqb", "wqb8", "qout", "qsq", "wq_all", "rbc_a", "xsq_a")
    bar()

    xbf = SA("xbf", [128, 8 * T], BF16)
    xfull_f = [SA("xfull_f0", [128, T], F32)]
    xsq_b = SA("xsq_b", [128, T2], BF16)
    rbc_b = SA("rbc_b", [128, T], BF16)

    cd = []
    for c in range(8):
        if c >= 1:
            wait("sync", "vs", cd[c - 1])
        op("qs", lambda e, c=c: e.dma_start(out=xfull_f[0][:], in_=x_full.ap()[c]))
        wait("vector", "qs")
        cd.append(op("vs", lambda e, c=c: e.tensor_copy(
            xbf[:, c * T:(c + 1) * T], xfull_f[0][:])))
    ln_stats(lambda c: xbf[:, c * T:(c + 1) * T], T, 1, extra_sq=xsq_b)
    bcast_row(rows_sb[5:6, :], T, rbc_b)
    for c in range(8):
        op("vs", lambda e, c=c: e.tensor_mul(
            xbf[:, c * T:(c + 1) * T], xbf[:, c * T:(c + 1) * T], rbc_b[:, 0:T]))
    v_xhatf = P.ctrs["vs"].n
    if "d_xhat1" in dbg_t:
        wait("gpsimd", "vs", v_xhatf)
        for c in range(8):
            op("qg", lambda e, c=c: e.dma_start(
                out=dbg_t["d_xhat1"].ap()[c], in_=xbf[:, c * T:(c + 1) * T]))
    SF("xfull_f0", "xsq_b", "rbc_b")

    wkvd_all = SA("wkvd_all", [128, 8 * KVR], BF16)
    wkvdf = SA("wkvdf", [128, KVR], F32)
    for c in range(8):
        wait("sync", "vs")
        op("qs", lambda e, c=c: e.dma_start(out=wkvdf[:], in_=wkvd.ap()[c]))
        wait("vector", "qs")
        op("vs", lambda e, c=c: e.tensor_copy(
            wkvd_all[:, c * KVR:(c + 1) * KVR], wkvdf[:]))
        wait("tensor", "vs")
        op("ts", lambda e, c=c: e.matmul(
            PS[0:1, 0:KVR], ones_sb[:, 0:1], wkvd_all[:, c * KVR:(c + 1) * KVR],
            start=(c == 0), stop=(c == 7), skip_group_check=True))
    wait("vector", "ts")
    op("vs", lambda e: e.tensor_copy(csum_sb[3:4, 0:KVR], PS[0:1, 0:KVR]))
    wait("tensor", "vs")
    lat_sb = SA("lat_sb", [128, T], BF16)
    wait("tensor", "vs", v_xhatf)
    ev = []
    for t0 in range(0, T, 512):
        bk = 2 + (t0 // 512) % 2
        if len(ev) >= 2:
            wait("tensor", "ss", ev[-2])
        for cc in range(8):
            op("ts", lambda e, t0=t0, cc=cc, bk=bk: e.matmul(
                PS[:, bk * 512:(bk + 1) * 512],
                wkvd_all[:, cc * KVR:(cc + 1) * KVR],
                xbf[:, cc * T + t0:cc * T + t0 + 512],
                start=(cc == 0), stop=False, skip_group_check=True))
        op("ts", lambda e, t0=t0, bk=bk: e.matmul(
            PS[:, bk * 512:(bk + 1) * 512], csum_sb[3:4, 0:KVR],
            rows_sb[1:2, t0:t0 + 512], start=False, stop=True,
            skip_group_check=True))
        wait("scalar", "ts")
        ev.append(op("ss", lambda e, t0=t0, bk=bk: e.activation(
            lat_sb[:, t0:t0 + 512], PS[:, bk * 512:(bk + 1) * 512], Act.Copy)))
    wkuk_sb = SA("wkuk_sb", [128, C], BF16)
    wkuv_sb = SA("wkuv_sb", [128, C], BF16)
    kchunk = SA("kchunk", [128, T], BF16)
    wkf = kchunk[:].bitcast(F32)[:, 0:C]
    op("qs", lambda e: e.dma_start(out=wkf, in_=wkvu_k.ap()))
    wait("vector", "qs")
    op("vs", lambda e: e.tensor_copy(wkuk_sb[:], wkf))
    wait("sync", "vs")
    op("qs", lambda e: e.dma_start(out=wkf, in_=wkvu_v.ap()))
    wait("vector", "qs")
    v_wkuv = op("vs", lambda e: e.tensor_copy(wkuv_sb[:], wkf))
    ksq = SA("ksq", [128, T2], BF16)
    wait("tensor", "vs", v_wkuv)
    wait("tensor", "ss")
    for fc in range(8):
        wait("scalar", "qg")
        ev = []
        for t0 in range(0, T, 512):
            bk = 2 + (t0 // 512) % 2
            if len(ev) >= 2:
                wait("tensor", "ss", ev[-2])
            op("ts", lambda e, fc=fc, t0=t0, bk=bk: e.matmul(
                PS[:, bk * 512:(bk + 1) * 512],
                wkuk_sb[:, fc * 128:(fc + 1) * 128],
                lat_sb[:, t0:t0 + 512], start=True, stop=True))
            wait("scalar", "ts")
            ev.append(op("ss", lambda e, t0=t0, bk=bk: e.activation(
                kchunk[:, t0:t0 + 512], PS[:, bk * 512:(bk + 1) * 512], Act.Copy)))
        wait("gpsimd", "ss")
        op("qg", lambda e, fc=fc: e.dma_start(out=kT_scr.ap()[fc], in_=kchunk[:]))
        if "d_k" in dbg_t:
            op("qg", lambda e, fc=fc: e.dma_start(out=dbg_t["d_k"].ap()[fc], in_=kchunk[:]))
        for half in range(2):
            wait("scalar", "ts")
            op("ss", lambda e, half=half: e.activation(
                ksq[:], kchunk[:, half * 2048:(half + 1) * 2048], Act.Square))
            wait("tensor", "ss")
            for t0 in range(0, 2048, 512):
                op("ts", lambda e, fc=fc, t0=t0, half=half: e.matmul(
                    PS[0:1, 2048 + t0:2048 + t0 + 512],
                    ones_sb[:, 0:1], ksq[:, t0:t0 + 512],
                    start=(fc == 0 and half == 0), stop=(fc == 7 and half == 1),
                    skip_group_check=True))
    wait("vector", "ts")
    op("vs", lambda e: e.reduce_max(c0s[0:1, 1:2], PS[0:1, 2048:4096],
                                    axis=mybir.AxisListType.X))
    vchunk = SA("vchunk", [128, C], BF16)
    wait("tensor", "vs")
    ev = []
    for tc in range(32):
        bk = 2 + (tc % 2) * 2
        wait("scalar", "qg")
        if len(ev) >= 2:
            wait("tensor", "ss", ev[-2])
        for d0 in range(0, C, 512):
            op("ts", lambda e, tc=tc, d0=d0, bk=bk: e.matmul(
                PS[:, (bk + d0 // 512) * 512:(bk + d0 // 512 + 1) * 512],
                lat_sb[:, tc * 128:(tc + 1) * 128],
                wkuv_sb[:, d0:d0 + 512], start=True, stop=True))
        wait("scalar", "ts")
        ev.append(op("ss", lambda e, bk=bk: e.activation(
            vchunk[:], PS[:, bk * 512:(bk + 2) * 512], Act.Copy)))
        wait("gpsimd", "ss")
        op("qg", lambda e, tc=tc: e.dma_start(out=v_scr.ap()[tc], in_=vchunk[:]))
    op("vs", lambda e: e.tensor_mul(c0s[0:1, 2:3], c0s[0:1, 0:1], c0s[0:1, 1:2]))
    wait("scalar", "vs")
    op("ss", lambda e: e.activation(c0s[0:1, 3:4], c0s[0:1, 2:3], Act.Sqrt))
    wait("vector", "ss")
    op("vs", lambda e: e.tensor_scalar_mul(c0s[0:1, 4:5], c0s[0:1, 3:4], -QSCALE))
    op("vs", lambda e: e.tensor_copy(c0b[0:1, 0:1], c0s[0:1, 4:5]))
    wait("tensor", "vs")
    op("ts", lambda e: e.matmul(PS[:, 0:1], ones_sb[0:1, :], c0b[0:1, 0:1],
                                start=True, stop=True))
    wait("vector", "ts")
    op("vs", lambda e: e.tensor_copy(cvec_sb[:, 0:1], PS[:, 0:1]))
    SF("wkvd_all", "wkvdf", "lat_sb", "wkuk_sb", "wkuv_sb",
       "kchunk", "ksq", "vchunk")
    bar()
    SF("xbf")

    dens_sb = SA("dens_sb", [16, T2], BF16)
    ysb = SA("ysb", [128, 8 * T2], BF16)
    rcp2 = SA("rcp2", [2, T2], BF16)
    khp = [SA("khp0", [128, T], BF16), SA("khp1", [128, T], BF16)]
    qhp = [SA("qhp0", [128, T2], BF16), SA("qhp1", [128, T2], BF16)]
    vstrip = [SA("vstrip0", [128, 32 * 130], BF16), SA("vstrip1", [128, 32 * 130], BF16)]
    ptile = SA("ptile", [128, 1024], BF16)
    for b in range(2):
        op("vs", lambda e, b=b: e.memset(
            vstrip[b][:].rearrange("p (c w) -> p c w", w=130)[:, :, 64:65], 1.0))
        op("vs", lambda e, b=b: e.memset(
            vstrip[b][:].rearrange("p (c w) -> p c w", w=130)[:, :, 129:130], 1.0))
    v_ones = P.ctrs["vs"].n
    wait("sync", "qg")

    attn_iters = []
    for hp in range(8):
        for slot in range(4):
            for c in range(8 * slot + 8):
                attn_iters.append((hp, slot, c))

    qload_snap = {}
    vload_snap = {}
    pv_done_snap = {}
    exp_snap = {}
    maskv_snap = {}
    st_ready = {}
    for hp in range(8):
        bq = hp % 2
        if hp >= 2:
            wait("sync", "ts", pv_done_snap[(hp - 2, 3)])
        op("qs", lambda e, hp=hp, bq=bq: e.dma_start(out=qhp[bq][:],
                                                     in_=q_scr.ap()[hp]))
        op("qs", lambda e, hp=hp, bq=bq: e.dma_start(out=khp[bq][:],
                                                     in_=kT_scr.ap()[hp]))
        qload_snap[hp] = P.ctrs["qs"].n
        for slot in range(4):
            ext = 8 * slot + 8
            bv = (hp * 4 + slot) % 2
            if (hp * 4 + slot) >= 2:
                prev = (hp, slot - 2) if slot >= 2 else (hp - 1, slot + 2)
                wait("sync", "ts", pv_done_snap[prev])
            for h in range(2):
                op("qs", lambda e, hp=hp, ext=ext, bv=bv, h=h: e.dma_start(
                    out=vstrip[bv][:].rearrange("p (c w) -> p c w", w=130)
                    [:, 0:ext, h * 65:h * 65 + 64],
                    in_=v_scr.ap()[0:ext, :, hp * 128 + h * 64:hp * 128 + h * 64 + 64]
                    .transpose([1, 0, 2])))
            vload_snap[(hp, slot)] = P.ctrs["qs"].n
            obank = 4 + 2 * (slot % 2)
            for c in range(ext):
                it = (hp, slot, c)
                jmin = max(0, (c - 8 * slot) // 2)
                W = 512 - jmin * 128
                ping = c % 2
                wait("tensor", "qs", qload_snap[hp])
                wait("tensor", "qs", vload_snap[(hp, slot)])
                wait("tensor", "vs", v_ones)
                if (hp, slot, c - 2) in exp_snap:
                    wait("tensor", "ss", exp_snap[(hp, slot, c - 2)])
                for h in range(2):
                    op("ts", lambda e, hp=hp, slot=slot, c=c, h=h, jmin=jmin,
                       ping=ping, bq=bq, W=W: e.matmul(
                           PS[:, (ping * 2 + h) * 512 + jmin * 128:
                              (ping * 2 + h) * 512 + 512],
                           khp[bq][h * 64:(h + 1) * 64,
                                   c * 128:c * 128 + 128],
                           qhp[bq][h * 64:(h + 1) * 64,
                                   slot * 512 + jmin * 128:slot * 512 + 512],
                           start=True, stop=True, tile_position=(h * 64, 0),
                           skip_group_check=True))
                st_ready[it] = P.ctrs["ts"].n
                wait("scalar", "ts", st_ready[it])
                if (hp, slot, c - 2) in maskv_snap:
                    wait("scalar", "vs", maskv_snap[(hp, slot, c - 2)])
                exp_snap[it] = op("ss", lambda e, ping=ping, jmin=jmin, W=W: e.activation(
                    ptile[:, 0:1024].rearrange("p (h w) -> p h w", h=2)
                    [:, :, jmin * 128:512],
                    PS[:, ping * 1024:ping * 1024 + 1024]
                    .rearrange("p (h w) -> p h w", h=2)[:, :, jmin * 128:512],
                    Act.Exp, bias=cvec_sb[:, 0:1], scale=QSCALE), sw=False)
                mj = None
                if c % 2 == 0 and 0 <= (c // 2) - 4 * slot <= 3:
                    mj = ((c // 2) - 4 * slot, maska_sb)
                if c % 2 == 1 and 0 <= ((c - 1) // 2) - 4 * slot <= 3:
                    mj = (((c - 1) // 2) - 4 * slot, maskb_sb)
                if mj is not None:
                    j, msk = mj
                    wait("vector", "ss", exp_snap[it])
                    for h in range(2):
                        op("vs", lambda e, h=h, j=j, msk=msk: e.tensor_mul(
                            ptile[:, h * 512 + j * 128:h * 512 + (j + 1) * 128],
                            ptile[:, h * 512 + j * 128:h * 512 + (j + 1) * 128],
                            msk[:]), sw=False)
                    maskv_snap[it] = P.ctrs["vs"].n
                    wait("tensor", "vs", maskv_snap[it])
                else:
                    wait("tensor", "ss", exp_snap[it])
                for h in range(2):
                    op("ts", lambda e, hp=hp, slot=slot, c=c, h=h, jmin=jmin,
                       bv=bv, obank=obank: e.matmul(
                           PS[0:65, (obank + h) * 512 + jmin * 128:
                              (obank + h) * 512 + 512],
                           vstrip[bv][:, c * 130 + h * 65:c * 130 + h * 65 + 65],
                           ptile[:, h * 512 + jmin * 128:h * 512 + 512],
                           start=(c == 0), stop=(c == ext - 1),
                           skip_group_check=True))
                pv_done_snap[(hp, slot)] = P.ctrs["ts"].n
            wait("vector", "ts", pv_done_snap[(hp, slot)])
            for h in range(2):
                op("vs", lambda e, hp=hp, slot=slot, h=h, obank=obank: e.tensor_copy(
                    ysb[h * 64:(h + 1) * 64, hp * T2 + slot * 512:hp * T2 + slot * 512 + 512],
                    PS[0:64, (obank + h) * 512:(obank + h) * 512 + 512]))
            op("vs", lambda e, obank=obank: e.tensor_copy(
                ptile[64:65, 0:1024],
                PS[64:65, obank * 512:obank * 512 + 1024]))
            wait("gpsimd", "vs")
            op("qg", lambda e, hp=hp, slot=slot: e.dma_start(
                out=dens_sb[2 * hp:2 * hp + 2, slot * 512:slot * 512 + 512],
                in_=ptile[64:65, 0:1024]))
            dens_g = P.ctrs["qg"].n
            P.wait("scalar", "qg", dens_g)
            wait("tensor", "vs")
    wait("vector", "qg")
    op("vs", lp(lambda e: e.reciprocal(dens_sb[:, :], dens_sb[:, :])))
    v_rcp = P.ctrs["vs"].n
    wait("tensor", "vs", v_rcp)
    wait("tensor", "ss")
    for hp in range(8):
        wait("sync", "vs")
        op("qs", lambda e, hp=hp: e.dma_start(
            out=rcp2[:], in_=dens_sb[2 * hp:2 * hp + 2, :]))
        q_r = P.ctrs["qs"].n
        wait("tensor", "qs", q_r)
        for t0 in range(0, T2, 512):
            bk = (t0 // 512) % 4
            wait("tensor", "vs")
            op("ts", lambda e, t0=t0, bk=bk: e.matmul(
                PS[:, bk * 512:(bk + 1) * 512], sel2_sb[:, :],
                rcp2[0:2, t0:t0 + 512], start=True, stop=True))
            wait("vector", "ts")
            op("vs", lambda e, hp=hp, t0=t0, bk=bk: e.tensor_mul(
                ysb[:, hp * T2 + t0:hp * T2 + t0 + 512],
                ysb[:, hp * T2 + t0:hp * T2 + t0 + 512],
                PS[:, bk * 512:(bk + 1) * 512]))
    SF("qhp0", "qhp1", "vstrip0", "vstrip1", "ptile", "khp0", "khp1")
    woutf = SA("woutf", [128, C], F32)
    wout_all = SA("wout_all", [128, 8 * C], BF16)
    for hc in range(8):
        wait("sync", "vs")
        op("qs", lambda e, hc=hc: e.dma_start(out=woutf[:], in_=wout.ap()[hc]))
        wait("vector", "qs")
        op("vs", lambda e, hc=hc: e.tensor_copy(
            wout_all[:, hc * C:(hc + 1) * C], woutf[:]))
    v_wout = P.ctrs["vs"].n
    wait("tensor", "vs", v_wout)
    evs = []
    for cc in range(8):
        for slot in range(4):
            bk = 4 + (cc * 4 + slot) % 2
            if len(evs) >= 2:
                wait("tensor", "vs", evs[-2])
            for hc in range(8):
                op("ts", lambda e, cc=cc, slot=slot, hc=hc, bk=bk: e.matmul(
                    PS[:, bk * 512:(bk + 1) * 512],
                    wout_all[:, hc * C + cc * 128:hc * C + cc * 128 + 128],
                    ysb[:, hc * T2 + slot * 512:hc * T2 + slot * 512 + 512],
                    start=(hc == 0), stop=(hc == 7), skip_group_check=True))
            wait("vector", "ts")
            evs.append(op("vs", lambda e, cc=cc, slot=slot, bk=bk: e.tensor_add(
                xfm(cc)[:, slot * 512:slot * 512 + 512],
                xfm(cc)[:, slot * 512:slot * 512 + 512],
                PS[:, bk * 512:(bk + 1) * 512])))
    SF("woutf", "wout_all", "rcp2", "ysb", "dens_sb")
    if "d_x1" in dbg_t:
        wait("gpsimd", "vs")
        for c in range(8):
            op("qg", lambda e, c=c: e.dma_start(out=dbg_t["d_x1"].ap()[c], in_=xfm(c)))
    bar()

    def emit_ffn(ln_row, dbg_name, sfx=""):
        xb = SA("f_xb" + sfx, [128, 8 * T2], BF16)
        xsq = SA("f_xsq" + sfx, [128, T2], BF16)
        rbc = SA("f_rbc" + sfx, [128, T2], BF16)
        wait("vector", "qs")
        for c in range(8):
            op("vs", lambda e, c=c: e.tensor_copy(xb[:, c * T2:(c + 1) * T2], xfm(c)))
        ln_stats(lambda c: xb[:, c * T2:(c + 1) * T2], T2, ln_row, extra_sq=xsq)
        bcast_row(rows_sb[5:6, :], T2, rbc)
        for c in range(8):
            op("vs", lambda e, c=c: e.tensor_mul(
                xb[:, c * T2:(c + 1) * T2], xb[:, c * T2:(c + 1) * T2], rbc[:, 0:T2]))
        v_xh = P.ctrs["vs"].n
        wait("tensor", "vs", v_xh)
        wait("sync", "vs", v_xh)
        wait("gpsimd", "vs", v_xh)
        SF("f_xsq" + sfx, "f_rbc" + sfx)

        wgt = [SA("f_wg0" + sfx, [128, 1024], BF16), SA("f_wg1" + sfx, [128, 1024], BF16)]
        wut = [SA("f_wu0" + sfx, [128, 1024], BF16)]
        wdt = [SA("f_wd0" + sfx, [128, 3072], BF16)]
        hact = SA("f_hact" + sfx, [128, 24 * 1024], BF16)
        tdone = {}
        hdone = {}
        dcons = {}
        for tt in range(2):
            t0 = tt * 1024
            for h in range(24):
                bw = h % 2
                key = (tt, h)
                prev = (tt, h - 2) if h >= 2 else ((tt - 1, 22 + h) if tt else None)
                if prev is not None and prev in hdone:
                    wait("sync", "vs", hdone[prev])
                op("qs", lambda e, h=h, bw=bw: e.dma_start(
                    out=wgt[bw][:].rearrange("p (c j) -> p c j", j=128),
                    in_=wg_scr.ap()[h].transpose([1, 0, 2])))
                wait("sync", "ts")
                op("qs", lambda e, h=h: e.dma_start(
                    out=wut[0][:].rearrange("p (c j) -> p c j", j=128),
                    in_=wu_scr.ap()[h].transpose([1, 0, 2])))
                wl = P.ctrs["qs"].n
                wait("tensor", "qs", wl)
                if (tt, h - 1) in hdone:
                    wait("tensor", "vs", hdone[(tt, h - 1)])
                elif tt and h == 0:
                    wait("tensor", "vs", hdone[(tt - 1, 23)])
                for half in range(2):
                    hh = t0 + half * 512
                    for cc in range(8):
                        op("ts", lambda e, cc=cc, hh=hh, bw=bw, half=half: e.matmul(
                            PS[:, half * 512:(half + 1) * 512],
                            wgt[bw][:, cc * 128:(cc + 1) * 128],
                            xb[:, cc * T2 + hh:cc * T2 + hh + 512],
                            start=(cc == 0), stop=False, skip_group_check=True))
                    op("ts", lambda e, h=h, hh=hh, half=half, ln_row=ln_row: e.matmul(
                        PS[:, half * 512:(half + 1) * 512],
                        csum_sb[0:1, h * 128:(h + 1) * 128],
                        rows_sb[ln_row:ln_row + 1, hh:hh + 512],
                        start=False, stop=True, skip_group_check=True))
                    for cc in range(8):
                        op("ts", lambda e, cc=cc, hh=hh, half=half: e.matmul(
                            PS[:, 1024 + half * 512:1024 + (half + 1) * 512],
                            wut[0][:, cc * 128:(cc + 1) * 128],
                            xb[:, cc * T2 + hh:cc * T2 + hh + 512],
                            start=(cc == 0), stop=False, skip_group_check=True))
                    t_up = op("ts", lambda e, h=h, hh=hh, half=half, ln_row=ln_row: e.matmul(
                        PS[:, 1024 + half * 512:1024 + (half + 1) * 512],
                        csum_sb[1:2, h * 128:(h + 1) * 128],
                        rows_sb[ln_row:ln_row + 1, hh:hh + 512],
                        start=False, stop=True, skip_group_check=True))
                tdone[key] = t_up
                wait("scalar", "ts", t_up)
                if (tt, h - 1) in hdone:
                    pass
                s_g = op("ss", lambda e, h=h: e.activation(
                    hact[:, h * 1024:(h + 1) * 1024], PS[:, 0:1024],
                    Act.Silu, scale=FSC))
                wait("vector", "ss", s_g)
                hdone[key] = op("vs", lambda e, h=h: e.scalar_tensor_tensor(
                    hact[:, h * 1024:(h + 1) * 1024], PS[:, 1024:2048],
                    FSC, hact[:, h * 1024:(h + 1) * 1024], Alu.mult, Alu.mult))
            wait("tensor", "vs", hdone[(tt, 23)])
            for cc in range(8):
                key = (tt, cc)
                prev = (tt, cc - 1) if cc >= 1 else ((tt - 1, 7) if tt else None)
                if prev is not None and prev in dcons:
                    wait("sync", "ts")
                op("qs", lambda e, cc=cc: e.dma_start(
                    out=wdt[0][:].rearrange("p (h j) -> p h j", j=128),
                    in_=wd_scr.ap()[cc].transpose([1, 0, 2])))
                dl = P.ctrs["qs"].n
                wait("tensor", "qs", dl)
                for half in range(2):
                    for h in range(24):
                        op("ts", lambda e, h=h, half=half: e.matmul(
                            PS[:, 2048 + half * 512:2048 + (half + 1) * 512],
                            wdt[0][:, h * 128:(h + 1) * 128],
                            hact[:, h * 1024 + half * 512:h * 1024 + half * 512 + 512],
                            start=(h == 0), stop=(h == 23), skip_group_check=True))
                wait("vector", "ts")
                dcons[key] = op("vs", lambda e, cc=cc, t0=t0: e.scalar_tensor_tensor(
                    xfm(cc)[:, t0:t0 + 1024], PS[:, 2048:3072],
                    FSC, xfm(cc)[:, t0:t0 + 1024], Alu.mult, Alu.add))
                wait("tensor", "vs", dcons[key])
        SF("f_xb" + sfx, "f_wg0" + sfx, "f_wg1" + sfx, "f_wu0" + sfx,
           "f_wd0" + sfx, "f_hact" + sfx)
        if dbg_name in dbg_t:
            wait("gpsimd", "vs")
            for c in range(8):
                op("qg", lambda e, c=c, n=dbg_name: e.dma_start(
                    out=dbg_t[n].ap()[c], in_=xfm(c)))
        bar()

    emit_ffn(2, "d_x2", "a")

    xb3 = SA("m_xb3", [128, 8 * T2], BF16)
    xsq3 = SA("m_xsq3", [128, T2], BF16)
    rbc3 = SA("m_rbc3", [128, T2], BF16)
    wait("vector", "qs")
    for c in range(8):
        op("vs", lambda e, c=c: e.tensor_copy(xb3[:, c * T2:(c + 1) * T2], xfm(c)))
    ln_stats(lambda c: xb3[:, c * T2:(c + 1) * T2], T2, 6, extra_sq=xsq3)
    bcast_row(rows_sb[5:6, :], T2, rbc3)
    for c in range(8):
        op("vs", lambda e, c=c: e.tensor_mul(
            xb3[:, c * T2:(c + 1) * T2], xb3[:, c * T2:(c + 1) * T2], rbc3[:, 0:T2]))
    v_x3h = P.ctrs["vs"].n
    wait("gpsimd", "vs", v_x3h)
    for c in range(8):
        op("qg", lambda e, c=c: e.dma_start(
            out=ln3_in.ap()[c * 128:(c + 1) * 128, :], in_=xb3[:, c * T2:(c + 1) * T2]))
    op("qg", lambda e: e.dma_start(out=ln3_in.ap()[1024:1025, :], in_=rows_sb[6:7, 0:T2]))
    for c in range(8):
        op("qg", lambda e, c=c: e.dma_start(out=park_scr.ap()[c], in_=xfm(c)))
    g_ln3 = P.ctrs["qg"].n
    wait("gpsimd", "qg", g_ln3)
    bar()
    op("gc", lambda e: e.collective_compute(
        "AllGather", Alu.bypass, replica_groups=GROUPS,
        ins=[ln3_in.ap().opt()], outs=[ln3_out.ap().opt()]))
    SF("m_xb3", "m_xsq3", "m_rbc3")
    bar()

    rhsln = xfm_h[0][:].bitcast(BF16)
    for c in range(8):
        for b in range(2):
            op("qs", lambda e, c=c, b=b: e.dma_start(
                out=rhsln[:, c * T:(c + 1) * T]
                .rearrange("p (g j) -> p g j", j=128)[:, b::2, :],
                in_=ln3_out.ap()[b, c * 128:(c + 1) * 128, :]
                .rearrange("p (g j) -> p g j", j=128).transpose([0, 1, 2])))
    for b in range(2):
        op("qs", lambda e, b=b: e.dma_start(
            out=rows_sb[3:4, 0:T].rearrange("o (g j) -> o g j", j=128)[:, b::2, :],
            in_=ln3_out.ap()[b, 1024:1025, :].rearrange("o (g j) -> o g j", j=128)))
    q_rhs = P.ctrs["qs"].n

    inpf = SA("m_inpf", [128, 1024], F32)
    inpb = [SA("m_inpb0", [128, 1024], BF16), SA("m_inpb1", [128, 1024], BF16)]
    xzc = SA("m_xzc", [128, T], F32)
    cacc = SA("m_cacc", [128, T], F32)
    xcsb = SA("m_xcsb", [128, T], BF16)
    wait("tensor", "qs", q_rhs)
    evx = []
    for ec in range(16):
        bw = ec % 2
        if ec > 0:
            wait("scalar", "qg")
        wait("sync", "vs")
        op("qs", lambda e, ec=ec: e.dma_start(
            out=inpf[:].rearrange("p (c j) -> p c j", j=128),
            in_=inpT.ap()[:, :, ec * 128:(ec + 1) * 128].transpose([1, 0, 2])))
        wait("vector", "qs")
        v_cv = op("vs", lambda e, bw=bw: e.tensor_copy(inpb[bw][:], inpf[:]))
        wait("tensor", "vs", v_cv)
        for cc in range(8):
            op("ts", lambda e, cc=cc, bw=bw: e.matmul(
                PS[0:1, 3584:3584 + 128],
                ones_sb[:, 0:1], inpb[bw][:, cc * 128:(cc + 1) * 128],
                start=(cc == 0), stop=(cc == 7), skip_group_check=True))
        wait("vector", "ts")
        op("vs", lambda e, ec=ec: e.tensor_copy(
            csum_sb[4:5, ec * 128:(ec + 1) * 128], PS[0:1, 3584:3584 + 128]))
        wait("tensor", "vs")
        for t0 in range(0, T, 512):
            bk = (t0 // 512) % 2
            if len(evx) >= 2:
                wait("tensor", "ss", evx[-2])
            for cc in range(8):
                op("ts", lambda e, cc=cc, t0=t0, bw=bw, bk=bk: e.matmul(
                    PS[:, bk * 512:(bk + 1) * 512],
                    inpb[bw][:, cc * 128:(cc + 1) * 128],
                    rhsln[:, cc * T + t0:cc * T + t0 + 512],
                    start=(cc == 0), stop=False, skip_group_check=True))
            op("ts", lambda e, ec=ec, t0=t0, bk=bk: e.matmul(
                PS[:, bk * 512:(bk + 1) * 512],
                csum_sb[4:5, ec * 128:ec * 128 + 128],
                rows_sb[3:4, t0:t0 + 512], start=False, stop=True,
                skip_group_check=True))
            wait("scalar", "ts")
            if ec < 8:
                evx.append(op("ss", lambda e, t0=t0, bk=bk: e.activation(
                    xzc[:, t0:t0 + 512], PS[:, bk * 512:(bk + 1) * 512], Act.Copy)))
            else:
                evx.append(op("ss", lambda e, t0=t0, bk=bk: e.activation(
                    xcsb[:, t0:t0 + 512], PS[:, bk * 512:(bk + 1) * 512], Act.Silu)))
        if ec >= 8:
            wait("gpsimd", "ss")
            op("qg", lambda e, ec=ec: e.dma_start(out=zs_scr.ap()[ec - 8], in_=xcsb[:]))
            continue
        wait("vector", "ss")
        op("vs", lambda e, ec=ec: e.tensor_scalar_mul(
            cacc[:], xzc[:], convw_sb[:, ec * 4 + 3:ec * 4 + 4]))
        for k in range(3):
            sh = 3 - k
            op("vs", lambda e, ec=ec, k=k, sh=sh: e.scalar_tensor_tensor(
                cacc[:, sh:T], xzc[:, 0:T - sh], convw_sb[:, ec * 4 + k:ec * 4 + k + 1],
                cacc[:, sh:T], Alu.mult, Alu.add))
        v_conv = P.ctrs["vs"].n
        wait("scalar", "vs", v_conv)
        op("ss", lambda e, ec=ec: e.activation(
            xcsb[:], cacc[:], Act.Silu, bias=convb_sb[:, ec:ec + 1]))
        wait("gpsimd", "ss")
        op("qg", lambda e, ec=ec: e.dma_start(out=xc_scr.ap()[ec], in_=xcsb[:]))
        if "d_xcs" in dbg_t:
            op("qg", lambda e, ec=ec: e.dma_start(out=dbg_t["d_xcs"].ap()[ec], in_=xcsb[:]))
    SF("m_inpf", "m_inpb0", "m_inpb1", "m_xzc", "m_cacc", "m_xcsb")
    bar()

    dbcb = SA("m_dbcb", [128, T], BF16)
    xpf = SA("m_xpf", [128, 768], F32)
    xpb = SA("m_xpb", [128, 768], BF16)
    op("qs", lambda e: e.dma_start(
        out=xpf[:].rearrange("p (c j) -> p c j", j=96), in_=xpT.ap().transpose([1, 0, 2])))
    wait("vector", "qs")
    op("vs", lambda e: e.tensor_copy(xpb[:], xpf[:]))
    dbcp = SA("m_dbcp", [128, T], F32)
    xcq = [SA("m_xcq0", [128, 512], BF16), SA("m_xcq1", [128, 512], BF16)]
    wait("tensor", "vs")
    for ts8 in range(8):
        t0 = ts8 * 512
        bk = ts8 % 2
        wait("tensor", "vs")
        for ec in range(8):
            bq = (ts8 * 8 + ec) % 2
            if ts8 * 8 + ec >= 2:
                wait("sync", "ts")
            op("qs", lambda e, ec=ec, t0=t0, bq=bq: e.dma_start(
                out=xcq[bq][:], in_=xc_scr.ap()[ec, :, t0:t0 + 512]))
            wait("tensor", "qs")
            op("ts", lambda e, ec=ec, t0=t0, bq=bq, bk=bk: e.matmul(
                PS[0:96, bk * 512:bk * 512 + 512],
                xpb[:, ec * 96:(ec + 1) * 96], xcq[bq][:],
                start=(ec == 0), stop=(ec == 7), skip_group_check=True))
        wait("vector", "ts")
        op("vs", lambda e, t0=t0, bk=bk: e.tensor_copy(
            dbcp[0:96, t0:t0 + 512], PS[0:96, bk * 512:bk * 512 + 512]))
    wait("gpsimd", "vs")
    op("qg", lambda e: e.dma_start(out=dbc_in.ap(), in_=dbcp[0:96, :]))
    wait("gpsimd", "qg")
    op("gc", lambda e: e.collective_compute(
        "AllReduce", Alu.add, replica_groups=GROUPS,
        ins=[dbc_in.ap().opt()], outs=[dbc_out.ap().opt()]))
    wait("sync", "gc")
    op("qs", lambda e: e.dma_start(out=dbcp[0:96, :], in_=dbc_out.ap()))
    q_dbc = P.ctrs["qs"].n
    if "d_dbc" in dbg_t:
        wait("gpsimd", "qs", q_dbc)
        op("qg", lambda e: e.dma_start(out=dbg_t["d_dbc"].ap()[0:96, :], in_=dbcp[0:96, :]))
    wait("vector", "qs", q_dbc)
    op("vs", lambda e: e.tensor_copy(dbcb[0:96, :], dbcp[0:96, :]))
    v_dbcb = P.ctrs["vs"].n
    wait("gpsimd", "vs", v_dbcb)
    op("qg", lambda e: e.dma_start(out=dbc_bf.ap(), in_=dbcb[64:96, :]))
    dttf = SA("m_dttf", [64, EO], F32)
    dttb = SA("m_dttb", [64, EO], BF16)
    op("qs", lambda e: e.dma_start(out=dttf[:], in_=dtT.ap()))
    wait("vector", "qs")
    op("vs", lambda e: e.tensor_copy(dttb[:], dttf[:]))
    v_dtt = P.ctrs["vs"].n
    dlt = SA("m_dlt", [128, T], F32)
    wait("tensor", "vs", max(v_dbcb, v_dtt))
    for ec in range(8):
        for t0 in range(0, T, 512):
            op("ts", lambda e, ec=ec, t0=t0: e.matmul(
                PS[:, t0:t0 + 512], dttb[:, ec * 128:(ec + 1) * 128],
                dbcb[0:64, t0:t0 + 512], start=True, stop=True))
        wait("scalar", "ts")
        op("ss", lambda e, ec=ec: e.activation(
            dlt[:], PS[:, 0:T], Act.Exp, bias=dtb_sb[:, ec:ec + 1]))
        op("ss", lambda e: e.activation(dlt[:], dlt[:], Act.Ln, bias=1.0))
        wait("gpsimd", "ss")
        op("qg", lambda e, ec=ec: e.dma_start(out=dl_scr.ap()[ec], in_=dlt[:]))
        if "d_delta" in dbg_t:
            op("qg", lambda e, ec=ec: e.dma_start(out=dbg_t["d_delta"].ap()[ec], in_=dlt[:]))
        wait("tensor", "qg")
    SF("m_xpf", "m_xpb", "m_dbcp", "m_xcq0", "m_xcq1", "m_dttf", "m_dttb", "m_dlt", "m_dbcb")
    bar()

    bcst = SA("m_bcst", [128, 32 * 1024], BF16)
    dq = [SA("m_dq0", [128, 1024], F32)]
    xq = [SA("m_xq0", [128, 1024], BF16), SA("m_xq1", [128, 1024], BF16)]
    uq = SA("m_uq", [128, 1024], BF16)
    dA = [SA("m_dA0", [128, 1024], F32)]
    bxq = SA("m_bxq", [128, 1024], BF16)
    hsq = SA("m_hsq", [128, 1024], F32)
    ytq = [SA("m_ytq0", [128, 1024], BF16)]
    zq = ytq
    ymq = uq
    dxq = bxq
    ld_snap = {}
    ymq_g = {}
    for tq in range(4):
        t0 = tq * 1024
        wait("sync", "vs")
        for n in range(NST):
            for r in range(2):
                dfrow = n if r == 0 else 16 + n
                slot = 2 * n + r
                op("qs", lambda e, dfrow=dfrow, slot=slot, t0=t0: e.dma_start(
                    out=bcst[:, slot * 1024:(slot + 1) * 1024],
                    in_=dbc_bf.ap()[dfrow:dfrow + 1, t0:t0 + 1024]
                    .partition_broadcast(128).squeeze(1)))
        bc_q = P.ctrs["qs"].n
        wait("vector", "qs", bc_q)
        for ec in range(8):
            bi = ec % 2
            if (tq, ec) != (0, 0):
                wait("sync", "vs")
            op("qs", lambda e, ec=ec, t0=t0: e.dma_start(
                out=dq[0][:], in_=dl_scr.ap()[ec, :, t0:t0 + 1024]))
            op("qs", lambda e, ec=ec, t0=t0, bi=bi: e.dma_start(
                out=xq[bi][:], in_=xc_scr.ap()[ec, :, t0:t0 + 1024]))
            ld_snap[(tq, ec)] = P.ctrs["qs"].n
            wait("vector", "qs", ld_snap[(tq, ec)])
            op("vs", lambda e, bi=bi: e.tensor_mul(uq[:], dq[0][:], xq[bi][:]))
            wait("scalar", "vs")
            acc_prev = None
            for n in range(NST):
                ba = n % 2
                col = ec * 16 + n
                if n >= 1:
                    wait("scalar", "vs")
                op("ss", lambda e, col=col: e.activation(
                    dA[0][:], dq[0][:], Act.Exp, scale=amat_sb[:, col:col + 1]),
                    sw=False)
                wait("vector", "ss")
                op("vs", lambda e, n=n: e.tensor_mul(
                    bxq[:], uq[:], bcst[:, 2 * n * 1024:(2 * n + 1) * 1024]))
                op("vs", lambda e, col=col: e.tensor_tensor_scan(
                    hsq[:], dA[0][:], bxq[:], state_sb[:, col:col + 1],
                    Alu.mult, Alu.add))
                op("vs", lambda e, col=col: e.tensor_copy(
                    state_sb[:, col:col + 1], hsq[:, 1023:1024]))
                if acc_prev is not None:
                    wait("vector", "ts", acc_prev)
                op("vs", lambda e, n=n: e.tensor_mul(
                    ytq[0][:], hsq[:],
                    bcst[:, (2 * n + 1) * 1024:(2 * n + 2) * 1024]))
                wait("tensor", "vs")
                for hh in range(2):
                    op("ts", lambda e, hh=hh, n=n: e.matmul(
                        PS[:, (4 + hh) * 512:(5 + hh) * 512],
                        eye_sb[:], ytq[0][:, hh * 512:(hh + 1) * 512],
                        start=(n == 0), stop=False, skip_group_check=True))
                acc_prev = P.ctrs["ts"].n
            if acc_prev is not None:
                wait("vector", "ts", acc_prev)
            op("vs", lambda e, bi=bi, ec=ec: e.tensor_scalar_mul(
                dxq[:], xq[bi][:], dpar_sb[:, ec:ec + 1]))
            wait("tensor", "vs")
            for hh in range(2):
                op("ts", lambda e, hh=hh: e.matmul(
                    PS[:, (4 + hh) * 512:(5 + hh) * 512],
                    eye_sb[:], dxq[:, hh * 512:(hh + 1) * 512],
                    start=False, stop=(hh == 1), skip_group_check=True))
            t_acc = P.ctrs["ts"].n
            wait("sync", "ts", t_acc)
            op("qs", lambda e, ec=ec, t0=t0: e.dma_start(
                out=zq[0][:], in_=zs_scr.ap()[ec, :, t0:t0 + 1024]))
            q_z = P.ctrs["qs"].n
            wait("vector", "ts", t_acc)
            wait("vector", "qs", q_z)
            ymq_g[(tq, ec)] = op("vs", lambda e: e.tensor_mul(
                ymq[:], PS[:, 2048:3072], zq[0][:]))
            wait("tensor", "vs")
            wait("gpsimd", "vs", ymq_g[(tq, ec)])
            op("qg", lambda e, ec=ec, t0=t0: e.dma_start(
                out=ym_scr.ap()[ec, :, t0:t0 + 1024], in_=ymq[:]))
            if "d_ymul" in dbg_t:
                op("qg", lambda e, ec=ec, t0=t0: e.dma_start(
                    out=dbg_t["d_ymul"].ap()[ec, :, t0:t0 + 1024], in_=ymq[:]))
            wait("vector", "qg")
    SF("m_bcst", "m_dq0", "m_xq0", "m_xq1",
       "m_uq", "m_dA0", "m_bxq", "m_hsq", "m_ytq0")
    bar()

    outf = SA("c4_outf", [128, C], F32)
    outt = SA("c4_outt", [128, 8 * C], BF16)
    for ec in range(8):
        wait("sync", "vs")
        op("qs", lambda e, ec=ec: e.dma_start(out=outf[:], in_=outT.ap()[ec]))
        wait("vector", "qs")
        op("vs", lambda e, ec=ec: e.tensor_copy(outt[:, ec * C:(ec + 1) * C], outf[:]))
    v_outt = P.ctrs["vs"].n
    wait("tensor", "vs", v_outt)
    ymgs = SA("c4_ymgs", [128, 8 * 512], BF16)
    prt = SA("c4_prt", [128, 512], F32)
    for gs in range(8):
        wait("sync", "ts")
        op("qs", lambda e, gs=gs: e.dma_start(
            out=ymgs[:].rearrange("p (c j) -> p c j", j=512),
            in_=ym_scr.ap()[:, :, gs * 512:(gs + 1) * 512].transpose([1, 0, 2])))
        q_ym = P.ctrs["qs"].n
        wait("tensor", "qs", q_ym)
        for cc in range(8):
            bk = (gs * 8 + cc) % 2
            wait("tensor", "vs")
            for ec in range(8):
                op("ts", lambda e, cc=cc, ec=ec, bk=bk: e.matmul(
                    PS[:, bk * 512:(bk + 1) * 512],
                    outt[:, ec * C + cc * 128:ec * C + cc * 128 + 128],
                    ymgs[:, ec * 512:(ec + 1) * 512],
                    start=(ec == 0), stop=(ec == 7), skip_group_check=True))
            wait("vector", "ts")
            op("vs", lambda e, bk=bk: e.tensor_copy(prt[:], PS[:, bk * 512:(bk + 1) * 512]))
            wait("gpsimd", "vs")
            for bb_ in range(2):
                op("qg", lambda e, cc=cc, gs=gs, bb_=bb_: e.dma_start(
                    out=rs_in.ap()[bb_, cc, :, gs * 256:gs * 256 + 256]
                    .rearrange("p (u j) -> p u j", j=128),
                    in_=prt[:].rearrange("p (u v j) -> p u v j", u=2, v=2)[:, :, bb_, :]))
            wait("vector", "qg")
    g_rs = P.ctrs["qg"].n
    wait("gpsimd", "qg", g_rs)
    op("gc", lambda e: e.collective_compute(
        "ReduceScatter", Alu.add, replica_groups=GROUPS,
        ins=[rs_in.ap().opt()], outs=[rs_out.ap().opt()]))
    SF("c4_outf", "c4_outt", "c4_ymgs", "c4_prt")
    bar()
    for c in range(8):
        op("qs", lambda e, c=c: e.dma_start(out=xfm(c), in_=park_scr.ap()[c]))
    rsl = [SA("c4_rsl0", [128, T2], F32), SA("c4_rsl1", [128, T2], F32)]
    wait("sync", "gc")
    for c in range(8):
        bi = c % 2
        if c >= 2:
            wait("sync", "vs")
        op("qs", lambda e, c=c, bi=bi: e.dma_start(out=rsl[bi][:], in_=rs_out.ap()[c]))
        wait("vector", "qs")
        op("vs", lambda e, c=c, bi=bi: e.tensor_add(xfm(c), xfm(c), rsl[bi][:]))
    SF("c4_rsl0", "c4_rsl1")
    if "d_x3" in dbg_t:
        wait("gpsimd", "vs")
        for c in range(8):
            op("qg", lambda e, c=c: e.dma_start(out=dbg_t["d_x3"].ap()[c], in_=xfm(c)))
    bar()

    emit_ffn(4, "none", "b")
    for c in range(8):
        op("qg", lambda e, c=c: e.dma_start(out=y_own.ap()[c], in_=xfm(c)))
    g_out = P.ctrs["qg"].n
    wait("gpsimd", "qg", g_out)
    bar()

    block = nc.Block().__enter__()
    P.finalize(block)
    block.__exit__(None, None, None)
    return nc


def _prep_core(inputs, core):
    seq, par = core // 2, core % 2
    f32 = np.float32
    x = np.asarray(inputs["L0_attn_a_x"], dtype=f32)
    xT = np.ascontiguousarray(x[seq].T)
    own_cols = np.concatenate(
        [np.arange((2 * i + par) * 128, (2 * i + par) * 128 + 128) for i in range(16)])
    m = {}
    m["x_full"] = np.ascontiguousarray(xT.reshape(8, 128, T))
    m["x_own"] = np.ascontiguousarray(xT[:, own_cols].reshape(8, 128, T2))
    m["wq"] = np.ascontiguousarray(
        np.asarray(inputs["L0_attn_a_w_q"], f32).reshape(8, 128, C))
    m["wkvd"] = np.ascontiguousarray(
        np.asarray(inputs["L0_attn_a_w_kv_down"], f32).reshape(8, 128, KVR))
    wkvu = np.asarray(inputs["L0_attn_a_w_kv_up"], f32)
    m["wkvu_k"] = np.ascontiguousarray(wkvu[:, :C])
    m["wkvu_v"] = np.ascontiguousarray(wkvu[:, C:])
    m["wout"] = np.ascontiguousarray(
        np.asarray(inputs["L0_attn_a_w_out"], f32).reshape(8, 128, C))
    wg = np.asarray(inputs["L0_ffn_w_gate"], np.int32)
    wu = np.asarray(inputs["L0_ffn_w_up"], np.int32)
    wd = np.asarray(inputs["L0_ffn_w_down"], np.int32)
    m["wgT"] = np.ascontiguousarray(wg.T.reshape(8, 128, FH))
    m["wuT"] = np.ascontiguousarray(wu.T.reshape(8, 128, FH))
    m["wdT"] = np.ascontiguousarray(wd.T.reshape(24, 128, C))
    ipw = np.asarray(inputs["L0_attn_b_in_proj_w"], f32)
    sel = np.concatenate([ipw[par * EO:(par + 1) * EO],
                          ipw[EF + par * EO:EF + (par + 1) * EO]])
    m["inpT"] = np.ascontiguousarray(sel.T.reshape(8, 128, EF))
    xpw = np.asarray(inputs["L0_attn_b_x_proj_w"], f32)
    m["xpT"] = np.ascontiguousarray(
        xpw[:, par * EO:(par + 1) * EO].T.reshape(8, 128, 96))
    dtw = np.asarray(inputs["L0_attn_b_dt_w"], f32)
    m["dtT"] = np.ascontiguousarray(dtw[par * EO:(par + 1) * EO].T)
    opw = np.asarray(inputs["L0_attn_b_out_proj_w"], f32)
    m["outT"] = np.ascontiguousarray(
        opw[:, par * EO:(par + 1) * EO].T.reshape(8, 128, C))
    alog = np.asarray(inputs["L0_attn_b_A_log"], f32)[par * EO:(par + 1) * EO]
    m["a_log"] = np.ascontiguousarray(
        alog.reshape(8, 128, 16).transpose(1, 0, 2).reshape(128, 128))
    cw = np.asarray(inputs["L0_attn_b_conv_w"], f32)[par * EO:(par + 1) * EO]
    m["convw"] = np.ascontiguousarray(
        cw.reshape(8, 128, 4).transpose(1, 0, 2).reshape(128, 32))
    for name, key in [("convb", "L0_attn_b_conv_b"), ("dtb", "L0_attn_b_dt_b"),
                      ("dpar", "L0_attn_b_D_param")]:
        v = np.asarray(inputs[key], f32)[par * EO:(par + 1) * EO]
        m[name] = np.ascontiguousarray(v.reshape(8, 128).T)
    import ml_dtypes
    bf = ml_dtypes.bfloat16
    m["ones128"] = np.ones((128, 128), dtype=bf)
    m["eye128"] = np.eye(128, dtype=np.float32).astype(bf)
    s2 = np.zeros((2, 128), np.float32)
    s2[0, 0:64] = 1.0
    s2[1, 64:128] = 1.0
    m["sel2"] = s2.astype(bf)
    tril = np.triu(np.ones((128, 128), np.float32))
    m["mask_a"] = (tril if par == 0 else np.ones((128, 128), np.float32)).astype(bf)
    m["mask_b"] = (np.zeros((128, 128), np.float32) if par == 0 else tril).astype(bf)
    return m


_NC_CACHE = {}


def kernel(**inputs):
    key = "main"
    if key not in _NC_CACHE:
        _NC_CACHE[key] = build_nc()
    nc = _NC_CACHE[key]
    in_maps = [_prep_core(inputs, core) for core in range(8)]
    res = run_bass_kernel_spmd(nc, in_maps, core_ids=list(range(8)))
    out = np.empty((B, T, C), dtype=np.float32)
    for core in range(8):
        seq, par = core // 2, core % 2
        yo = np.asarray(res.results[core]["y_own"])
        yo = yo.reshape(C, T2)
        cols = np.concatenate(
            [np.arange((2 * i + par) * 128, (2 * i + par) * 128 + 128)
             for i in range(16)])
        out[seq][cols, :] = yo.T
    return out


# version 02
# speedup vs baseline: 1.1643x; 1.1643x over previous
import numpy as np
from contextlib import ExitStack

import concourse.bass as bass
import concourse.mybir as mybir
from concourse.bass_utils import run_bass_kernel_spmd

F32 = mybir.dt.float32
BF16 = mybir.dt.bfloat16
I32 = mybir.dt.int32
Alu = mybir.AluOpType
Act = mybir.ActivationFunctionType

B, T, C = 4, 4096, 1024
T2 = T // 2
NH, HD, KVR = 16, 64, 128
EF, EO, NST, DTR = 2048, 1024, 16, 64
FH = 3072
QSCALE = 0.125
FSC = 1.0 / 127.0
EPS = 1e-5
GROUPS = [[0, 1], [2, 3], [4, 5], [6, 7]]
ENGS = ("tensor", "vector", "scalar", "sync", "gpsimd")


class Ctr:
    __slots__ = ("name", "sem", "step", "n", "is_dma", "watermark", "settled")

    def __init__(self, name, sem, step):
        self.name, self.sem, self.step, self.n = name, sem, step, 0
        self.is_dma = step == 16
        self.watermark = 0
        self.settled = 0


class Prog:

    def __init__(self, nc, E):
        self.nc = nc
        self.parts = {e: [] for e in ENGS}
        self.ctrs = {}
        for name, eng, step in [("ts", "tensor", 1), ("vs", "vector", 1),
                                ("ss", "scalar", 1), ("gc", "gpsimd", 1),
                                ("qs", "sync", 16), ("qg", "gpsimd", 16)]:
            self.ctrs[name] = Ctr(name, E(nc.semaphore(name)), step)
        self.home = {"ts": "tensor", "vs": "vector", "ss": "scalar",
                     "gc": "gpsimd", "qs": "sync", "qg": "gpsimd"}
        self.waited = {e: {} for e in ENGS}

    def wait(self, eng, cname, val=None):
        c = self.ctrs[cname]
        val = c.n if val is None else val
        if val <= 0 or self.waited[eng].get(cname, -1) >= val:
            return
        if c.is_dma and val < c.n:
            val = c.n
            if self.waited[eng].get(cname, -1) >= val:
                return
        self.waited[eng][cname] = val
        if c.is_dma and eng != self.home[cname]:
            c.watermark = max(c.watermark, val)
        self.parts[eng].append(lambda e, s=c.sem, v=val: e.wait_ge(s, v))

    def op(self, cname, fn, sw=True):
        c = self.ctrs[cname]
        eng = self.home[cname]
        if sw and eng in ("vector", "scalar"):
            self.wait(eng, cname, c.n)
        if c.is_dma and c.watermark > c.settled and c.n > c.settled:
            self.waited[eng][cname] = max(self.waited[eng].get(cname, -1), c.n)
            self.parts[eng].append(lambda e, s=c.sem, v=c.n: e.wait_ge(s, v))
            c.settled = c.n
        c.n += c.step
        self.parts[eng].append(
            lambda e, fn=fn, s=c.sem, k=c.step: fn(e).then_inc(s, k))
        return c.n

    def raw(self, eng, fn):
        self.parts[eng].append(fn)

    def bar(self):
        vals = {k: c.n for k, c in self.ctrs.items()}
        for k, c in self.ctrs.items():
            if c.is_dma:
                self.wait(self.home[k], k, c.n)
                c.settled = c.n
        for e in ENGS:
            for k, v in vals.items():
                self.wait(e, k, v)

    def finalize(self, block):
        getattr(block, "tensor")(lambda e: [f(e) for f in self.parts["tensor"]])
        getattr(block, "vector")(lambda e: [f(e) for f in self.parts["vector"]])
        getattr(block, "scalar")(lambda e: [f(e) for f in self.parts["scalar"]])
        getattr(block, "sync")(lambda e: [f(e) for f in self.parts["sync"]])
        getattr(block, "gpsimd")(lambda e: [f(e) for f in self.parts["gpsimd"]])


def build_nc(dbg=(), groups=None):
    nc = bass.Bass("TRN2", target_bir_lowering=False)
    dbg = set(dbg)
    global GROUPS
    GROUPS = groups if groups is not None else [[0, 1], [2, 3], [4, 5], [6, 7]]
    ctx = ExitStack()
    E = ctx.enter_context

    def DI(name, shape, dtype=F32):
        return nc.dram_tensor(name, list(shape), dtype, kind="ExternalInput")

    x_own = DI("x_own", [8, 128, T2])
    x_full = DI("x_full", [8, 128, T])
    wq = DI("wq", [8, 128, C])
    wkvd = DI("wkvd", [8, 128, KVR])
    wkvu_k = DI("wkvu_k", [128, C])
    wkvu_v = DI("wkvu_v", [128, C])
    wout = DI("wout", [8, 128, C])
    wgT = DI("wgT", [8, 128, FH], I32)
    wuT = DI("wuT", [8, 128, FH], I32)
    wdT = DI("wdT", [24, 128, C], I32)
    inpT = DI("inpT", [8, 128, EF])
    xpT = DI("xpT", [8, 128, 96])
    dtT = DI("dtT", [64, EO])
    outT = DI("outT", [8, 128, C])
    a_log = DI("a_log", [128, 128])
    convw = DI("convw", [128, 32])
    convb = DI("convb", [128, 8])
    dtb = DI("dtb", [128, 8])
    dpar = DI("dpar", [128, 8])
    ones128 = DI("ones128", [128, 128], BF16)
    eye128 = DI("eye128", [128, 128], BF16)
    sel2 = DI("sel2", [2, 128], BF16)
    mask_a = DI("mask_a", [128, 128], BF16)
    mask_b = DI("mask_b", [128, 128], BF16)

    y_own = nc.dram_tensor("y_own", [8, 128, T2], F32, kind="ExternalOutput")

    dbg_t = {}
    for name, shape, dt in [
        ("d_xhat1", [8, 128, T], BF16), ("d_q", [8, 128, T2], BF16),
        ("d_k", [8, 128, T], BF16), ("d_x1", [8, 128, T2], F32),
        ("d_x2", [8, 128, T2], F32), ("d_dbc", [128, T], F32),
        ("d_delta", [8, 128, T], F32), ("d_xcs", [8, 128, T], BF16),
        ("d_x3", [8, 128, T2], F32), ("d_ymul", [8, 128, T], BF16),
    ]:
        if name in dbg:
            dbg_t[name] = nc.dram_tensor(name, shape, dt, kind="ExternalOutput")

    kT_scr = nc.dram_tensor("kT_scr", [8, 128, T], BF16)
    v_scr = nc.dram_tensor("v_scr", [32, 128, C], BF16)
    q_scr = nc.dram_tensor("q_scr", [8, 128, T2], BF16)
    wg_scr = nc.dram_tensor("wg_scr", [24, 8, 128, 128], BF16)
    wu_scr = nc.dram_tensor("wu_scr", [24, 8, 128, 128], BF16)
    wd_scr = nc.dram_tensor("wd_scr", [8, 24, 128, 128], BF16)
    xc_scr = nc.dram_tensor("xc_scr", [8, 128, T], BF16)
    zs_scr = nc.dram_tensor("zs_scr", [8, 128, T], BF16)
    dl_scr = nc.dram_tensor("dl_scr", [8, 128, T], F32)
    ym_scr = nc.dram_tensor("ym_scr", [8, 128, T], BF16)
    park_scr = nc.dram_tensor("park_scr", [8, 128, T2], F32)
    ln3_in = nc.dram_tensor("ln3_in", [1025, T2], BF16)
    ln3_out = nc.dram_tensor("ln3_out", [2, 1025, T2], BF16)
    dbc_in = nc.dram_tensor("dbc_in", [96, T], F32)
    dbc_out = nc.dram_tensor("dbc_out", [96, T], F32)
    dbc_bf = nc.dram_tensor("dbc_bf", [32, T], BF16)
    rs_in = nc.dram_tensor("rs_in", [2, 8, 128, T2], F32)
    rs_out = nc.dram_tensor("rs_out", [8, 128, T2], F32)

    P = Prog(nc, E)
    sb_frees = {}
    sb_order = []

    def SA(name, shape, dtype):
        cm = nc.sbuf_tensor(name, list(shape), dtype)
        t = cm.__enter__()
        sb_frees[name] = cm
        sb_order.append(name)
        return t

    def SF(*names):
        for n in sorted(names, key=lambda n: -sb_order.index(n)):
            assert n == sb_order[-1], f"non-LIFO free {n} vs stack {sb_order[-6:]}"
            sb_order.pop()
            sb_frees.pop(n).__exit__(None, None, None)

    PS = nc.psum_tensor("PS", [128, 4096], F32).__enter__()

    ones_sb = SA("ones_sb", [128, 128], BF16)
    eye_sb = SA("eye_sb", [128, 128], BF16)
    sel2_sb = SA("sel2_sb", [2, 128], BF16)
    maska_sb = SA("maska_sb", [128, 128], BF16)
    maskb_sb = SA("maskb_sb", [128, 128], BF16)
    class _RowMap:
        def __init__(self, raw, width, slots):
            self.raw, self.w, self.slots = raw, width, slots
        def __getitem__(self, key):
            rs, cs = key
            off = self.slots[rs.start]
            a = cs.start or 0
            b = cs.stop if cs.stop is not None else self.w
            return self.raw[0:1, off + a:off + b]

    rows_raw = SA("rows_raw", [1, 3 * T], BF16)
    rows_sb = _RowMap(rows_raw, T, {0: 0, 2: 0, 4: 0, 6: 0, 5: T, 1: 2 * T, 3: 2 * T})
    csum_raw = SA("csum_raw", [1, 8192], BF16)
    csum_sb = _RowMap(csum_raw, 8192, {0: 0, 1: 3072, 2: 6144, 3: 7168, 4: 6144})
    srow = SA("srow", [1, 2048], F32)
    scol = SA("scol", [128, 160], F32)
    sbcol = SA("sbcol", [128, 64], BF16)
    c0s = SA("c0s", [1, 8], F32)
    c0b = SA("c0b", [1, 2], BF16)
    cvec_sb = SA("cvec_sb", [128, 8], F32)
    amat_sb = SA("amat_sb", [128, 128], F32)
    convw_sb = SA("convw_sb", [128, 32], F32)
    convb_sb = SA("convb_sb", [128, 8], F32)
    dtb_sb = SA("dtb_sb", [128, 8], F32)
    dpar_sb = SA("dpar_sb", [128, 8], F32)
    state_sb = SA("state_sb", [128, 128], F32)
    xfm_h = [SA("x_fm", [128, 8 * T2], F32)]

    def xfm(c):
        return xfm_h[0][:, c * T2:(c + 1) * T2]

    op, wait, bar = P.op, P.wait, P.bar

    def lp(fn):
        def go(e):
            with nc.allow_low_precision(reason="intentional bf16 elementwise"):
                return fn(e)
        return go

    al_sb = SA("al_sb", [128, 128], F32)
    for dst, src in [(ones_sb, ones128), (eye_sb, eye128), (sel2_sb, sel2),
                     (maska_sb, mask_a), (maskb_sb, mask_b), (convw_sb, convw),
                     (convb_sb, convb), (dtb_sb, dtb), (dpar_sb, dpar),
                     (al_sb, a_log)]:
        op("qs", lambda e, d=dst, s=src: e.dma_start(out=d[:], in_=s.ap()))
    wait("scalar", "qs")
    op("ss", lambda e: e.activation(amat_sb[:], al_sb[:], Act.Exp))
    wait("vector", "ss")
    op("vs", lambda e: e.tensor_scalar_mul(amat_sb[:], amat_sb[:], -1.0))
    op("vs", lambda e: e.memset(state_sb[:], 0.0))
    op("vs", lambda e: e.memset(cvec_sb[:, 1:2], EPS))

    wi = [SA("p0_wi0", [128, FH], I32), SA("p0_wi1", [128, FH], I32)]
    wb = [SA("p0_wb0", [128, FH], BF16), SA("p0_wb1", [128, FH], BF16)]
    conv_sets = [(wgT, wg_scr, 0, 8, FH), (wuT, wu_scr, 1, 8, FH),
                 (wdT, wd_scr, None, 24, C)]
    conv_v = {}
    store_g = {}
    colsum_t = {}
    it = 0
    for src, dst, crow, nchunk, w in conv_sets:
        for c in range(nchunk):
            b = it % 2
            if it >= 2:
                wait("sync", "vs", conv_v[it - 2])
            op("qs", lambda e, b=b, src=src, c=c, w=w:
               e.dma_start(out=wi[b][:, 0:w], in_=src.ap()[c]))
            wait("vector", "qs")
            if it >= 2:
                wait("vector", "qg", store_g[it - 2])
                if it - 2 in colsum_t:
                    wait("vector", "ts", colsum_t[it - 2])
            conv_v[it] = op("vs", lambda e, b=b, w=w:
                            e.tensor_copy(wb[b][:, 0:w], wi[b][:, 0:w]))
            wait("gpsimd", "vs", conv_v[it])
            if crow is None:
                store_g[it] = op("qg", lambda e, b=b, dst=dst, c=c:
                                 e.dma_start(out=dst.ap()[:, c].transpose([1, 0, 2]),
                                             in_=wb[b][:, 0:C].rearrange("p (f j) -> p f j", j=128)))
            else:
                store_g[it] = op("qg", lambda e, b=b, dst=dst, c=c:
                                 e.dma_start(out=dst.ap()[:, c].transpose([1, 0, 2]),
                                             in_=wb[b][:, 0:FH].rearrange("p (h j) -> p h j", j=128)))
            if crow is not None:
                wait("tensor", "vs", conv_v[it])
                for h in range(6):
                    op("ts", lambda e, b=b, h=h, c=c, nchunk=nchunk:
                       e.matmul(PS[0:1, h * 512:(h + 1) * 512], ones_sb[:, 0:1],
                                wb[b][:, h * 512:(h + 1) * 512],
                                start=(c == 0), stop=(c == nchunk - 1),
                                skip_group_check=True))
                colsum_t[it] = P.ctrs["ts"].n
            it += 1
        if crow is not None:
            wait("vector", "ts")
            op("vs", lambda e, crow=crow: e.tensor_copy(
                csum_sb[crow:crow + 1, 0:FH], PS[0:1, 0:FH]))
            wait("tensor", "vs")
    SF("al_sb", "p0_wi0", "p0_wi1", "p0_wb0", "p0_wb1")
    bar()

    def ln_stats(xb_chunks, ntok, row_idx, nchunks=8, extra_sq=None):
        xsq = extra_sq
        nc_cols = ntok // 128
        nhalf = max(1, ntok // 2048)
        for half in range(nhalf):
            h0 = half * 2048
            hw = min(2048, ntok - h0)
            hc = hw // 128
            wait("tensor", "vs")
            for c in range(nchunks):
                wait("scalar", "vs")
                wait("scalar", "ts")
                op("ss", lambda e, c=c, h0=h0, hw=hw:
                   e.activation(xsq[:, 0:hw], xb_chunks(c)[:, h0:h0 + hw], Act.Square))
                wait("tensor", "ss")
                for t0 in range(0, hw, 512):
                    op("ts", lambda e, c=c, h0=h0, t0=t0:
                       e.matmul(PS[0:1, t0:t0 + 512], ones_sb[:, 0:1],
                                xb_chunks(c)[:, h0 + t0:h0 + t0 + 512],
                                start=(c == 0), stop=(c == nchunks - 1),
                                skip_group_check=True))
                    op("ts", lambda e, c=c, t0=t0:
                       e.matmul(PS[0:1, 2048 + t0:2048 + t0 + 512], ones_sb[:, 0:1],
                                xsq[:, t0:t0 + 512],
                                start=(c == 0), stop=(c == nchunks - 1),
                                skip_group_check=True))
            wait("vector", "ts")
            wait("vector", "qs")
            op("vs", lambda e, hw=hw: e.tensor_copy(srow[0:1, 0:hw], PS[0:1, 0:hw]))
            wait("sync", "vs")
            op("qs", lambda e, half=half, hc=hc, hw=hw: e.dma_start(
                out=scol[:, half * 16:half * 16 + hc], in_=srow[0:1, 0:hw]))
            qsx = P.ctrs["qs"].n
            wait("vector", "qs", qsx)
            op("vs", lambda e, hw=hw: e.tensor_copy(srow[0:1, 0:hw],
                                                    PS[0:1, 2048:2048 + hw]))
            wait("sync", "vs")
            op("qs", lambda e, half=half, hc=hc, hw=hw: e.dma_start(
                out=scol[:, 32 + half * 16:32 + half * 16 + hc], in_=srow[0:1, 0:hw]))
        wait("vector", "qs")
        ncc = nc_cols
        op("vs", lambda e: e.tensor_scalar_mul(
            scol[:, 64:64 + ncc], scol[:, 0:ncc], 1.0 / 1024.0))
        op("vs", lambda e: e.tensor_mul(
            scol[:, 96:96 + ncc], scol[:, 64:64 + ncc], scol[:, 64:64 + ncc]))
        op("vs", lambda e: e.scalar_tensor_tensor(
            scol[:, 96:96 + ncc], scol[:, 32:32 + ncc], 1.0 / 1024.0,
            scol[:, 96:96 + ncc], Alu.mult, Alu.subtract))
        wait("scalar", "vs")
        op("ss", lambda e: e.activation(
            scol[:, 96:96 + ncc], scol[:, 96:96 + ncc], Act.Ln,
            bias=cvec_sb[:, 1:2]))
        op("ss", lambda e: e.activation(
            scol[:, 96:96 + ncc], scol[:, 96:96 + ncc], Act.Exp, scale=-0.5))
        wait("vector", "ss")
        op("vs", lambda e: e.scalar_tensor_tensor(
            scol[:, 128:128 + ncc], scol[:, 64:64 + ncc], -1.0,
            scol[:, 96:96 + ncc], Alu.mult, Alu.mult))
        op("vs", lambda e: e.tensor_copy(sbcol[:, 0:ncc], scol[:, 96:96 + ncc]))
        op("vs", lambda e: e.tensor_copy(sbcol[:, 32:32 + ncc], scol[:, 128:128 + ncc]))
        wait("sync", "vs")
        for half in range(nhalf):
            h0 = half * 2048
            hw = min(2048, ntok - h0)
            hc = hw // 128
            op("qs", lambda e, h0=h0, hw=hw, half=half, hc=hc: e.dma_start(
                out=rows_sb[5:6, h0:h0 + hw],
                in_=sbcol[:, half * 16:half * 16 + hc]))
            op("qs", lambda e, h0=h0, hw=hw, half=half, hc=hc, r=row_idx: e.dma_start(
                out=rows_sb[r:r + 1, h0:h0 + hw],
                in_=sbcol[:, 32 + half * 16:32 + half * 16 + hc]))
        wait("tensor", "qs")
        wait("vector", "qs")

    def bcast_row(row_ap, ntok, dst_sb, pbank=0):
        for t0 in range(0, ntok, 512):
            bk = pbank + (t0 // 512) % 2
            wait("tensor", "vs")
            op("ts", lambda e, t0=t0, bk=bk: e.matmul(
                PS[:, bk * 512:(bk + 1) * 512], ones_sb[0:1, :],
                row_ap[:, t0:t0 + 512], start=True, stop=True))
            wait("vector", "ts")
            op("vs", lambda e, t0=t0, bk=bk: e.tensor_copy(
                dst_sb[:, t0:t0 + 512], PS[:, bk * 512:(bk + 1) * 512]))

    for c in range(8):
        op("qs", lambda e, c=c: e.dma_start(out=xfm(c), in_=x_own.ap()[c]))
    q_xown = P.ctrs["qs"].n

    xb1 = SA("xb1", [128, 8 * T2], BF16)
    xsq_a = SA("xsq_a", [128, T2], BF16)
    rbc_a = SA("rbc_a", [128, T], BF16)
    wait("vector", "qs", q_xown)
    for c in range(8):
        op("vs", lambda e, c=c: e.tensor_copy(xb1[:, c * T2:(c + 1) * T2], xfm(c)))
    ln_stats(lambda c: xb1[:, c * T2:(c + 1) * T2], T2, 0, extra_sq=xsq_a)
    bcast_row(rows_sb[5:6, :], T2, rbc_a)
    wait("vector", "vs")
    for c in range(8):
        op("vs", lambda e, c=c: e.tensor_mul(
            xb1[:, c * T2:(c + 1) * T2], xb1[:, c * T2:(c + 1) * T2], rbc_a[:, 0:T2]))
    v_xhat1 = P.ctrs["vs"].n

    wqf = [SA("wqf0", [128, C], F32), SA("wqf1", [128, C], F32)]
    wqb = SA("wqb", [128, 8 * 128], BF16)
    wqb8 = SA("wqb8", [8, 1], F32)
    qout = SA("qout", [128, T2], BF16)
    qsq = SA("qsq", [128, T2], BF16)
    wq_all = SA("wq_all", [128, 8 * C], BF16)
    conv_done = []
    for c in range(8):
        b = c % 2
        if c >= 2:
            wait("sync", "vs", conv_done[c - 2])
        op("qs", lambda e, c=c, b=b: e.dma_start(out=wqf[b][:], in_=wq.ap()[c]))
        wait("vector", "qs")
        conv_done.append(op("vs", lambda e, c=c, b=b: e.tensor_copy(
            wq_all[:, c * C:(c + 1) * C], wqf[b][:])))
        wait("tensor", "vs", conv_done[-1])
        for h in range(2):
            op("ts", lambda e, c=c, h=h: e.matmul(
                PS[0:1, h * 512:(h + 1) * 512], ones_sb[:, 0:1],
                wq_all[:, c * C + h * 512:c * C + (h + 1) * 512],
                start=(c == 0), stop=(c == 7), skip_group_check=True))
    wait("vector", "ts")
    op("vs", lambda e: e.tensor_copy(csum_sb[2:3, 0:C], PS[0:1, 0:C]))
    wait("tensor", "vs")
    wait("tensor", "vs", v_xhat1)
    qevs = []
    for fc in range(8):
        wait("scalar", "qg")
        for t0 in range(0, T2, 512):
            bk = 2 + ((fc * 4 + t0 // 512) % 2)
            if len(qevs) >= 2:
                wait("tensor", "ss", qevs[-2])
            for cc in range(8):
                op("ts", lambda e, fc=fc, t0=t0, cc=cc, bk=bk: e.matmul(
                    PS[:, bk * 512:(bk + 1) * 512],
                    wq_all[:, cc * C + fc * 128:cc * C + fc * 128 + 128],
                    xb1[:, cc * T2 + t0:cc * T2 + t0 + 512],
                    start=(cc == 0), stop=False, skip_group_check=True))
            op("ts", lambda e, fc=fc, t0=t0, bk=bk: e.matmul(
                PS[:, bk * 512:(bk + 1) * 512],
                csum_sb[2:3, fc * 128:fc * 128 + 128],
                rows_sb[0:1, t0:t0 + 512],
                start=False, stop=True, skip_group_check=True))
            wait("scalar", "ts")
            qevs.append(op("ss", lambda e, t0=t0, bk=bk: e.activation(
                qout[:, t0:t0 + 512], PS[:, bk * 512:(bk + 1) * 512], Act.Copy)))
        wait("gpsimd", "ss")
        op("qg", lambda e, fc=fc: e.dma_start(out=q_scr.ap()[fc], in_=qout[:]))
        wait("scalar", "ts")
        op("ss", lambda e: e.activation(qsq[:], qout[:], Act.Square))
        wait("tensor", "ss")
        for t0 in range(0, T2, 512):
            op("ts", lambda e, fc=fc, t0=t0: e.matmul(
                PS[0:1, 2048 + t0:2048 + t0 + 512], ones_sb[:, 0:1],
                qsq[:, t0:t0 + 512], start=(fc == 0), stop=(fc == 7),
                skip_group_check=True))
        if "d_q" in dbg_t:
            op("qg", lambda e, fc=fc: e.dma_start(out=dbg_t["d_q"].ap()[fc], in_=qout[:]))
    wait("vector", "ts")
    op("vs", lambda e: e.reduce_max(c0s[0:1, 0:1], PS[0:1, 2048:4096],
                                    axis=mybir.AxisListType.X))
    SF("xb1", "wqf0", "wqf1", "wqb", "wqb8", "qout", "qsq", "wq_all", "rbc_a", "xsq_a")
    bar()

    xbf = SA("xbf", [128, 8 * T], BF16)
    xfull_f = [SA("xfull_f0", [128, T], F32)]
    xsq_b = SA("xsq_b", [128, T2], BF16)
    rbc_b = SA("rbc_b", [128, T], BF16)

    cd = []
    for c in range(8):
        if c >= 1:
            wait("sync", "vs", cd[c - 1])
        op("qs", lambda e, c=c: e.dma_start(out=xfull_f[0][:], in_=x_full.ap()[c]))
        wait("vector", "qs")
        cd.append(op("vs", lambda e, c=c: e.tensor_copy(
            xbf[:, c * T:(c + 1) * T], xfull_f[0][:])))
    ln_stats(lambda c: xbf[:, c * T:(c + 1) * T], T, 1, extra_sq=xsq_b)
    bcast_row(rows_sb[5:6, :], T, rbc_b)
    for c in range(8):
        op("vs", lambda e, c=c: e.tensor_mul(
            xbf[:, c * T:(c + 1) * T], xbf[:, c * T:(c + 1) * T], rbc_b[:, 0:T]))
    v_xhatf = P.ctrs["vs"].n
    if "d_xhat1" in dbg_t:
        wait("gpsimd", "vs", v_xhatf)
        for c in range(8):
            op("qg", lambda e, c=c: e.dma_start(
                out=dbg_t["d_xhat1"].ap()[c], in_=xbf[:, c * T:(c + 1) * T]))
    SF("xfull_f0", "xsq_b", "rbc_b")

    wkvd_all = SA("wkvd_all", [128, 8 * KVR], BF16)
    wkvdf = SA("wkvdf", [128, KVR], F32)
    for c in range(8):
        wait("sync", "vs")
        op("qs", lambda e, c=c: e.dma_start(out=wkvdf[:], in_=wkvd.ap()[c]))
        wait("vector", "qs")
        op("vs", lambda e, c=c: e.tensor_copy(
            wkvd_all[:, c * KVR:(c + 1) * KVR], wkvdf[:]))
        wait("tensor", "vs")
        op("ts", lambda e, c=c: e.matmul(
            PS[0:1, 0:KVR], ones_sb[:, 0:1], wkvd_all[:, c * KVR:(c + 1) * KVR],
            start=(c == 0), stop=(c == 7), skip_group_check=True))
    wait("vector", "ts")
    op("vs", lambda e: e.tensor_copy(csum_sb[3:4, 0:KVR], PS[0:1, 0:KVR]))
    wait("tensor", "vs")
    lat_sb = SA("lat_sb", [128, T], BF16)
    wait("tensor", "vs", v_xhatf)
    ev = []
    for t0 in range(0, T, 512):
        bk = 2 + (t0 // 512) % 2
        if len(ev) >= 2:
            wait("tensor", "ss", ev[-2])
        for cc in range(8):
            op("ts", lambda e, t0=t0, cc=cc, bk=bk: e.matmul(
                PS[:, bk * 512:(bk + 1) * 512],
                wkvd_all[:, cc * KVR:(cc + 1) * KVR],
                xbf[:, cc * T + t0:cc * T + t0 + 512],
                start=(cc == 0), stop=False, skip_group_check=True))
        op("ts", lambda e, t0=t0, bk=bk: e.matmul(
            PS[:, bk * 512:(bk + 1) * 512], csum_sb[3:4, 0:KVR],
            rows_sb[1:2, t0:t0 + 512], start=False, stop=True,
            skip_group_check=True))
        wait("scalar", "ts")
        ev.append(op("ss", lambda e, t0=t0, bk=bk: e.activation(
            lat_sb[:, t0:t0 + 512], PS[:, bk * 512:(bk + 1) * 512], Act.Copy)))
    wkuk_sb = SA("wkuk_sb", [128, C], BF16)
    wkuv_sb = SA("wkuv_sb", [128, C], BF16)
    kchunk = SA("kchunk", [128, T], BF16)
    wkf = kchunk[:].bitcast(F32)[:, 0:C]
    op("qs", lambda e: e.dma_start(out=wkf, in_=wkvu_k.ap()))
    wait("vector", "qs")
    op("vs", lambda e: e.tensor_copy(wkuk_sb[:], wkf))
    wait("sync", "vs")
    op("qs", lambda e: e.dma_start(out=wkf, in_=wkvu_v.ap()))
    wait("vector", "qs")
    v_wkuv = op("vs", lambda e: e.tensor_copy(wkuv_sb[:], wkf))
    ksq = SA("ksq", [128, T2], BF16)
    wait("tensor", "vs", v_wkuv)
    wait("tensor", "ss")
    for fc in range(8):
        wait("scalar", "qg")
        ev = []
        for t0 in range(0, T, 512):
            bk = 2 + (t0 // 512) % 2
            if len(ev) >= 2:
                wait("tensor", "ss", ev[-2])
            op("ts", lambda e, fc=fc, t0=t0, bk=bk: e.matmul(
                PS[:, bk * 512:(bk + 1) * 512],
                wkuk_sb[:, fc * 128:(fc + 1) * 128],
                lat_sb[:, t0:t0 + 512], start=True, stop=True))
            wait("scalar", "ts")
            ev.append(op("ss", lambda e, t0=t0, bk=bk: e.activation(
                kchunk[:, t0:t0 + 512], PS[:, bk * 512:(bk + 1) * 512], Act.Copy)))
        wait("gpsimd", "ss")
        op("qg", lambda e, fc=fc: e.dma_start(out=kT_scr.ap()[fc], in_=kchunk[:]))
        if "d_k" in dbg_t:
            op("qg", lambda e, fc=fc: e.dma_start(out=dbg_t["d_k"].ap()[fc], in_=kchunk[:]))
        for half in range(2):
            wait("scalar", "ts")
            op("ss", lambda e, half=half: e.activation(
                ksq[:], kchunk[:, half * 2048:(half + 1) * 2048], Act.Square))
            wait("tensor", "ss")
            for t0 in range(0, 2048, 512):
                op("ts", lambda e, fc=fc, t0=t0, half=half: e.matmul(
                    PS[0:1, 2048 + t0:2048 + t0 + 512],
                    ones_sb[:, 0:1], ksq[:, t0:t0 + 512],
                    start=(fc == 0 and half == 0), stop=(fc == 7 and half == 1),
                    skip_group_check=True))
    wait("vector", "ts")
    op("vs", lambda e: e.reduce_max(c0s[0:1, 1:2], PS[0:1, 2048:4096],
                                    axis=mybir.AxisListType.X))
    vchunk = SA("vchunk", [128, C], BF16)
    wait("tensor", "vs")
    ev = []
    for tc in range(32):
        bk = 2 + (tc % 2) * 2
        wait("scalar", "qg")
        if len(ev) >= 2:
            wait("tensor", "ss", ev[-2])
        for d0 in range(0, C, 512):
            op("ts", lambda e, tc=tc, d0=d0, bk=bk: e.matmul(
                PS[:, (bk + d0 // 512) * 512:(bk + d0 // 512 + 1) * 512],
                lat_sb[:, tc * 128:(tc + 1) * 128],
                wkuv_sb[:, d0:d0 + 512], start=True, stop=True))
        wait("scalar", "ts")
        ev.append(op("ss", lambda e, bk=bk: e.activation(
            vchunk[:], PS[:, bk * 512:(bk + 2) * 512], Act.Copy)))
        wait("gpsimd", "ss")
        op("qg", lambda e, tc=tc: e.dma_start(out=v_scr.ap()[tc], in_=vchunk[:]))
    op("vs", lambda e: e.tensor_mul(c0s[0:1, 2:3], c0s[0:1, 0:1], c0s[0:1, 1:2]))
    wait("scalar", "vs")
    op("ss", lambda e: e.activation(c0s[0:1, 3:4], c0s[0:1, 2:3], Act.Sqrt))
    wait("vector", "ss")
    op("vs", lambda e: e.tensor_scalar_mul(c0s[0:1, 4:5], c0s[0:1, 3:4], -QSCALE))
    op("vs", lambda e: e.tensor_copy(c0b[0:1, 0:1], c0s[0:1, 4:5]))
    wait("tensor", "vs")
    op("ts", lambda e: e.matmul(PS[:, 0:1], ones_sb[0:1, :], c0b[0:1, 0:1],
                                start=True, stop=True))
    wait("vector", "ts")
    op("vs", lambda e: e.tensor_copy(cvec_sb[:, 0:1], PS[:, 0:1]))
    SF("wkvd_all", "wkvdf", "lat_sb", "wkuk_sb", "wkuv_sb",
       "kchunk", "ksq", "vchunk")
    bar()
    SF("xbf")

    dens_sb = SA("dens_sb", [16, T2], BF16)
    ysb = SA("ysb", [128, 8 * T2], BF16)
    rcp2 = SA("rcp2", [2, T2], BF16)
    khp = [SA("khp0", [128, T], BF16), SA("khp1", [128, T], BF16)]
    qhp = [SA("qhp0", [128, T2], BF16), SA("qhp1", [128, T2], BF16)]
    vstrip = [SA("vstrip0", [128, 32 * 130], BF16), SA("vstrip1", [128, 32 * 130], BF16)]
    ptile = [SA("ptile0", [128, 1024], BF16), SA("ptile1", [128, 1024], BF16)]
    for b in range(2):
        op("vs", lambda e, b=b: e.memset(
            vstrip[b][:].rearrange("p (c w) -> p c w", w=130)[:, :, 64:65], 1.0))
        op("vs", lambda e, b=b: e.memset(
            vstrip[b][:].rearrange("p (c w) -> p c w", w=130)[:, :, 129:130], 1.0))
    v_ones = P.ctrs["vs"].n
    wait("sync", "qg")

    attn_iters = []
    for hp in range(8):
        for slot in range(4):
            for c in range(8 * slot + 8):
                attn_iters.append((hp, slot, c))

    qload_snap = {}
    vload_snap = {}
    pv_done_snap = {}
    exp_snap = {}
    maskv_snap = {}
    st_ready = {}
    for hp in range(8):
        bq = hp % 2
        if hp >= 2:
            wait("sync", "ts", pv_done_snap[(hp - 2, 3)])
        op("qs", lambda e, hp=hp, bq=bq: e.dma_start(out=qhp[bq][:],
                                                     in_=q_scr.ap()[hp]))
        op("qs", lambda e, hp=hp, bq=bq: e.dma_start(out=khp[bq][:],
                                                     in_=kT_scr.ap()[hp]))
        qload_snap[hp] = P.ctrs["qs"].n
        for slot in range(4):
            ext = 8 * slot + 8
            bv = (hp * 4 + slot) % 2
            if (hp * 4 + slot) >= 2:
                prev = (hp, slot - 2) if slot >= 2 else (hp - 1, slot + 2)
                wait("sync", "ts", pv_done_snap[prev])
            for h in range(2):
                op("qs", lambda e, hp=hp, ext=ext, bv=bv, h=h: e.dma_start(
                    out=vstrip[bv][:].rearrange("p (c w) -> p c w", w=130)
                    [:, 0:ext, h * 65:h * 65 + 64],
                    in_=v_scr.ap()[0:ext, :, hp * 128 + h * 64:hp * 128 + h * 64 + 64]
                    .transpose([1, 0, 2])))
            vload_snap[(hp, slot)] = P.ctrs["qs"].n
            obank = 4 + 2 * (slot % 2)
            pv_args = {}
            for c in range(ext):
                it = (hp, slot, c)
                jmin = max(0, (c - 8 * slot) // 2)
                W = 512 - jmin * 128
                ping = c % 2
                wait("tensor", "qs", qload_snap[hp])
                wait("tensor", "qs", vload_snap[(hp, slot)])
                wait("tensor", "vs", v_ones)
                if (hp, slot, c - 2) in exp_snap:
                    wait("tensor", "ss", exp_snap[(hp, slot, c - 2)])
                for h in range(2):
                    op("ts", lambda e, hp=hp, slot=slot, c=c, h=h, jmin=jmin,
                       ping=ping, bq=bq, W=W: e.matmul(
                           PS[:, (ping * 2 + h) * 512 + jmin * 128:
                              (ping * 2 + h) * 512 + 512],
                           khp[bq][h * 64:(h + 1) * 64,
                                   c * 128:c * 128 + 128],
                           qhp[bq][h * 64:(h + 1) * 64,
                                   slot * 512 + jmin * 128:slot * 512 + 512],
                           start=True, stop=True, tile_position=(h * 64, 0),
                           skip_group_check=True))
                st_ready[it] = P.ctrs["ts"].n

                def emit_pv(cc):
                    itp = (hp, slot, cc)
                    if itp in maskv_snap:
                        wait("tensor", "vs", maskv_snap[itp])
                    else:
                        wait("tensor", "ss", exp_snap[itp])
                    jm, bvv = pv_args[cc]
                    for h in range(2):
                        op("ts", lambda e, cc=cc, h=h, jm=jm, bvv=bvv, ob=obank,
                           ex=ext: e.matmul(
                            PS[0:65, (ob + h) * 512 + jm * 128:
                               (ob + h) * 512 + 512],
                            vstrip[bvv][:, cc * 130 + h * 65:cc * 130 + h * 65 + 65],
                            ptile[cc % 2][:, h * 512 + jm * 128:h * 512 + 512],
                            start=(cc == 0), stop=(cc == ex - 1),
                            skip_group_check=True))
                pv_args[c] = (jmin, bv)
                if c >= 1:
                    emit_pv(c - 1)
                    pv_done_snap[(hp, slot)] = P.ctrs["ts"].n

                wait("scalar", "ts", st_ready[it])
                exp_snap[it] = op("ss", lambda e, ping=ping, jmin=jmin, W=W, c=c:
                                  e.activation(
                    ptile[c % 2][:, 0:1024].rearrange("p (h w) -> p h w", h=2)
                    [:, :, jmin * 128:512],
                    PS[:, ping * 1024:ping * 1024 + 1024]
                    .rearrange("p (h w) -> p h w", h=2)[:, :, jmin * 128:512],
                    Act.Exp, bias=cvec_sb[:, 0:1], scale=QSCALE), sw=False)
                mj = None
                if c % 2 == 0 and 0 <= (c // 2) - 4 * slot <= 3:
                    mj = ((c // 2) - 4 * slot, maska_sb)
                if c % 2 == 1 and 0 <= ((c - 1) // 2) - 4 * slot <= 3:
                    mj = (((c - 1) // 2) - 4 * slot, maskb_sb)
                if mj is not None:
                    j, msk = mj
                    wait("vector", "ss", exp_snap[it])
                    for h in range(2):
                        op("vs", lambda e, h=h, j=j, msk=msk, c=c: e.tensor_mul(
                            ptile[c % 2][:, h * 512 + j * 128:h * 512 + (j + 1) * 128],
                            ptile[c % 2][:, h * 512 + j * 128:h * 512 + (j + 1) * 128],
                            msk[:]), sw=False)
                    maskv_snap[it] = P.ctrs["vs"].n
            emit_pv(ext - 1)
            pv_done_snap[(hp, slot)] = P.ctrs["ts"].n
            wait("vector", "ts", pv_done_snap[(hp, slot)])
            for h in range(2):
                op("vs", lambda e, hp=hp, slot=slot, h=h, obank=obank: e.tensor_copy(
                    ysb[h * 64:(h + 1) * 64, hp * T2 + slot * 512:hp * T2 + slot * 512 + 512],
                    PS[0:64, (obank + h) * 512:(obank + h) * 512 + 512]))
            op("vs", lambda e, obank=obank: e.tensor_copy(
                ptile[0][64:65, 0:1024],
                PS[64:65, obank * 512:obank * 512 + 1024]))
            wait("gpsimd", "vs")
            op("qg", lambda e, hp=hp, slot=slot: e.dma_start(
                out=dens_sb[2 * hp:2 * hp + 2, slot * 512:slot * 512 + 512],
                in_=ptile[0][64:65, 0:1024]))
            dens_g = P.ctrs["qg"].n
            P.wait("scalar", "qg", dens_g)
            wait("tensor", "vs")
    wait("vector", "qg")
    op("vs", lp(lambda e: e.reciprocal(dens_sb[:, :], dens_sb[:, :])))
    v_rcp = P.ctrs["vs"].n
    wait("tensor", "vs", v_rcp)
    wait("tensor", "ss")
    for hp in range(8):
        wait("sync", "vs")
        op("qs", lambda e, hp=hp: e.dma_start(
            out=rcp2[:], in_=dens_sb[2 * hp:2 * hp + 2, :]))
        q_r = P.ctrs["qs"].n
        wait("tensor", "qs", q_r)
        for t0 in range(0, T2, 512):
            bk = (t0 // 512) % 4
            wait("tensor", "vs")
            op("ts", lambda e, t0=t0, bk=bk: e.matmul(
                PS[:, bk * 512:(bk + 1) * 512], sel2_sb[:, :],
                rcp2[0:2, t0:t0 + 512], start=True, stop=True))
            wait("vector", "ts")
            op("vs", lambda e, hp=hp, t0=t0, bk=bk: e.tensor_mul(
                ysb[:, hp * T2 + t0:hp * T2 + t0 + 512],
                ysb[:, hp * T2 + t0:hp * T2 + t0 + 512],
                PS[:, bk * 512:(bk + 1) * 512]))
    SF("qhp0", "qhp1", "vstrip0", "vstrip1", "ptile0", "ptile1", "khp0", "khp1")
    woutf = SA("woutf", [128, C], F32)
    wout_all = SA("wout_all", [128, 8 * C], BF16)
    for hc in range(8):
        wait("sync", "vs")
        op("qs", lambda e, hc=hc: e.dma_start(out=woutf[:], in_=wout.ap()[hc]))
        wait("vector", "qs")
        op("vs", lambda e, hc=hc: e.tensor_copy(
            wout_all[:, hc * C:(hc + 1) * C], woutf[:]))
    v_wout = P.ctrs["vs"].n
    wait("tensor", "vs", v_wout)
    evs = []
    for cc in range(8):
        for slot in range(4):
            bk = 4 + (cc * 4 + slot) % 2
            if len(evs) >= 2:
                wait("tensor", "vs", evs[-2])
            for hc in range(8):
                op("ts", lambda e, cc=cc, slot=slot, hc=hc, bk=bk: e.matmul(
                    PS[:, bk * 512:(bk + 1) * 512],
                    wout_all[:, hc * C + cc * 128:hc * C + cc * 128 + 128],
                    ysb[:, hc * T2 + slot * 512:hc * T2 + slot * 512 + 512],
                    start=(hc == 0), stop=(hc == 7), skip_group_check=True))
            wait("vector", "ts")
            evs.append(op("vs", lambda e, cc=cc, slot=slot, bk=bk: e.tensor_add(
                xfm(cc)[:, slot * 512:slot * 512 + 512],
                xfm(cc)[:, slot * 512:slot * 512 + 512],
                PS[:, bk * 512:(bk + 1) * 512])))
    SF("woutf", "wout_all", "rcp2", "ysb", "dens_sb")
    if "d_x1" in dbg_t:
        wait("gpsimd", "vs")
        for c in range(8):
            op("qg", lambda e, c=c: e.dma_start(out=dbg_t["d_x1"].ap()[c], in_=xfm(c)))
    bar()

    def emit_ffn(ln_row, dbg_name, sfx=""):
        xb = SA("f_xb" + sfx, [128, 8 * T2], BF16)
        xsq = SA("f_xsq" + sfx, [128, T2], BF16)
        rbc = SA("f_rbc" + sfx, [128, T2], BF16)
        wait("vector", "qs")
        for c in range(8):
            op("vs", lambda e, c=c: e.tensor_copy(xb[:, c * T2:(c + 1) * T2], xfm(c)))
        ln_stats(lambda c: xb[:, c * T2:(c + 1) * T2], T2, ln_row, extra_sq=xsq)
        bcast_row(rows_sb[5:6, :], T2, rbc)
        for c in range(8):
            op("vs", lambda e, c=c: e.tensor_mul(
                xb[:, c * T2:(c + 1) * T2], xb[:, c * T2:(c + 1) * T2], rbc[:, 0:T2]))
        v_xh = P.ctrs["vs"].n
        wait("tensor", "vs", v_xh)
        wait("sync", "vs", v_xh)
        wait("gpsimd", "vs", v_xh)
        SF("f_xsq" + sfx, "f_rbc" + sfx)

        wgt = [SA("f_wg0" + sfx, [128, 1024], BF16), SA("f_wg1" + sfx, [128, 1024], BF16)]
        wut = [SA("f_wu0" + sfx, [128, 1024], BF16)]
        wdt = [SA("f_wd0" + sfx, [128, 3072], BF16)]
        hact = SA("f_hact" + sfx, [128, 24 * 1024], BF16)
        tdone = {}
        hdone = {}
        dcons = {}
        for tt in range(2):
            t0 = tt * 1024
            for h in range(24):
                bw = h % 2
                key = (tt, h)
                prev = (tt, h - 2) if h >= 2 else ((tt - 1, 22 + h) if tt else None)
                if prev is not None and prev in hdone:
                    wait("sync", "vs", hdone[prev])
                op("qs", lambda e, h=h, bw=bw: e.dma_start(
                    out=wgt[bw][:].rearrange("p (c j) -> p c j", j=128),
                    in_=wg_scr.ap()[h].transpose([1, 0, 2])))
                wait("sync", "ts")
                op("qs", lambda e, h=h: e.dma_start(
                    out=wut[0][:].rearrange("p (c j) -> p c j", j=128),
                    in_=wu_scr.ap()[h].transpose([1, 0, 2])))
                wl = P.ctrs["qs"].n
                wait("tensor", "qs", wl)
                if (tt, h - 1) in hdone:
                    wait("tensor", "vs", hdone[(tt, h - 1)])
                elif tt and h == 0:
                    wait("tensor", "vs", hdone[(tt - 1, 23)])
                for half in range(2):
                    hh = t0 + half * 512
                    for cc in range(8):
                        op("ts", lambda e, cc=cc, hh=hh, bw=bw, half=half: e.matmul(
                            PS[:, half * 512:(half + 1) * 512],
                            wgt[bw][:, cc * 128:(cc + 1) * 128],
                            xb[:, cc * T2 + hh:cc * T2 + hh + 512],
                            start=(cc == 0), stop=False, skip_group_check=True))
                    op("ts", lambda e, h=h, hh=hh, half=half, ln_row=ln_row: e.matmul(
                        PS[:, half * 512:(half + 1) * 512],
                        csum_sb[0:1, h * 128:(h + 1) * 128],
                        rows_sb[ln_row:ln_row + 1, hh:hh + 512],
                        start=False, stop=True, skip_group_check=True))
                    for cc in range(8):
                        op("ts", lambda e, cc=cc, hh=hh, half=half: e.matmul(
                            PS[:, 1024 + half * 512:1024 + (half + 1) * 512],
                            wut[0][:, cc * 128:(cc + 1) * 128],
                            xb[:, cc * T2 + hh:cc * T2 + hh + 512],
                            start=(cc == 0), stop=False, skip_group_check=True))
                    t_up = op("ts", lambda e, h=h, hh=hh, half=half, ln_row=ln_row: e.matmul(
                        PS[:, 1024 + half * 512:1024 + (half + 1) * 512],
                        csum_sb[1:2, h * 128:(h + 1) * 128],
                        rows_sb[ln_row:ln_row + 1, hh:hh + 512],
                        start=False, stop=True, skip_group_check=True))
                tdone[key] = t_up
                wait("scalar", "ts", t_up)
                if (tt, h - 1) in hdone:
                    pass
                s_g = op("ss", lambda e, h=h: e.activation(
                    hact[:, h * 1024:(h + 1) * 1024], PS[:, 0:1024],
                    Act.Silu, scale=FSC))
                wait("vector", "ss", s_g)
                hdone[key] = op("vs", lambda e, h=h: e.scalar_tensor_tensor(
                    hact[:, h * 1024:(h + 1) * 1024], PS[:, 1024:2048],
                    FSC, hact[:, h * 1024:(h + 1) * 1024], Alu.mult, Alu.mult))
            wait("tensor", "vs", hdone[(tt, 23)])
            for cc in range(8):
                key = (tt, cc)
                prev = (tt, cc - 1) if cc >= 1 else ((tt - 1, 7) if tt else None)
                if prev is not None and prev in dcons:
                    wait("sync", "ts")
                op("qs", lambda e, cc=cc: e.dma_start(
                    out=wdt[0][:].rearrange("p (h j) -> p h j", j=128),
                    in_=wd_scr.ap()[cc].transpose([1, 0, 2])))
                dl = P.ctrs["qs"].n
                wait("tensor", "qs", dl)
                for half in range(2):
                    for h in range(24):
                        op("ts", lambda e, h=h, half=half: e.matmul(
                            PS[:, 2048 + half * 512:2048 + (half + 1) * 512],
                            wdt[0][:, h * 128:(h + 1) * 128],
                            hact[:, h * 1024 + half * 512:h * 1024 + half * 512 + 512],
                            start=(h == 0), stop=(h == 23), skip_group_check=True))
                wait("vector", "ts")
                dcons[key] = op("vs", lambda e, cc=cc, t0=t0: e.scalar_tensor_tensor(
                    xfm(cc)[:, t0:t0 + 1024], PS[:, 2048:3072],
                    FSC, xfm(cc)[:, t0:t0 + 1024], Alu.mult, Alu.add))
                wait("tensor", "vs", dcons[key])
        SF("f_xb" + sfx, "f_wg0" + sfx, "f_wg1" + sfx, "f_wu0" + sfx,
           "f_wd0" + sfx, "f_hact" + sfx)
        if dbg_name in dbg_t:
            wait("gpsimd", "vs")
            for c in range(8):
                op("qg", lambda e, c=c, n=dbg_name: e.dma_start(
                    out=dbg_t[n].ap()[c], in_=xfm(c)))
        bar()

    emit_ffn(2, "d_x2", "a")

    xb3 = SA("m_xb3", [128, 8 * T2], BF16)
    xsq3 = SA("m_xsq3", [128, T2], BF16)
    rbc3 = SA("m_rbc3", [128, T2], BF16)
    wait("vector", "qs")
    for c in range(8):
        op("vs", lambda e, c=c: e.tensor_copy(xb3[:, c * T2:(c + 1) * T2], xfm(c)))
    ln_stats(lambda c: xb3[:, c * T2:(c + 1) * T2], T2, 6, extra_sq=xsq3)
    bcast_row(rows_sb[5:6, :], T2, rbc3)
    for c in range(8):
        op("vs", lambda e, c=c: e.tensor_mul(
            xb3[:, c * T2:(c + 1) * T2], xb3[:, c * T2:(c + 1) * T2], rbc3[:, 0:T2]))
    v_x3h = P.ctrs["vs"].n
    wait("gpsimd", "vs", v_x3h)
    for c in range(8):
        op("qg", lambda e, c=c: e.dma_start(
            out=ln3_in.ap()[c * 128:(c + 1) * 128, :], in_=xb3[:, c * T2:(c + 1) * T2]))
    op("qg", lambda e: e.dma_start(out=ln3_in.ap()[1024:1025, :], in_=rows_sb[6:7, 0:T2]))
    for c in range(8):
        op("qg", lambda e, c=c: e.dma_start(out=park_scr.ap()[c], in_=xfm(c)))
    g_ln3 = P.ctrs["qg"].n
    wait("gpsimd", "qg", g_ln3)
    bar()
    op("gc", lambda e: e.collective_compute(
        "AllGather", Alu.bypass, replica_groups=GROUPS,
        ins=[ln3_in.ap().opt()], outs=[ln3_out.ap().opt()]))
    SF("m_xb3", "m_xsq3", "m_rbc3")
    bar()

    rhsln = xfm_h[0][:].bitcast(BF16)
    for c in range(8):
        for b in range(2):
            op("qs", lambda e, c=c, b=b: e.dma_start(
                out=rhsln[:, c * T:(c + 1) * T]
                .rearrange("p (g j) -> p g j", j=128)[:, b::2, :],
                in_=ln3_out.ap()[b, c * 128:(c + 1) * 128, :]
                .rearrange("p (g j) -> p g j", j=128).transpose([0, 1, 2])))
    for b in range(2):
        op("qs", lambda e, b=b: e.dma_start(
            out=rows_sb[3:4, 0:T].rearrange("o (g j) -> o g j", j=128)[:, b::2, :],
            in_=ln3_out.ap()[b, 1024:1025, :].rearrange("o (g j) -> o g j", j=128)))
    q_rhs = P.ctrs["qs"].n

    inpf = SA("m_inpf", [128, 1024], F32)
    inpb = [SA("m_inpb0", [128, 1024], BF16), SA("m_inpb1", [128, 1024], BF16)]
    xzc = SA("m_xzc", [128, T], F32)
    cacc = SA("m_cacc", [128, T], F32)
    xcsb = SA("m_xcsb", [128, T], BF16)
    wait("tensor", "qs", q_rhs)
    evx = []
    for ec in range(16):
        bw = ec % 2
        if ec > 0:
            wait("scalar", "qg")
        wait("sync", "vs")
        op("qs", lambda e, ec=ec: e.dma_start(
            out=inpf[:].rearrange("p (c j) -> p c j", j=128),
            in_=inpT.ap()[:, :, ec * 128:(ec + 1) * 128].transpose([1, 0, 2])))
        wait("vector", "qs")
        v_cv = op("vs", lambda e, bw=bw: e.tensor_copy(inpb[bw][:], inpf[:]))
        wait("tensor", "vs", v_cv)
        for cc in range(8):
            op("ts", lambda e, cc=cc, bw=bw: e.matmul(
                PS[0:1, 3584:3584 + 128],
                ones_sb[:, 0:1], inpb[bw][:, cc * 128:(cc + 1) * 128],
                start=(cc == 0), stop=(cc == 7), skip_group_check=True))
        wait("vector", "ts")
        op("vs", lambda e, ec=ec: e.tensor_copy(
            csum_sb[4:5, ec * 128:(ec + 1) * 128], PS[0:1, 3584:3584 + 128]))
        wait("tensor", "vs")
        for t0 in range(0, T, 512):
            bk = (t0 // 512) % 2
            if len(evx) >= 2:
                wait("tensor", "ss", evx[-2])
            for cc in range(8):
                op("ts", lambda e, cc=cc, t0=t0, bw=bw, bk=bk: e.matmul(
                    PS[:, bk * 512:(bk + 1) * 512],
                    inpb[bw][:, cc * 128:(cc + 1) * 128],
                    rhsln[:, cc * T + t0:cc * T + t0 + 512],
                    start=(cc == 0), stop=False, skip_group_check=True))
            op("ts", lambda e, ec=ec, t0=t0, bk=bk: e.matmul(
                PS[:, bk * 512:(bk + 1) * 512],
                csum_sb[4:5, ec * 128:ec * 128 + 128],
                rows_sb[3:4, t0:t0 + 512], start=False, stop=True,
                skip_group_check=True))
            wait("scalar", "ts")
            if ec < 8:
                evx.append(op("ss", lambda e, t0=t0, bk=bk: e.activation(
                    xzc[:, t0:t0 + 512], PS[:, bk * 512:(bk + 1) * 512], Act.Copy)))
            else:
                evx.append(op("ss", lambda e, t0=t0, bk=bk: e.activation(
                    xcsb[:, t0:t0 + 512], PS[:, bk * 512:(bk + 1) * 512], Act.Silu)))
        if ec >= 8:
            wait("gpsimd", "ss")
            op("qg", lambda e, ec=ec: e.dma_start(out=zs_scr.ap()[ec - 8], in_=xcsb[:]))
            continue
        wait("vector", "ss")
        op("vs", lambda e, ec=ec: e.tensor_scalar_mul(
            cacc[:], xzc[:], convw_sb[:, ec * 4 + 3:ec * 4 + 4]))
        for k in range(3):
            sh = 3 - k
            op("vs", lambda e, ec=ec, k=k, sh=sh: e.scalar_tensor_tensor(
                cacc[:, sh:T], xzc[:, 0:T - sh], convw_sb[:, ec * 4 + k:ec * 4 + k + 1],
                cacc[:, sh:T], Alu.mult, Alu.add))
        v_conv = P.ctrs["vs"].n
        wait("scalar", "vs", v_conv)
        op("ss", lambda e, ec=ec: e.activation(
            xcsb[:], cacc[:], Act.Silu, bias=convb_sb[:, ec:ec + 1]))
        wait("gpsimd", "ss")
        op("qg", lambda e, ec=ec: e.dma_start(out=xc_scr.ap()[ec], in_=xcsb[:]))
        if "d_xcs" in dbg_t:
            op("qg", lambda e, ec=ec: e.dma_start(out=dbg_t["d_xcs"].ap()[ec], in_=xcsb[:]))
    SF("m_inpf", "m_inpb0", "m_inpb1", "m_xzc", "m_cacc", "m_xcsb")
    bar()

    dbcb = SA("m_dbcb", [128, T], BF16)
    xpf = SA("m_xpf", [128, 768], F32)
    xpb = SA("m_xpb", [128, 768], BF16)
    op("qs", lambda e: e.dma_start(
        out=xpf[:].rearrange("p (c j) -> p c j", j=96), in_=xpT.ap().transpose([1, 0, 2])))
    wait("vector", "qs")
    op("vs", lambda e: e.tensor_copy(xpb[:], xpf[:]))
    dbcp = SA("m_dbcp", [128, T], F32)
    xcq = [SA("m_xcq0", [128, 512], BF16), SA("m_xcq1", [128, 512], BF16)]
    wait("tensor", "vs")
    for ts8 in range(8):
        t0 = ts8 * 512
        bk = ts8 % 2
        wait("tensor", "vs")
        for ec in range(8):
            bq = (ts8 * 8 + ec) % 2
            if ts8 * 8 + ec >= 2:
                wait("sync", "ts")
            op("qs", lambda e, ec=ec, t0=t0, bq=bq: e.dma_start(
                out=xcq[bq][:], in_=xc_scr.ap()[ec, :, t0:t0 + 512]))
            wait("tensor", "qs")
            op("ts", lambda e, ec=ec, t0=t0, bq=bq, bk=bk: e.matmul(
                PS[0:96, bk * 512:bk * 512 + 512],
                xpb[:, ec * 96:(ec + 1) * 96], xcq[bq][:],
                start=(ec == 0), stop=(ec == 7), skip_group_check=True))
        wait("vector", "ts")
        op("vs", lambda e, t0=t0, bk=bk: e.tensor_copy(
            dbcp[0:96, t0:t0 + 512], PS[0:96, bk * 512:bk * 512 + 512]))
    wait("gpsimd", "vs")
    op("qg", lambda e: e.dma_start(out=dbc_in.ap(), in_=dbcp[0:96, :]))
    wait("gpsimd", "qg")
    op("gc", lambda e: e.collective_compute(
        "AllReduce", Alu.add, replica_groups=GROUPS,
        ins=[dbc_in.ap().opt()], outs=[dbc_out.ap().opt()]))
    wait("sync", "gc")
    op("qs", lambda e: e.dma_start(out=dbcp[0:96, :], in_=dbc_out.ap()))
    q_dbc = P.ctrs["qs"].n
    if "d_dbc" in dbg_t:
        wait("gpsimd", "qs", q_dbc)
        op("qg", lambda e: e.dma_start(out=dbg_t["d_dbc"].ap()[0:96, :], in_=dbcp[0:96, :]))
    wait("vector", "qs", q_dbc)
    op("vs", lambda e: e.tensor_copy(dbcb[0:96, :], dbcp[0:96, :]))
    v_dbcb = P.ctrs["vs"].n
    wait("gpsimd", "vs", v_dbcb)
    op("qg", lambda e: e.dma_start(out=dbc_bf.ap(), in_=dbcb[64:96, :]))
    dttf = SA("m_dttf", [64, EO], F32)
    dttb = SA("m_dttb", [64, EO], BF16)
    op("qs", lambda e: e.dma_start(out=dttf[:], in_=dtT.ap()))
    wait("vector", "qs")
    op("vs", lambda e: e.tensor_copy(dttb[:], dttf[:]))
    v_dtt = P.ctrs["vs"].n
    dlt = SA("m_dlt", [128, T], F32)
    wait("tensor", "vs", max(v_dbcb, v_dtt))
    for ec in range(8):
        for t0 in range(0, T, 512):
            op("ts", lambda e, ec=ec, t0=t0: e.matmul(
                PS[:, t0:t0 + 512], dttb[:, ec * 128:(ec + 1) * 128],
                dbcb[0:64, t0:t0 + 512], start=True, stop=True))
        wait("scalar", "ts")
        op("ss", lambda e, ec=ec: e.activation(
            dlt[:], PS[:, 0:T], Act.Exp, bias=dtb_sb[:, ec:ec + 1]))
        op("ss", lambda e: e.activation(dlt[:], dlt[:], Act.Ln, bias=1.0))
        wait("gpsimd", "ss")
        op("qg", lambda e, ec=ec: e.dma_start(out=dl_scr.ap()[ec], in_=dlt[:]))
        if "d_delta" in dbg_t:
            op("qg", lambda e, ec=ec: e.dma_start(out=dbg_t["d_delta"].ap()[ec], in_=dlt[:]))
        wait("tensor", "qg")
    SF("m_xpf", "m_xpb", "m_dbcp", "m_xcq0", "m_xcq1", "m_dttf", "m_dttb", "m_dlt", "m_dbcb")
    bar()

    bcst = SA("m_bcst", [128, 32 * 1024], BF16)
    dq = [SA("m_dq0", [128, 1024], F32)]
    xq = [SA("m_xq0", [128, 1024], BF16), SA("m_xq1", [128, 1024], BF16)]
    uq = SA("m_uq", [128, 1024], BF16)
    dA = [PS[:, 0:1024], PS[:, 1024:2048]]
    bxq = SA("m_bxq", [128, 1024], BF16)
    hsq = SA("m_hsq", [128, 1024], F32)
    ytq = [SA("m_ytq0", [128, 1024], BF16)]
    zq = ytq
    ymq = uq
    dxq = bxq
    ld_snap = {}
    ymq_g = {}
    for tq in range(4):
        t0 = tq * 1024
        wait("sync", "vs")
        for n in range(NST):
            for r in range(2):
                dfrow = n if r == 0 else 16 + n
                slot = 2 * n + r
                op("qs", lambda e, dfrow=dfrow, slot=slot, t0=t0: e.dma_start(
                    out=bcst[:, slot * 1024:(slot + 1) * 1024],
                    in_=dbc_bf.ap()[dfrow:dfrow + 1, t0:t0 + 1024]
                    .partition_broadcast(128).squeeze(1)))
        bc_q = P.ctrs["qs"].n
        wait("vector", "qs", bc_q)
        for ec in range(8):
            bi = ec % 2
            if (tq, ec) != (0, 0):
                wait("sync", "vs")
            op("qs", lambda e, ec=ec, t0=t0: e.dma_start(
                out=dq[0][:], in_=dl_scr.ap()[ec, :, t0:t0 + 1024]))
            op("qs", lambda e, ec=ec, t0=t0, bi=bi: e.dma_start(
                out=xq[bi][:], in_=xc_scr.ap()[ec, :, t0:t0 + 1024]))
            ld_snap[(tq, ec)] = P.ctrs["qs"].n
            wait("vector", "qs", ld_snap[(tq, ec)])
            op("vs", lambda e, bi=bi: e.tensor_mul(uq[:], dq[0][:], xq[bi][:]))
            wait("scalar", "vs")
            acc_prev = None
            scan_v = {}
            for n in range(NST):
                ba = n % 2
                col = ec * 16 + n
                if n >= 2:
                    wait("scalar", "vs", scan_v[n - 2])
                op("ss", lambda e, ba=ba, col=col: e.activation(
                    dA[ba], dq[0][:], Act.Exp, scale=amat_sb[:, col:col + 1]),
                    sw=False)
                exp_s = P.ctrs["ss"].n
                op("vs", lambda e, n=n: e.tensor_mul(
                    bxq[:], uq[:], bcst[:, 2 * n * 1024:(2 * n + 1) * 1024]))
                wait("vector", "ss", exp_s)
                scan_v[n] = op("vs", lambda e, ba=ba, col=col: e.tensor_tensor_scan(
                    hsq[:], dA[ba], bxq[:], state_sb[:, col:col + 1],
                    Alu.mult, Alu.add))
                op("vs", lambda e, col=col: e.tensor_copy(
                    state_sb[:, col:col + 1], hsq[:, 1023:1024]))
                if acc_prev is not None:
                    wait("vector", "ts", acc_prev)
                op("vs", lambda e, n=n: e.tensor_mul(
                    ytq[0][:], hsq[:],
                    bcst[:, (2 * n + 1) * 1024:(2 * n + 2) * 1024]))
                wait("tensor", "vs")
                for hh in range(2):
                    op("ts", lambda e, hh=hh, n=n: e.matmul(
                        PS[:, (4 + hh) * 512:(5 + hh) * 512],
                        eye_sb[:], ytq[0][:, hh * 512:(hh + 1) * 512],
                        start=(n == 0), stop=False, skip_group_check=True))
                acc_prev = P.ctrs["ts"].n
            if acc_prev is not None:
                wait("vector", "ts", acc_prev)
            op("vs", lambda e, bi=bi, ec=ec: e.tensor_scalar_mul(
                dxq[:], xq[bi][:], dpar_sb[:, ec:ec + 1]))
            wait("tensor", "vs")
            for hh in range(2):
                op("ts", lambda e, hh=hh: e.matmul(
                    PS[:, (4 + hh) * 512:(5 + hh) * 512],
                    eye_sb[:], dxq[:, hh * 512:(hh + 1) * 512],
                    start=False, stop=(hh == 1), skip_group_check=True))
            t_acc = P.ctrs["ts"].n
            wait("sync", "ts", t_acc)
            op("qs", lambda e, ec=ec, t0=t0: e.dma_start(
                out=zq[0][:], in_=zs_scr.ap()[ec, :, t0:t0 + 1024]))
            q_z = P.ctrs["qs"].n
            wait("vector", "ts", t_acc)
            wait("vector", "qs", q_z)
            ymq_g[(tq, ec)] = op("vs", lambda e: e.tensor_mul(
                ymq[:], PS[:, 2048:3072], zq[0][:]))
            wait("tensor", "vs")
            wait("gpsimd", "vs", ymq_g[(tq, ec)])
            op("qg", lambda e, ec=ec, t0=t0: e.dma_start(
                out=ym_scr.ap()[ec, :, t0:t0 + 1024], in_=ymq[:]))
            if "d_ymul" in dbg_t:
                op("qg", lambda e, ec=ec, t0=t0: e.dma_start(
                    out=dbg_t["d_ymul"].ap()[ec, :, t0:t0 + 1024], in_=ymq[:]))
            wait("vector", "qg")
    SF("m_bcst", "m_dq0", "m_xq0", "m_xq1",
       "m_uq", "m_bxq", "m_hsq", "m_ytq0")
    bar()

    outf = SA("c4_outf", [128, C], F32)
    outt = SA("c4_outt", [128, 8 * C], BF16)
    for ec in range(8):
        wait("sync", "vs")
        op("qs", lambda e, ec=ec: e.dma_start(out=outf[:], in_=outT.ap()[ec]))
        wait("vector", "qs")
        op("vs", lambda e, ec=ec: e.tensor_copy(outt[:, ec * C:(ec + 1) * C], outf[:]))
    v_outt = P.ctrs["vs"].n
    wait("tensor", "vs", v_outt)
    ymgs = SA("c4_ymgs", [128, 8 * 512], BF16)
    prt = SA("c4_prt", [128, 512], F32)
    for gs in range(8):
        wait("sync", "ts")
        op("qs", lambda e, gs=gs: e.dma_start(
            out=ymgs[:].rearrange("p (c j) -> p c j", j=512),
            in_=ym_scr.ap()[:, :, gs * 512:(gs + 1) * 512].transpose([1, 0, 2])))
        q_ym = P.ctrs["qs"].n
        wait("tensor", "qs", q_ym)
        for cc in range(8):
            bk = (gs * 8 + cc) % 2
            wait("tensor", "vs")
            for ec in range(8):
                op("ts", lambda e, cc=cc, ec=ec, bk=bk: e.matmul(
                    PS[:, bk * 512:(bk + 1) * 512],
                    outt[:, ec * C + cc * 128:ec * C + cc * 128 + 128],
                    ymgs[:, ec * 512:(ec + 1) * 512],
                    start=(ec == 0), stop=(ec == 7), skip_group_check=True))
            wait("vector", "ts")
            op("vs", lambda e, bk=bk: e.tensor_copy(prt[:], PS[:, bk * 512:(bk + 1) * 512]))
            wait("gpsimd", "vs")
            for bb_ in range(2):
                op("qg", lambda e, cc=cc, gs=gs, bb_=bb_: e.dma_start(
                    out=rs_in.ap()[bb_, cc, :, gs * 256:gs * 256 + 256]
                    .rearrange("p (u j) -> p u j", j=128),
                    in_=prt[:].rearrange("p (u v j) -> p u v j", u=2, v=2)[:, :, bb_, :]))
            wait("vector", "qg")
    g_rs = P.ctrs["qg"].n
    wait("gpsimd", "qg", g_rs)
    op("gc", lambda e: e.collective_compute(
        "ReduceScatter", Alu.add, replica_groups=GROUPS,
        ins=[rs_in.ap().opt()], outs=[rs_out.ap().opt()]))
    SF("c4_outf", "c4_outt", "c4_ymgs", "c4_prt")
    bar()
    for c in range(8):
        op("qs", lambda e, c=c: e.dma_start(out=xfm(c), in_=park_scr.ap()[c]))
    rsl = [SA("c4_rsl0", [128, T2], F32), SA("c4_rsl1", [128, T2], F32)]
    wait("sync", "gc")
    for c in range(8):
        bi = c % 2
        if c >= 2:
            wait("sync", "vs")
        op("qs", lambda e, c=c, bi=bi: e.dma_start(out=rsl[bi][:], in_=rs_out.ap()[c]))
        wait("vector", "qs")
        op("vs", lambda e, c=c, bi=bi: e.tensor_add(xfm(c), xfm(c), rsl[bi][:]))
    SF("c4_rsl0", "c4_rsl1")
    if "d_x3" in dbg_t:
        wait("gpsimd", "vs")
        for c in range(8):
            op("qg", lambda e, c=c: e.dma_start(out=dbg_t["d_x3"].ap()[c], in_=xfm(c)))
    bar()

    emit_ffn(4, "none", "b")
    for c in range(8):
        op("qg", lambda e, c=c: e.dma_start(out=y_own.ap()[c], in_=xfm(c)))
    g_out = P.ctrs["qg"].n
    wait("gpsimd", "qg", g_out)
    bar()

    block = nc.Block().__enter__()
    P.finalize(block)
    block.__exit__(None, None, None)
    return nc


def _prep_core(inputs, core):
    seq, par = core // 2, core % 2
    f32 = np.float32
    x = np.asarray(inputs["L0_attn_a_x"], dtype=f32)
    xT = np.ascontiguousarray(x[seq].T)
    own_cols = np.concatenate(
        [np.arange((2 * i + par) * 128, (2 * i + par) * 128 + 128) for i in range(16)])
    m = {}
    m["x_full"] = np.ascontiguousarray(xT.reshape(8, 128, T))
    m["x_own"] = np.ascontiguousarray(xT[:, own_cols].reshape(8, 128, T2))
    m["wq"] = np.ascontiguousarray(
        np.asarray(inputs["L0_attn_a_w_q"], f32).reshape(8, 128, C))
    m["wkvd"] = np.ascontiguousarray(
        np.asarray(inputs["L0_attn_a_w_kv_down"], f32).reshape(8, 128, KVR))
    wkvu = np.asarray(inputs["L0_attn_a_w_kv_up"], f32)
    m["wkvu_k"] = np.ascontiguousarray(wkvu[:, :C])
    m["wkvu_v"] = np.ascontiguousarray(wkvu[:, C:])
    m["wout"] = np.ascontiguousarray(
        np.asarray(inputs["L0_attn_a_w_out"], f32).reshape(8, 128, C))
    wg = np.asarray(inputs["L0_ffn_w_gate"], np.int32)
    wu = np.asarray(inputs["L0_ffn_w_up"], np.int32)
    wd = np.asarray(inputs["L0_ffn_w_down"], np.int32)
    m["wgT"] = np.ascontiguousarray(wg.T.reshape(8, 128, FH))
    m["wuT"] = np.ascontiguousarray(wu.T.reshape(8, 128, FH))
    m["wdT"] = np.ascontiguousarray(wd.T.reshape(24, 128, C))
    ipw = np.asarray(inputs["L0_attn_b_in_proj_w"], f32)
    sel = np.concatenate([ipw[par * EO:(par + 1) * EO],
                          ipw[EF + par * EO:EF + (par + 1) * EO]])
    m["inpT"] = np.ascontiguousarray(sel.T.reshape(8, 128, EF))
    xpw = np.asarray(inputs["L0_attn_b_x_proj_w"], f32)
    m["xpT"] = np.ascontiguousarray(
        xpw[:, par * EO:(par + 1) * EO].T.reshape(8, 128, 96))
    dtw = np.asarray(inputs["L0_attn_b_dt_w"], f32)
    m["dtT"] = np.ascontiguousarray(dtw[par * EO:(par + 1) * EO].T)
    opw = np.asarray(inputs["L0_attn_b_out_proj_w"], f32)
    m["outT"] = np.ascontiguousarray(
        opw[:, par * EO:(par + 1) * EO].T.reshape(8, 128, C))
    alog = np.asarray(inputs["L0_attn_b_A_log"], f32)[par * EO:(par + 1) * EO]
    m["a_log"] = np.ascontiguousarray(
        alog.reshape(8, 128, 16).transpose(1, 0, 2).reshape(128, 128))
    cw = np.asarray(inputs["L0_attn_b_conv_w"], f32)[par * EO:(par + 1) * EO]
    m["convw"] = np.ascontiguousarray(
        cw.reshape(8, 128, 4).transpose(1, 0, 2).reshape(128, 32))
    for name, key in [("convb", "L0_attn_b_conv_b"), ("dtb", "L0_attn_b_dt_b"),
                      ("dpar", "L0_attn_b_D_param")]:
        v = np.asarray(inputs[key], f32)[par * EO:(par + 1) * EO]
        m[name] = np.ascontiguousarray(v.reshape(8, 128).T)
    import ml_dtypes
    bf = ml_dtypes.bfloat16
    m["ones128"] = np.ones((128, 128), dtype=bf)
    m["eye128"] = np.eye(128, dtype=np.float32).astype(bf)
    s2 = np.zeros((2, 128), np.float32)
    s2[0, 0:64] = 1.0
    s2[1, 64:128] = 1.0
    m["sel2"] = s2.astype(bf)
    tril = np.triu(np.ones((128, 128), np.float32))
    m["mask_a"] = (tril if par == 0 else np.ones((128, 128), np.float32)).astype(bf)
    m["mask_b"] = (np.zeros((128, 128), np.float32) if par == 0 else tril).astype(bf)
    return m


_NC_CACHE = {}


def kernel(**inputs):
    key = "main"
    if key not in _NC_CACHE:
        _NC_CACHE[key] = build_nc()
    nc = _NC_CACHE[key]
    in_maps = [_prep_core(inputs, core) for core in range(8)]
    res = run_bass_kernel_spmd(nc, in_maps, core_ids=list(range(8)))
    out = np.empty((B, T, C), dtype=np.float32)
    for core in range(8):
        seq, par = core // 2, core % 2
        yo = np.asarray(res.results[core]["y_own"])
        yo = yo.reshape(C, T2)
        cols = np.concatenate(
            [np.arange((2 * i + par) * 128, (2 * i + par) * 128 + 128)
             for i in range(16)])
        out[seq][cols, :] = yo.T
    return out


# version 04
# speedup vs baseline: 1.2515x; 1.0749x over previous
import numpy as np
from contextlib import ExitStack

import concourse.bass as bass
import concourse.mybir as mybir
from concourse.bass_utils import run_bass_kernel_spmd

F32 = mybir.dt.float32
BF16 = mybir.dt.bfloat16
I32 = mybir.dt.int32
Alu = mybir.AluOpType
Act = mybir.ActivationFunctionType

B, T, C = 4, 4096, 1024
T2 = T // 2
NH, HD, KVR = 16, 64, 128
EF, EO, NST, DTR = 2048, 1024, 16, 64
FH = 3072
QSCALE = 0.125
FSC = 1.0 / 127.0
EPS = 1e-5
GROUPS = [[0, 1], [2, 3], [4, 5], [6, 7]]
ENGS = ("tensor", "vector", "scalar", "sync", "gpsimd")


class Ctr:
    __slots__ = ("name", "sem", "step", "n", "is_dma", "watermark", "settled")

    def __init__(self, name, sem, step):
        self.name, self.sem, self.step, self.n = name, sem, step, 0
        self.is_dma = step == 16
        self.watermark = 0
        self.settled = 0


class Prog:

    def __init__(self, nc, E):
        self.nc = nc
        self.parts = {e: [] for e in ENGS}
        self.ctrs = {}
        for name, eng, step in [("ts", "tensor", 1), ("vs", "vector", 1),
                                ("ss", "scalar", 1), ("gc", "gpsimd", 1),
                                ("qs", "sync", 16), ("qg", "gpsimd", 16)]:
            self.ctrs[name] = Ctr(name, E(nc.semaphore(name)), step)
        self.home = {"ts": "tensor", "vs": "vector", "ss": "scalar",
                     "gc": "gpsimd", "qs": "sync", "qg": "gpsimd"}
        self.waited = {e: {} for e in ENGS}

    def wait(self, eng, cname, val=None):
        c = self.ctrs[cname]
        val = c.n if val is None else val
        if val <= 0 or self.waited[eng].get(cname, -1) >= val:
            return
        if c.is_dma and val < c.n:
            val = c.n
            if self.waited[eng].get(cname, -1) >= val:
                return
        self.waited[eng][cname] = val
        if c.is_dma and eng != self.home[cname]:
            c.watermark = max(c.watermark, val)
        self.parts[eng].append(lambda e, s=c.sem, v=val: e.wait_ge(s, v))

    def op(self, cname, fn, sw=True):
        c = self.ctrs[cname]
        eng = self.home[cname]
        if sw and eng in ("vector", "scalar"):
            self.wait(eng, cname, c.n)
        if c.is_dma and c.watermark > c.settled and c.n > c.settled:
            self.waited[eng][cname] = max(self.waited[eng].get(cname, -1), c.n)
            self.parts[eng].append(lambda e, s=c.sem, v=c.n: e.wait_ge(s, v))
            c.settled = c.n
        c.n += c.step
        self.parts[eng].append(
            lambda e, fn=fn, s=c.sem, k=c.step: fn(e).then_inc(s, k))
        return c.n

    def raw(self, eng, fn):
        self.parts[eng].append(fn)

    def bar(self):
        vals = {k: c.n for k, c in self.ctrs.items()}
        for k, c in self.ctrs.items():
            if c.is_dma:
                self.wait(self.home[k], k, c.n)
                c.settled = c.n
        for e in ENGS:
            for k, v in vals.items():
                self.wait(e, k, v)

    def finalize(self, block):
        getattr(block, "tensor")(lambda e: [f(e) for f in self.parts["tensor"]])
        getattr(block, "vector")(lambda e: [f(e) for f in self.parts["vector"]])
        getattr(block, "scalar")(lambda e: [f(e) for f in self.parts["scalar"]])
        getattr(block, "sync")(lambda e: [f(e) for f in self.parts["sync"]])
        getattr(block, "gpsimd")(lambda e: [f(e) for f in self.parts["gpsimd"]])


def build_nc(dbg=(), groups=None):
    nc = bass.Bass("TRN2", target_bir_lowering=False)
    dbg = set(dbg)
    global GROUPS
    GROUPS = groups if groups is not None else [[0, 1], [2, 3], [4, 5], [6, 7]]
    ctx = ExitStack()
    E = ctx.enter_context

    def DI(name, shape, dtype=F32):
        return nc.dram_tensor(name, list(shape), dtype, kind="ExternalInput")

    x_own = DI("x_own", [8, 128, T2])
    x_full = DI("x_full", [8, 128, T])
    wq = DI("wq", [8, 128, C])
    wkvd = DI("wkvd", [8, 128, KVR])
    wkvu_k = DI("wkvu_k", [128, C])
    wkvu_v = DI("wkvu_v", [128, C])
    wout = DI("wout", [8, 128, C])
    wgT = DI("wgT", [8, 128, FH], I32)
    wuT = DI("wuT", [8, 128, FH], I32)
    wdT = DI("wdT", [24, 128, C], I32)
    inpT = DI("inpT", [8, 128, EF])
    xpT = DI("xpT", [8, 128, 96])
    dtT = DI("dtT", [64, EO])
    outT = DI("outT", [8, 128, C])
    a_log = DI("a_log", [128, 128])
    convw = DI("convw", [128, 32])
    convb = DI("convb", [128, 8])
    dtb = DI("dtb", [128, 8])
    dpar = DI("dpar", [128, 8])
    ones128 = DI("ones128", [128, 128], BF16)
    eye128 = DI("eye128", [128, 128], BF16)
    sel2 = DI("sel2", [2, 128], BF16)
    mask_a = DI("mask_a", [128, 128], BF16)
    mask_b = DI("mask_b", [128, 128], BF16)

    y_own = nc.dram_tensor("y_own", [8, 128, T2], F32, kind="ExternalOutput")

    dbg_t = {}
    for name, shape, dt in [
        ("d_xhat1", [8, 128, T], BF16), ("d_q", [8, 128, T2], BF16),
        ("d_k", [8, 128, T], BF16), ("d_x1", [8, 128, T2], F32),
        ("d_x2", [8, 128, T2], F32), ("d_dbc", [128, T], F32),
        ("d_delta", [8, 128, T], F32), ("d_xcs", [8, 128, T], BF16),
        ("d_x3", [8, 128, T2], F32), ("d_ymul", [8, 128, T], BF16),
    ]:
        if name in dbg:
            dbg_t[name] = nc.dram_tensor(name, shape, dt, kind="ExternalOutput")

    kT_scr = nc.dram_tensor("kT_scr", [8, 128, T], BF16)
    v_scr = nc.dram_tensor("v_scr", [32, 128, C], BF16)
    q_scr = nc.dram_tensor("q_scr", [8, 128, T2], BF16)
    wg_scr = nc.dram_tensor("wg_scr", [24, 8, 128, 128], BF16)
    wu_scr = nc.dram_tensor("wu_scr", [24, 8, 128, 128], BF16)
    wd_scr = nc.dram_tensor("wd_scr", [8, 24, 128, 128], BF16)
    xc_scr = nc.dram_tensor("xc_scr", [8, 128, T], BF16)
    zs_scr = nc.dram_tensor("zs_scr", [8, 128, T], BF16)
    dl_scr = nc.dram_tensor("dl_scr", [8, 128, T], F32)
    ym_scr = nc.dram_tensor("ym_scr", [8, 128, T], BF16)
    park_scr = nc.dram_tensor("park_scr", [8, 128, T2], F32)
    ln3_in = nc.dram_tensor("ln3_in", [1025, T2], BF16)
    ln3_out = nc.dram_tensor("ln3_out", [2, 1025, T2], BF16)
    dbc_in = nc.dram_tensor("dbc_in", [96, T], F32)
    dbc_out = nc.dram_tensor("dbc_out", [96, T], F32)
    dbc_bf = nc.dram_tensor("dbc_bf", [32, T], BF16)
    rs_in = nc.dram_tensor("rs_in", [2, 8, 128, T2], F32)
    rs_out = nc.dram_tensor("rs_out", [8, 128, T2], F32)

    P = Prog(nc, E)
    sb_frees = {}
    sb_order = []

    def SA(name, shape, dtype):
        cm = nc.sbuf_tensor(name, list(shape), dtype)
        t = cm.__enter__()
        sb_frees[name] = cm
        sb_order.append(name)
        return t

    def SF(*names):
        for n in sorted(names, key=lambda n: -sb_order.index(n)):
            assert n == sb_order[-1], f"non-LIFO free {n} vs stack {sb_order[-6:]}"
            sb_order.pop()
            sb_frees.pop(n).__exit__(None, None, None)

    PS = nc.psum_tensor("PS", [128, 4096], F32).__enter__()

    ones_sb = SA("ones_sb", [128, 128], BF16)
    eye_sb = SA("eye_sb", [128, 128], BF16)
    sel2_sb = SA("sel2_sb", [2, 128], BF16)
    maska_sb = SA("maska_sb", [128, 128], BF16)
    maskb_sb = SA("maskb_sb", [128, 128], BF16)
    class _RowMap:
        def __init__(self, raw, width, slots):
            self.raw, self.w, self.slots = raw, width, slots
        def __getitem__(self, key):
            rs, cs = key
            off = self.slots[rs.start]
            a = cs.start or 0
            b = cs.stop if cs.stop is not None else self.w
            return self.raw[0:1, off + a:off + b]

    rows_raw = SA("rows_raw", [1, 3 * T], BF16)
    rows_sb = _RowMap(rows_raw, T, {0: 0, 2: 0, 4: 0, 6: 0, 5: T, 1: 2 * T, 3: 2 * T})
    csum_raw = SA("csum_raw", [1, 8192], BF16)
    csum_sb = _RowMap(csum_raw, 8192, {0: 0, 1: 3072, 2: 6144, 3: 7168, 4: 6144})
    srow = SA("srow", [1, 2048], F32)
    scol = SA("scol", [128, 160], F32)
    sbcol = SA("sbcol", [128, 64], BF16)
    c0s = SA("c0s", [1, 8], F32)
    c0b = SA("c0b", [1, 2], BF16)
    cvec_sb = SA("cvec_sb", [128, 8], F32)
    amat_sb = SA("amat_sb", [128, 128], F32)
    convw_sb = SA("convw_sb", [128, 32], F32)
    convb_sb = SA("convb_sb", [128, 8], F32)
    dtb_sb = SA("dtb_sb", [128, 8], F32)
    dpar_sb = SA("dpar_sb", [128, 8], F32)
    state_sb = SA("state_sb", [128, 128], F32)
    xfm_h = [SA("x_fm", [128, 8 * T2], F32)]

    def xfm(c):
        return xfm_h[0][:, c * T2:(c + 1) * T2]

    op, wait, bar = P.op, P.wait, P.bar

    def lp(fn):
        def go(e):
            with nc.allow_low_precision(reason="intentional bf16 elementwise"):
                return fn(e)
        return go

    al_sb = SA("al_sb", [128, 128], F32)
    for dst, src in [(ones_sb, ones128), (eye_sb, eye128), (sel2_sb, sel2),
                     (maska_sb, mask_a), (maskb_sb, mask_b), (convw_sb, convw),
                     (convb_sb, convb), (dtb_sb, dtb), (dpar_sb, dpar),
                     (al_sb, a_log)]:
        op("qs", lambda e, d=dst, s=src: e.dma_start(out=d[:], in_=s.ap()))
    wait("scalar", "qs")
    op("ss", lambda e: e.activation(amat_sb[:], al_sb[:], Act.Exp))
    wait("vector", "ss")
    op("vs", lambda e: e.tensor_scalar_mul(amat_sb[:], amat_sb[:], -1.0))
    op("vs", lambda e: e.memset(state_sb[:], 0.0))
    op("vs", lambda e: e.memset(cvec_sb[:, 1:2], EPS))

    wi = [SA("p0_wi0", [128, FH], I32), SA("p0_wi1", [128, FH], I32)]
    wb = [SA("p0_wb0", [128, FH], BF16), SA("p0_wb1", [128, FH], BF16)]
    conv_sets = [(wgT, wg_scr, 0, 8, FH), (wuT, wu_scr, 1, 8, FH),
                 (wdT, wd_scr, None, 24, C)]
    conv_v = {}
    store_g = {}
    colsum_t = {}
    it = 0
    for src, dst, crow, nchunk, w in conv_sets:
        for c in range(nchunk):
            b = it % 2
            if it >= 2:
                wait("sync", "vs", conv_v[it - 2])
            op("qs", lambda e, b=b, src=src, c=c, w=w:
               e.dma_start(out=wi[b][:, 0:w], in_=src.ap()[c]))
            wait("vector", "qs")
            if it >= 2:
                wait("vector", "qg", store_g[it - 2])
                if it - 2 in colsum_t:
                    wait("vector", "ts", colsum_t[it - 2])
            conv_v[it] = op("vs", lambda e, b=b, w=w:
                            e.tensor_copy(wb[b][:, 0:w], wi[b][:, 0:w]))
            wait("gpsimd", "vs", conv_v[it])
            if crow is None:
                store_g[it] = op("qg", lambda e, b=b, dst=dst, c=c:
                                 e.dma_start(out=dst.ap()[:, c].transpose([1, 0, 2]),
                                             in_=wb[b][:, 0:C].rearrange("p (f j) -> p f j", j=128)))
            else:
                store_g[it] = op("qg", lambda e, b=b, dst=dst, c=c:
                                 e.dma_start(out=dst.ap()[:, c].transpose([1, 0, 2]),
                                             in_=wb[b][:, 0:FH].rearrange("p (h j) -> p h j", j=128)))
            if crow is not None:
                wait("tensor", "vs", conv_v[it])
                for h in range(6):
                    op("ts", lambda e, b=b, h=h, c=c, nchunk=nchunk:
                       e.matmul(PS[0:1, h * 512:(h + 1) * 512], ones_sb[:, 0:1],
                                wb[b][:, h * 512:(h + 1) * 512],
                                start=(c == 0), stop=(c == nchunk - 1),
                                skip_group_check=True))
                colsum_t[it] = P.ctrs["ts"].n
            it += 1
        if crow is not None:
            wait("vector", "ts")
            op("vs", lambda e, crow=crow: e.tensor_copy(
                csum_sb[crow:crow + 1, 0:FH], PS[0:1, 0:FH]))
            wait("tensor", "vs")
    SF("al_sb", "p0_wi0", "p0_wi1", "p0_wb0", "p0_wb1")
    bar()

    def ln_stats(xb_chunks, ntok, row_idx, nchunks=8, extra_sq=None):
        xsq = extra_sq
        nc_cols = ntok // 128
        nhalf = max(1, ntok // 2048)
        for half in range(nhalf):
            h0 = half * 2048
            hw = min(2048, ntok - h0)
            hc = hw // 128
            wait("tensor", "vs")
            for c in range(nchunks):
                wait("scalar", "vs")
                wait("scalar", "ts")
                op("ss", lambda e, c=c, h0=h0, hw=hw:
                   e.activation(xsq[:, 0:hw], xb_chunks(c)[:, h0:h0 + hw], Act.Square))
                wait("tensor", "ss")
                for t0 in range(0, hw, 512):
                    op("ts", lambda e, c=c, h0=h0, t0=t0:
                       e.matmul(PS[0:1, t0:t0 + 512], ones_sb[:, 0:1],
                                xb_chunks(c)[:, h0 + t0:h0 + t0 + 512],
                                start=(c == 0), stop=(c == nchunks - 1),
                                skip_group_check=True))
                    op("ts", lambda e, c=c, t0=t0:
                       e.matmul(PS[0:1, 2048 + t0:2048 + t0 + 512], ones_sb[:, 0:1],
                                xsq[:, t0:t0 + 512],
                                start=(c == 0), stop=(c == nchunks - 1),
                                skip_group_check=True))
            wait("vector", "ts")
            wait("vector", "qs")
            op("vs", lambda e, hw=hw: e.tensor_copy(srow[0:1, 0:hw], PS[0:1, 0:hw]))
            wait("sync", "vs")
            op("qs", lambda e, half=half, hc=hc, hw=hw: e.dma_start(
                out=scol[:, half * 16:half * 16 + hc], in_=srow[0:1, 0:hw]))
            qsx = P.ctrs["qs"].n
            wait("vector", "qs", qsx)
            op("vs", lambda e, hw=hw: e.tensor_copy(srow[0:1, 0:hw],
                                                    PS[0:1, 2048:2048 + hw]))
            wait("sync", "vs")
            op("qs", lambda e, half=half, hc=hc, hw=hw: e.dma_start(
                out=scol[:, 32 + half * 16:32 + half * 16 + hc], in_=srow[0:1, 0:hw]))
        wait("vector", "qs")
        ncc = nc_cols
        op("vs", lambda e: e.tensor_scalar_mul(
            scol[:, 64:64 + ncc], scol[:, 0:ncc], 1.0 / 1024.0))
        op("vs", lambda e: e.tensor_mul(
            scol[:, 96:96 + ncc], scol[:, 64:64 + ncc], scol[:, 64:64 + ncc]))
        op("vs", lambda e: e.scalar_tensor_tensor(
            scol[:, 96:96 + ncc], scol[:, 32:32 + ncc], 1.0 / 1024.0,
            scol[:, 96:96 + ncc], Alu.mult, Alu.subtract))
        wait("scalar", "vs")
        op("ss", lambda e: e.activation(
            scol[:, 96:96 + ncc], scol[:, 96:96 + ncc], Act.Ln,
            bias=cvec_sb[:, 1:2]))
        op("ss", lambda e: e.activation(
            scol[:, 96:96 + ncc], scol[:, 96:96 + ncc], Act.Exp, scale=-0.5))
        wait("vector", "ss")
        op("vs", lambda e: e.scalar_tensor_tensor(
            scol[:, 128:128 + ncc], scol[:, 64:64 + ncc], -1.0,
            scol[:, 96:96 + ncc], Alu.mult, Alu.mult))
        op("vs", lambda e: e.tensor_copy(sbcol[:, 0:ncc], scol[:, 96:96 + ncc]))
        op("vs", lambda e: e.tensor_copy(sbcol[:, 32:32 + ncc], scol[:, 128:128 + ncc]))
        wait("sync", "vs")
        for half in range(nhalf):
            h0 = half * 2048
            hw = min(2048, ntok - h0)
            hc = hw // 128
            op("qs", lambda e, h0=h0, hw=hw, half=half, hc=hc: e.dma_start(
                out=rows_sb[5:6, h0:h0 + hw],
                in_=sbcol[:, half * 16:half * 16 + hc]))
            op("qs", lambda e, h0=h0, hw=hw, half=half, hc=hc, r=row_idx: e.dma_start(
                out=rows_sb[r:r + 1, h0:h0 + hw],
                in_=sbcol[:, 32 + half * 16:32 + half * 16 + hc]))
        wait("tensor", "qs")
        wait("vector", "qs")

    def bcast_row(row_ap, ntok, dst_sb, pbank=0):
        for t0 in range(0, ntok, 512):
            bk = pbank + (t0 // 512) % 2
            wait("tensor", "vs")
            op("ts", lambda e, t0=t0, bk=bk: e.matmul(
                PS[:, bk * 512:(bk + 1) * 512], ones_sb[0:1, :],
                row_ap[:, t0:t0 + 512], start=True, stop=True))
            wait("vector", "ts")
            op("vs", lambda e, t0=t0, bk=bk: e.tensor_copy(
                dst_sb[:, t0:t0 + 512], PS[:, bk * 512:(bk + 1) * 512]))

    for c in range(8):
        op("qs", lambda e, c=c: e.dma_start(out=xfm(c), in_=x_own.ap()[c]))
    q_xown = P.ctrs["qs"].n

    xb1 = SA("xb1", [128, 8 * T2], BF16)
    xsq_a = SA("xsq_a", [128, T2], BF16)
    rbc_a = SA("rbc_a", [128, T], BF16)
    wait("vector", "qs", q_xown)
    for c in range(8):
        op("vs", lambda e, c=c: e.tensor_copy(xb1[:, c * T2:(c + 1) * T2], xfm(c)))
    ln_stats(lambda c: xb1[:, c * T2:(c + 1) * T2], T2, 0, extra_sq=xsq_a)
    bcast_row(rows_sb[5:6, :], T2, rbc_a)
    wait("vector", "vs")
    for c in range(8):
        op("vs", lambda e, c=c: e.tensor_mul(
            xb1[:, c * T2:(c + 1) * T2], xb1[:, c * T2:(c + 1) * T2], rbc_a[:, 0:T2]))
    v_xhat1 = P.ctrs["vs"].n

    wqf = [SA("wqf0", [128, C], F32), SA("wqf1", [128, C], F32)]
    wqb = SA("wqb", [128, 8 * 128], BF16)
    wqb8 = SA("wqb8", [8, 1], F32)
    qout = SA("qout", [128, T2], BF16)
    qsq = SA("qsq", [128, T2], BF16)
    wq_all = SA("wq_all", [128, 8 * C], BF16)
    conv_done = []
    for c in range(8):
        b = c % 2
        if c >= 2:
            wait("sync", "vs", conv_done[c - 2])
        op("qs", lambda e, c=c, b=b: e.dma_start(out=wqf[b][:], in_=wq.ap()[c]))
        wait("vector", "qs")
        conv_done.append(op("vs", lambda e, c=c, b=b: e.tensor_copy(
            wq_all[:, c * C:(c + 1) * C], wqf[b][:])))
        wait("tensor", "vs", conv_done[-1])
        for h in range(2):
            op("ts", lambda e, c=c, h=h: e.matmul(
                PS[0:1, h * 512:(h + 1) * 512], ones_sb[:, 0:1],
                wq_all[:, c * C + h * 512:c * C + (h + 1) * 512],
                start=(c == 0), stop=(c == 7), skip_group_check=True))
    wait("vector", "ts")
    op("vs", lambda e: e.tensor_copy(csum_sb[2:3, 0:C], PS[0:1, 0:C]))
    wait("tensor", "vs")
    wait("tensor", "vs", v_xhat1)
    qevs = []
    for fc in range(8):
        wait("scalar", "qg")
        for t0 in range(0, T2, 512):
            bk = 2 + ((fc * 4 + t0 // 512) % 2)
            if len(qevs) >= 2:
                wait("tensor", "ss", qevs[-2])
            for cc in range(8):
                op("ts", lambda e, fc=fc, t0=t0, cc=cc, bk=bk: e.matmul(
                    PS[:, bk * 512:(bk + 1) * 512],
                    wq_all[:, cc * C + fc * 128:cc * C + fc * 128 + 128],
                    xb1[:, cc * T2 + t0:cc * T2 + t0 + 512],
                    start=(cc == 0), stop=False, skip_group_check=True))
            op("ts", lambda e, fc=fc, t0=t0, bk=bk: e.matmul(
                PS[:, bk * 512:(bk + 1) * 512],
                csum_sb[2:3, fc * 128:fc * 128 + 128],
                rows_sb[0:1, t0:t0 + 512],
                start=False, stop=True, skip_group_check=True))
            wait("scalar", "ts")
            qevs.append(op("ss", lambda e, t0=t0, bk=bk: e.activation(
                qout[:, t0:t0 + 512], PS[:, bk * 512:(bk + 1) * 512], Act.Copy)))
        wait("gpsimd", "ss")
        op("qg", lambda e, fc=fc: e.dma_start(out=q_scr.ap()[fc], in_=qout[:]))
        wait("scalar", "ts")
        op("ss", lambda e: e.activation(qsq[:], qout[:], Act.Square))
        wait("tensor", "ss")
        for t0 in range(0, T2, 512):
            op("ts", lambda e, fc=fc, t0=t0: e.matmul(
                PS[0:1, 2048 + t0:2048 + t0 + 512], ones_sb[:, 0:1],
                qsq[:, t0:t0 + 512], start=(fc == 0), stop=(fc == 7),
                skip_group_check=True))
        if "d_q" in dbg_t:
            op("qg", lambda e, fc=fc: e.dma_start(out=dbg_t["d_q"].ap()[fc], in_=qout[:]))
    wait("vector", "ts")
    op("vs", lambda e: e.reduce_max(c0s[0:1, 0:1], PS[0:1, 2048:4096],
                                    axis=mybir.AxisListType.X))
    SF("xb1", "wqf0", "wqf1", "wqb", "wqb8", "qout", "qsq", "wq_all", "rbc_a", "xsq_a")
    bar()

    xbf = SA("xbf", [128, 8 * T], BF16)
    xfull_f = [SA("xfull_f0", [128, T], F32)]
    xsq_b = SA("xsq_b", [128, T2], BF16)
    rbc_b = SA("rbc_b", [128, T], BF16)

    cd = []
    for c in range(8):
        if c >= 1:
            wait("sync", "vs", cd[c - 1])
        op("qs", lambda e, c=c: e.dma_start(out=xfull_f[0][:], in_=x_full.ap()[c]))
        wait("vector", "qs")
        cd.append(op("vs", lambda e, c=c: e.tensor_copy(
            xbf[:, c * T:(c + 1) * T], xfull_f[0][:])))
    ln_stats(lambda c: xbf[:, c * T:(c + 1) * T], T, 1, extra_sq=xsq_b)
    bcast_row(rows_sb[5:6, :], T, rbc_b)
    for c in range(8):
        op("vs", lambda e, c=c: e.tensor_mul(
            xbf[:, c * T:(c + 1) * T], xbf[:, c * T:(c + 1) * T], rbc_b[:, 0:T]))
    v_xhatf = P.ctrs["vs"].n
    if "d_xhat1" in dbg_t:
        wait("gpsimd", "vs", v_xhatf)
        for c in range(8):
            op("qg", lambda e, c=c: e.dma_start(
                out=dbg_t["d_xhat1"].ap()[c], in_=xbf[:, c * T:(c + 1) * T]))
    SF("xfull_f0", "xsq_b", "rbc_b")

    wkvd_all = SA("wkvd_all", [128, 8 * KVR], BF16)
    wkvdf = SA("wkvdf", [128, KVR], F32)
    for c in range(8):
        wait("sync", "vs")
        op("qs", lambda e, c=c: e.dma_start(out=wkvdf[:], in_=wkvd.ap()[c]))
        wait("vector", "qs")
        op("vs", lambda e, c=c: e.tensor_copy(
            wkvd_all[:, c * KVR:(c + 1) * KVR], wkvdf[:]))
        wait("tensor", "vs")
        op("ts", lambda e, c=c: e.matmul(
            PS[0:1, 0:KVR], ones_sb[:, 0:1], wkvd_all[:, c * KVR:(c + 1) * KVR],
            start=(c == 0), stop=(c == 7), skip_group_check=True))
    wait("vector", "ts")
    op("vs", lambda e: e.tensor_copy(csum_sb[3:4, 0:KVR], PS[0:1, 0:KVR]))
    wait("tensor", "vs")
    lat_sb = SA("lat_sb", [128, T], BF16)
    wait("tensor", "vs", v_xhatf)
    ev = []
    for t0 in range(0, T, 512):
        bk = 2 + (t0 // 512) % 2
        if len(ev) >= 2:
            wait("tensor", "ss", ev[-2])
        for cc in range(8):
            op("ts", lambda e, t0=t0, cc=cc, bk=bk: e.matmul(
                PS[:, bk * 512:(bk + 1) * 512],
                wkvd_all[:, cc * KVR:(cc + 1) * KVR],
                xbf[:, cc * T + t0:cc * T + t0 + 512],
                start=(cc == 0), stop=False, skip_group_check=True))
        op("ts", lambda e, t0=t0, bk=bk: e.matmul(
            PS[:, bk * 512:(bk + 1) * 512], csum_sb[3:4, 0:KVR],
            rows_sb[1:2, t0:t0 + 512], start=False, stop=True,
            skip_group_check=True))
        wait("scalar", "ts")
        ev.append(op("ss", lambda e, t0=t0, bk=bk: e.activation(
            lat_sb[:, t0:t0 + 512], PS[:, bk * 512:(bk + 1) * 512], Act.Copy)))
    wkuk_sb = SA("wkuk_sb", [128, C], BF16)
    wkuv_sb = SA("wkuv_sb", [128, C], BF16)
    kchunk = SA("kchunk", [128, T], BF16)
    wkf = kchunk[:].bitcast(F32)[:, 0:C]
    op("qs", lambda e: e.dma_start(out=wkf, in_=wkvu_k.ap()))
    wait("vector", "qs")
    op("vs", lambda e: e.tensor_copy(wkuk_sb[:], wkf))
    wait("sync", "vs")
    op("qs", lambda e: e.dma_start(out=wkf, in_=wkvu_v.ap()))
    wait("vector", "qs")
    v_wkuv = op("vs", lambda e: e.tensor_copy(wkuv_sb[:], wkf))
    ksq = SA("ksq", [128, T2], BF16)
    wait("tensor", "vs", v_wkuv)
    wait("tensor", "ss")
    for fc in range(8):
        wait("scalar", "qg")
        ev = []
        for t0 in range(0, T, 512):
            bk = 2 + (t0 // 512) % 2
            if len(ev) >= 2:
                wait("tensor", "ss", ev[-2])
            op("ts", lambda e, fc=fc, t0=t0, bk=bk: e.matmul(
                PS[:, bk * 512:(bk + 1) * 512],
                wkuk_sb[:, fc * 128:(fc + 1) * 128],
                lat_sb[:, t0:t0 + 512], start=True, stop=True))
            wait("scalar", "ts")
            ev.append(op("ss", lambda e, t0=t0, bk=bk: e.activation(
                kchunk[:, t0:t0 + 512], PS[:, bk * 512:(bk + 1) * 512], Act.Copy)))
        wait("gpsimd", "ss")
        op("qg", lambda e, fc=fc: e.dma_start(out=kT_scr.ap()[fc], in_=kchunk[:]))
        if "d_k" in dbg_t:
            op("qg", lambda e, fc=fc: e.dma_start(out=dbg_t["d_k"].ap()[fc], in_=kchunk[:]))
        for half in range(2):
            wait("scalar", "ts")
            op("ss", lambda e, half=half: e.activation(
                ksq[:], kchunk[:, half * 2048:(half + 1) * 2048], Act.Square))
            wait("tensor", "ss")
            for t0 in range(0, 2048, 512):
                op("ts", lambda e, fc=fc, t0=t0, half=half: e.matmul(
                    PS[0:1, 2048 + t0:2048 + t0 + 512],
                    ones_sb[:, 0:1], ksq[:, t0:t0 + 512],
                    start=(fc == 0 and half == 0), stop=(fc == 7 and half == 1),
                    skip_group_check=True))
    wait("vector", "ts")
    op("vs", lambda e: e.reduce_max(c0s[0:1, 1:2], PS[0:1, 2048:4096],
                                    axis=mybir.AxisListType.X))
    vchunk = SA("vchunk", [128, C], BF16)
    wait("tensor", "vs")
    ev = []
    for tc in range(32):
        bk = 2 + (tc % 2) * 2
        wait("scalar", "qg")
        if len(ev) >= 2:
            wait("tensor", "ss", ev[-2])
        for d0 in range(0, C, 512):
            op("ts", lambda e, tc=tc, d0=d0, bk=bk: e.matmul(
                PS[:, (bk + d0 // 512) * 512:(bk + d0 // 512 + 1) * 512],
                lat_sb[:, tc * 128:(tc + 1) * 128],
                wkuv_sb[:, d0:d0 + 512], start=True, stop=True))
        wait("scalar", "ts")
        ev.append(op("ss", lambda e, bk=bk: e.activation(
            vchunk[:], PS[:, bk * 512:(bk + 2) * 512], Act.Copy)))
        wait("gpsimd", "ss")
        op("qg", lambda e, tc=tc: e.dma_start(out=v_scr.ap()[tc], in_=vchunk[:]))
    op("vs", lambda e: e.tensor_mul(c0s[0:1, 2:3], c0s[0:1, 0:1], c0s[0:1, 1:2]))
    wait("scalar", "vs")
    op("ss", lambda e: e.activation(c0s[0:1, 3:4], c0s[0:1, 2:3], Act.Sqrt))
    wait("vector", "ss")
    op("vs", lambda e: e.tensor_scalar_mul(c0s[0:1, 4:5], c0s[0:1, 3:4], -QSCALE))
    op("vs", lambda e: e.tensor_copy(c0b[0:1, 0:1], c0s[0:1, 4:5]))
    wait("tensor", "vs")
    op("ts", lambda e: e.matmul(PS[:, 0:1], ones_sb[0:1, :], c0b[0:1, 0:1],
                                start=True, stop=True))
    wait("vector", "ts")
    op("vs", lambda e: e.tensor_copy(cvec_sb[:, 0:1], PS[:, 0:1]))
    SF("wkvd_all", "wkvdf", "lat_sb", "wkuk_sb", "wkuv_sb",
       "kchunk", "ksq", "vchunk")
    bar()
    SF("xbf")

    dens_sb = SA("dens_sb", [16, T2], BF16)
    ysb = SA("ysb", [128, 8 * T2], BF16)
    rcp2 = SA("rcp2", [2, T2], BF16)
    khp = [SA("khp0", [128, T], BF16), SA("khp1", [128, T], BF16)]
    qhp = [SA("qhp0", [128, T2], BF16), SA("qhp1", [128, T2], BF16)]
    vstrip = [SA("vstrip0", [128, 32 * 130], BF16), SA("vstrip1", [128, 32 * 130], BF16)]
    ptile = [SA("ptile0", [128, 1024], BF16), SA("ptile1", [128, 1024], BF16)]
    for b in range(2):
        op("vs", lambda e, b=b: e.memset(
            vstrip[b][:].rearrange("p (c w) -> p c w", w=130)[:, :, 64:65], 1.0))
        op("vs", lambda e, b=b: e.memset(
            vstrip[b][:].rearrange("p (c w) -> p c w", w=130)[:, :, 129:130], 1.0))
    v_ones = P.ctrs["vs"].n
    wait("sync", "qg")

    attn_iters = []
    for hp in range(8):
        for slot in range(4):
            for c in range(8 * slot + 8):
                attn_iters.append((hp, slot, c))

    qload_snap = {}
    vload_snap = {}
    pv_done_snap = {}
    exp_snap = {}
    maskv_snap = {}
    st_ready = {}
    for hp in range(8):
        bq = hp % 2
        if hp >= 2:
            wait("sync", "ts", pv_done_snap[(hp - 2, 3)])
        op("qs", lambda e, hp=hp, bq=bq: e.dma_start(out=qhp[bq][:],
                                                     in_=q_scr.ap()[hp]))
        op("qs", lambda e, hp=hp, bq=bq: e.dma_start(out=khp[bq][:],
                                                     in_=kT_scr.ap()[hp]))
        qload_snap[hp] = P.ctrs["qs"].n
        for slot in range(4):
            ext = 8 * slot + 8
            bv = (hp * 4 + slot) % 2
            if (hp * 4 + slot) >= 2:
                prev = (hp, slot - 2) if slot >= 2 else (hp - 1, slot + 2)
                wait("sync", "ts", pv_done_snap[prev])
            for h in range(2):
                op("qs", lambda e, hp=hp, ext=ext, bv=bv, h=h: e.dma_start(
                    out=vstrip[bv][:].rearrange("p (c w) -> p c w", w=130)
                    [:, 0:ext, h * 65:h * 65 + 64],
                    in_=v_scr.ap()[0:ext, :, hp * 128 + h * 64:hp * 128 + h * 64 + 64]
                    .transpose([1, 0, 2])))
            vload_snap[(hp, slot)] = P.ctrs["qs"].n
            obank = 4 + 2 * (slot % 2)
            pv_args = {}
            for c in range(ext):
                it = (hp, slot, c)
                jmin = max(0, (c - 8 * slot) // 2)
                W = 512 - jmin * 128
                ping = c % 2
                wait("tensor", "qs", qload_snap[hp])
                wait("tensor", "qs", vload_snap[(hp, slot)])
                wait("tensor", "vs", v_ones)
                if (hp, slot, c - 2) in exp_snap:
                    wait("tensor", "ss", exp_snap[(hp, slot, c - 2)])
                for h in range(2):
                    op("ts", lambda e, hp=hp, slot=slot, c=c, h=h, jmin=jmin,
                       ping=ping, bq=bq, W=W: e.matmul(
                           PS[:, (ping * 2 + h) * 512 + jmin * 128:
                              (ping * 2 + h) * 512 + 512],
                           khp[bq][h * 64:(h + 1) * 64,
                                   c * 128:c * 128 + 128],
                           qhp[bq][h * 64:(h + 1) * 64,
                                   slot * 512 + jmin * 128:slot * 512 + 512],
                           start=True, stop=True, tile_position=(h * 64, 0),
                           skip_group_check=True))
                st_ready[it] = P.ctrs["ts"].n

                def emit_pv(cc):
                    itp = (hp, slot, cc)
                    if itp in maskv_snap:
                        wait("tensor", "vs", maskv_snap[itp])
                    else:
                        wait("tensor", "ss", exp_snap[itp])
                    jm, bvv = pv_args[cc]
                    for h in range(2):
                        op("ts", lambda e, cc=cc, h=h, jm=jm, bvv=bvv, ob=obank,
                           ex=ext: e.matmul(
                            PS[0:65, (ob + h) * 512 + jm * 128:
                               (ob + h) * 512 + 512],
                            vstrip[bvv][:, cc * 130 + h * 65:cc * 130 + h * 65 + 65],
                            ptile[cc % 2][:, h * 512 + jm * 128:h * 512 + 512],
                            start=(cc == 0), stop=(cc == ex - 1),
                            skip_group_check=True))
                pv_args[c] = (jmin, bv)
                if c >= 1:
                    emit_pv(c - 1)
                    pv_done_snap[(hp, slot)] = P.ctrs["ts"].n

                wait("scalar", "ts", st_ready[it])
                exp_snap[it] = op("ss", lambda e, ping=ping, jmin=jmin, W=W, c=c:
                                  e.activation(
                    ptile[c % 2][:, 0:1024].rearrange("p (h w) -> p h w", h=2)
                    [:, :, jmin * 128:512],
                    PS[:, ping * 1024:ping * 1024 + 1024]
                    .rearrange("p (h w) -> p h w", h=2)[:, :, jmin * 128:512],
                    Act.Exp, bias=cvec_sb[:, 0:1], scale=QSCALE), sw=False)
                mj = None
                if c % 2 == 0 and 0 <= (c // 2) - 4 * slot <= 3:
                    mj = ((c // 2) - 4 * slot, maska_sb)
                if c % 2 == 1 and 0 <= ((c - 1) // 2) - 4 * slot <= 3:
                    mj = (((c - 1) // 2) - 4 * slot, maskb_sb)
                if mj is not None:
                    j, msk = mj
                    wait("vector", "ss", exp_snap[it])
                    for h in range(2):
                        op("vs", lambda e, h=h, j=j, msk=msk, c=c: e.tensor_mul(
                            ptile[c % 2][:, h * 512 + j * 128:h * 512 + (j + 1) * 128],
                            ptile[c % 2][:, h * 512 + j * 128:h * 512 + (j + 1) * 128],
                            msk[:]), sw=False)
                    maskv_snap[it] = P.ctrs["vs"].n
            emit_pv(ext - 1)
            pv_done_snap[(hp, slot)] = P.ctrs["ts"].n
            wait("vector", "ts", pv_done_snap[(hp, slot)])
            for h in range(2):
                op("vs", lambda e, hp=hp, slot=slot, h=h, obank=obank: e.tensor_copy(
                    ysb[h * 64:(h + 1) * 64, hp * T2 + slot * 512:hp * T2 + slot * 512 + 512],
                    PS[0:64, (obank + h) * 512:(obank + h) * 512 + 512]))
            op("vs", lambda e, obank=obank: e.tensor_copy(
                ptile[0][64:65, 0:1024],
                PS[64:65, obank * 512:obank * 512 + 1024]))
            wait("gpsimd", "vs")
            op("qg", lambda e, hp=hp, slot=slot: e.dma_start(
                out=dens_sb[2 * hp:2 * hp + 2, slot * 512:slot * 512 + 512],
                in_=ptile[0][64:65, 0:1024]))
            dens_g = P.ctrs["qg"].n
            P.wait("scalar", "qg", dens_g)
            wait("tensor", "vs")
    wait("vector", "qg")
    op("vs", lp(lambda e: e.reciprocal(dens_sb[:, :], dens_sb[:, :])))
    v_rcp = P.ctrs["vs"].n
    wait("tensor", "vs", v_rcp)
    wait("tensor", "ss")
    for hp in range(8):
        wait("sync", "vs")
        op("qs", lambda e, hp=hp: e.dma_start(
            out=rcp2[:], in_=dens_sb[2 * hp:2 * hp + 2, :]))
        q_r = P.ctrs["qs"].n
        wait("tensor", "qs", q_r)
        for t0 in range(0, T2, 512):
            bk = (t0 // 512) % 4
            wait("tensor", "vs")
            op("ts", lambda e, t0=t0, bk=bk: e.matmul(
                PS[:, bk * 512:(bk + 1) * 512], sel2_sb[:, :],
                rcp2[0:2, t0:t0 + 512], start=True, stop=True))
            wait("vector", "ts")
            op("vs", lambda e, hp=hp, t0=t0, bk=bk: e.tensor_mul(
                ysb[:, hp * T2 + t0:hp * T2 + t0 + 512],
                ysb[:, hp * T2 + t0:hp * T2 + t0 + 512],
                PS[:, bk * 512:(bk + 1) * 512]))
    SF("qhp0", "qhp1", "vstrip0", "vstrip1", "ptile0", "ptile1", "khp0", "khp1")
    woutf = SA("woutf", [128, C], F32)
    wout_all = SA("wout_all", [128, 8 * C], BF16)
    for hc in range(8):
        wait("sync", "vs")
        op("qs", lambda e, hc=hc: e.dma_start(out=woutf[:], in_=wout.ap()[hc]))
        wait("vector", "qs")
        op("vs", lambda e, hc=hc: e.tensor_copy(
            wout_all[:, hc * C:(hc + 1) * C], woutf[:]))
    v_wout = P.ctrs["vs"].n
    wait("tensor", "vs", v_wout)
    evs = []
    for cc in range(8):
        for slot in range(4):
            bk = 4 + (cc * 4 + slot) % 2
            if len(evs) >= 2:
                wait("tensor", "vs", evs[-2])
            for hc in range(8):
                op("ts", lambda e, cc=cc, slot=slot, hc=hc, bk=bk: e.matmul(
                    PS[:, bk * 512:(bk + 1) * 512],
                    wout_all[:, hc * C + cc * 128:hc * C + cc * 128 + 128],
                    ysb[:, hc * T2 + slot * 512:hc * T2 + slot * 512 + 512],
                    start=(hc == 0), stop=(hc == 7), skip_group_check=True))
            wait("vector", "ts")
            evs.append(op("vs", lambda e, cc=cc, slot=slot, bk=bk: e.tensor_add(
                xfm(cc)[:, slot * 512:slot * 512 + 512],
                xfm(cc)[:, slot * 512:slot * 512 + 512],
                PS[:, bk * 512:(bk + 1) * 512])))
    SF("woutf", "wout_all", "rcp2", "ysb", "dens_sb")
    if "d_x1" in dbg_t:
        wait("gpsimd", "vs")
        for c in range(8):
            op("qg", lambda e, c=c: e.dma_start(out=dbg_t["d_x1"].ap()[c], in_=xfm(c)))
    bar()

    def emit_ffn(ln_row, dbg_name, sfx=""):
        xb = SA("f_xb" + sfx, [128, 8 * T2], BF16)
        xsq = SA("f_xsq" + sfx, [128, T2], BF16)
        rbc = SA("f_rbc" + sfx, [128, T2], BF16)
        wait("vector", "qs")
        for c in range(8):
            op("vs", lambda e, c=c: e.tensor_copy(xb[:, c * T2:(c + 1) * T2], xfm(c)))
        ln_stats(lambda c: xb[:, c * T2:(c + 1) * T2], T2, ln_row, extra_sq=xsq)
        bcast_row(rows_sb[5:6, :], T2, rbc)
        for c in range(8):
            op("vs", lambda e, c=c: e.tensor_mul(
                xb[:, c * T2:(c + 1) * T2], xb[:, c * T2:(c + 1) * T2], rbc[:, 0:T2]))
        v_xh = P.ctrs["vs"].n
        wait("tensor", "vs", v_xh)
        wait("sync", "vs", v_xh)
        wait("gpsimd", "vs", v_xh)
        SF("f_xsq" + sfx, "f_rbc" + sfx)

        wgt = [SA("f_wg0" + sfx, [128, 1024], BF16), SA("f_wg1" + sfx, [128, 1024], BF16)]
        wdt = [SA("f_wd0" + sfx, [128, 3072], BF16)]
        wu0 = SA("f_wu0" + sfx, [128, 1024], BF16)
        wut = [wu0[:], wdt[0][:, 0:1024]]
        hact = SA("f_hact" + sfx, [128, 24 * 1024], BF16)
        tdone = {}
        hdone = {}
        dcons = {}
        for tt in range(2):
            t0 = tt * 1024
            for h in range(24):
                bw = h % 2
                key = (tt, h)
                if h >= 2:
                    wait("sync", "ts", tdone[(tt, h - 2)])
                elif tt:
                    wait("sync", "ts")
                op("qs", lambda e, h=h, bw=bw: e.dma_start(
                    out=wgt[bw][:].rearrange("p (c j) -> p c j", j=128),
                    in_=wg_scr.ap()[h].transpose([1, 0, 2])))
                op("qs", lambda e, h=h, bw=bw: e.dma_start(
                    out=wut[bw].rearrange("p (c j) -> p c j", j=128),
                    in_=wu_scr.ap()[h].transpose([1, 0, 2])))
                wl = P.ctrs["qs"].n
                wait("tensor", "qs", wl)
                gb = (h % 2) * 2048
                if (tt, h - 2) in hdone:
                    wait("tensor", "vs", hdone[(tt, h - 2)])
                elif tt and h <= 1:
                    wait("tensor", "vs", dcons[(tt - 1, 7)])
                for half in range(2):
                    hh = t0 + half * 512
                    for cc in range(8):
                        op("ts", lambda e, cc=cc, hh=hh, bw=bw, half=half, gb=gb: e.matmul(
                            PS[:, gb + half * 512:gb + (half + 1) * 512],
                            wgt[bw][:, cc * 128:(cc + 1) * 128],
                            xb[:, cc * T2 + hh:cc * T2 + hh + 512],
                            start=(cc == 0), stop=False, skip_group_check=True))
                    op("ts", lambda e, h=h, hh=hh, half=half, ln_row=ln_row, gb=gb: e.matmul(
                        PS[:, gb + half * 512:gb + (half + 1) * 512],
                        csum_sb[0:1, h * 128:(h + 1) * 128],
                        rows_sb[ln_row:ln_row + 1, hh:hh + 512],
                        start=False, stop=True, skip_group_check=True))
                    for cc in range(8):
                        op("ts", lambda e, cc=cc, hh=hh, half=half, gb=gb, bw=bw: e.matmul(
                            PS[:, gb + 1024 + half * 512:gb + 1024 + (half + 1) * 512],
                            wut[bw][:, cc * 128:(cc + 1) * 128],
                            xb[:, cc * T2 + hh:cc * T2 + hh + 512],
                            start=(cc == 0), stop=False, skip_group_check=True))
                    t_up = op("ts", lambda e, h=h, hh=hh, half=half, ln_row=ln_row, gb=gb: e.matmul(
                        PS[:, gb + 1024 + half * 512:gb + 1024 + (half + 1) * 512],
                        csum_sb[1:2, h * 128:(h + 1) * 128],
                        rows_sb[ln_row:ln_row + 1, hh:hh + 512],
                        start=False, stop=True, skip_group_check=True))
                tdone[key] = t_up
                wait("scalar", "ts", t_up)
                s_g = op("ss", lambda e, h=h, gb=gb: e.activation(
                    hact[:, h * 1024:(h + 1) * 1024], PS[:, gb:gb + 1024],
                    Act.Silu, scale=FSC))
                wait("vector", "ss", s_g)
                hdone[key] = op("vs", lambda e, h=h, gb=gb: e.scalar_tensor_tensor(
                    hact[:, h * 1024:(h + 1) * 1024], PS[:, gb + 1024:gb + 2048],
                    FSC, hact[:, h * 1024:(h + 1) * 1024], Alu.mult, Alu.mult))
            wait("tensor", "vs", hdone[(tt, 23)])
            for cc in range(8):
                key = (tt, cc)
                db = (cc % 2) * 1024
                wait("sync", "ts")
                op("qs", lambda e, cc=cc: e.dma_start(
                    out=wdt[0][:].rearrange("p (h j) -> p h j", j=128),
                    in_=wd_scr.ap()[cc].transpose([1, 0, 2])))
                dl = P.ctrs["qs"].n
                wait("tensor", "qs", dl)
                prev2 = (tt, cc - 2) if cc >= 2 else ((tt - 1, 6 + cc) if tt else None)
                if prev2 is not None and prev2 in dcons:
                    wait("tensor", "vs", dcons[prev2])
                for half in range(2):
                    for h in range(24):
                        op("ts", lambda e, h=h, half=half, db=db: e.matmul(
                            PS[:, db + half * 512:db + (half + 1) * 512],
                            wdt[0][:, h * 128:(h + 1) * 128],
                            hact[:, h * 1024 + half * 512:h * 1024 + half * 512 + 512],
                            start=(h == 0), stop=(h == 23), skip_group_check=True))
                wait("vector", "ts")
                dcons[key] = op("vs", lambda e, cc=cc, t0=t0, db=db: e.scalar_tensor_tensor(
                    xfm(cc)[:, t0:t0 + 1024], PS[:, db:db + 1024],
                    FSC, xfm(cc)[:, t0:t0 + 1024], Alu.mult, Alu.add))
        SF("f_xb" + sfx, "f_wg0" + sfx, "f_wg1" + sfx, "f_wd0" + sfx,
           "f_wu0" + sfx, "f_hact" + sfx)
        if dbg_name in dbg_t:
            wait("gpsimd", "vs")
            for c in range(8):
                op("qg", lambda e, c=c, n=dbg_name: e.dma_start(
                    out=dbg_t[n].ap()[c], in_=xfm(c)))
        bar()

    emit_ffn(2, "d_x2", "a")

    xb3 = SA("m_xb3", [128, 8 * T2], BF16)
    xsq3 = SA("m_xsq3", [128, T2], BF16)
    rbc3 = SA("m_rbc3", [128, T2], BF16)
    wait("vector", "qs")
    for c in range(8):
        op("vs", lambda e, c=c: e.tensor_copy(xb3[:, c * T2:(c + 1) * T2], xfm(c)))
    ln_stats(lambda c: xb3[:, c * T2:(c + 1) * T2], T2, 6, extra_sq=xsq3)
    bcast_row(rows_sb[5:6, :], T2, rbc3)
    for c in range(8):
        op("vs", lambda e, c=c: e.tensor_mul(
            xb3[:, c * T2:(c + 1) * T2], xb3[:, c * T2:(c + 1) * T2], rbc3[:, 0:T2]))
    v_x3h = P.ctrs["vs"].n
    wait("gpsimd", "vs", v_x3h)
    for c in range(8):
        op("qg", lambda e, c=c: e.dma_start(
            out=ln3_in.ap()[c * 128:(c + 1) * 128, :], in_=xb3[:, c * T2:(c + 1) * T2]))
    op("qg", lambda e: e.dma_start(out=ln3_in.ap()[1024:1025, :], in_=rows_sb[6:7, 0:T2]))
    for c in range(8):
        op("qg", lambda e, c=c: e.dma_start(out=park_scr.ap()[c], in_=xfm(c)))
    g_ln3 = P.ctrs["qg"].n
    wait("gpsimd", "qg", g_ln3)
    bar()
    op("gc", lambda e: e.collective_compute(
        "AllGather", Alu.bypass, replica_groups=GROUPS,
        ins=[ln3_in.ap().opt()], outs=[ln3_out.ap().opt()]))
    SF("m_xb3", "m_xsq3", "m_rbc3")
    bar()

    rhsln = xfm_h[0][:].bitcast(BF16)
    for c in range(8):
        for b in range(2):
            op("qs", lambda e, c=c, b=b: e.dma_start(
                out=rhsln[:, c * T:(c + 1) * T]
                .rearrange("p (g j) -> p g j", j=128)[:, b::2, :],
                in_=ln3_out.ap()[b, c * 128:(c + 1) * 128, :]
                .rearrange("p (g j) -> p g j", j=128).transpose([0, 1, 2])))
    for b in range(2):
        op("qs", lambda e, b=b: e.dma_start(
            out=rows_sb[3:4, 0:T].rearrange("o (g j) -> o g j", j=128)[:, b::2, :],
            in_=ln3_out.ap()[b, 1024:1025, :].rearrange("o (g j) -> o g j", j=128)))
    q_rhs = P.ctrs["qs"].n

    inpf = SA("m_inpf", [128, 1024], F32)
    inpb = [SA("m_inpb0", [128, 1024], BF16), SA("m_inpb1", [128, 1024], BF16)]
    xzc = SA("m_xzc", [128, T], F32)
    cacc = SA("m_cacc", [128, T], F32)
    xcsb = SA("m_xcsb", [128, T], BF16)
    wait("tensor", "qs", q_rhs)
    evx = []
    for ec in range(16):
        bw = ec % 2
        if ec > 0:
            wait("scalar", "qg")
        wait("sync", "vs")
        op("qs", lambda e, ec=ec: e.dma_start(
            out=inpf[:].rearrange("p (c j) -> p c j", j=128),
            in_=inpT.ap()[:, :, ec * 128:(ec + 1) * 128].transpose([1, 0, 2])))
        wait("vector", "qs")
        v_cv = op("vs", lambda e, bw=bw: e.tensor_copy(inpb[bw][:], inpf[:]))
        wait("tensor", "vs", v_cv)
        for cc in range(8):
            op("ts", lambda e, cc=cc, bw=bw: e.matmul(
                PS[0:1, 3584:3584 + 128],
                ones_sb[:, 0:1], inpb[bw][:, cc * 128:(cc + 1) * 128],
                start=(cc == 0), stop=(cc == 7), skip_group_check=True))
        wait("vector", "ts")
        op("vs", lambda e, ec=ec: e.tensor_copy(
            csum_sb[4:5, ec * 128:(ec + 1) * 128], PS[0:1, 3584:3584 + 128]))
        wait("tensor", "vs")
        for t0 in range(0, T, 512):
            bk = (t0 // 512) % 2
            if len(evx) >= 2:
                wait("tensor", "ss", evx[-2])
            for cc in range(8):
                op("ts", lambda e, cc=cc, t0=t0, bw=bw, bk=bk: e.matmul(
                    PS[:, bk * 512:(bk + 1) * 512],
                    inpb[bw][:, cc * 128:(cc + 1) * 128],
                    rhsln[:, cc * T + t0:cc * T + t0 + 512],
                    start=(cc == 0), stop=False, skip_group_check=True))
            op("ts", lambda e, ec=ec, t0=t0, bk=bk: e.matmul(
                PS[:, bk * 512:(bk + 1) * 512],
                csum_sb[4:5, ec * 128:ec * 128 + 128],
                rows_sb[3:4, t0:t0 + 512], start=False, stop=True,
                skip_group_check=True))
            wait("scalar", "ts")
            if ec < 8:
                evx.append(op("ss", lambda e, t0=t0, bk=bk: e.activation(
                    xzc[:, t0:t0 + 512], PS[:, bk * 512:(bk + 1) * 512], Act.Copy)))
            else:
                evx.append(op("ss", lambda e, t0=t0, bk=bk: e.activation(
                    xcsb[:, t0:t0 + 512], PS[:, bk * 512:(bk + 1) * 512], Act.Silu)))
        if ec >= 8:
            wait("gpsimd", "ss")
            op("qg", lambda e, ec=ec: e.dma_start(out=zs_scr.ap()[ec - 8], in_=xcsb[:]))
            continue
        wait("vector", "ss")
        op("vs", lambda e, ec=ec: e.tensor_scalar_mul(
            cacc[:], xzc[:], convw_sb[:, ec * 4 + 3:ec * 4 + 4]))
        for k in range(3):
            sh = 3 - k
            op("vs", lambda e, ec=ec, k=k, sh=sh: e.scalar_tensor_tensor(
                cacc[:, sh:T], xzc[:, 0:T - sh], convw_sb[:, ec * 4 + k:ec * 4 + k + 1],
                cacc[:, sh:T], Alu.mult, Alu.add))
        v_conv = P.ctrs["vs"].n
        wait("scalar", "vs", v_conv)
        op("ss", lambda e, ec=ec: e.activation(
            xcsb[:], cacc[:], Act.Silu, bias=convb_sb[:, ec:ec + 1]))
        wait("gpsimd", "ss")
        op("qg", lambda e, ec=ec: e.dma_start(out=xc_scr.ap()[ec], in_=xcsb[:]))
        if "d_xcs" in dbg_t:
            op("qg", lambda e, ec=ec: e.dma_start(out=dbg_t["d_xcs"].ap()[ec], in_=xcsb[:]))
    SF("m_inpf", "m_inpb0", "m_inpb1", "m_xzc", "m_cacc", "m_xcsb")
    bar()

    dbcb = SA("m_dbcb", [128, T], BF16)
    xpf = SA("m_xpf", [128, 768], F32)
    xpb = SA("m_xpb", [128, 768], BF16)
    op("qs", lambda e: e.dma_start(
        out=xpf[:].rearrange("p (c j) -> p c j", j=96), in_=xpT.ap().transpose([1, 0, 2])))
    wait("vector", "qs")
    op("vs", lambda e: e.tensor_copy(xpb[:], xpf[:]))
    dbcp = SA("m_dbcp", [128, T], F32)
    xcq = [SA("m_xcq0", [128, 512], BF16), SA("m_xcq1", [128, 512], BF16)]
    wait("tensor", "vs")
    for ts8 in range(8):
        t0 = ts8 * 512
        bk = ts8 % 2
        wait("tensor", "vs")
        for ec in range(8):
            bq = (ts8 * 8 + ec) % 2
            if ts8 * 8 + ec >= 2:
                wait("sync", "ts")
            op("qs", lambda e, ec=ec, t0=t0, bq=bq: e.dma_start(
                out=xcq[bq][:], in_=xc_scr.ap()[ec, :, t0:t0 + 512]))
            wait("tensor", "qs")
            op("ts", lambda e, ec=ec, t0=t0, bq=bq, bk=bk: e.matmul(
                PS[0:96, bk * 512:bk * 512 + 512],
                xpb[:, ec * 96:(ec + 1) * 96], xcq[bq][:],
                start=(ec == 0), stop=(ec == 7), skip_group_check=True))
        wait("vector", "ts")
        op("vs", lambda e, t0=t0, bk=bk: e.tensor_copy(
            dbcp[0:96, t0:t0 + 512], PS[0:96, bk * 512:bk * 512 + 512]))
    wait("gpsimd", "vs")
    op("qg", lambda e: e.dma_start(out=dbc_in.ap(), in_=dbcp[0:96, :]))
    wait("gpsimd", "qg")
    op("gc", lambda e: e.collective_compute(
        "AllReduce", Alu.add, replica_groups=GROUPS,
        ins=[dbc_in.ap().opt()], outs=[dbc_out.ap().opt()]))
    wait("sync", "gc")
    op("qs", lambda e: e.dma_start(out=dbcp[0:96, :], in_=dbc_out.ap()))
    q_dbc = P.ctrs["qs"].n
    if "d_dbc" in dbg_t:
        wait("gpsimd", "qs", q_dbc)
        op("qg", lambda e: e.dma_start(out=dbg_t["d_dbc"].ap()[0:96, :], in_=dbcp[0:96, :]))
    wait("vector", "qs", q_dbc)
    op("vs", lambda e: e.tensor_copy(dbcb[0:96, :], dbcp[0:96, :]))
    v_dbcb = P.ctrs["vs"].n
    wait("gpsimd", "vs", v_dbcb)
    op("qg", lambda e: e.dma_start(out=dbc_bf.ap(), in_=dbcb[64:96, :]))
    dttf = SA("m_dttf", [64, EO], F32)
    dttb = SA("m_dttb", [64, EO], BF16)
    op("qs", lambda e: e.dma_start(out=dttf[:], in_=dtT.ap()))
    wait("vector", "qs")
    op("vs", lambda e: e.tensor_copy(dttb[:], dttf[:]))
    v_dtt = P.ctrs["vs"].n
    dlt = SA("m_dlt", [128, T], F32)
    wait("tensor", "vs", max(v_dbcb, v_dtt))
    for ec in range(8):
        for t0 in range(0, T, 512):
            op("ts", lambda e, ec=ec, t0=t0: e.matmul(
                PS[:, t0:t0 + 512], dttb[:, ec * 128:(ec + 1) * 128],
                dbcb[0:64, t0:t0 + 512], start=True, stop=True))
        wait("scalar", "ts")
        op("ss", lambda e, ec=ec: e.activation(
            dlt[:], PS[:, 0:T], Act.Exp, bias=dtb_sb[:, ec:ec + 1]))
        op("ss", lambda e: e.activation(dlt[:], dlt[:], Act.Ln, bias=1.0))
        wait("gpsimd", "ss")
        op("qg", lambda e, ec=ec: e.dma_start(out=dl_scr.ap()[ec], in_=dlt[:]))
        if "d_delta" in dbg_t:
            op("qg", lambda e, ec=ec: e.dma_start(out=dbg_t["d_delta"].ap()[ec], in_=dlt[:]))
        wait("tensor", "qg")
    SF("m_xpf", "m_xpb", "m_dbcp", "m_xcq0", "m_xcq1", "m_dttf", "m_dttb", "m_dlt", "m_dbcb")
    bar()

    bcst = SA("m_bcst", [128, 32 * 1024], BF16)
    dq = [SA("m_dq0", [128, 1024], F32), SA("m_dq1", [128, 1024], F32)]
    xq = [SA("m_xq0", [128, 1024], BF16), SA("m_xq1", [128, 1024], BF16)]
    uq = SA("m_uq", [128, 1024], BF16)
    dA = [PS[:, 0:1024], PS[:, 1024:2048]]
    bxq = SA("m_bxq", [128, 1024], BF16)
    hsq = SA("m_hsq", [128, 1024], F32)
    ytq = [SA("m_ytq0", [128, 1024], BF16)]
    zq = [SA("m_zq0", [128, 1024], BF16), SA("m_zq1", [128, 1024], BF16)]
    ymq = uq
    dxq = bxq
    ld_snap = {}
    ymq_g = {}
    for tq in range(4):
        t0 = tq * 1024
        wait("sync", "vs")
        for n in range(NST):
            for r in range(2):
                dfrow = n if r == 0 else 16 + n
                slot = 2 * n + r
                op("qs", lambda e, dfrow=dfrow, slot=slot, t0=t0: e.dma_start(
                    out=bcst[:, slot * 1024:(slot + 1) * 1024],
                    in_=dbc_bf.ap()[dfrow:dfrow + 1, t0:t0 + 1024]
                    .partition_broadcast(128).squeeze(1)))
        bc_q = P.ctrs["qs"].n
        wait("vector", "qs", bc_q)
        def emit_loads(ec_):
            bi_ = ec_ % 2
            wait("sync", "vs")
            op("qs", lambda e, ec=ec_, t0=t0, bi=bi_: e.dma_start(
                out=dq[bi][:], in_=dl_scr.ap()[ec, :, t0:t0 + 1024]))
            op("qs", lambda e, ec=ec_, t0=t0, bi=bi_: e.dma_start(
                out=xq[bi][:], in_=xc_scr.ap()[ec, :, t0:t0 + 1024]))
            op("qs", lambda e, ec=ec_, t0=t0, bi=bi_: e.dma_start(
                out=zq[bi][:], in_=zs_scr.ap()[ec, :, t0:t0 + 1024]))
            ld_snap[(tq, ec_)] = P.ctrs["qs"].n
        for ec in range(8):
            bi = ec % 2
            yb = 2048 + (ec % 2) * 1024
            if ec == 0:
                emit_loads(0)
            wait("vector", "qs", ld_snap[(tq, ec)])
            op("vs", lambda e, bi=bi: e.tensor_mul(uq[:], dq[bi][:], xq[bi][:]))
            if ec < 7:
                emit_loads(ec + 1)
            wait("scalar", "vs")
            acc_prev = None
            scan_v = {}
            for n in range(NST):
                ba = n % 2
                col = ec * 16 + n
                if n >= 2:
                    wait("scalar", "vs", scan_v[n - 2])
                op("ss", lambda e, ba=ba, col=col, bi=bi: e.activation(
                    dA[ba], dq[bi][:], Act.Exp, scale=amat_sb[:, col:col + 1]),
                    sw=False)
                exp_s = P.ctrs["ss"].n
                op("vs", lambda e, n=n: e.tensor_mul(
                    bxq[:], uq[:], bcst[:, 2 * n * 1024:(2 * n + 1) * 1024]))
                wait("vector", "ss", exp_s)
                scan_v[n] = op("vs", lambda e, ba=ba, col=col: e.tensor_tensor_scan(
                    hsq[:], dA[ba], bxq[:], state_sb[:, col:col + 1],
                    Alu.mult, Alu.add))
                op("vs", lambda e, col=col: e.tensor_copy(
                    state_sb[:, col:col + 1], hsq[:, 1023:1024]))
                if acc_prev is not None:
                    wait("vector", "ts", acc_prev)
                op("vs", lambda e, n=n: e.tensor_mul(
                    ytq[0][:], hsq[:],
                    bcst[:, (2 * n + 1) * 1024:(2 * n + 2) * 1024]))
                wait("tensor", "vs")
                for hh in range(2):
                    op("ts", lambda e, hh=hh, n=n, yb=yb: e.matmul(
                        PS[:, yb + hh * 512:yb + (hh + 1) * 512],
                        eye_sb[:], ytq[0][:, hh * 512:(hh + 1) * 512],
                        start=(n == 0), stop=False, skip_group_check=True))
                acc_prev = P.ctrs["ts"].n
            if acc_prev is not None:
                wait("vector", "ts", acc_prev)
            op("vs", lambda e, bi=bi, ec=ec: e.tensor_scalar_mul(
                dxq[:], xq[bi][:], dpar_sb[:, ec:ec + 1]))
            wait("tensor", "vs")
            for hh in range(2):
                op("ts", lambda e, hh=hh, yb=yb: e.matmul(
                    PS[:, yb + hh * 512:yb + (hh + 1) * 512],
                    eye_sb[:], dxq[:, hh * 512:(hh + 1) * 512],
                    start=False, stop=(hh == 1), skip_group_check=True))
            t_acc = P.ctrs["ts"].n
            wait("vector", "ts", t_acc)
            ymq_g[(tq, ec)] = op("vs", lambda e, bi=bi, yb=yb: e.tensor_mul(
                ymq[:], PS[:, yb:yb + 1024], zq[bi][:]))
            wait("gpsimd", "vs", ymq_g[(tq, ec)])
            op("qg", lambda e, ec=ec, t0=t0: e.dma_start(
                out=ym_scr.ap()[ec, :, t0:t0 + 1024], in_=ymq[:]))
            if "d_ymul" in dbg_t:
                op("qg", lambda e, ec=ec, t0=t0: e.dma_start(
                    out=dbg_t["d_ymul"].ap()[ec, :, t0:t0 + 1024], in_=ymq[:]))
            wait("vector", "qg")
    SF("m_bcst", "m_dq0", "m_xq0", "m_xq1",
       "m_dq1", "m_uq", "m_bxq", "m_hsq", "m_ytq0", "m_zq0", "m_zq1")
    bar()

    outf = SA("c4_outf", [128, C], F32)
    outt = SA("c4_outt", [128, 8 * C], BF16)
    for ec in range(8):
        wait("sync", "vs")
        op("qs", lambda e, ec=ec: e.dma_start(out=outf[:], in_=outT.ap()[ec]))
        wait("vector", "qs")
        op("vs", lambda e, ec=ec: e.tensor_copy(outt[:, ec * C:(ec + 1) * C], outf[:]))
    v_outt = P.ctrs["vs"].n
    wait("tensor", "vs", v_outt)
    ymgs = SA("c4_ymgs", [128, 8 * 512], BF16)
    prt = [SA("c4_prt", [128, 512], F32), SA("c4_prt1", [128, 512], F32)]
    prt_g = {}
    for gs in range(8):
        wait("sync", "ts")
        op("qs", lambda e, gs=gs: e.dma_start(
            out=ymgs[:].rearrange("p (c j) -> p c j", j=512),
            in_=ym_scr.ap()[:, :, gs * 512:(gs + 1) * 512].transpose([1, 0, 2])))
        q_ym = P.ctrs["qs"].n
        wait("tensor", "qs", q_ym)
        for cc in range(8):
            bk = (gs * 8 + cc) % 2
            wait("tensor", "vs")
            for ec in range(8):
                op("ts", lambda e, cc=cc, ec=ec, bk=bk: e.matmul(
                    PS[:, bk * 512:(bk + 1) * 512],
                    outt[:, ec * C + cc * 128:ec * C + cc * 128 + 128],
                    ymgs[:, ec * 512:(ec + 1) * 512],
                    start=(ec == 0), stop=(ec == 7), skip_group_check=True))
            wait("vector", "ts")
            bp = (gs * 8 + cc) % 2
            wait("vector", "qg", prt_g.get(bp, 0))
            op("vs", lambda e, bk=bk, bp=bp: e.tensor_copy(
                prt[bp][:], PS[:, bk * 512:(bk + 1) * 512]))
            wait("gpsimd", "vs")
            for bb_ in range(2):
                op("qg", lambda e, cc=cc, gs=gs, bb_=bb_, bp=bp: e.dma_start(
                    out=rs_in.ap()[bb_, cc, :, gs * 256:gs * 256 + 256]
                    .rearrange("p (u j) -> p u j", j=128),
                    in_=prt[bp][:].rearrange("p (u v j) -> p u v j", u=2, v=2)[:, :, bb_, :]))
            prt_g[bp] = P.ctrs["qg"].n
    g_rs = P.ctrs["qg"].n
    wait("gpsimd", "qg", g_rs)
    op("gc", lambda e: e.collective_compute(
        "ReduceScatter", Alu.add, replica_groups=GROUPS,
        ins=[rs_in.ap().opt()], outs=[rs_out.ap().opt()]))
    SF("c4_outf", "c4_outt", "c4_ymgs", "c4_prt", "c4_prt1")
    bar()
    for c in range(8):
        op("qs", lambda e, c=c: e.dma_start(out=xfm(c), in_=park_scr.ap()[c]))
    rsl = [SA("c4_rsl0", [128, T2], F32), SA("c4_rsl1", [128, T2], F32)]
    wait("sync", "gc")
    for c in range(8):
        bi = c % 2
        if c >= 2:
            wait("sync", "vs")
        op("qs", lambda e, c=c, bi=bi: e.dma_start(out=rsl[bi][:], in_=rs_out.ap()[c]))
        wait("vector", "qs")
        op("vs", lambda e, c=c, bi=bi: e.tensor_add(xfm(c), xfm(c), rsl[bi][:]))
    SF("c4_rsl0", "c4_rsl1")
    if "d_x3" in dbg_t:
        wait("gpsimd", "vs")
        for c in range(8):
            op("qg", lambda e, c=c: e.dma_start(out=dbg_t["d_x3"].ap()[c], in_=xfm(c)))
    bar()

    emit_ffn(4, "none", "b")
    for c in range(8):
        op("qg", lambda e, c=c: e.dma_start(out=y_own.ap()[c], in_=xfm(c)))
    g_out = P.ctrs["qg"].n
    wait("gpsimd", "qg", g_out)
    bar()

    block = nc.Block().__enter__()
    P.finalize(block)
    block.__exit__(None, None, None)
    return nc


def _prep_core(inputs, core):
    seq, par = core // 2, core % 2
    f32 = np.float32
    x = np.asarray(inputs["L0_attn_a_x"], dtype=f32)
    xT = np.ascontiguousarray(x[seq].T)
    own_cols = np.concatenate(
        [np.arange((2 * i + par) * 128, (2 * i + par) * 128 + 128) for i in range(16)])
    m = {}
    m["x_full"] = np.ascontiguousarray(xT.reshape(8, 128, T))
    m["x_own"] = np.ascontiguousarray(xT[:, own_cols].reshape(8, 128, T2))
    m["wq"] = np.ascontiguousarray(
        np.asarray(inputs["L0_attn_a_w_q"], f32).reshape(8, 128, C))
    m["wkvd"] = np.ascontiguousarray(
        np.asarray(inputs["L0_attn_a_w_kv_down"], f32).reshape(8, 128, KVR))
    wkvu = np.asarray(inputs["L0_attn_a_w_kv_up"], f32)
    m["wkvu_k"] = np.ascontiguousarray(wkvu[:, :C])
    m["wkvu_v"] = np.ascontiguousarray(wkvu[:, C:])
    m["wout"] = np.ascontiguousarray(
        np.asarray(inputs["L0_attn_a_w_out"], f32).reshape(8, 128, C))
    wg = np.asarray(inputs["L0_ffn_w_gate"], np.int32)
    wu = np.asarray(inputs["L0_ffn_w_up"], np.int32)
    wd = np.asarray(inputs["L0_ffn_w_down"], np.int32)
    m["wgT"] = np.ascontiguousarray(wg.T.reshape(8, 128, FH))
    m["wuT"] = np.ascontiguousarray(wu.T.reshape(8, 128, FH))
    m["wdT"] = np.ascontiguousarray(wd.T.reshape(24, 128, C))
    ipw = np.asarray(inputs["L0_attn_b_in_proj_w"], f32)
    sel = np.concatenate([ipw[par * EO:(par + 1) * EO],
                          ipw[EF + par * EO:EF + (par + 1) * EO]])
    m["inpT"] = np.ascontiguousarray(sel.T.reshape(8, 128, EF))
    xpw = np.asarray(inputs["L0_attn_b_x_proj_w"], f32)
    m["xpT"] = np.ascontiguousarray(
        xpw[:, par * EO:(par + 1) * EO].T.reshape(8, 128, 96))
    dtw = np.asarray(inputs["L0_attn_b_dt_w"], f32)
    m["dtT"] = np.ascontiguousarray(dtw[par * EO:(par + 1) * EO].T)
    opw = np.asarray(inputs["L0_attn_b_out_proj_w"], f32)
    m["outT"] = np.ascontiguousarray(
        opw[:, par * EO:(par + 1) * EO].T.reshape(8, 128, C))
    alog = np.asarray(inputs["L0_attn_b_A_log"], f32)[par * EO:(par + 1) * EO]
    m["a_log"] = np.ascontiguousarray(
        alog.reshape(8, 128, 16).transpose(1, 0, 2).reshape(128, 128))
    cw = np.asarray(inputs["L0_attn_b_conv_w"], f32)[par * EO:(par + 1) * EO]
    m["convw"] = np.ascontiguousarray(
        cw.reshape(8, 128, 4).transpose(1, 0, 2).reshape(128, 32))
    for name, key in [("convb", "L0_attn_b_conv_b"), ("dtb", "L0_attn_b_dt_b"),
                      ("dpar", "L0_attn_b_D_param")]:
        v = np.asarray(inputs[key], f32)[par * EO:(par + 1) * EO]
        m[name] = np.ascontiguousarray(v.reshape(8, 128).T)
    import ml_dtypes
    bf = ml_dtypes.bfloat16
    m["ones128"] = np.ones((128, 128), dtype=bf)
    m["eye128"] = np.eye(128, dtype=np.float32).astype(bf)
    s2 = np.zeros((2, 128), np.float32)
    s2[0, 0:64] = 1.0
    s2[1, 64:128] = 1.0
    m["sel2"] = s2.astype(bf)
    tril = np.triu(np.ones((128, 128), np.float32))
    m["mask_a"] = (tril if par == 0 else np.ones((128, 128), np.float32)).astype(bf)
    m["mask_b"] = (np.zeros((128, 128), np.float32) if par == 0 else tril).astype(bf)
    return m


_NC_CACHE = {}


def kernel(**inputs):
    key = "main"
    if key not in _NC_CACHE:
        _NC_CACHE[key] = build_nc()
    nc = _NC_CACHE[key]
    in_maps = [_prep_core(inputs, core) for core in range(8)]
    res = run_bass_kernel_spmd(nc, in_maps, core_ids=list(range(8)))
    out = np.empty((B, T, C), dtype=np.float32)
    for core in range(8):
        seq, par = core // 2, core % 2
        yo = np.asarray(res.results[core]["y_own"])
        yo = yo.reshape(C, T2)
        cols = np.concatenate(
            [np.arange((2 * i + par) * 128, (2 * i + par) * 128 + 128)
             for i in range(16)])
        out[seq][cols, :] = yo.T
    return out


# version 05
# speedup vs baseline: 1.3066x; 1.0441x over previous
import numpy as np
from contextlib import ExitStack

import concourse.bass as bass
import concourse.mybir as mybir
from concourse.bass_utils import run_bass_kernel_spmd

F32 = mybir.dt.float32
BF16 = mybir.dt.bfloat16
I32 = mybir.dt.int32
Alu = mybir.AluOpType
Act = mybir.ActivationFunctionType

B, T, C = 4, 4096, 1024
T2 = T // 2
NH, HD, KVR = 16, 64, 128
EF, EO, NST, DTR = 2048, 1024, 16, 64
FH = 3072
QSCALE = 0.125
FSC = 1.0 / 127.0
EPS = 1e-5
GROUPS = [[0, 1], [2, 3], [4, 5], [6, 7]]
ENGS = ("tensor", "vector", "scalar", "sync", "gpsimd")


class Ctr:
    __slots__ = ("name", "sem", "step", "n", "is_dma", "watermark", "settled")

    def __init__(self, name, sem, step):
        self.name, self.sem, self.step, self.n = name, sem, step, 0
        self.is_dma = step == 16
        self.watermark = 0
        self.settled = 0


class Prog:

    def __init__(self, nc, E):
        self.nc = nc
        self.parts = {e: [] for e in ENGS}
        self.ctrs = {}
        for name, eng, step in [("ts", "tensor", 1), ("vs", "vector", 1),
                                ("ss", "scalar", 1), ("gc", "gpsimd", 1),
                                ("qs", "sync", 16), ("qg", "gpsimd", 16)]:
            self.ctrs[name] = Ctr(name, E(nc.semaphore(name)), step)
        self.home = {"ts": "tensor", "vs": "vector", "ss": "scalar",
                     "gc": "gpsimd", "qs": "sync", "qg": "gpsimd"}
        self.waited = {e: {} for e in ENGS}

    def wait(self, eng, cname, val=None):
        c = self.ctrs[cname]
        val = c.n if val is None else val
        if val <= 0 or self.waited[eng].get(cname, -1) >= val:
            return
        if c.is_dma and val < c.n:
            val = c.n
            if self.waited[eng].get(cname, -1) >= val:
                return
        self.waited[eng][cname] = val
        if c.is_dma and eng != self.home[cname]:
            c.watermark = max(c.watermark, val)
        self.parts[eng].append(lambda e, s=c.sem, v=val: e.wait_ge(s, v))

    def op(self, cname, fn, sw=True):
        c = self.ctrs[cname]
        eng = self.home[cname]
        if sw and eng in ("vector", "scalar"):
            self.wait(eng, cname, c.n)
        if c.is_dma and c.watermark > c.settled and c.n > c.settled:
            self.waited[eng][cname] = max(self.waited[eng].get(cname, -1), c.n)
            self.parts[eng].append(lambda e, s=c.sem, v=c.n: e.wait_ge(s, v))
            c.settled = c.n
        c.n += c.step
        self.parts[eng].append(
            lambda e, fn=fn, s=c.sem, k=c.step: fn(e).then_inc(s, k))
        return c.n

    def raw(self, eng, fn):
        self.parts[eng].append(fn)

    def bar(self):
        vals = {k: c.n for k, c in self.ctrs.items()}
        for k, c in self.ctrs.items():
            if c.is_dma:
                self.wait(self.home[k], k, c.n)
                c.settled = c.n
        for e in ENGS:
            for k, v in vals.items():
                self.wait(e, k, v)

    def finalize(self, block):
        getattr(block, "tensor")(lambda e: [f(e) for f in self.parts["tensor"]])
        getattr(block, "vector")(lambda e: [f(e) for f in self.parts["vector"]])
        getattr(block, "scalar")(lambda e: [f(e) for f in self.parts["scalar"]])
        getattr(block, "sync")(lambda e: [f(e) for f in self.parts["sync"]])
        getattr(block, "gpsimd")(lambda e: [f(e) for f in self.parts["gpsimd"]])


def build_nc(dbg=(), groups=None):
    nc = bass.Bass("TRN2", target_bir_lowering=False)
    dbg = set(dbg)
    global GROUPS
    GROUPS = groups if groups is not None else [[0, 1], [2, 3], [4, 5], [6, 7]]
    ctx = ExitStack()
    E = ctx.enter_context

    def DI(name, shape, dtype=F32):
        return nc.dram_tensor(name, list(shape), dtype, kind="ExternalInput")

    x_own = DI("x_own", [8, 128, T2])
    x_full = DI("x_full", [8, 128, T])
    wq = DI("wq", [8, 128, C])
    wkvd = DI("wkvd", [8, 128, KVR])
    wkvu_k = DI("wkvu_k", [128, C])
    wkvu_v = DI("wkvu_v", [128, C])
    wout = DI("wout", [8, 128, C])
    wgT = DI("wgT", [8, 128, FH], I32)
    wuT = DI("wuT", [8, 128, FH], I32)
    wdT = DI("wdT", [24, 128, C], I32)
    inpT = DI("inpT", [8, 128, EF])
    xpT = DI("xpT", [8, 128, 96])
    dtT = DI("dtT", [64, EO])
    outT = DI("outT", [8, 128, C])
    a_log = DI("a_log", [128, 128])
    convw = DI("convw", [128, 32])
    convb = DI("convb", [128, 8])
    dtb = DI("dtb", [128, 8])
    dpar = DI("dpar", [128, 8])
    ones128 = DI("ones128", [128, 128], BF16)
    eye128 = DI("eye128", [128, 128], BF16)
    sel2 = DI("sel2", [2, 128], BF16)
    mask_a = DI("mask_a", [128, 128], BF16)
    mask_b = DI("mask_b", [128, 128], BF16)

    y_own = nc.dram_tensor("y_own", [8, 128, T2], F32, kind="ExternalOutput")

    dbg_t = {}
    for name, shape, dt in [
        ("d_xhat1", [8, 128, T], BF16), ("d_q", [8, 128, T2], BF16),
        ("d_k", [8, 128, T], BF16), ("d_x1", [8, 128, T2], F32),
        ("d_x2", [8, 128, T2], F32), ("d_dbc", [128, T], F32),
        ("d_delta", [8, 128, T], F32), ("d_xcs", [8, 128, T], BF16),
        ("d_x3", [8, 128, T2], F32), ("d_ymul", [8, 128, T], BF16),
    ]:
        if name in dbg:
            dbg_t[name] = nc.dram_tensor(name, shape, dt, kind="ExternalOutput")

    kT_scr = nc.dram_tensor("kT_scr", [8, 128, T], BF16)
    v_scr = nc.dram_tensor("v_scr", [32, 128, C], BF16)
    q_scr = nc.dram_tensor("q_scr", [8, 128, T2], BF16)
    wg_scr = nc.dram_tensor("wg_scr", [24, 8, 128, 128], BF16)
    wu_scr = nc.dram_tensor("wu_scr", [24, 8, 128, 128], BF16)
    wd_scr = nc.dram_tensor("wd_scr", [8, 24, 128, 128], BF16)
    xc_scr = nc.dram_tensor("xc_scr", [8, 128, T], BF16)
    zs_scr = nc.dram_tensor("zs_scr", [8, 128, T], BF16)
    dl_scr = nc.dram_tensor("dl_scr", [8, 128, T], F32)
    ym_scr = nc.dram_tensor("ym_scr", [8, 128, T], BF16)
    park_scr = nc.dram_tensor("park_scr", [8, 128, T2], F32)
    ln3_in = nc.dram_tensor("ln3_in", [1025, T2], BF16)
    ln3_out = nc.dram_tensor("ln3_out", [2, 1025, T2], BF16)
    dbc_in = nc.dram_tensor("dbc_in", [96, T], F32)
    dbc_out = nc.dram_tensor("dbc_out", [96, T], F32)
    dbc_bf = nc.dram_tensor("dbc_bf", [32, T], BF16)
    rs_in = nc.dram_tensor("rs_in", [2, 8, 128, T2], F32)
    rs_out = nc.dram_tensor("rs_out", [8, 128, T2], F32)

    P = Prog(nc, E)
    sb_frees = {}
    sb_order = []

    def SA(name, shape, dtype):
        cm = nc.sbuf_tensor(name, list(shape), dtype)
        t = cm.__enter__()
        sb_frees[name] = cm
        sb_order.append(name)
        return t

    def SF(*names):
        for n in sorted(names, key=lambda n: -sb_order.index(n)):
            assert n == sb_order[-1], f"non-LIFO free {n} vs stack {sb_order[-6:]}"
            sb_order.pop()
            sb_frees.pop(n).__exit__(None, None, None)

    PS = nc.psum_tensor("PS", [128, 4096], F32).__enter__()

    ones_sb = SA("ones_sb", [128, 128], BF16)
    eye_sb = SA("eye_sb", [128, 128], BF16)
    sel2_sb = SA("sel2_sb", [2, 128], BF16)
    maska_sb = SA("maska_sb", [128, 128], BF16)
    maskb_sb = SA("maskb_sb", [128, 128], BF16)
    class _RowMap:
        def __init__(self, raw, width, slots):
            self.raw, self.w, self.slots = raw, width, slots
        def __getitem__(self, key):
            rs, cs = key
            off = self.slots[rs.start]
            a = cs.start or 0
            b = cs.stop if cs.stop is not None else self.w
            return self.raw[0:1, off + a:off + b]

    rows_raw = SA("rows_raw", [1, T2 + 2 * T], BF16)
    rows_sb = _RowMap(rows_raw, T, {0: 0, 2: 0, 4: 0, 6: 0, 5: T2,
                                    1: T2 + T, 3: T2 + T})
    csum_raw = SA("csum_raw", [1, 8192], BF16)
    csum_sb = _RowMap(csum_raw, 8192, {0: 0, 1: 3072, 2: 6144, 3: 7168, 4: 6144})
    srow = SA("srow", [1, 2048], F32)
    scol = SA("scol", [128, 160], F32)
    sbcol = SA("sbcol", [128, 64], BF16)
    c0s = SA("c0s", [1, 8], F32)
    c0b = SA("c0b", [1, 2], BF16)
    cvec_sb = SA("cvec_sb", [128, 8], F32)
    amat_sb = SA("amat_sb", [128, 128], F32)
    convw_sb = SA("convw_sb", [128, 32], F32)
    convb_sb = SA("convb_sb", [128, 8], F32)
    dtb_sb = SA("dtb_sb", [128, 8], F32)
    dpar_sb = SA("dpar_sb", [128, 8], F32)
    state_sb = SA("state_sb", [128, 128], F32)
    xfm_h = [SA("x_fm", [128, 8 * T2], F32)]

    def xfm(c):
        return xfm_h[0][:, c * T2:(c + 1) * T2]

    op, wait, bar = P.op, P.wait, P.bar

    def lp(fn):
        def go(e):
            with nc.allow_low_precision(reason="intentional bf16 elementwise"):
                return fn(e)
        return go

    al_sb = SA("al_sb", [128, 128], F32)
    for dst, src in [(ones_sb, ones128), (eye_sb, eye128), (sel2_sb, sel2),
                     (maska_sb, mask_a), (maskb_sb, mask_b), (convw_sb, convw),
                     (convb_sb, convb), (dtb_sb, dtb), (dpar_sb, dpar),
                     (al_sb, a_log)]:
        op("qs", lambda e, d=dst, s=src: e.dma_start(out=d[:], in_=s.ap()))
    wait("scalar", "qs")
    op("ss", lambda e: e.activation(amat_sb[:], al_sb[:], Act.Exp))
    wait("vector", "ss")
    op("vs", lambda e: e.tensor_scalar_mul(amat_sb[:], amat_sb[:], -1.0))
    op("vs", lambda e: e.memset(state_sb[:], 0.0))
    op("vs", lambda e: e.memset(cvec_sb[:, 1:2], EPS))

    wi = [SA("p0_wi0", [128, FH], I32), SA("p0_wi1", [128, FH], I32)]
    wb = [SA("p0_wb0", [128, FH], BF16), SA("p0_wb1", [128, FH], BF16)]
    conv_sets = [(wgT, wg_scr, 0, 8, FH), (wuT, wu_scr, 1, 8, FH),
                 (wdT, wd_scr, None, 24, C)]
    conv_v = {}
    store_g = {}
    colsum_t = {}
    it = 0
    for src, dst, crow, nchunk, w in conv_sets:
        for c in range(nchunk):
            b = it % 2
            if it >= 2:
                wait("sync", "vs", conv_v[it - 2])
            op("qs", lambda e, b=b, src=src, c=c, w=w:
               e.dma_start(out=wi[b][:, 0:w], in_=src.ap()[c]))
            wait("vector", "qs")
            if it >= 2:
                wait("vector", "qg", store_g[it - 2])
                if it - 2 in colsum_t:
                    wait("vector", "ts", colsum_t[it - 2])
            conv_v[it] = op("vs", lambda e, b=b, w=w:
                            e.tensor_copy(wb[b][:, 0:w], wi[b][:, 0:w]))
            wait("gpsimd", "vs", conv_v[it])
            if crow is None:
                store_g[it] = op("qg", lambda e, b=b, dst=dst, c=c:
                                 e.dma_start(out=dst.ap()[:, c].transpose([1, 0, 2]),
                                             in_=wb[b][:, 0:C].rearrange("p (f j) -> p f j", j=128)))
            else:
                store_g[it] = op("qg", lambda e, b=b, dst=dst, c=c:
                                 e.dma_start(out=dst.ap()[:, c].transpose([1, 0, 2]),
                                             in_=wb[b][:, 0:FH].rearrange("p (h j) -> p h j", j=128)))
            if crow is not None:
                wait("tensor", "vs", conv_v[it])
                for h in range(6):
                    op("ts", lambda e, b=b, h=h, c=c, nchunk=nchunk:
                       e.matmul(PS[0:1, h * 512:(h + 1) * 512], ones_sb[:, 0:1],
                                wb[b][:, h * 512:(h + 1) * 512],
                                start=(c == 0), stop=(c == nchunk - 1),
                                skip_group_check=True))
                colsum_t[it] = P.ctrs["ts"].n
            it += 1
        if crow is not None:
            wait("vector", "ts")
            op("vs", lambda e, crow=crow: e.tensor_copy(
                csum_sb[crow:crow + 1, 0:FH], PS[0:1, 0:FH]))
            wait("tensor", "vs")
    SF("al_sb", "p0_wi0", "p0_wi1", "p0_wb0", "p0_wb1")
    bar()

    def ln_stats(xb_chunks, ntok, row_idx, nchunks=8, extra_sq=None):
        xsq = extra_sq
        nc_cols = ntok // 128
        nhalf = max(1, ntok // 2048)
        for half in range(nhalf):
            h0 = half * 2048
            hw = min(2048, ntok - h0)
            hc = hw // 128
            wait("tensor", "vs")
            for c in range(nchunks):
                wait("scalar", "vs")
                wait("scalar", "ts")
                op("ss", lambda e, c=c, h0=h0, hw=hw:
                   e.activation(xsq[:, 0:hw], xb_chunks(c)[:, h0:h0 + hw], Act.Square))
                wait("tensor", "ss")
                for t0 in range(0, hw, 512):
                    op("ts", lambda e, c=c, h0=h0, t0=t0:
                       e.matmul(PS[0:1, t0:t0 + 512], ones_sb[:, 0:1],
                                xb_chunks(c)[:, h0 + t0:h0 + t0 + 512],
                                start=(c == 0), stop=(c == nchunks - 1),
                                skip_group_check=True))
                    op("ts", lambda e, c=c, t0=t0:
                       e.matmul(PS[0:1, 2048 + t0:2048 + t0 + 512], ones_sb[:, 0:1],
                                xsq[:, t0:t0 + 512],
                                start=(c == 0), stop=(c == nchunks - 1),
                                skip_group_check=True))
            wait("vector", "ts")
            wait("vector", "qs")
            op("vs", lambda e, hw=hw: e.tensor_copy(srow[0:1, 0:hw], PS[0:1, 0:hw]))
            wait("sync", "vs")
            op("qs", lambda e, half=half, hc=hc, hw=hw: e.dma_start(
                out=scol[:, half * 16:half * 16 + hc], in_=srow[0:1, 0:hw]))
            qsx = P.ctrs["qs"].n
            wait("vector", "qs", qsx)
            op("vs", lambda e, hw=hw: e.tensor_copy(srow[0:1, 0:hw],
                                                    PS[0:1, 2048:2048 + hw]))
            wait("sync", "vs")
            op("qs", lambda e, half=half, hc=hc, hw=hw: e.dma_start(
                out=scol[:, 32 + half * 16:32 + half * 16 + hc], in_=srow[0:1, 0:hw]))
        wait("vector", "qs")
        ncc = nc_cols
        op("vs", lambda e: e.tensor_scalar_mul(
            scol[:, 64:64 + ncc], scol[:, 0:ncc], 1.0 / 1024.0))
        op("vs", lambda e: e.tensor_mul(
            scol[:, 96:96 + ncc], scol[:, 64:64 + ncc], scol[:, 64:64 + ncc]))
        op("vs", lambda e: e.scalar_tensor_tensor(
            scol[:, 96:96 + ncc], scol[:, 32:32 + ncc], 1.0 / 1024.0,
            scol[:, 96:96 + ncc], Alu.mult, Alu.subtract))
        wait("scalar", "vs")
        op("ss", lambda e: e.activation(
            scol[:, 96:96 + ncc], scol[:, 96:96 + ncc], Act.Ln,
            bias=cvec_sb[:, 1:2]))
        op("ss", lambda e: e.activation(
            scol[:, 96:96 + ncc], scol[:, 96:96 + ncc], Act.Exp, scale=-0.5))
        wait("vector", "ss")
        op("vs", lambda e: e.scalar_tensor_tensor(
            scol[:, 128:128 + ncc], scol[:, 64:64 + ncc], -1.0,
            scol[:, 96:96 + ncc], Alu.mult, Alu.mult))
        op("vs", lambda e: e.tensor_copy(sbcol[:, 0:ncc], scol[:, 96:96 + ncc]))
        op("vs", lambda e: e.tensor_copy(sbcol[:, 32:32 + ncc], scol[:, 128:128 + ncc]))
        wait("sync", "vs")
        for half in range(nhalf):
            h0 = half * 2048
            hw = min(2048, ntok - h0)
            hc = hw // 128
            op("qs", lambda e, h0=h0, hw=hw, half=half, hc=hc: e.dma_start(
                out=rows_sb[5:6, h0:h0 + hw],
                in_=sbcol[:, half * 16:half * 16 + hc]))
            op("qs", lambda e, h0=h0, hw=hw, half=half, hc=hc, r=row_idx: e.dma_start(
                out=rows_sb[r:r + 1, h0:h0 + hw],
                in_=sbcol[:, 32 + half * 16:32 + half * 16 + hc]))
        wait("tensor", "qs")
        wait("vector", "qs")

    def bcast_row(row_ap, ntok, dst_sb, pbank=0):
        for t0 in range(0, ntok, 512):
            bk = pbank + (t0 // 512) % 2
            wait("tensor", "vs")
            op("ts", lambda e, t0=t0, bk=bk: e.matmul(
                PS[:, bk * 512:(bk + 1) * 512], ones_sb[0:1, :],
                row_ap[:, t0:t0 + 512], start=True, stop=True))
            wait("vector", "ts")
            op("vs", lambda e, t0=t0, bk=bk: e.tensor_copy(
                dst_sb[:, t0:t0 + 512], PS[:, bk * 512:(bk + 1) * 512]))

    for c in range(8):
        op("qs", lambda e, c=c: e.dma_start(out=xfm(c), in_=x_own.ap()[c]))
    q_xown = P.ctrs["qs"].n

    xb1 = SA("xb1", [128, 8 * T2], BF16)
    xsq_a = SA("xsq_a", [128, T2], BF16)
    rbc_a = SA("rbc_a", [128, T], BF16)
    wait("vector", "qs", q_xown)
    for c in range(8):
        op("vs", lambda e, c=c: e.tensor_copy(xb1[:, c * T2:(c + 1) * T2], xfm(c)))
    ln_stats(lambda c: xb1[:, c * T2:(c + 1) * T2], T2, 0, extra_sq=xsq_a)
    bcast_row(rows_sb[5:6, :], T2, rbc_a)
    wait("vector", "vs")
    for c in range(8):
        op("vs", lambda e, c=c: e.tensor_mul(
            xb1[:, c * T2:(c + 1) * T2], xb1[:, c * T2:(c + 1) * T2], rbc_a[:, 0:T2]))
    v_xhat1 = P.ctrs["vs"].n

    wqf = [SA("wqf0", [128, C], F32), SA("wqf1", [128, C], F32)]
    wqb = SA("wqb", [128, 8 * 128], BF16)
    wqb8 = SA("wqb8", [8, 1], F32)
    qout = SA("qout", [128, T2], BF16)
    qsq = SA("qsq", [128, T2], BF16)
    wq_all = SA("wq_all", [128, 8 * C], BF16)
    conv_done = []
    for c in range(8):
        b = c % 2
        if c >= 2:
            wait("sync", "vs", conv_done[c - 2])
        op("qs", lambda e, c=c, b=b: e.dma_start(out=wqf[b][:], in_=wq.ap()[c]))
        wait("vector", "qs")
        conv_done.append(op("vs", lambda e, c=c, b=b: e.tensor_copy(
            wq_all[:, c * C:(c + 1) * C], wqf[b][:])))
        wait("tensor", "vs", conv_done[-1])
        for h in range(2):
            op("ts", lambda e, c=c, h=h: e.matmul(
                PS[0:1, h * 512:(h + 1) * 512], ones_sb[:, 0:1],
                wq_all[:, c * C + h * 512:c * C + (h + 1) * 512],
                start=(c == 0), stop=(c == 7), skip_group_check=True))
    wait("vector", "ts")
    op("vs", lambda e: e.tensor_copy(csum_sb[2:3, 0:C], PS[0:1, 0:C]))
    wait("tensor", "vs")
    wait("tensor", "vs", v_xhat1)
    qevs = []
    for fc in range(8):
        wait("scalar", "qg")
        for t0 in range(0, T2, 512):
            bk = 2 + ((fc * 4 + t0 // 512) % 2)
            if len(qevs) >= 2:
                wait("tensor", "ss", qevs[-2])
            for cc in range(8):
                op("ts", lambda e, fc=fc, t0=t0, cc=cc, bk=bk: e.matmul(
                    PS[:, bk * 512:(bk + 1) * 512],
                    wq_all[:, cc * C + fc * 128:cc * C + fc * 128 + 128],
                    xb1[:, cc * T2 + t0:cc * T2 + t0 + 512],
                    start=(cc == 0), stop=False, skip_group_check=True))
            op("ts", lambda e, fc=fc, t0=t0, bk=bk: e.matmul(
                PS[:, bk * 512:(bk + 1) * 512],
                csum_sb[2:3, fc * 128:fc * 128 + 128],
                rows_sb[0:1, t0:t0 + 512],
                start=False, stop=True, skip_group_check=True))
            wait("scalar", "ts")
            qevs.append(op("ss", lambda e, t0=t0, bk=bk: e.activation(
                qout[:, t0:t0 + 512], PS[:, bk * 512:(bk + 1) * 512], Act.Copy)))
        wait("gpsimd", "ss")
        op("qg", lambda e, fc=fc: e.dma_start(out=q_scr.ap()[fc], in_=qout[:]))
        wait("scalar", "ts")
        op("ss", lambda e: e.activation(qsq[:], qout[:], Act.Square))
        wait("tensor", "ss")
        for t0 in range(0, T2, 512):
            op("ts", lambda e, fc=fc, t0=t0: e.matmul(
                PS[0:1, 2048 + t0:2048 + t0 + 512], ones_sb[:, 0:1],
                qsq[:, t0:t0 + 512], start=(fc == 0), stop=(fc == 7),
                skip_group_check=True))
        if "d_q" in dbg_t:
            op("qg", lambda e, fc=fc: e.dma_start(out=dbg_t["d_q"].ap()[fc], in_=qout[:]))
    wait("vector", "ts")
    op("vs", lambda e: e.reduce_max(c0s[0:1, 0:1], PS[0:1, 2048:4096],
                                    axis=mybir.AxisListType.X))
    SF("xb1", "wqf0", "wqf1", "wqb", "wqb8", "qout", "qsq", "wq_all", "rbc_a", "xsq_a")
    bar()

    xbf = SA("xbf", [128, 8 * T], BF16)
    xfull_f = [SA("xfull_f0", [128, T], F32)]
    xsq_b = SA("xsq_b", [128, T2], BF16)
    rbc_b = SA("rbc_b", [128, T], BF16)

    cd = []
    for c in range(8):
        if c >= 1:
            wait("sync", "vs", cd[c - 1])
        op("qs", lambda e, c=c: e.dma_start(out=xfull_f[0][:], in_=x_full.ap()[c]))
        wait("vector", "qs")
        cd.append(op("vs", lambda e, c=c: e.tensor_copy(
            xbf[:, c * T:(c + 1) * T], xfull_f[0][:])))
    ln_stats(lambda c: xbf[:, c * T:(c + 1) * T], T, 1, extra_sq=xsq_b)
    bcast_row(rows_sb[5:6, :], T, rbc_b)
    for c in range(8):
        op("vs", lambda e, c=c: e.tensor_mul(
            xbf[:, c * T:(c + 1) * T], xbf[:, c * T:(c + 1) * T], rbc_b[:, 0:T]))
    v_xhatf = P.ctrs["vs"].n
    if "d_xhat1" in dbg_t:
        wait("gpsimd", "vs", v_xhatf)
        for c in range(8):
            op("qg", lambda e, c=c: e.dma_start(
                out=dbg_t["d_xhat1"].ap()[c], in_=xbf[:, c * T:(c + 1) * T]))
    SF("xfull_f0", "xsq_b", "rbc_b")

    wkvd_all = SA("wkvd_all", [128, 8 * KVR], BF16)
    wkvdf = SA("wkvdf", [128, KVR], F32)
    for c in range(8):
        wait("sync", "vs")
        op("qs", lambda e, c=c: e.dma_start(out=wkvdf[:], in_=wkvd.ap()[c]))
        wait("vector", "qs")
        op("vs", lambda e, c=c: e.tensor_copy(
            wkvd_all[:, c * KVR:(c + 1) * KVR], wkvdf[:]))
        wait("tensor", "vs")
        op("ts", lambda e, c=c: e.matmul(
            PS[0:1, 0:KVR], ones_sb[:, 0:1], wkvd_all[:, c * KVR:(c + 1) * KVR],
            start=(c == 0), stop=(c == 7), skip_group_check=True))
    wait("vector", "ts")
    op("vs", lambda e: e.tensor_copy(csum_sb[3:4, 0:KVR], PS[0:1, 0:KVR]))
    wait("tensor", "vs")
    lat_sb = SA("lat_sb", [128, T], BF16)
    wait("tensor", "vs", v_xhatf)
    ev = []
    for t0 in range(0, T, 512):
        bk = 2 + (t0 // 512) % 2
        if len(ev) >= 2:
            wait("tensor", "ss", ev[-2])
        for cc in range(8):
            op("ts", lambda e, t0=t0, cc=cc, bk=bk: e.matmul(
                PS[:, bk * 512:(bk + 1) * 512],
                wkvd_all[:, cc * KVR:(cc + 1) * KVR],
                xbf[:, cc * T + t0:cc * T + t0 + 512],
                start=(cc == 0), stop=False, skip_group_check=True))
        op("ts", lambda e, t0=t0, bk=bk: e.matmul(
            PS[:, bk * 512:(bk + 1) * 512], csum_sb[3:4, 0:KVR],
            rows_sb[1:2, t0:t0 + 512], start=False, stop=True,
            skip_group_check=True))
        wait("scalar", "ts")
        ev.append(op("ss", lambda e, t0=t0, bk=bk: e.activation(
            lat_sb[:, t0:t0 + 512], PS[:, bk * 512:(bk + 1) * 512], Act.Copy)))
    wkuk_sb = SA("wkuk_sb", [128, C], BF16)
    wkuv_sb = SA("wkuv_sb", [128, C], BF16)
    kchunk = SA("kchunk", [128, T], BF16)
    wkf = kchunk[:].bitcast(F32)[:, 0:C]
    op("qs", lambda e: e.dma_start(out=wkf, in_=wkvu_k.ap()))
    wait("vector", "qs")
    op("vs", lambda e: e.tensor_copy(wkuk_sb[:], wkf))
    wait("sync", "vs")
    op("qs", lambda e: e.dma_start(out=wkf, in_=wkvu_v.ap()))
    wait("vector", "qs")
    v_wkuv = op("vs", lambda e: e.tensor_copy(wkuv_sb[:], wkf))
    ksq = SA("ksq", [128, T2], BF16)
    wait("tensor", "vs", v_wkuv)
    wait("tensor", "ss")
    for fc in range(8):
        wait("scalar", "qg")
        ev = []
        for t0 in range(0, T, 512):
            bk = 2 + (t0 // 512) % 2
            if len(ev) >= 2:
                wait("tensor", "ss", ev[-2])
            op("ts", lambda e, fc=fc, t0=t0, bk=bk: e.matmul(
                PS[:, bk * 512:(bk + 1) * 512],
                wkuk_sb[:, fc * 128:(fc + 1) * 128],
                lat_sb[:, t0:t0 + 512], start=True, stop=True))
            wait("scalar", "ts")
            ev.append(op("ss", lambda e, t0=t0, bk=bk: e.activation(
                kchunk[:, t0:t0 + 512], PS[:, bk * 512:(bk + 1) * 512], Act.Copy)))
        wait("gpsimd", "ss")
        op("qg", lambda e, fc=fc: e.dma_start(out=kT_scr.ap()[fc], in_=kchunk[:]))
        if "d_k" in dbg_t:
            op("qg", lambda e, fc=fc: e.dma_start(out=dbg_t["d_k"].ap()[fc], in_=kchunk[:]))
        for half in range(2):
            wait("scalar", "ts")
            op("ss", lambda e, half=half: e.activation(
                ksq[:], kchunk[:, half * 2048:(half + 1) * 2048], Act.Square))
            wait("tensor", "ss")
            for t0 in range(0, 2048, 512):
                op("ts", lambda e, fc=fc, t0=t0, half=half: e.matmul(
                    PS[0:1, 2048 + t0:2048 + t0 + 512],
                    ones_sb[:, 0:1], ksq[:, t0:t0 + 512],
                    start=(fc == 0 and half == 0), stop=(fc == 7 and half == 1),
                    skip_group_check=True))
    wait("vector", "ts")
    op("vs", lambda e: e.reduce_max(c0s[0:1, 1:2], PS[0:1, 2048:4096],
                                    axis=mybir.AxisListType.X))
    vchunk = SA("vchunk", [128, C], BF16)
    wait("tensor", "vs")
    ev = []
    for tc in range(32):
        bk = 2 + (tc % 2) * 2
        wait("scalar", "qg")
        if len(ev) >= 2:
            wait("tensor", "ss", ev[-2])
        for d0 in range(0, C, 512):
            op("ts", lambda e, tc=tc, d0=d0, bk=bk: e.matmul(
                PS[:, (bk + d0 // 512) * 512:(bk + d0 // 512 + 1) * 512],
                lat_sb[:, tc * 128:(tc + 1) * 128],
                wkuv_sb[:, d0:d0 + 512], start=True, stop=True))
        wait("scalar", "ts")
        ev.append(op("ss", lambda e, bk=bk: e.activation(
            vchunk[:], PS[:, bk * 512:(bk + 2) * 512], Act.Copy)))
        wait("gpsimd", "ss")
        op("qg", lambda e, tc=tc: e.dma_start(out=v_scr.ap()[tc], in_=vchunk[:]))
    op("vs", lambda e: e.tensor_mul(c0s[0:1, 2:3], c0s[0:1, 0:1], c0s[0:1, 1:2]))
    wait("scalar", "vs")
    op("ss", lambda e: e.activation(c0s[0:1, 3:4], c0s[0:1, 2:3], Act.Sqrt))
    wait("vector", "ss")
    op("vs", lambda e: e.tensor_scalar_mul(c0s[0:1, 4:5], c0s[0:1, 3:4], -QSCALE))
    op("vs", lambda e: e.tensor_copy(c0b[0:1, 0:1], c0s[0:1, 4:5]))
    wait("tensor", "vs")
    op("ts", lambda e: e.matmul(PS[:, 0:1], ones_sb[0:1, :], c0b[0:1, 0:1],
                                start=True, stop=True))
    wait("vector", "ts")
    op("vs", lambda e: e.tensor_copy(cvec_sb[:, 0:1], PS[:, 0:1]))
    SF("wkvd_all", "wkvdf", "lat_sb", "wkuk_sb", "wkuv_sb",
       "kchunk", "ksq", "vchunk")
    bar()
    SF("xbf")

    dens_sb = SA("dens_sb", [16, T2], BF16)
    ysb = SA("ysb", [128, 8 * T2], BF16)
    rcp2 = SA("rcp2", [2, T2], BF16)
    khp = [SA("khp0", [128, T], BF16), SA("khp1", [128, T], BF16)]
    qhp = [SA("qhp0", [128, T2], BF16), SA("qhp1", [128, T2], BF16)]
    vstrip = [SA("vstrip0", [128, 32 * 130], BF16), SA("vstrip1", [128, 32 * 130], BF16)]
    ptile = [SA("ptile0", [128, 1024], BF16), SA("ptile1", [128, 1024], BF16)]
    for b in range(2):
        op("vs", lambda e, b=b: e.memset(
            vstrip[b][:].rearrange("p (c w) -> p c w", w=130)[:, :, 64:65], 1.0))
        op("vs", lambda e, b=b: e.memset(
            vstrip[b][:].rearrange("p (c w) -> p c w", w=130)[:, :, 129:130], 1.0))
    v_ones = P.ctrs["vs"].n
    wait("sync", "qg")

    attn_iters = []
    for hp in range(8):
        for slot in range(4):
            for c in range(8 * slot + 8):
                attn_iters.append((hp, slot, c))

    qload_snap = {}
    vload_snap = {}
    pv_done_snap = {}
    exp_snap = {}
    maskv_snap = {}
    st_ready = {}
    for hp in range(8):
        bq = hp % 2
        if hp >= 2:
            wait("sync", "ts", pv_done_snap[(hp - 2, 3)])
        op("qs", lambda e, hp=hp, bq=bq: e.dma_start(out=qhp[bq][:],
                                                     in_=q_scr.ap()[hp]))
        op("qs", lambda e, hp=hp, bq=bq: e.dma_start(out=khp[bq][:],
                                                     in_=kT_scr.ap()[hp]))
        qload_snap[hp] = P.ctrs["qs"].n
        for slot in range(4):
            ext = 8 * slot + 8
            bv = (hp * 4 + slot) % 2
            if (hp * 4 + slot) >= 2:
                prev = (hp, slot - 2) if slot >= 2 else (hp - 1, slot + 2)
                wait("sync", "ts", pv_done_snap[prev])
            for h in range(2):
                op("qs", lambda e, hp=hp, ext=ext, bv=bv, h=h: e.dma_start(
                    out=vstrip[bv][:].rearrange("p (c w) -> p c w", w=130)
                    [:, 0:ext, h * 65:h * 65 + 64],
                    in_=v_scr.ap()[0:ext, :, hp * 128 + h * 64:hp * 128 + h * 64 + 64]
                    .transpose([1, 0, 2])))
            vload_snap[(hp, slot)] = P.ctrs["qs"].n
            obank = 4 + 2 * (slot % 2)
            pv_args = {}
            for c in range(ext):
                it = (hp, slot, c)
                jmin = max(0, (c - 8 * slot) // 2)
                W = 512 - jmin * 128
                ping = c % 2
                wait("tensor", "qs", qload_snap[hp])
                wait("tensor", "qs", vload_snap[(hp, slot)])
                wait("tensor", "vs", v_ones)
                if (hp, slot, c - 2) in exp_snap:
                    wait("tensor", "ss", exp_snap[(hp, slot, c - 2)])
                for h in range(2):
                    op("ts", lambda e, hp=hp, slot=slot, c=c, h=h, jmin=jmin,
                       ping=ping, bq=bq, W=W: e.matmul(
                           PS[:, (ping * 2 + h) * 512 + jmin * 128:
                              (ping * 2 + h) * 512 + 512],
                           khp[bq][h * 64:(h + 1) * 64,
                                   c * 128:c * 128 + 128],
                           qhp[bq][h * 64:(h + 1) * 64,
                                   slot * 512 + jmin * 128:slot * 512 + 512],
                           start=True, stop=True, tile_position=(h * 64, 0),
                           skip_group_check=True))
                st_ready[it] = P.ctrs["ts"].n

                def emit_pv(cc):
                    itp = (hp, slot, cc)
                    if itp in maskv_snap:
                        wait("tensor", "vs", maskv_snap[itp])
                    else:
                        wait("tensor", "ss", exp_snap[itp])
                    jm, bvv = pv_args[cc]
                    for h in range(2):
                        op("ts", lambda e, cc=cc, h=h, jm=jm, bvv=bvv, ob=obank,
                           ex=ext: e.matmul(
                            PS[0:65, (ob + h) * 512 + jm * 128:
                               (ob + h) * 512 + 512],
                            vstrip[bvv][:, cc * 130 + h * 65:cc * 130 + h * 65 + 65],
                            ptile[cc % 2][:, h * 512 + jm * 128:h * 512 + 512],
                            start=(cc == 0), stop=(cc == ex - 1),
                            skip_group_check=True))
                pv_args[c] = (jmin, bv)
                if c >= 1:
                    emit_pv(c - 1)
                    pv_done_snap[(hp, slot)] = P.ctrs["ts"].n

                wait("scalar", "ts", st_ready[it])
                exp_snap[it] = op("ss", lambda e, ping=ping, jmin=jmin, W=W, c=c:
                                  e.activation(
                    ptile[c % 2][:, 0:1024].rearrange("p (h w) -> p h w", h=2)
                    [:, :, jmin * 128:512],
                    PS[:, ping * 1024:ping * 1024 + 1024]
                    .rearrange("p (h w) -> p h w", h=2)[:, :, jmin * 128:512],
                    Act.Exp, bias=cvec_sb[:, 0:1], scale=QSCALE), sw=False)
                mj = None
                if c % 2 == 0 and 0 <= (c // 2) - 4 * slot <= 3:
                    mj = ((c // 2) - 4 * slot, maska_sb)
                if c % 2 == 1 and 0 <= ((c - 1) // 2) - 4 * slot <= 3:
                    mj = (((c - 1) // 2) - 4 * slot, maskb_sb)
                if mj is not None:
                    j, msk = mj
                    wait("vector", "ss", exp_snap[it])
                    for h in range(2):
                        op("vs", lambda e, h=h, j=j, msk=msk, c=c: e.tensor_mul(
                            ptile[c % 2][:, h * 512 + j * 128:h * 512 + (j + 1) * 128],
                            ptile[c % 2][:, h * 512 + j * 128:h * 512 + (j + 1) * 128],
                            msk[:]), sw=False)
                    maskv_snap[it] = P.ctrs["vs"].n
            emit_pv(ext - 1)
            pv_done_snap[(hp, slot)] = P.ctrs["ts"].n
            wait("vector", "ts", pv_done_snap[(hp, slot)])
            for h in range(2):
                op("vs", lambda e, hp=hp, slot=slot, h=h, obank=obank: e.tensor_copy(
                    ysb[h * 64:(h + 1) * 64, hp * T2 + slot * 512:hp * T2 + slot * 512 + 512],
                    PS[0:64, (obank + h) * 512:(obank + h) * 512 + 512]))
            op("vs", lambda e, obank=obank: e.tensor_copy(
                ptile[0][64:65, 0:1024],
                PS[64:65, obank * 512:obank * 512 + 1024]))
            wait("gpsimd", "vs")
            op("qg", lambda e, hp=hp, slot=slot: e.dma_start(
                out=dens_sb[2 * hp:2 * hp + 2, slot * 512:slot * 512 + 512],
                in_=ptile[0][64:65, 0:1024]))
            dens_g = P.ctrs["qg"].n
            P.wait("scalar", "qg", dens_g)
            wait("tensor", "vs")
    wait("vector", "qg")
    op("vs", lp(lambda e: e.reciprocal(dens_sb[:, :], dens_sb[:, :])))
    v_rcp = P.ctrs["vs"].n
    wait("tensor", "vs", v_rcp)
    wait("tensor", "ss")
    for hp in range(8):
        wait("sync", "vs")
        op("qs", lambda e, hp=hp: e.dma_start(
            out=rcp2[:], in_=dens_sb[2 * hp:2 * hp + 2, :]))
        q_r = P.ctrs["qs"].n
        wait("tensor", "qs", q_r)
        for t0 in range(0, T2, 512):
            bk = (t0 // 512) % 4
            wait("tensor", "vs")
            op("ts", lambda e, t0=t0, bk=bk: e.matmul(
                PS[:, bk * 512:(bk + 1) * 512], sel2_sb[:, :],
                rcp2[0:2, t0:t0 + 512], start=True, stop=True))
            wait("vector", "ts")
            op("vs", lambda e, hp=hp, t0=t0, bk=bk: e.tensor_mul(
                ysb[:, hp * T2 + t0:hp * T2 + t0 + 512],
                ysb[:, hp * T2 + t0:hp * T2 + t0 + 512],
                PS[:, bk * 512:(bk + 1) * 512]))
    SF("qhp0", "qhp1", "vstrip0", "vstrip1", "ptile0", "ptile1", "khp0", "khp1")
    woutf = SA("woutf", [128, C], F32)
    wout_all = SA("wout_all", [128, 8 * C], BF16)
    for hc in range(8):
        wait("sync", "vs")
        op("qs", lambda e, hc=hc: e.dma_start(out=woutf[:], in_=wout.ap()[hc]))
        wait("vector", "qs")
        op("vs", lambda e, hc=hc: e.tensor_copy(
            wout_all[:, hc * C:(hc + 1) * C], woutf[:]))
    v_wout = P.ctrs["vs"].n
    wait("tensor", "vs", v_wout)
    evs = []
    for cc in range(8):
        for slot in range(4):
            bk = 4 + (cc * 4 + slot) % 2
            if len(evs) >= 2:
                wait("tensor", "vs", evs[-2])
            for hc in range(8):
                op("ts", lambda e, cc=cc, slot=slot, hc=hc, bk=bk: e.matmul(
                    PS[:, bk * 512:(bk + 1) * 512],
                    wout_all[:, hc * C + cc * 128:hc * C + cc * 128 + 128],
                    ysb[:, hc * T2 + slot * 512:hc * T2 + slot * 512 + 512],
                    start=(hc == 0), stop=(hc == 7), skip_group_check=True))
            wait("vector", "ts")
            evs.append(op("vs", lambda e, cc=cc, slot=slot, bk=bk: e.tensor_add(
                xfm(cc)[:, slot * 512:slot * 512 + 512],
                xfm(cc)[:, slot * 512:slot * 512 + 512],
                PS[:, bk * 512:(bk + 1) * 512])))
    SF("woutf", "wout_all", "rcp2", "ysb", "dens_sb")
    if "d_x1" in dbg_t:
        wait("gpsimd", "vs")
        for c in range(8):
            op("qg", lambda e, c=c: e.dma_start(out=dbg_t["d_x1"].ap()[c], in_=xfm(c)))
    bar()

    def emit_ffn(ln_row, dbg_name, sfx=""):
        xb = SA("f_xb" + sfx, [128, 8 * T2], BF16)
        xsq = SA("f_xsq" + sfx, [128, T2], BF16)
        rbc = SA("f_rbc" + sfx, [128, T2], BF16)
        wait("vector", "qs")
        for c in range(8):
            op("vs", lambda e, c=c: e.tensor_copy(xb[:, c * T2:(c + 1) * T2], xfm(c)))
        ln_stats(lambda c: xb[:, c * T2:(c + 1) * T2], T2, ln_row, extra_sq=xsq)
        bcast_row(rows_sb[5:6, :], T2, rbc)
        for c in range(8):
            op("vs", lambda e, c=c: e.tensor_mul(
                xb[:, c * T2:(c + 1) * T2], xb[:, c * T2:(c + 1) * T2], rbc[:, 0:T2]))
        v_xh = P.ctrs["vs"].n
        wait("tensor", "vs", v_xh)
        wait("sync", "vs", v_xh)
        wait("gpsimd", "vs", v_xh)
        SF("f_xsq" + sfx, "f_rbc" + sfx)

        wgt = [SA("f_wg0" + sfx, [128, 1024], BF16), SA("f_wg1" + sfx, [128, 1024], BF16)]
        wdt = [SA("f_wd0" + sfx, [128, 3072], BF16)]
        wu0 = SA("f_wu0" + sfx, [128, 1024], BF16)
        wut = [wu0[:], wdt[0][:, 0:1024]]
        hact = SA("f_hact" + sfx, [128, 24 * 1024], BF16)
        tdone = {}
        hdone = {}
        dcons = {}
        for tt in range(2):
            t0 = tt * 1024
            for h in range(24):
                bw = h % 2
                key = (tt, h)
                if h >= 2:
                    wait("sync", "ts", tdone[(tt, h - 2)])
                elif tt:
                    wait("sync", "ts")
                op("qs", lambda e, h=h, bw=bw: e.dma_start(
                    out=wgt[bw][:].rearrange("p (c j) -> p c j", j=128),
                    in_=wg_scr.ap()[h].transpose([1, 0, 2])))
                op("qs", lambda e, h=h, bw=bw: e.dma_start(
                    out=wut[bw].rearrange("p (c j) -> p c j", j=128),
                    in_=wu_scr.ap()[h].transpose([1, 0, 2])))
                wl = P.ctrs["qs"].n
                wait("tensor", "qs", wl)
                gb = (h % 2) * 2048
                if (tt, h - 2) in hdone:
                    wait("tensor", "vs", hdone[(tt, h - 2)])
                elif tt and h <= 1:
                    wait("tensor", "vs", dcons[(tt - 1, 7)])
                for half in range(2):
                    hh = t0 + half * 512
                    for cc in range(8):
                        op("ts", lambda e, cc=cc, hh=hh, bw=bw, half=half, gb=gb: e.matmul(
                            PS[:, gb + half * 512:gb + (half + 1) * 512],
                            wgt[bw][:, cc * 128:(cc + 1) * 128],
                            xb[:, cc * T2 + hh:cc * T2 + hh + 512],
                            start=(cc == 0), stop=False, skip_group_check=True))
                    op("ts", lambda e, h=h, hh=hh, half=half, ln_row=ln_row, gb=gb: e.matmul(
                        PS[:, gb + half * 512:gb + (half + 1) * 512],
                        csum_sb[0:1, h * 128:(h + 1) * 128],
                        rows_sb[ln_row:ln_row + 1, hh:hh + 512],
                        start=False, stop=True, skip_group_check=True))
                    for cc in range(8):
                        op("ts", lambda e, cc=cc, hh=hh, half=half, gb=gb, bw=bw: e.matmul(
                            PS[:, gb + 1024 + half * 512:gb + 1024 + (half + 1) * 512],
                            wut[bw][:, cc * 128:(cc + 1) * 128],
                            xb[:, cc * T2 + hh:cc * T2 + hh + 512],
                            start=(cc == 0), stop=False, skip_group_check=True))
                    t_up = op("ts", lambda e, h=h, hh=hh, half=half, ln_row=ln_row, gb=gb: e.matmul(
                        PS[:, gb + 1024 + half * 512:gb + 1024 + (half + 1) * 512],
                        csum_sb[1:2, h * 128:(h + 1) * 128],
                        rows_sb[ln_row:ln_row + 1, hh:hh + 512],
                        start=False, stop=True, skip_group_check=True))
                tdone[key] = t_up
                wait("scalar", "ts", t_up)
                s_g = op("ss", lambda e, h=h, gb=gb: e.activation(
                    hact[:, h * 1024:(h + 1) * 1024], PS[:, gb:gb + 1024],
                    Act.Silu, scale=FSC))
                wait("vector", "ss", s_g)
                hdone[key] = op("vs", lambda e, h=h, gb=gb: e.scalar_tensor_tensor(
                    hact[:, h * 1024:(h + 1) * 1024], PS[:, gb + 1024:gb + 2048],
                    FSC, hact[:, h * 1024:(h + 1) * 1024], Alu.mult, Alu.mult))
            wait("tensor", "vs", hdone[(tt, 23)])
            for cc in range(8):
                key = (tt, cc)
                db = (cc % 2) * 1024
                wait("sync", "ts")
                op("qs", lambda e, cc=cc: e.dma_start(
                    out=wdt[0][:].rearrange("p (h j) -> p h j", j=128),
                    in_=wd_scr.ap()[cc].transpose([1, 0, 2])))
                dl = P.ctrs["qs"].n
                wait("tensor", "qs", dl)
                prev2 = (tt, cc - 2) if cc >= 2 else ((tt - 1, 6 + cc) if tt else None)
                if prev2 is not None and prev2 in dcons:
                    wait("tensor", "vs", dcons[prev2])
                for half in range(2):
                    for h in range(24):
                        op("ts", lambda e, h=h, half=half, db=db: e.matmul(
                            PS[:, db + half * 512:db + (half + 1) * 512],
                            wdt[0][:, h * 128:(h + 1) * 128],
                            hact[:, h * 1024 + half * 512:h * 1024 + half * 512 + 512],
                            start=(h == 0), stop=(h == 23), skip_group_check=True))
                wait("vector", "ts")
                dcons[key] = op("vs", lambda e, cc=cc, t0=t0, db=db: e.scalar_tensor_tensor(
                    xfm(cc)[:, t0:t0 + 1024], PS[:, db:db + 1024],
                    FSC, xfm(cc)[:, t0:t0 + 1024], Alu.mult, Alu.add))
        SF("f_xb" + sfx, "f_wg0" + sfx, "f_wg1" + sfx, "f_wd0" + sfx,
           "f_wu0" + sfx, "f_hact" + sfx)
        if dbg_name in dbg_t:
            wait("gpsimd", "vs")
            for c in range(8):
                op("qg", lambda e, c=c, n=dbg_name: e.dma_start(
                    out=dbg_t[n].ap()[c], in_=xfm(c)))
        bar()

    emit_ffn(2, "d_x2", "a")

    xb3 = SA("m_xb3", [128, 8 * T2], BF16)
    xsq3 = SA("m_xsq3", [128, T2], BF16)
    rbc3 = SA("m_rbc3", [128, T2], BF16)
    wait("vector", "qs")
    for c in range(8):
        op("vs", lambda e, c=c: e.tensor_copy(xb3[:, c * T2:(c + 1) * T2], xfm(c)))
    ln_stats(lambda c: xb3[:, c * T2:(c + 1) * T2], T2, 6, extra_sq=xsq3)
    bcast_row(rows_sb[5:6, :], T2, rbc3)
    for c in range(8):
        op("vs", lambda e, c=c: e.tensor_mul(
            xb3[:, c * T2:(c + 1) * T2], xb3[:, c * T2:(c + 1) * T2], rbc3[:, 0:T2]))
    v_x3h = P.ctrs["vs"].n
    wait("gpsimd", "vs", v_x3h)
    for c in range(8):
        op("qg", lambda e, c=c: e.dma_start(
            out=ln3_in.ap()[c * 128:(c + 1) * 128, :], in_=xb3[:, c * T2:(c + 1) * T2]))
    op("qg", lambda e: e.dma_start(out=ln3_in.ap()[1024:1025, :], in_=rows_sb[6:7, 0:T2]))
    for c in range(8):
        op("qg", lambda e, c=c: e.dma_start(out=park_scr.ap()[c], in_=xfm(c)))
    g_ln3 = P.ctrs["qg"].n
    wait("gpsimd", "qg", g_ln3)
    bar()
    op("gc", lambda e: e.collective_compute(
        "AllGather", Alu.bypass, replica_groups=GROUPS,
        ins=[ln3_in.ap().opt()], outs=[ln3_out.ap().opt()]))
    SF("m_xb3", "m_xsq3", "m_rbc3")
    bar()

    rhsln = xfm_h[0][:].bitcast(BF16)
    for c in range(8):
        for b in range(2):
            op("qs", lambda e, c=c, b=b: e.dma_start(
                out=rhsln[:, c * T:(c + 1) * T]
                .rearrange("p (g j) -> p g j", j=128)[:, b::2, :],
                in_=ln3_out.ap()[b, c * 128:(c + 1) * 128, :]
                .rearrange("p (g j) -> p g j", j=128).transpose([0, 1, 2])))
    for b in range(2):
        op("qs", lambda e, b=b: e.dma_start(
            out=rows_sb[3:4, 0:T].rearrange("o (g j) -> o g j", j=128)[:, b::2, :],
            in_=ln3_out.ap()[b, 1024:1025, :].rearrange("o (g j) -> o g j", j=128)))
    q_rhs = P.ctrs["qs"].n

    inpf = SA("m_inpf", [128, 1024], F32)
    inpb = [SA("m_inpb0", [128, 1024], BF16), SA("m_inpb1", [128, 1024], BF16)]
    xzc = [SA("m_xzc", [128, T], F32), SA("m_xzc1", [128, T], F32)]
    cacc = SA("m_cacc", [128, T], F32)
    xcsb = SA("m_xcsb", [128, T], BF16)
    wait("tensor", "qs", q_rhs)
    evx = []
    for ec in range(16):
        bw = ec % 2
        if ec > 0:
            wait("scalar", "qg")
        wait("sync", "vs")
        op("qs", lambda e, ec=ec: e.dma_start(
            out=inpf[:].rearrange("p (c j) -> p c j", j=128),
            in_=inpT.ap()[:, :, ec * 128:(ec + 1) * 128].transpose([1, 0, 2])))
        wait("vector", "qs")
        v_cv = op("vs", lambda e, bw=bw: e.tensor_copy(inpb[bw][:], inpf[:]))
        wait("tensor", "vs", v_cv)
        for cc in range(8):
            op("ts", lambda e, cc=cc, bw=bw: e.matmul(
                PS[0:1, 3584:3584 + 128],
                ones_sb[:, 0:1], inpb[bw][:, cc * 128:(cc + 1) * 128],
                start=(cc == 0), stop=(cc == 7), skip_group_check=True))
        wait("vector", "ts")
        op("vs", lambda e, ec=ec: e.tensor_copy(
            csum_sb[4:5, ec * 128:(ec + 1) * 128], PS[0:1, 3584:3584 + 128]))
        wait("tensor", "vs")
        for t0 in range(0, T, 512):
            bk = (t0 // 512) % 2
            if len(evx) >= 2:
                wait("tensor", "ss", evx[-2])
            for cc in range(8):
                op("ts", lambda e, cc=cc, t0=t0, bw=bw, bk=bk: e.matmul(
                    PS[:, bk * 512:(bk + 1) * 512],
                    inpb[bw][:, cc * 128:(cc + 1) * 128],
                    rhsln[:, cc * T + t0:cc * T + t0 + 512],
                    start=(cc == 0), stop=False, skip_group_check=True))
            op("ts", lambda e, ec=ec, t0=t0, bk=bk: e.matmul(
                PS[:, bk * 512:(bk + 1) * 512],
                csum_sb[4:5, ec * 128:ec * 128 + 128],
                rows_sb[3:4, t0:t0 + 512], start=False, stop=True,
                skip_group_check=True))
            if ec < 8 and evx:
                wait("scalar", "vs")
            wait("scalar", "ts")
            if ec < 8:
                evx.append(op("ss", lambda e, t0=t0, bk=bk, bz=ec % 2: e.activation(
                    xzc[bz][:, t0:t0 + 512], PS[:, bk * 512:(bk + 1) * 512], Act.Copy)))
            else:
                evx.append(op("ss", lambda e, t0=t0, bk=bk: e.activation(
                    xcsb[:, t0:t0 + 512], PS[:, bk * 512:(bk + 1) * 512], Act.Silu)))
        if ec >= 8:
            wait("gpsimd", "ss")
            op("qg", lambda e, ec=ec: e.dma_start(out=zs_scr.ap()[ec - 8], in_=xcsb[:]))
            continue
        wait("vector", "ss")
        op("vs", lambda e, ec=ec, bz=ec % 2: e.tensor_scalar_mul(
            cacc[:], xzc[bz][:], convw_sb[:, ec * 4 + 3:ec * 4 + 4]))
        for k in range(3):
            sh = 3 - k
            op("vs", lambda e, ec=ec, k=k, sh=sh, bz=ec % 2: e.scalar_tensor_tensor(
                cacc[:, sh:T], xzc[bz][:, 0:T - sh], convw_sb[:, ec * 4 + k:ec * 4 + k + 1],
                cacc[:, sh:T], Alu.mult, Alu.add))
        v_conv = P.ctrs["vs"].n
        wait("scalar", "vs", v_conv)
        op("ss", lambda e, ec=ec: e.activation(
            xcsb[:], cacc[:], Act.Silu, bias=convb_sb[:, ec:ec + 1]))
        wait("gpsimd", "ss")
        op("qg", lambda e, ec=ec: e.dma_start(out=xc_scr.ap()[ec], in_=xcsb[:]))
        if "d_xcs" in dbg_t:
            op("qg", lambda e, ec=ec: e.dma_start(out=dbg_t["d_xcs"].ap()[ec], in_=xcsb[:]))
    SF("m_inpf", "m_inpb0", "m_inpb1", "m_xzc", "m_xzc1", "m_cacc", "m_xcsb")
    bar()

    dbcb = SA("m_dbcb", [128, T], BF16)
    xpf = SA("m_xpf", [128, 768], F32)
    xpb = SA("m_xpb", [128, 768], BF16)
    op("qs", lambda e: e.dma_start(
        out=xpf[:].rearrange("p (c j) -> p c j", j=96), in_=xpT.ap().transpose([1, 0, 2])))
    wait("vector", "qs")
    op("vs", lambda e: e.tensor_copy(xpb[:], xpf[:]))
    dbcp = SA("m_dbcp", [128, T], F32)
    xcq = [SA("m_xcq0", [128, 512], BF16), SA("m_xcq1", [128, 512], BF16)]
    wait("tensor", "vs")
    for ts8 in range(8):
        t0 = ts8 * 512
        bk = ts8 % 2
        wait("tensor", "vs")
        for ec in range(8):
            bq = (ts8 * 8 + ec) % 2
            if ts8 * 8 + ec >= 2:
                wait("sync", "ts")
            op("qs", lambda e, ec=ec, t0=t0, bq=bq: e.dma_start(
                out=xcq[bq][:], in_=xc_scr.ap()[ec, :, t0:t0 + 512]))
            wait("tensor", "qs")
            op("ts", lambda e, ec=ec, t0=t0, bq=bq, bk=bk: e.matmul(
                PS[0:96, bk * 512:bk * 512 + 512],
                xpb[:, ec * 96:(ec + 1) * 96], xcq[bq][:],
                start=(ec == 0), stop=(ec == 7), skip_group_check=True))
        wait("vector", "ts")
        op("vs", lambda e, t0=t0, bk=bk: e.tensor_copy(
            dbcp[0:96, t0:t0 + 512], PS[0:96, bk * 512:bk * 512 + 512]))
    wait("gpsimd", "vs")
    op("qg", lambda e: e.dma_start(out=dbc_in.ap(), in_=dbcp[0:96, :]))
    wait("gpsimd", "qg")
    op("gc", lambda e: e.collective_compute(
        "AllReduce", Alu.add, replica_groups=GROUPS,
        ins=[dbc_in.ap().opt()], outs=[dbc_out.ap().opt()]))
    wait("sync", "gc")
    op("qs", lambda e: e.dma_start(out=dbcp[0:96, :], in_=dbc_out.ap()))
    q_dbc = P.ctrs["qs"].n
    if "d_dbc" in dbg_t:
        wait("gpsimd", "qs", q_dbc)
        op("qg", lambda e: e.dma_start(out=dbg_t["d_dbc"].ap()[0:96, :], in_=dbcp[0:96, :]))
    wait("vector", "qs", q_dbc)
    op("vs", lambda e: e.tensor_copy(dbcb[0:96, :], dbcp[0:96, :]))
    v_dbcb = P.ctrs["vs"].n
    wait("gpsimd", "vs", v_dbcb)
    op("qg", lambda e: e.dma_start(out=dbc_bf.ap(), in_=dbcb[64:96, :]))
    dttf = SA("m_dttf", [64, EO], F32)
    dttb = SA("m_dttb", [64, EO], BF16)
    op("qs", lambda e: e.dma_start(out=dttf[:], in_=dtT.ap()))
    wait("vector", "qs")
    op("vs", lambda e: e.tensor_copy(dttb[:], dttf[:]))
    v_dtt = P.ctrs["vs"].n
    dlt = SA("m_dlt", [128, T], F32)
    wait("tensor", "vs", max(v_dbcb, v_dtt))
    for ec in range(8):
        for t0 in range(0, T, 512):
            op("ts", lambda e, ec=ec, t0=t0: e.matmul(
                PS[:, t0:t0 + 512], dttb[:, ec * 128:(ec + 1) * 128],
                dbcb[0:64, t0:t0 + 512], start=True, stop=True))
        wait("scalar", "ts")
        op("ss", lambda e, ec=ec: e.activation(
            dlt[:], PS[:, 0:T], Act.Exp, bias=dtb_sb[:, ec:ec + 1]))
        op("ss", lambda e: e.activation(dlt[:], dlt[:], Act.Ln, bias=1.0))
        wait("gpsimd", "ss")
        op("qg", lambda e, ec=ec: e.dma_start(out=dl_scr.ap()[ec], in_=dlt[:]))
        if "d_delta" in dbg_t:
            op("qg", lambda e, ec=ec: e.dma_start(out=dbg_t["d_delta"].ap()[ec], in_=dlt[:]))
        wait("tensor", "qg")
    SF("m_xpf", "m_xpb", "m_dbcp", "m_xcq0", "m_xcq1", "m_dttf", "m_dttb", "m_dlt", "m_dbcb")
    bar()

    bcst = SA("m_bcst", [128, 32 * 1024], BF16)
    dq = [SA("m_dq0", [128, 1024], F32), SA("m_dq1", [128, 1024], F32)]
    xq = [SA("m_xq0", [128, 1024], BF16), SA("m_xq1", [128, 1024], BF16)]
    uq = SA("m_uq", [128, 1024], BF16)
    dA = [PS[:, 0:1024], PS[:, 1024:2048]]
    bxq = [SA("m_bxq", [128, 1024], BF16), SA("m_bxq1", [128, 1024], BF16)]
    hsq = SA("m_hsq", [128, 1024], BF16)
    ytq = [SA("m_ytq0", [128, 1024], BF16), SA("m_ytq1", [128, 1024], BF16)]
    zq = [SA("m_zq0", [128, 1024], BF16), SA("m_zq1", [128, 1024], BF16)]
    ymq = uq
    dxq = bxq[0]
    ld_snap = {}
    ymq_g = {}
    for tq in range(4):
        t0 = tq * 1024
        wait("sync", "vs")
        for n in range(NST):
            for r in range(2):
                dfrow = n if r == 0 else 16 + n
                slot = 2 * n + r
                op("qs", lambda e, dfrow=dfrow, slot=slot, t0=t0: e.dma_start(
                    out=bcst[:, slot * 1024:(slot + 1) * 1024],
                    in_=dbc_bf.ap()[dfrow:dfrow + 1, t0:t0 + 1024]
                    .partition_broadcast(128).squeeze(1)))
        bc_q = P.ctrs["qs"].n
        wait("vector", "qs", bc_q)
        def emit_loads(ec_):
            bi_ = ec_ % 2
            wait("sync", "vs")
            op("qs", lambda e, ec=ec_, t0=t0, bi=bi_: e.dma_start(
                out=dq[bi][:], in_=dl_scr.ap()[ec, :, t0:t0 + 1024]))
            op("qs", lambda e, ec=ec_, t0=t0, bi=bi_: e.dma_start(
                out=xq[bi][:], in_=xc_scr.ap()[ec, :, t0:t0 + 1024]))
            op("qs", lambda e, ec=ec_, t0=t0, bi=bi_: e.dma_start(
                out=zq[bi][:], in_=zs_scr.ap()[ec, :, t0:t0 + 1024]))
            ld_snap[(tq, ec_)] = P.ctrs["qs"].n
        for ec in range(8):
            bi = ec % 2
            yb = 2048 + (ec % 2) * 1024
            if ec == 0:
                emit_loads(0)
            wait("vector", "qs", ld_snap[(tq, ec)])
            op("vs", lambda e, bi=bi: e.tensor_mul(uq[:], dq[bi][:], xq[bi][:]))
            if ec < 7:
                emit_loads(ec + 1)
            wait("scalar", "vs")
            acc_prev = None
            acc_prev2 = None
            scan_v = {}
            for n in range(NST):
                ba = n % 2
                col = ec * 16 + n
                if n >= 2:
                    wait("scalar", "vs", scan_v[n - 2])
                op("ss", lambda e, ba=ba, col=col, bi=bi: e.activation(
                    dA[ba], dq[bi][:], Act.Exp, scale=amat_sb[:, col:col + 1]),
                    sw=False)
                exp_s = P.ctrs["ss"].n
                op("vs", lambda e, n=n, ba=ba: e.tensor_mul(
                    bxq[ba][:], uq[:], bcst[:, 2 * n * 1024:(2 * n + 1) * 1024]),
                    sw=(n < 2))
                wait("vector", "ss", exp_s)
                scan_v[n] = op("vs", lambda e, ba=ba, col=col: e.tensor_tensor_scan(
                    hsq[:], dA[ba], bxq[ba][:], state_sb[:, col:col + 1],
                    Alu.mult, Alu.add))
                op("vs", lambda e, col=col: e.tensor_copy(
                    state_sb[:, col:col + 1], hsq[:, 1023:1024]))
                if acc_prev2 is not None:
                    wait("vector", "ts", acc_prev2)
                op("vs", lambda e, n=n, ba=ba: e.tensor_mul(
                    ytq[ba][:], hsq[:],
                    bcst[:, (2 * n + 1) * 1024:(2 * n + 2) * 1024]),
                    sw=(n < 2))
                wait("tensor", "vs")
                for hh in range(2):
                    op("ts", lambda e, hh=hh, n=n, yb=yb, ba=ba: e.matmul(
                        PS[:, yb + hh * 512:yb + (hh + 1) * 512],
                        eye_sb[:], ytq[ba][:, hh * 512:(hh + 1) * 512],
                        start=(n == 0), stop=False, skip_group_check=True))
                acc_prev2 = acc_prev
                acc_prev = P.ctrs["ts"].n
            if acc_prev is not None:
                wait("vector", "ts", acc_prev)
            op("vs", lambda e, bi=bi, ec=ec: e.tensor_scalar_mul(
                dxq[:], xq[bi][:], dpar_sb[:, ec:ec + 1]))
            wait("tensor", "vs")
            for hh in range(2):
                op("ts", lambda e, hh=hh, yb=yb: e.matmul(
                    PS[:, yb + hh * 512:yb + (hh + 1) * 512],
                    eye_sb[:], dxq[:, hh * 512:(hh + 1) * 512],
                    start=False, stop=(hh == 1), skip_group_check=True))
            t_acc = P.ctrs["ts"].n
            wait("vector", "ts", t_acc)
            ymq_g[(tq, ec)] = op("vs", lambda e, bi=bi, yb=yb: e.tensor_mul(
                ymq[:], PS[:, yb:yb + 1024], zq[bi][:]))
            wait("gpsimd", "vs", ymq_g[(tq, ec)])
            op("qg", lambda e, ec=ec, t0=t0: e.dma_start(
                out=ym_scr.ap()[ec, :, t0:t0 + 1024], in_=ymq[:]))
            if "d_ymul" in dbg_t:
                op("qg", lambda e, ec=ec, t0=t0: e.dma_start(
                    out=dbg_t["d_ymul"].ap()[ec, :, t0:t0 + 1024], in_=ymq[:]))
            wait("vector", "qg")
    SF("m_bcst", "m_dq0", "m_xq0", "m_xq1",
       "m_dq1", "m_uq", "m_bxq", "m_bxq1", "m_hsq", "m_ytq0", "m_ytq1",
       "m_zq0", "m_zq1")
    bar()

    outf = SA("c4_outf", [128, C], F32)
    outt = SA("c4_outt", [128, 8 * C], BF16)
    for ec in range(8):
        wait("sync", "vs")
        op("qs", lambda e, ec=ec: e.dma_start(out=outf[:], in_=outT.ap()[ec]))
        wait("vector", "qs")
        op("vs", lambda e, ec=ec: e.tensor_copy(outt[:, ec * C:(ec + 1) * C], outf[:]))
    v_outt = P.ctrs["vs"].n
    wait("tensor", "vs", v_outt)
    ymgs = SA("c4_ymgs", [128, 8 * 512], BF16)
    prt = [SA("c4_prt", [128, 512], F32), SA("c4_prt1", [128, 512], F32)]
    prt_g = {}
    for gs in range(8):
        wait("sync", "ts")
        op("qs", lambda e, gs=gs: e.dma_start(
            out=ymgs[:].rearrange("p (c j) -> p c j", j=512),
            in_=ym_scr.ap()[:, :, gs * 512:(gs + 1) * 512].transpose([1, 0, 2])))
        q_ym = P.ctrs["qs"].n
        wait("tensor", "qs", q_ym)
        for cc in range(8):
            bk = (gs * 8 + cc) % 2
            wait("tensor", "vs")
            for ec in range(8):
                op("ts", lambda e, cc=cc, ec=ec, bk=bk: e.matmul(
                    PS[:, bk * 512:(bk + 1) * 512],
                    outt[:, ec * C + cc * 128:ec * C + cc * 128 + 128],
                    ymgs[:, ec * 512:(ec + 1) * 512],
                    start=(ec == 0), stop=(ec == 7), skip_group_check=True))
            wait("vector", "ts")
            bp = (gs * 8 + cc) % 2
            wait("vector", "qg", prt_g.get(bp, 0))
            op("vs", lambda e, bk=bk, bp=bp: e.tensor_copy(
                prt[bp][:], PS[:, bk * 512:(bk + 1) * 512]))
            wait("gpsimd", "vs")
            for bb_ in range(2):
                op("qg", lambda e, cc=cc, gs=gs, bb_=bb_, bp=bp: e.dma_start(
                    out=rs_in.ap()[bb_, cc, :, gs * 256:gs * 256 + 256]
                    .rearrange("p (u j) -> p u j", j=128),
                    in_=prt[bp][:].rearrange("p (u v j) -> p u v j", u=2, v=2)[:, :, bb_, :]))
            prt_g[bp] = P.ctrs["qg"].n
    g_rs = P.ctrs["qg"].n
    wait("gpsimd", "qg", g_rs)
    op("gc", lambda e: e.collective_compute(
        "ReduceScatter", Alu.add, replica_groups=GROUPS,
        ins=[rs_in.ap().opt()], outs=[rs_out.ap().opt()]))
    SF("c4_outf", "c4_outt", "c4_ymgs", "c4_prt", "c4_prt1")
    bar()
    for c in range(8):
        op("qs", lambda e, c=c: e.dma_start(out=xfm(c), in_=park_scr.ap()[c]))
    rsl = [SA("c4_rsl0", [128, T2], F32), SA("c4_rsl1", [128, T2], F32)]
    wait("sync", "gc")
    for c in range(8):
        bi = c % 2
        if c >= 2:
            wait("sync", "vs")
        op("qs", lambda e, c=c, bi=bi: e.dma_start(out=rsl[bi][:], in_=rs_out.ap()[c]))
        wait("vector", "qs")
        op("vs", lambda e, c=c, bi=bi: e.tensor_add(xfm(c), xfm(c), rsl[bi][:]))
    SF("c4_rsl0", "c4_rsl1")
    if "d_x3" in dbg_t:
        wait("gpsimd", "vs")
        for c in range(8):
            op("qg", lambda e, c=c: e.dma_start(out=dbg_t["d_x3"].ap()[c], in_=xfm(c)))
    bar()

    emit_ffn(4, "none", "b")
    for c in range(8):
        op("qg", lambda e, c=c: e.dma_start(out=y_own.ap()[c], in_=xfm(c)))
    g_out = P.ctrs["qg"].n
    wait("gpsimd", "qg", g_out)
    bar()

    block = nc.Block().__enter__()
    P.finalize(block)
    block.__exit__(None, None, None)
    return nc


def _prep_core(inputs, core):
    seq, par = core // 2, core % 2
    f32 = np.float32
    x = np.asarray(inputs["L0_attn_a_x"], dtype=f32)
    xT = np.ascontiguousarray(x[seq].T)
    own_cols = np.concatenate(
        [np.arange((2 * i + par) * 128, (2 * i + par) * 128 + 128) for i in range(16)])
    m = {}
    m["x_full"] = np.ascontiguousarray(xT.reshape(8, 128, T))
    m["x_own"] = np.ascontiguousarray(xT[:, own_cols].reshape(8, 128, T2))
    m["wq"] = np.ascontiguousarray(
        np.asarray(inputs["L0_attn_a_w_q"], f32).reshape(8, 128, C))
    m["wkvd"] = np.ascontiguousarray(
        np.asarray(inputs["L0_attn_a_w_kv_down"], f32).reshape(8, 128, KVR))
    wkvu = np.asarray(inputs["L0_attn_a_w_kv_up"], f32)
    m["wkvu_k"] = np.ascontiguousarray(wkvu[:, :C])
    m["wkvu_v"] = np.ascontiguousarray(wkvu[:, C:])
    m["wout"] = np.ascontiguousarray(
        np.asarray(inputs["L0_attn_a_w_out"], f32).reshape(8, 128, C))
    wg = np.asarray(inputs["L0_ffn_w_gate"], np.int32)
    wu = np.asarray(inputs["L0_ffn_w_up"], np.int32)
    wd = np.asarray(inputs["L0_ffn_w_down"], np.int32)
    m["wgT"] = np.ascontiguousarray(wg.T.reshape(8, 128, FH))
    m["wuT"] = np.ascontiguousarray(wu.T.reshape(8, 128, FH))
    m["wdT"] = np.ascontiguousarray(wd.T.reshape(24, 128, C))
    ipw = np.asarray(inputs["L0_attn_b_in_proj_w"], f32)
    sel = np.concatenate([ipw[par * EO:(par + 1) * EO],
                          ipw[EF + par * EO:EF + (par + 1) * EO]])
    m["inpT"] = np.ascontiguousarray(sel.T.reshape(8, 128, EF))
    xpw = np.asarray(inputs["L0_attn_b_x_proj_w"], f32)
    m["xpT"] = np.ascontiguousarray(
        xpw[:, par * EO:(par + 1) * EO].T.reshape(8, 128, 96))
    dtw = np.asarray(inputs["L0_attn_b_dt_w"], f32)
    m["dtT"] = np.ascontiguousarray(dtw[par * EO:(par + 1) * EO].T)
    opw = np.asarray(inputs["L0_attn_b_out_proj_w"], f32)
    m["outT"] = np.ascontiguousarray(
        opw[:, par * EO:(par + 1) * EO].T.reshape(8, 128, C))
    alog = np.asarray(inputs["L0_attn_b_A_log"], f32)[par * EO:(par + 1) * EO]
    m["a_log"] = np.ascontiguousarray(
        alog.reshape(8, 128, 16).transpose(1, 0, 2).reshape(128, 128))
    cw = np.asarray(inputs["L0_attn_b_conv_w"], f32)[par * EO:(par + 1) * EO]
    m["convw"] = np.ascontiguousarray(
        cw.reshape(8, 128, 4).transpose(1, 0, 2).reshape(128, 32))
    for name, key in [("convb", "L0_attn_b_conv_b"), ("dtb", "L0_attn_b_dt_b"),
                      ("dpar", "L0_attn_b_D_param")]:
        v = np.asarray(inputs[key], f32)[par * EO:(par + 1) * EO]
        m[name] = np.ascontiguousarray(v.reshape(8, 128).T)
    import ml_dtypes
    bf = ml_dtypes.bfloat16
    m["ones128"] = np.ones((128, 128), dtype=bf)
    m["eye128"] = np.eye(128, dtype=np.float32).astype(bf)
    s2 = np.zeros((2, 128), np.float32)
    s2[0, 0:64] = 1.0
    s2[1, 64:128] = 1.0
    m["sel2"] = s2.astype(bf)
    tril = np.triu(np.ones((128, 128), np.float32))
    m["mask_a"] = (tril if par == 0 else np.ones((128, 128), np.float32)).astype(bf)
    m["mask_b"] = (np.zeros((128, 128), np.float32) if par == 0 else tril).astype(bf)
    return m


_NC_CACHE = {}


def kernel(**inputs):
    key = "main"
    if key not in _NC_CACHE:
        _NC_CACHE[key] = build_nc()
    nc = _NC_CACHE[key]
    in_maps = [_prep_core(inputs, core) for core in range(8)]
    res = run_bass_kernel_spmd(nc, in_maps, core_ids=list(range(8)))
    out = np.empty((B, T, C), dtype=np.float32)
    for core in range(8):
        seq, par = core // 2, core % 2
        yo = np.asarray(res.results[core]["y_own"])
        yo = yo.reshape(C, T2)
        cols = np.concatenate(
            [np.arange((2 * i + par) * 128, (2 * i + par) * 128 + 128)
             for i in range(16)])
        out[seq][cols, :] = yo.T
    return out


# version 06
# speedup vs baseline: 1.3175x; 1.0084x over previous
import numpy as np
from contextlib import ExitStack

import concourse.bass as bass
import concourse.mybir as mybir
from concourse.bass_utils import run_bass_kernel_spmd

F32 = mybir.dt.float32
BF16 = mybir.dt.bfloat16
I32 = mybir.dt.int32
Alu = mybir.AluOpType
Act = mybir.ActivationFunctionType

B, T, C = 4, 4096, 1024
T2 = T // 2
NH, HD, KVR = 16, 64, 128
EF, EO, NST, DTR = 2048, 1024, 16, 64
FH = 3072
QSCALE = 0.125
FSC = 1.0 / 127.0
EPS = 1e-5
GROUPS = [[0, 1], [2, 3], [4, 5], [6, 7]]
ENGS = ("tensor", "vector", "scalar", "sync", "gpsimd")


class Ctr:
    __slots__ = ("name", "sem", "step", "n", "is_dma", "watermark", "settled")

    def __init__(self, name, sem, step):
        self.name, self.sem, self.step, self.n = name, sem, step, 0
        self.is_dma = step == 16
        self.watermark = 0
        self.settled = 0


class Prog:

    def __init__(self, nc, E):
        self.nc = nc
        self.parts = {e: [] for e in ENGS}
        self.ctrs = {}
        for name, eng, step in [("ts", "tensor", 1), ("vs", "vector", 1),
                                ("ss", "scalar", 1), ("gc", "gpsimd", 1),
                                ("qs", "sync", 16), ("qg", "gpsimd", 16)]:
            self.ctrs[name] = Ctr(name, E(nc.semaphore(name)), step)
        self.home = {"ts": "tensor", "vs": "vector", "ss": "scalar",
                     "gc": "gpsimd", "qs": "sync", "qg": "gpsimd"}
        self.waited = {e: {} for e in ENGS}

    def wait(self, eng, cname, val=None):
        c = self.ctrs[cname]
        val = c.n if val is None else val
        if val <= 0 or self.waited[eng].get(cname, -1) >= val:
            return
        if c.is_dma and val < c.n:
            val = c.n
            if self.waited[eng].get(cname, -1) >= val:
                return
        self.waited[eng][cname] = val
        if c.is_dma and eng != self.home[cname]:
            c.watermark = max(c.watermark, val)
        self.parts[eng].append(lambda e, s=c.sem, v=val: e.wait_ge(s, v))

    def op(self, cname, fn, sw=True):
        c = self.ctrs[cname]
        eng = self.home[cname]
        if sw and eng in ("vector", "scalar"):
            self.wait(eng, cname, c.n)
        if c.is_dma and c.watermark > c.settled and c.n > c.settled:
            self.waited[eng][cname] = max(self.waited[eng].get(cname, -1), c.n)
            self.parts[eng].append(lambda e, s=c.sem, v=c.n: e.wait_ge(s, v))
            c.settled = c.n
        c.n += c.step
        self.parts[eng].append(
            lambda e, fn=fn, s=c.sem, k=c.step: fn(e).then_inc(s, k))
        return c.n

    def raw(self, eng, fn):
        self.parts[eng].append(fn)

    def bar(self):
        vals = {k: c.n for k, c in self.ctrs.items()}
        for k, c in self.ctrs.items():
            if c.is_dma:
                self.wait(self.home[k], k, c.n)
                c.settled = c.n
        for e in ENGS:
            for k, v in vals.items():
                self.wait(e, k, v)

    def finalize(self, block):
        getattr(block, "tensor")(lambda e: [f(e) for f in self.parts["tensor"]])
        getattr(block, "vector")(lambda e: [f(e) for f in self.parts["vector"]])
        getattr(block, "scalar")(lambda e: [f(e) for f in self.parts["scalar"]])
        getattr(block, "sync")(lambda e: [f(e) for f in self.parts["sync"]])
        getattr(block, "gpsimd")(lambda e: [f(e) for f in self.parts["gpsimd"]])


def build_nc(dbg=(), groups=None):
    nc = bass.Bass("TRN2", target_bir_lowering=False)
    dbg = set(dbg)
    global GROUPS
    GROUPS = groups if groups is not None else [[0, 1], [2, 3], [4, 5], [6, 7]]
    ctx = ExitStack()
    E = ctx.enter_context

    def DI(name, shape, dtype=F32):
        return nc.dram_tensor(name, list(shape), dtype, kind="ExternalInput")

    x_own = DI("x_own", [8, 128, T2])
    x_full = DI("x_full", [8, 128, T])
    wq = DI("wq", [8, 128, C])
    wkvd = DI("wkvd", [8, 128, KVR])
    wkvu_k = DI("wkvu_k", [128, C])
    wkvu_v = DI("wkvu_v", [128, C])
    wout = DI("wout", [8, 128, C])
    wgT = DI("wgT", [8, 128, FH], I32)
    wuT = DI("wuT", [8, 128, FH], I32)
    wdT = DI("wdT", [24, 128, C], I32)
    inpT = DI("inpT", [8, 128, EF])
    xpT = DI("xpT", [8, 128, 96])
    dtT = DI("dtT", [64, EO])
    outT = DI("outT", [8, 128, C])
    a_log = DI("a_log", [128, 128])
    convw = DI("convw", [128, 32])
    convb = DI("convb", [128, 8])
    dtb = DI("dtb", [128, 8])
    dpar = DI("dpar", [128, 8])
    ones128 = DI("ones128", [128, 128], BF16)
    eye128 = DI("eye128", [128, 128], BF16)
    sel2 = DI("sel2", [2, 128], BF16)
    mask_a = DI("mask_a", [128, 128], BF16)
    mask_b = DI("mask_b", [128, 128], BF16)

    y_own = nc.dram_tensor("y_own", [8, 128, T2], F32, kind="ExternalOutput")

    dbg_t = {}
    for name, shape, dt in [
        ("d_xhat1", [8, 128, T], BF16), ("d_q", [8, 128, T2], BF16),
        ("d_k", [8, 128, T], BF16), ("d_x1", [8, 128, T2], F32),
        ("d_x2", [8, 128, T2], F32), ("d_dbc", [128, T], F32),
        ("d_delta", [8, 128, T], F32), ("d_xcs", [8, 128, T], BF16),
        ("d_x3", [8, 128, T2], F32), ("d_ymul", [8, 128, T], BF16),
    ]:
        if name in dbg:
            dbg_t[name] = nc.dram_tensor(name, shape, dt, kind="ExternalOutput")

    kT_scr = nc.dram_tensor("kT_scr", [8, 128, T], BF16)
    v_scr = nc.dram_tensor("v_scr", [32, 128, C], BF16)
    q_scr = nc.dram_tensor("q_scr", [8, 128, T2], BF16)
    wg_scr = nc.dram_tensor("wg_scr", [24, 8, 128, 128], BF16)
    wu_scr = nc.dram_tensor("wu_scr", [24, 8, 128, 128], BF16)
    wd_scr = nc.dram_tensor("wd_scr", [8, 24, 128, 128], BF16)
    xc_scr = nc.dram_tensor("xc_scr", [8, 128, T], BF16)
    zs_scr = nc.dram_tensor("zs_scr", [8, 128, T], BF16)
    dl_scr = nc.dram_tensor("dl_scr", [8, 128, T], F32)
    ym_scr = nc.dram_tensor("ym_scr", [8, 128, T], BF16)
    park_scr = nc.dram_tensor("park_scr", [8, 128, T2], F32)
    ln3_in = nc.dram_tensor("ln3_in", [1025, T2], BF16)
    ln3_out = nc.dram_tensor("ln3_out", [2, 1025, T2], BF16)
    dbc_in = nc.dram_tensor("dbc_in", [96, T], F32)
    dbc_out = nc.dram_tensor("dbc_out", [96, T], F32)
    dbc_bf = nc.dram_tensor("dbc_bf", [32, T], BF16)
    rs_in = nc.dram_tensor("rs_in", [2, 8, 128, T2], F32)
    rs_out = nc.dram_tensor("rs_out", [8, 128, T2], F32)

    P = Prog(nc, E)
    sb_frees = {}
    sb_order = []

    def SA(name, shape, dtype):
        cm = nc.sbuf_tensor(name, list(shape), dtype)
        t = cm.__enter__()
        sb_frees[name] = cm
        sb_order.append(name)
        return t

    def SF(*names):
        for n in sorted(names, key=lambda n: -sb_order.index(n)):
            assert n == sb_order[-1], f"non-LIFO free {n} vs stack {sb_order[-6:]}"
            sb_order.pop()
            sb_frees.pop(n).__exit__(None, None, None)

    PS = nc.psum_tensor("PS", [128, 4096], F32).__enter__()

    ones_sb = SA("ones_sb", [128, 128], BF16)
    eye_sb = SA("eye_sb", [128, 128], BF16)
    sel2_sb = SA("sel2_sb", [2, 128], BF16)
    maska_sb = SA("maska_sb", [128, 128], BF16)
    maskb_sb = SA("maskb_sb", [128, 128], BF16)
    class _RowMap:
        def __init__(self, raw, width, slots):
            self.raw, self.w, self.slots = raw, width, slots
        def __getitem__(self, key):
            rs, cs = key
            off = self.slots[rs.start]
            a = cs.start or 0
            b = cs.stop if cs.stop is not None else self.w
            return self.raw[0:1, off + a:off + b]

    rows_raw = SA("rows_raw", [1, T2 + 2 * T], BF16)
    rows_sb = _RowMap(rows_raw, T, {0: 0, 2: 0, 4: 0, 6: 0, 5: T2,
                                    1: T2 + T, 3: T2 + T})
    csum_raw = SA("csum_raw", [1, 8192], BF16)
    csum_sb = _RowMap(csum_raw, 8192, {0: 0, 1: 3072, 2: 6144, 3: 7168, 4: 6144})
    srow = SA("srow", [1, 2048], F32)
    scol = SA("scol", [128, 160], F32)
    sbcol = SA("sbcol", [128, 64], BF16)
    c0s = SA("c0s", [1, 8], F32)
    c0b = SA("c0b", [1, 2], BF16)
    cvec_sb = SA("cvec_sb", [128, 8], F32)
    amat_sb = SA("amat_sb", [128, 128], F32)
    convw_sb = SA("convw_sb", [128, 32], F32)
    convb_sb = SA("convb_sb", [128, 8], F32)
    dtb_sb = SA("dtb_sb", [128, 8], F32)
    dpar_sb = SA("dpar_sb", [128, 8], F32)
    state_sb = SA("state_sb", [128, 128], F32)
    xfm_h = [SA("x_fm", [128, 8 * T2], F32)]

    def xfm(c):
        return xfm_h[0][:, c * T2:(c + 1) * T2]

    op, wait, bar = P.op, P.wait, P.bar

    def lp(fn):
        def go(e):
            with nc.allow_low_precision(reason="intentional bf16 elementwise"):
                return fn(e)
        return go

    al_sb = SA("al_sb", [128, 128], F32)
    for dst, src in [(ones_sb, ones128), (eye_sb, eye128), (sel2_sb, sel2),
                     (maska_sb, mask_a), (maskb_sb, mask_b), (convw_sb, convw),
                     (convb_sb, convb), (dtb_sb, dtb), (dpar_sb, dpar),
                     (al_sb, a_log)]:
        op("qs", lambda e, d=dst, s=src: e.dma_start(out=d[:], in_=s.ap()))
    wait("scalar", "qs")
    op("ss", lambda e: e.activation(amat_sb[:], al_sb[:], Act.Exp))
    wait("vector", "ss")
    op("vs", lambda e: e.tensor_scalar_mul(amat_sb[:], amat_sb[:], -1.0))
    op("vs", lambda e: e.memset(state_sb[:], 0.0))
    op("vs", lambda e: e.memset(cvec_sb[:, 1:2], EPS))

    wi = [SA("p0_wi0", [128, FH], I32), SA("p0_wi1", [128, FH], I32)]
    wb = [SA("p0_wb0", [128, FH], BF16), SA("p0_wb1", [128, FH], BF16)]
    conv_sets = [(wgT, wg_scr, 0, 8, FH), (wuT, wu_scr, 1, 8, FH)]
    conv_v = {}
    store_g = {}
    colsum_t = {}
    it = 0
    for src, dst, crow, nchunk, w in conv_sets:
        for c in range(nchunk):
            b = it % 2
            if it >= 2:
                wait("sync", "vs", conv_v[it - 2])
            op("qs", lambda e, b=b, src=src, c=c, w=w:
               e.dma_start(out=wi[b][:, 0:w], in_=src.ap()[c]))
            wait("vector", "qs")
            if it >= 2:
                wait("vector", "qg", store_g[it - 2])
                if it - 2 in colsum_t:
                    wait("vector", "ts", colsum_t[it - 2])
            conv_v[it] = op("vs", lambda e, b=b, w=w:
                            e.tensor_copy(wb[b][:, 0:w], wi[b][:, 0:w]))
            wait("gpsimd", "vs", conv_v[it])
            if crow is None:
                store_g[it] = op("qg", lambda e, b=b, dst=dst, c=c:
                                 e.dma_start(out=dst.ap()[:, c].transpose([1, 0, 2]),
                                             in_=wb[b][:, 0:C].rearrange("p (f j) -> p f j", j=128)))
            else:
                store_g[it] = op("qg", lambda e, b=b, dst=dst, c=c:
                                 e.dma_start(out=dst.ap()[:, c].transpose([1, 0, 2]),
                                             in_=wb[b][:, 0:FH].rearrange("p (h j) -> p h j", j=128)))
            if crow is not None:
                wait("tensor", "vs", conv_v[it])
                for h in range(6):
                    op("ts", lambda e, b=b, h=h, c=c, nchunk=nchunk:
                       e.matmul(PS[0:1, h * 512:(h + 1) * 512], ones_sb[:, 0:1],
                                wb[b][:, h * 512:(h + 1) * 512],
                                start=(c == 0), stop=(c == nchunk - 1),
                                skip_group_check=True))
                colsum_t[it] = P.ctrs["ts"].n
            it += 1
        if crow is not None:
            wait("vector", "ts")
            op("vs", lambda e, crow=crow: e.tensor_copy(
                csum_sb[crow:crow + 1, 0:FH], PS[0:1, 0:FH]))
            wait("tensor", "vs")
    SF("al_sb", "p0_wi0", "p0_wi1", "p0_wb0", "p0_wb1")
    bar()
    WD_PENDING = True

    def ln_stats(xb_chunks, ntok, row_idx, nchunks=8, extra_sq=None):
        xsq = extra_sq
        nc_cols = ntok // 128
        nhalf = max(1, ntok // 2048)
        for half in range(nhalf):
            h0 = half * 2048
            hw = min(2048, ntok - h0)
            hc = hw // 128
            wait("tensor", "vs")
            for c in range(nchunks):
                wait("scalar", "vs")
                wait("scalar", "ts")
                op("ss", lambda e, c=c, h0=h0, hw=hw:
                   e.activation(xsq[:, 0:hw], xb_chunks(c)[:, h0:h0 + hw], Act.Square))
                wait("tensor", "ss")
                for t0 in range(0, hw, 512):
                    op("ts", lambda e, c=c, h0=h0, t0=t0:
                       e.matmul(PS[0:1, t0:t0 + 512], ones_sb[:, 0:1],
                                xb_chunks(c)[:, h0 + t0:h0 + t0 + 512],
                                start=(c == 0), stop=(c == nchunks - 1),
                                skip_group_check=True))
                    op("ts", lambda e, c=c, t0=t0:
                       e.matmul(PS[0:1, 2048 + t0:2048 + t0 + 512], ones_sb[:, 0:1],
                                xsq[:, t0:t0 + 512],
                                start=(c == 0), stop=(c == nchunks - 1),
                                skip_group_check=True))
            wait("vector", "ts")
            wait("vector", "qs")
            op("vs", lambda e, hw=hw: e.tensor_copy(srow[0:1, 0:hw], PS[0:1, 0:hw]))
            wait("sync", "vs")
            op("qs", lambda e, half=half, hc=hc, hw=hw: e.dma_start(
                out=scol[:, half * 16:half * 16 + hc], in_=srow[0:1, 0:hw]))
            qsx = P.ctrs["qs"].n
            wait("vector", "qs", qsx)
            op("vs", lambda e, hw=hw: e.tensor_copy(srow[0:1, 0:hw],
                                                    PS[0:1, 2048:2048 + hw]))
            wait("sync", "vs")
            op("qs", lambda e, half=half, hc=hc, hw=hw: e.dma_start(
                out=scol[:, 32 + half * 16:32 + half * 16 + hc], in_=srow[0:1, 0:hw]))
        wait("vector", "qs")
        ncc = nc_cols
        op("vs", lambda e: e.tensor_scalar_mul(
            scol[:, 64:64 + ncc], scol[:, 0:ncc], 1.0 / 1024.0))
        op("vs", lambda e: e.tensor_mul(
            scol[:, 96:96 + ncc], scol[:, 64:64 + ncc], scol[:, 64:64 + ncc]))
        op("vs", lambda e: e.scalar_tensor_tensor(
            scol[:, 96:96 + ncc], scol[:, 32:32 + ncc], 1.0 / 1024.0,
            scol[:, 96:96 + ncc], Alu.mult, Alu.subtract))
        wait("scalar", "vs")
        op("ss", lambda e: e.activation(
            scol[:, 96:96 + ncc], scol[:, 96:96 + ncc], Act.Ln,
            bias=cvec_sb[:, 1:2]))
        op("ss", lambda e: e.activation(
            scol[:, 96:96 + ncc], scol[:, 96:96 + ncc], Act.Exp, scale=-0.5))
        wait("vector", "ss")
        op("vs", lambda e: e.scalar_tensor_tensor(
            scol[:, 128:128 + ncc], scol[:, 64:64 + ncc], -1.0,
            scol[:, 96:96 + ncc], Alu.mult, Alu.mult))
        op("vs", lambda e: e.tensor_copy(sbcol[:, 0:ncc], scol[:, 96:96 + ncc]))
        op("vs", lambda e: e.tensor_copy(sbcol[:, 32:32 + ncc], scol[:, 128:128 + ncc]))
        wait("sync", "vs")
        for half in range(nhalf):
            h0 = half * 2048
            hw = min(2048, ntok - h0)
            hc = hw // 128
            op("qs", lambda e, h0=h0, hw=hw, half=half, hc=hc: e.dma_start(
                out=rows_sb[5:6, h0:h0 + hw],
                in_=sbcol[:, half * 16:half * 16 + hc]))
            op("qs", lambda e, h0=h0, hw=hw, half=half, hc=hc, r=row_idx: e.dma_start(
                out=rows_sb[r:r + 1, h0:h0 + hw],
                in_=sbcol[:, 32 + half * 16:32 + half * 16 + hc]))
        wait("tensor", "qs")
        wait("vector", "qs")

    def bcast_row(row_ap, ntok, dst_sb, pbank=0):
        for t0 in range(0, ntok, 512):
            bk = pbank + (t0 // 512) % 2
            wait("tensor", "vs")
            op("ts", lambda e, t0=t0, bk=bk: e.matmul(
                PS[:, bk * 512:(bk + 1) * 512], ones_sb[0:1, :],
                row_ap[:, t0:t0 + 512], start=True, stop=True))
            wait("vector", "ts")
            op("vs", lambda e, t0=t0, bk=bk: e.tensor_copy(
                dst_sb[:, t0:t0 + 512], PS[:, bk * 512:(bk + 1) * 512]))

    for c in range(8):
        op("qs", lambda e, c=c: e.dma_start(out=xfm(c), in_=x_own.ap()[c]))
    q_xown = P.ctrs["qs"].n

    xb1 = SA("xb1", [128, 8 * T2], BF16)
    xsq_a = SA("xsq_a", [128, T2], BF16)
    rbc_a = SA("rbc_a", [128, T], BF16)
    wait("vector", "qs", q_xown)
    for c in range(8):
        op("vs", lambda e, c=c: e.tensor_copy(xb1[:, c * T2:(c + 1) * T2], xfm(c)))
    ln_stats(lambda c: xb1[:, c * T2:(c + 1) * T2], T2, 0, extra_sq=xsq_a)
    bcast_row(rows_sb[5:6, :], T2, rbc_a)
    wait("vector", "vs")
    for c in range(8):
        op("vs", lambda e, c=c: e.tensor_mul(
            xb1[:, c * T2:(c + 1) * T2], xb1[:, c * T2:(c + 1) * T2], rbc_a[:, 0:T2]))
    v_xhat1 = P.ctrs["vs"].n

    wqf = [SA("wqf0", [128, C], F32), SA("wqf1", [128, C], F32)]
    wqb = SA("wqb", [128, 8 * 128], BF16)
    wqb8 = SA("wqb8", [8, 1], F32)
    qout = SA("qout", [128, T2], BF16)
    qsq = SA("qsq", [128, T2], BF16)
    wq_all = SA("wq_all", [128, 8 * C], BF16)
    conv_done = []
    for c in range(8):
        b = c % 2
        if c >= 2:
            wait("sync", "vs", conv_done[c - 2])
        op("qs", lambda e, c=c, b=b: e.dma_start(out=wqf[b][:], in_=wq.ap()[c]))
        wait("vector", "qs")
        conv_done.append(op("vs", lambda e, c=c, b=b: e.tensor_copy(
            wq_all[:, c * C:(c + 1) * C], wqf[b][:])))
        wait("tensor", "vs", conv_done[-1])
        for h in range(2):
            op("ts", lambda e, c=c, h=h: e.matmul(
                PS[0:1, h * 512:(h + 1) * 512], ones_sb[:, 0:1],
                wq_all[:, c * C + h * 512:c * C + (h + 1) * 512],
                start=(c == 0), stop=(c == 7), skip_group_check=True))
    wait("vector", "ts")
    op("vs", lambda e: e.tensor_copy(csum_sb[2:3, 0:C], PS[0:1, 0:C]))
    wait("tensor", "vs")
    wait("tensor", "vs", v_xhat1)
    qevs = []
    for fc in range(8):
        wait("scalar", "qg")
        for t0 in range(0, T2, 512):
            bk = 2 + ((fc * 4 + t0 // 512) % 2)
            if len(qevs) >= 2:
                wait("tensor", "ss", qevs[-2])
            for cc in range(8):
                op("ts", lambda e, fc=fc, t0=t0, cc=cc, bk=bk: e.matmul(
                    PS[:, bk * 512:(bk + 1) * 512],
                    wq_all[:, cc * C + fc * 128:cc * C + fc * 128 + 128],
                    xb1[:, cc * T2 + t0:cc * T2 + t0 + 512],
                    start=(cc == 0), stop=False, skip_group_check=True))
            op("ts", lambda e, fc=fc, t0=t0, bk=bk: e.matmul(
                PS[:, bk * 512:(bk + 1) * 512],
                csum_sb[2:3, fc * 128:fc * 128 + 128],
                rows_sb[0:1, t0:t0 + 512],
                start=False, stop=True, skip_group_check=True))
            wait("scalar", "ts")
            qevs.append(op("ss", lambda e, t0=t0, bk=bk: e.activation(
                qout[:, t0:t0 + 512], PS[:, bk * 512:(bk + 1) * 512], Act.Copy)))
        wait("gpsimd", "ss")
        op("qg", lambda e, fc=fc: e.dma_start(out=q_scr.ap()[fc], in_=qout[:]))
        wait("scalar", "ts")
        op("ss", lambda e: e.activation(qsq[:], qout[:], Act.Square))
        wait("tensor", "ss")
        for t0 in range(0, T2, 512):
            op("ts", lambda e, fc=fc, t0=t0: e.matmul(
                PS[0:1, 2048 + t0:2048 + t0 + 512], ones_sb[:, 0:1],
                qsq[:, t0:t0 + 512], start=(fc == 0), stop=(fc == 7),
                skip_group_check=True))
        if "d_q" in dbg_t:
            op("qg", lambda e, fc=fc: e.dma_start(out=dbg_t["d_q"].ap()[fc], in_=qout[:]))
    wait("vector", "ts")
    op("vs", lambda e: e.reduce_max(c0s[0:1, 0:1], PS[0:1, 2048:4096],
                                    axis=mybir.AxisListType.X))
    SF("xb1", "wqf0", "wqf1", "wqb", "wqb8", "qout", "qsq", "wq_all", "rbc_a", "xsq_a")
    bar()

    xbf = SA("xbf", [128, 8 * T], BF16)
    xfull_f = [SA("xfull_f0", [128, T], F32)]
    xsq_b = SA("xsq_b", [128, T2], BF16)
    rbc_b = SA("rbc_b", [128, T], BF16)

    cd = []
    for c in range(8):
        if c >= 1:
            wait("sync", "vs", cd[c - 1])
        op("qs", lambda e, c=c: e.dma_start(out=xfull_f[0][:], in_=x_full.ap()[c]))
        wait("vector", "qs")
        cd.append(op("vs", lambda e, c=c: e.tensor_copy(
            xbf[:, c * T:(c + 1) * T], xfull_f[0][:])))
    ln_stats(lambda c: xbf[:, c * T:(c + 1) * T], T, 1, extra_sq=xsq_b)
    bcast_row(rows_sb[5:6, :], T, rbc_b)
    for c in range(8):
        op("vs", lambda e, c=c: e.tensor_mul(
            xbf[:, c * T:(c + 1) * T], xbf[:, c * T:(c + 1) * T], rbc_b[:, 0:T]))
    v_xhatf = P.ctrs["vs"].n
    if "d_xhat1" in dbg_t:
        wait("gpsimd", "vs", v_xhatf)
        for c in range(8):
            op("qg", lambda e, c=c: e.dma_start(
                out=dbg_t["d_xhat1"].ap()[c], in_=xbf[:, c * T:(c + 1) * T]))
    SF("xfull_f0", "xsq_b", "rbc_b")

    wkvd_all = SA("wkvd_all", [128, 8 * KVR], BF16)
    wkvdf = SA("wkvdf", [128, KVR], F32)
    for c in range(8):
        wait("sync", "vs")
        op("qs", lambda e, c=c: e.dma_start(out=wkvdf[:], in_=wkvd.ap()[c]))
        wait("vector", "qs")
        op("vs", lambda e, c=c: e.tensor_copy(
            wkvd_all[:, c * KVR:(c + 1) * KVR], wkvdf[:]))
        wait("tensor", "vs")
        op("ts", lambda e, c=c: e.matmul(
            PS[0:1, 0:KVR], ones_sb[:, 0:1], wkvd_all[:, c * KVR:(c + 1) * KVR],
            start=(c == 0), stop=(c == 7), skip_group_check=True))
    wait("vector", "ts")
    op("vs", lambda e: e.tensor_copy(csum_sb[3:4, 0:KVR], PS[0:1, 0:KVR]))
    wait("tensor", "vs")
    lat_sb = SA("lat_sb", [128, T], BF16)
    wait("tensor", "vs", v_xhatf)
    ev = []
    for t0 in range(0, T, 512):
        bk = 2 + (t0 // 512) % 2
        if len(ev) >= 2:
            wait("tensor", "ss", ev[-2])
        for cc in range(8):
            op("ts", lambda e, t0=t0, cc=cc, bk=bk: e.matmul(
                PS[:, bk * 512:(bk + 1) * 512],
                wkvd_all[:, cc * KVR:(cc + 1) * KVR],
                xbf[:, cc * T + t0:cc * T + t0 + 512],
                start=(cc == 0), stop=False, skip_group_check=True))
        op("ts", lambda e, t0=t0, bk=bk: e.matmul(
            PS[:, bk * 512:(bk + 1) * 512], csum_sb[3:4, 0:KVR],
            rows_sb[1:2, t0:t0 + 512], start=False, stop=True,
            skip_group_check=True))
        wait("scalar", "ts")
        ev.append(op("ss", lambda e, t0=t0, bk=bk: e.activation(
            lat_sb[:, t0:t0 + 512], PS[:, bk * 512:(bk + 1) * 512], Act.Copy)))
    wkuk_sb = SA("wkuk_sb", [128, C], BF16)
    wkuv_sb = SA("wkuv_sb", [128, C], BF16)
    kchunk = SA("kchunk", [128, T], BF16)
    wkf = kchunk[:].bitcast(F32)[:, 0:C]
    op("qs", lambda e: e.dma_start(out=wkf, in_=wkvu_k.ap()))
    wait("vector", "qs")
    op("vs", lambda e: e.tensor_copy(wkuk_sb[:], wkf))
    wait("sync", "vs")
    op("qs", lambda e: e.dma_start(out=wkf, in_=wkvu_v.ap()))
    wait("vector", "qs")
    v_wkuv = op("vs", lambda e: e.tensor_copy(wkuv_sb[:], wkf))
    ksq = SA("ksq", [128, T2], BF16)
    wait("tensor", "vs", v_wkuv)
    wait("tensor", "ss")
    for fc in range(8):
        wait("scalar", "qg")
        ev = []
        for t0 in range(0, T, 512):
            bk = 2 + (t0 // 512) % 2
            if len(ev) >= 2:
                wait("tensor", "ss", ev[-2])
            op("ts", lambda e, fc=fc, t0=t0, bk=bk: e.matmul(
                PS[:, bk * 512:(bk + 1) * 512],
                wkuk_sb[:, fc * 128:(fc + 1) * 128],
                lat_sb[:, t0:t0 + 512], start=True, stop=True))
            wait("scalar", "ts")
            ev.append(op("ss", lambda e, t0=t0, bk=bk: e.activation(
                kchunk[:, t0:t0 + 512], PS[:, bk * 512:(bk + 1) * 512], Act.Copy)))
        wait("gpsimd", "ss")
        op("qg", lambda e, fc=fc: e.dma_start(out=kT_scr.ap()[fc], in_=kchunk[:]))
        if "d_k" in dbg_t:
            op("qg", lambda e, fc=fc: e.dma_start(out=dbg_t["d_k"].ap()[fc], in_=kchunk[:]))
        for half in range(2):
            wait("scalar", "ts")
            op("ss", lambda e, half=half: e.activation(
                ksq[:], kchunk[:, half * 2048:(half + 1) * 2048], Act.Square))
            wait("tensor", "ss")
            for t0 in range(0, 2048, 512):
                op("ts", lambda e, fc=fc, t0=t0, half=half: e.matmul(
                    PS[0:1, 2048 + t0:2048 + t0 + 512],
                    ones_sb[:, 0:1], ksq[:, t0:t0 + 512],
                    start=(fc == 0 and half == 0), stop=(fc == 7 and half == 1),
                    skip_group_check=True))
    wait("vector", "ts")
    op("vs", lambda e: e.reduce_max(c0s[0:1, 1:2], PS[0:1, 2048:4096],
                                    axis=mybir.AxisListType.X))
    vchunk = SA("vchunk", [128, C], BF16)
    wait("tensor", "vs")
    ev = []
    for tc in range(32):
        bk = 2 + (tc % 2) * 2
        wait("scalar", "qg")
        if len(ev) >= 2:
            wait("tensor", "ss", ev[-2])
        for d0 in range(0, C, 512):
            op("ts", lambda e, tc=tc, d0=d0, bk=bk: e.matmul(
                PS[:, (bk + d0 // 512) * 512:(bk + d0 // 512 + 1) * 512],
                lat_sb[:, tc * 128:(tc + 1) * 128],
                wkuv_sb[:, d0:d0 + 512], start=True, stop=True))
        wait("scalar", "ts")
        ev.append(op("ss", lambda e, bk=bk: e.activation(
            vchunk[:], PS[:, bk * 512:(bk + 2) * 512], Act.Copy)))
        wait("gpsimd", "ss")
        op("qg", lambda e, tc=tc: e.dma_start(out=v_scr.ap()[tc], in_=vchunk[:]))
    op("vs", lambda e: e.tensor_mul(c0s[0:1, 2:3], c0s[0:1, 0:1], c0s[0:1, 1:2]))
    wait("scalar", "vs")
    op("ss", lambda e: e.activation(c0s[0:1, 3:4], c0s[0:1, 2:3], Act.Sqrt))
    wait("vector", "ss")
    op("vs", lambda e: e.tensor_scalar_mul(c0s[0:1, 4:5], c0s[0:1, 3:4], -QSCALE))
    op("vs", lambda e: e.tensor_copy(c0b[0:1, 0:1], c0s[0:1, 4:5]))
    wait("tensor", "vs")
    op("ts", lambda e: e.matmul(PS[:, 0:1], ones_sb[0:1, :], c0b[0:1, 0:1],
                                start=True, stop=True))
    wait("vector", "ts")
    op("vs", lambda e: e.tensor_copy(cvec_sb[:, 0:1], PS[:, 0:1]))
    SF("wkvd_all", "wkvdf", "lat_sb", "wkuk_sb", "wkuv_sb",
       "kchunk", "ksq", "vchunk")
    bar()
    SF("xbf")

    dens_sb = SA("dens_sb", [16, T2], BF16)
    ysb = SA("ysb", [128, 8 * T2], BF16)
    rcp2 = SA("rcp2", [2, T2], BF16)
    khp = [SA("khp0", [128, T], BF16), SA("khp1", [128, T], BF16)]
    qhp = [SA("qhp0", [128, T2], BF16), SA("qhp1", [128, T2], BF16)]
    vstrip = [SA("vstrip0", [128, 32 * 130], BF16), SA("vstrip1", [128, 32 * 130], BF16)]
    ptile = [SA("ptile0", [128, 1024], BF16), SA("ptile1", [128, 1024], BF16)]
    for b in range(2):
        op("vs", lambda e, b=b: e.memset(
            vstrip[b][:].rearrange("p (c w) -> p c w", w=130)[:, :, 64:65], 1.0))
        op("vs", lambda e, b=b: e.memset(
            vstrip[b][:].rearrange("p (c w) -> p c w", w=130)[:, :, 129:130], 1.0))
    v_ones = P.ctrs["vs"].n
    wait("sync", "qg")

    wdi = [SA("wdi0", [128, C], I32), SA("wdi1", [128, C], I32)]
    wdb = [SA("wdb0", [128, C], BF16), SA("wdb1", [128, C], BF16)]
    wd_v = {}
    wd_g = {}
    for c24 in range(24):
        b = c24 % 2
        if c24 >= 2:
            wait("sync", "vs", wd_v[c24 - 2])
        op("qs", lambda e, c24=c24, b=b: e.dma_start(out=wdi[b][:], in_=wdT.ap()[c24]))
        wait("vector", "qs")
        if c24 >= 2:
            wait("vector", "qg", wd_g[c24 - 2])
        wd_v[c24] = op("vs", lambda e, b=b: e.tensor_copy(wdb[b][:], wdi[b][:]))
        wait("gpsimd", "vs", wd_v[c24])
        wd_g[c24] = op("qg", lambda e, c24=c24, b=b: e.dma_start(
            out=wd_scr.ap()[:, c24].transpose([1, 0, 2]),
            in_=wdb[b][:].rearrange("p (f j) -> p f j", j=128)))
    SF("wdi0", "wdi1", "wdb0", "wdb1")

    attn_iters = []
    for hp in range(8):
        for slot in range(4):
            for c in range(8 * slot + 8):
                attn_iters.append((hp, slot, c))

    qload_snap = {}
    vload_snap = {}
    pv_done_snap = {}
    exp_snap = {}
    maskv_snap = {}
    st_ready = {}
    for hp in range(8):
        bq = hp % 2
        if hp >= 2:
            wait("sync", "ts", pv_done_snap[(hp - 2, 3)])
        op("qs", lambda e, hp=hp, bq=bq: e.dma_start(out=qhp[bq][:],
                                                     in_=q_scr.ap()[hp]))
        op("qs", lambda e, hp=hp, bq=bq: e.dma_start(out=khp[bq][:],
                                                     in_=kT_scr.ap()[hp]))
        qload_snap[hp] = P.ctrs["qs"].n
        for slot in range(4):
            ext = 8 * slot + 8
            bv = (hp * 4 + slot) % 2
            if (hp * 4 + slot) >= 2:
                prev = (hp, slot - 2) if slot >= 2 else (hp - 1, slot + 2)
                wait("sync", "ts", pv_done_snap[prev])
            for h in range(2):
                op("qs", lambda e, hp=hp, ext=ext, bv=bv, h=h: e.dma_start(
                    out=vstrip[bv][:].rearrange("p (c w) -> p c w", w=130)
                    [:, 0:ext, h * 65:h * 65 + 64],
                    in_=v_scr.ap()[0:ext, :, hp * 128 + h * 64:hp * 128 + h * 64 + 64]
                    .transpose([1, 0, 2])))
            vload_snap[(hp, slot)] = P.ctrs["qs"].n
            obank = 4 + 2 * (slot % 2)
            pv_args = {}
            for c in range(ext):
                it = (hp, slot, c)
                jmin = max(0, (c - 8 * slot) // 2)
                W = 512 - jmin * 128
                ping = c % 2
                wait("tensor", "qs", qload_snap[hp])
                wait("tensor", "qs", vload_snap[(hp, slot)])
                wait("tensor", "vs", v_ones)
                if (hp, slot, c - 2) in exp_snap:
                    wait("tensor", "ss", exp_snap[(hp, slot, c - 2)])
                for h in range(2):
                    op("ts", lambda e, hp=hp, slot=slot, c=c, h=h, jmin=jmin,
                       ping=ping, bq=bq, W=W: e.matmul(
                           PS[:, (ping * 2 + h) * 512 + jmin * 128:
                              (ping * 2 + h) * 512 + 512],
                           khp[bq][h * 64:(h + 1) * 64,
                                   c * 128:c * 128 + 128],
                           qhp[bq][h * 64:(h + 1) * 64,
                                   slot * 512 + jmin * 128:slot * 512 + 512],
                           start=True, stop=True, tile_position=(h * 64, 0),
                           skip_group_check=True))
                st_ready[it] = P.ctrs["ts"].n

                def emit_pv(cc):
                    itp = (hp, slot, cc)
                    if itp in maskv_snap:
                        wait("tensor", "vs", maskv_snap[itp])
                    else:
                        wait("tensor", "ss", exp_snap[itp])
                    jm, bvv = pv_args[cc]
                    for h in range(2):
                        op("ts", lambda e, cc=cc, h=h, jm=jm, bvv=bvv, ob=obank,
                           ex=ext: e.matmul(
                            PS[0:65, (ob + h) * 512 + jm * 128:
                               (ob + h) * 512 + 512],
                            vstrip[bvv][:, cc * 130 + h * 65:cc * 130 + h * 65 + 65],
                            ptile[cc % 2][:, h * 512 + jm * 128:h * 512 + 512],
                            start=(cc == 0), stop=(cc == ex - 1),
                            skip_group_check=True))
                pv_args[c] = (jmin, bv)
                if c >= 1:
                    emit_pv(c - 1)
                    pv_done_snap[(hp, slot)] = P.ctrs["ts"].n

                wait("scalar", "ts", st_ready[it])
                exp_snap[it] = op("ss", lambda e, ping=ping, jmin=jmin, W=W, c=c:
                                  e.activation(
                    ptile[c % 2][:, 0:1024].rearrange("p (h w) -> p h w", h=2)
                    [:, :, jmin * 128:512],
                    PS[:, ping * 1024:ping * 1024 + 1024]
                    .rearrange("p (h w) -> p h w", h=2)[:, :, jmin * 128:512],
                    Act.Exp, bias=cvec_sb[:, 0:1], scale=QSCALE), sw=False)
                mj = None
                if c % 2 == 0 and 0 <= (c // 2) - 4 * slot <= 3:
                    mj = ((c // 2) - 4 * slot, maska_sb)
                if c % 2 == 1 and 0 <= ((c - 1) // 2) - 4 * slot <= 3:
                    mj = (((c - 1) // 2) - 4 * slot, maskb_sb)
                if mj is not None:
                    j, msk = mj
                    wait("vector", "ss", exp_snap[it])
                    for h in range(2):
                        op("vs", lambda e, h=h, j=j, msk=msk, c=c: e.tensor_mul(
                            ptile[c % 2][:, h * 512 + j * 128:h * 512 + (j + 1) * 128],
                            ptile[c % 2][:, h * 512 + j * 128:h * 512 + (j + 1) * 128],
                            msk[:]), sw=False)
                    maskv_snap[it] = P.ctrs["vs"].n
            emit_pv(ext - 1)
            pv_done_snap[(hp, slot)] = P.ctrs["ts"].n
            wait("vector", "ts", pv_done_snap[(hp, slot)])
            for h in range(2):
                op("vs", lambda e, hp=hp, slot=slot, h=h, obank=obank: e.tensor_copy(
                    ysb[h * 64:(h + 1) * 64, hp * T2 + slot * 512:hp * T2 + slot * 512 + 512],
                    PS[0:64, (obank + h) * 512:(obank + h) * 512 + 512]))
            op("vs", lambda e, obank=obank: e.tensor_copy(
                ptile[0][64:65, 0:1024],
                PS[64:65, obank * 512:obank * 512 + 1024]))
            wait("gpsimd", "vs")
            op("qg", lambda e, hp=hp, slot=slot: e.dma_start(
                out=dens_sb[2 * hp:2 * hp + 2, slot * 512:slot * 512 + 512],
                in_=ptile[0][64:65, 0:1024]))
            dens_g = P.ctrs["qg"].n
            P.wait("scalar", "qg", dens_g)
            wait("tensor", "vs")
    wait("vector", "qg")
    op("vs", lp(lambda e: e.reciprocal(dens_sb[:, :], dens_sb[:, :])))
    v_rcp = P.ctrs["vs"].n
    wait("tensor", "vs", v_rcp)
    wait("tensor", "ss")
    for hp in range(8):
        wait("sync", "vs")
        op("qs", lambda e, hp=hp: e.dma_start(
            out=rcp2[:], in_=dens_sb[2 * hp:2 * hp + 2, :]))
        q_r = P.ctrs["qs"].n
        wait("tensor", "qs", q_r)
        for t0 in range(0, T2, 512):
            bk = (t0 // 512) % 4
            wait("tensor", "vs")
            op("ts", lambda e, t0=t0, bk=bk: e.matmul(
                PS[:, bk * 512:(bk + 1) * 512], sel2_sb[:, :],
                rcp2[0:2, t0:t0 + 512], start=True, stop=True))
            wait("vector", "ts")
            op("vs", lambda e, hp=hp, t0=t0, bk=bk: e.tensor_mul(
                ysb[:, hp * T2 + t0:hp * T2 + t0 + 512],
                ysb[:, hp * T2 + t0:hp * T2 + t0 + 512],
                PS[:, bk * 512:(bk + 1) * 512]))
    SF("qhp0", "qhp1", "vstrip0", "vstrip1", "ptile0", "ptile1", "khp0", "khp1")
    woutf = SA("woutf", [128, C], F32)
    wout_all = SA("wout_all", [128, 8 * C], BF16)
    for hc in range(8):
        wait("sync", "vs")
        op("qs", lambda e, hc=hc: e.dma_start(out=woutf[:], in_=wout.ap()[hc]))
        wait("vector", "qs")
        op("vs", lambda e, hc=hc: e.tensor_copy(
            wout_all[:, hc * C:(hc + 1) * C], woutf[:]))
    v_wout = P.ctrs["vs"].n
    wait("tensor", "vs", v_wout)
    evs = []
    for cc in range(8):
        for slot in range(4):
            bk = 4 + (cc * 4 + slot) % 2
            if len(evs) >= 2:
                wait("tensor", "vs", evs[-2])
            for hc in range(8):
                op("ts", lambda e, cc=cc, slot=slot, hc=hc, bk=bk: e.matmul(
                    PS[:, bk * 512:(bk + 1) * 512],
                    wout_all[:, hc * C + cc * 128:hc * C + cc * 128 + 128],
                    ysb[:, hc * T2 + slot * 512:hc * T2 + slot * 512 + 512],
                    start=(hc == 0), stop=(hc == 7), skip_group_check=True))
            wait("vector", "ts")
            evs.append(op("vs", lambda e, cc=cc, slot=slot, bk=bk: e.tensor_add(
                xfm(cc)[:, slot * 512:slot * 512 + 512],
                xfm(cc)[:, slot * 512:slot * 512 + 512],
                PS[:, bk * 512:(bk + 1) * 512])))
    SF("woutf", "wout_all", "rcp2", "ysb", "dens_sb")
    if "d_x1" in dbg_t:
        wait("gpsimd", "vs")
        for c in range(8):
            op("qg", lambda e, c=c: e.dma_start(out=dbg_t["d_x1"].ap()[c], in_=xfm(c)))
    bar()

    def emit_ffn(ln_row, dbg_name, sfx=""):
        xb = SA("f_xb" + sfx, [128, 8 * T2], BF16)
        xsq = SA("f_xsq" + sfx, [128, T2], BF16)
        rbc = SA("f_rbc" + sfx, [128, T2], BF16)
        wait("vector", "qs")
        for c in range(8):
            op("vs", lambda e, c=c: e.tensor_copy(xb[:, c * T2:(c + 1) * T2], xfm(c)))
        ln_stats(lambda c: xb[:, c * T2:(c + 1) * T2], T2, ln_row, extra_sq=xsq)
        bcast_row(rows_sb[5:6, :], T2, rbc)
        for c in range(8):
            op("vs", lambda e, c=c: e.tensor_mul(
                xb[:, c * T2:(c + 1) * T2], xb[:, c * T2:(c + 1) * T2], rbc[:, 0:T2]))
        v_xh = P.ctrs["vs"].n
        wait("tensor", "vs", v_xh)
        wait("sync", "vs", v_xh)
        wait("gpsimd", "vs", v_xh)
        SF("f_xsq" + sfx, "f_rbc" + sfx)

        wgt = [SA("f_wg0" + sfx, [128, 1024], BF16), SA("f_wg1" + sfx, [128, 1024], BF16)]
        wdt = [SA("f_wd0" + sfx, [128, 3072], BF16)]
        wu0 = SA("f_wu0" + sfx, [128, 1024], BF16)
        wut = [wu0[:], wdt[0][:, 0:1024]]
        hact = SA("f_hact" + sfx, [128, 24 * 1024], BF16)
        tdone = {}
        hdone = {}
        dcons = {}
        for tt in range(2):
            t0 = tt * 1024
            for h in range(24):
                bw = h % 2
                key = (tt, h)
                if h >= 2:
                    wait("sync", "ts", tdone[(tt, h - 2)])
                elif tt:
                    wait("sync", "ts")
                op("qs", lambda e, h=h, bw=bw: e.dma_start(
                    out=wgt[bw][:].rearrange("p (c j) -> p c j", j=128),
                    in_=wg_scr.ap()[h].transpose([1, 0, 2])))
                op("qs", lambda e, h=h, bw=bw: e.dma_start(
                    out=wut[bw].rearrange("p (c j) -> p c j", j=128),
                    in_=wu_scr.ap()[h].transpose([1, 0, 2])))
                wl = P.ctrs["qs"].n
                wait("tensor", "qs", wl)
                gb = (h % 2) * 2048
                if (tt, h - 2) in hdone:
                    wait("tensor", "vs", hdone[(tt, h - 2)])
                elif tt and h <= 1:
                    wait("tensor", "vs", dcons[(tt - 1, 7)])
                for half in range(2):
                    hh = t0 + half * 512
                    for cc in range(8):
                        op("ts", lambda e, cc=cc, hh=hh, bw=bw, half=half, gb=gb: e.matmul(
                            PS[:, gb + half * 512:gb + (half + 1) * 512],
                            wgt[bw][:, cc * 128:(cc + 1) * 128],
                            xb[:, cc * T2 + hh:cc * T2 + hh + 512],
                            start=(cc == 0), stop=False, skip_group_check=True))
                    op("ts", lambda e, h=h, hh=hh, half=half, ln_row=ln_row, gb=gb: e.matmul(
                        PS[:, gb + half * 512:gb + (half + 1) * 512],
                        csum_sb[0:1, h * 128:(h + 1) * 128],
                        rows_sb[ln_row:ln_row + 1, hh:hh + 512],
                        start=False, stop=True, skip_group_check=True))
                    for cc in range(8):
                        op("ts", lambda e, cc=cc, hh=hh, half=half, gb=gb, bw=bw: e.matmul(
                            PS[:, gb + 1024 + half * 512:gb + 1024 + (half + 1) * 512],
                            wut[bw][:, cc * 128:(cc + 1) * 128],
                            xb[:, cc * T2 + hh:cc * T2 + hh + 512],
                            start=(cc == 0), stop=False, skip_group_check=True))
                    t_up = op("ts", lambda e, h=h, hh=hh, half=half, ln_row=ln_row, gb=gb: e.matmul(
                        PS[:, gb + 1024 + half * 512:gb + 1024 + (half + 1) * 512],
                        csum_sb[1:2, h * 128:(h + 1) * 128],
                        rows_sb[ln_row:ln_row + 1, hh:hh + 512],
                        start=False, stop=True, skip_group_check=True))
                tdone[key] = t_up
                wait("scalar", "ts", t_up)
                s_g = op("ss", lambda e, h=h, gb=gb: e.activation(
                    hact[:, h * 1024:(h + 1) * 1024], PS[:, gb:gb + 1024],
                    Act.Silu, scale=FSC))
                wait("vector", "ss", s_g)
                hdone[key] = op("vs", lambda e, h=h, gb=gb: e.scalar_tensor_tensor(
                    hact[:, h * 1024:(h + 1) * 1024], PS[:, gb + 1024:gb + 2048],
                    FSC, hact[:, h * 1024:(h + 1) * 1024], Alu.mult, Alu.mult))
            wait("tensor", "vs", hdone[(tt, 23)])
            for cc in range(8):
                key = (tt, cc)
                db = (cc % 2) * 1024
                wait("sync", "ts")
                op("qs", lambda e, cc=cc: e.dma_start(
                    out=wdt[0][:].rearrange("p (h j) -> p h j", j=128),
                    in_=wd_scr.ap()[cc].transpose([1, 0, 2])))
                dl = P.ctrs["qs"].n
                wait("tensor", "qs", dl)
                prev2 = (tt, cc - 2) if cc >= 2 else ((tt - 1, 6 + cc) if tt else None)
                if prev2 is not None and prev2 in dcons:
                    wait("tensor", "vs", dcons[prev2])
                for half in range(2):
                    for h in range(24):
                        op("ts", lambda e, h=h, half=half, db=db: e.matmul(
                            PS[:, db + half * 512:db + (half + 1) * 512],
                            wdt[0][:, h * 128:(h + 1) * 128],
                            hact[:, h * 1024 + half * 512:h * 1024 + half * 512 + 512],
                            start=(h == 0), stop=(h == 23), skip_group_check=True))
                wait("vector", "ts")
                dcons[key] = op("vs", lambda e, cc=cc, t0=t0, db=db: e.scalar_tensor_tensor(
                    xfm(cc)[:, t0:t0 + 1024], PS[:, db:db + 1024],
                    FSC, xfm(cc)[:, t0:t0 + 1024], Alu.mult, Alu.add))
        SF("f_xb" + sfx, "f_wg0" + sfx, "f_wg1" + sfx, "f_wd0" + sfx,
           "f_wu0" + sfx, "f_hact" + sfx)
        if dbg_name in dbg_t:
            wait("gpsimd", "vs")
            for c in range(8):
                op("qg", lambda e, c=c, n=dbg_name: e.dma_start(
                    out=dbg_t[n].ap()[c], in_=xfm(c)))
        bar()

    emit_ffn(2, "d_x2", "a")

    xb3 = SA("m_xb3", [128, 8 * T2], BF16)
    xsq3 = SA("m_xsq3", [128, T2], BF16)
    rbc3 = SA("m_rbc3", [128, T2], BF16)
    wait("vector", "qs")
    for c in range(8):
        op("vs", lambda e, c=c: e.tensor_copy(xb3[:, c * T2:(c + 1) * T2], xfm(c)))
    ln_stats(lambda c: xb3[:, c * T2:(c + 1) * T2], T2, 6, extra_sq=xsq3)
    bcast_row(rows_sb[5:6, :], T2, rbc3)
    for c in range(8):
        op("vs", lambda e, c=c: e.tensor_mul(
            xb3[:, c * T2:(c + 1) * T2], xb3[:, c * T2:(c + 1) * T2], rbc3[:, 0:T2]))
    v_x3h = P.ctrs["vs"].n
    wait("gpsimd", "vs", v_x3h)
    for c in range(8):
        op("qg", lambda e, c=c: e.dma_start(
            out=ln3_in.ap()[c * 128:(c + 1) * 128, :], in_=xb3[:, c * T2:(c + 1) * T2]))
    op("qg", lambda e: e.dma_start(out=ln3_in.ap()[1024:1025, :], in_=rows_sb[6:7, 0:T2]))
    for c in range(8):
        op("qg", lambda e, c=c: e.dma_start(out=park_scr.ap()[c], in_=xfm(c)))
    g_ln3 = P.ctrs["qg"].n
    wait("gpsimd", "qg", g_ln3)
    bar()
    op("gc", lambda e: e.collective_compute(
        "AllGather", Alu.bypass, replica_groups=GROUPS,
        ins=[ln3_in.ap().opt()], outs=[ln3_out.ap().opt()]))
    SF("m_xb3", "m_xsq3", "m_rbc3")
    bar()

    rhsln = xfm_h[0][:].bitcast(BF16)
    for c in range(8):
        for b in range(2):
            op("qs", lambda e, c=c, b=b: e.dma_start(
                out=rhsln[:, c * T:(c + 1) * T]
                .rearrange("p (g j) -> p g j", j=128)[:, b::2, :],
                in_=ln3_out.ap()[b, c * 128:(c + 1) * 128, :]
                .rearrange("p (g j) -> p g j", j=128).transpose([0, 1, 2])))
    for b in range(2):
        op("qs", lambda e, b=b: e.dma_start(
            out=rows_sb[3:4, 0:T].rearrange("o (g j) -> o g j", j=128)[:, b::2, :],
            in_=ln3_out.ap()[b, 1024:1025, :].rearrange("o (g j) -> o g j", j=128)))
    q_rhs = P.ctrs["qs"].n

    inpf = SA("m_inpf", [128, 1024], F32)
    inpb = [SA("m_inpb0", [128, 1024], BF16), SA("m_inpb1", [128, 1024], BF16)]
    xzc = [SA("m_xzc", [128, T], F32), SA("m_xzc1", [128, T], F32)]
    cacc = SA("m_cacc", [128, T], F32)
    xcsb = SA("m_xcsb", [128, T], BF16)
    wait("tensor", "qs", q_rhs)
    evx = []
    for ec in range(16):
        bw = ec % 2
        if ec > 0:
            wait("scalar", "qg")
        wait("sync", "vs")
        op("qs", lambda e, ec=ec: e.dma_start(
            out=inpf[:].rearrange("p (c j) -> p c j", j=128),
            in_=inpT.ap()[:, :, ec * 128:(ec + 1) * 128].transpose([1, 0, 2])))
        wait("vector", "qs")
        v_cv = op("vs", lambda e, bw=bw: e.tensor_copy(inpb[bw][:], inpf[:]))
        wait("tensor", "vs", v_cv)
        for cc in range(8):
            op("ts", lambda e, cc=cc, bw=bw: e.matmul(
                PS[0:1, 3584:3584 + 128],
                ones_sb[:, 0:1], inpb[bw][:, cc * 128:(cc + 1) * 128],
                start=(cc == 0), stop=(cc == 7), skip_group_check=True))
        wait("vector", "ts")
        op("vs", lambda e, ec=ec: e.tensor_copy(
            csum_sb[4:5, ec * 128:(ec + 1) * 128], PS[0:1, 3584:3584 + 128]))
        wait("tensor", "vs")
        for t0 in range(0, T, 512):
            bk = (t0 // 512) % 2
            if len(evx) >= 2:
                wait("tensor", "ss", evx[-2])
            for cc in range(8):
                op("ts", lambda e, cc=cc, t0=t0, bw=bw, bk=bk: e.matmul(
                    PS[:, bk * 512:(bk + 1) * 512],
                    inpb[bw][:, cc * 128:(cc + 1) * 128],
                    rhsln[:, cc * T + t0:cc * T + t0 + 512],
                    start=(cc == 0), stop=False, skip_group_check=True))
            op("ts", lambda e, ec=ec, t0=t0, bk=bk: e.matmul(
                PS[:, bk * 512:(bk + 1) * 512],
                csum_sb[4:5, ec * 128:ec * 128 + 128],
                rows_sb[3:4, t0:t0 + 512], start=False, stop=True,
                skip_group_check=True))
            if ec < 8 and evx:
                wait("scalar", "vs")
            wait("scalar", "ts")
            if ec < 8:
                evx.append(op("ss", lambda e, t0=t0, bk=bk, bz=ec % 2: e.activation(
                    xzc[bz][:, t0:t0 + 512], PS[:, bk * 512:(bk + 1) * 512], Act.Copy)))
            else:
                evx.append(op("ss", lambda e, t0=t0, bk=bk: e.activation(
                    xcsb[:, t0:t0 + 512], PS[:, bk * 512:(bk + 1) * 512], Act.Silu)))
        if ec >= 8:
            wait("gpsimd", "ss")
            op("qg", lambda e, ec=ec: e.dma_start(out=zs_scr.ap()[ec - 8], in_=xcsb[:]))
            continue
        wait("vector", "ss")
        op("vs", lambda e, ec=ec, bz=ec % 2: e.tensor_scalar_mul(
            cacc[:], xzc[bz][:], convw_sb[:, ec * 4 + 3:ec * 4 + 4]))
        for k in range(3):
            sh = 3 - k
            op("vs", lambda e, ec=ec, k=k, sh=sh, bz=ec % 2: e.scalar_tensor_tensor(
                cacc[:, sh:T], xzc[bz][:, 0:T - sh], convw_sb[:, ec * 4 + k:ec * 4 + k + 1],
                cacc[:, sh:T], Alu.mult, Alu.add))
        v_conv = P.ctrs["vs"].n
        wait("scalar", "vs", v_conv)
        op("ss", lambda e, ec=ec: e.activation(
            xcsb[:], cacc[:], Act.Silu, bias=convb_sb[:, ec:ec + 1]))
        wait("gpsimd", "ss")
        op("qg", lambda e, ec=ec: e.dma_start(out=xc_scr.ap()[ec], in_=xcsb[:]))
        if "d_xcs" in dbg_t:
            op("qg", lambda e, ec=ec: e.dma_start(out=dbg_t["d_xcs"].ap()[ec], in_=xcsb[:]))
    SF("m_inpf", "m_inpb0", "m_inpb1", "m_xzc", "m_xzc1", "m_cacc", "m_xcsb")
    bar()

    dbcb = SA("m_dbcb", [128, T], BF16)
    xpf = SA("m_xpf", [128, 768], F32)
    xpb = SA("m_xpb", [128, 768], BF16)
    op("qs", lambda e: e.dma_start(
        out=xpf[:].rearrange("p (c j) -> p c j", j=96), in_=xpT.ap().transpose([1, 0, 2])))
    wait("vector", "qs")
    op("vs", lambda e: e.tensor_copy(xpb[:], xpf[:]))
    dbcp = SA("m_dbcp", [128, T], F32)
    xcq = [SA("m_xcq0", [128, 512], BF16), SA("m_xcq1", [128, 512], BF16)]
    wait("tensor", "vs")
    for ts8 in range(8):
        t0 = ts8 * 512
        bk = ts8 % 2
        wait("tensor", "vs")
        for ec in range(8):
            bq = (ts8 * 8 + ec) % 2
            if ts8 * 8 + ec >= 2:
                wait("sync", "ts")
            op("qs", lambda e, ec=ec, t0=t0, bq=bq: e.dma_start(
                out=xcq[bq][:], in_=xc_scr.ap()[ec, :, t0:t0 + 512]))
            wait("tensor", "qs")
            op("ts", lambda e, ec=ec, t0=t0, bq=bq, bk=bk: e.matmul(
                PS[0:96, bk * 512:bk * 512 + 512],
                xpb[:, ec * 96:(ec + 1) * 96], xcq[bq][:],
                start=(ec == 0), stop=(ec == 7), skip_group_check=True))
        wait("vector", "ts")
        op("vs", lambda e, t0=t0, bk=bk: e.tensor_copy(
            dbcp[0:96, t0:t0 + 512], PS[0:96, bk * 512:bk * 512 + 512]))
    wait("gpsimd", "vs")
    op("qg", lambda e: e.dma_start(out=dbc_in.ap(), in_=dbcp[0:96, :]))
    wait("gpsimd", "qg")
    op("gc", lambda e: e.collective_compute(
        "AllReduce", Alu.add, replica_groups=GROUPS,
        ins=[dbc_in.ap().opt()], outs=[dbc_out.ap().opt()]))
    wait("sync", "gc")
    op("qs", lambda e: e.dma_start(out=dbcp[0:96, :], in_=dbc_out.ap()))
    q_dbc = P.ctrs["qs"].n
    if "d_dbc" in dbg_t:
        wait("gpsimd", "qs", q_dbc)
        op("qg", lambda e: e.dma_start(out=dbg_t["d_dbc"].ap()[0:96, :], in_=dbcp[0:96, :]))
    wait("vector", "qs", q_dbc)
    op("vs", lambda e: e.tensor_copy(dbcb[0:96, :], dbcp[0:96, :]))
    v_dbcb = P.ctrs["vs"].n
    wait("gpsimd", "vs", v_dbcb)
    op("qg", lambda e: e.dma_start(out=dbc_bf.ap(), in_=dbcb[64:96, :]))
    dttf = SA("m_dttf", [64, EO], F32)
    dttb = SA("m_dttb", [64, EO], BF16)
    op("qs", lambda e: e.dma_start(out=dttf[:], in_=dtT.ap()))
    wait("vector", "qs")
    op("vs", lambda e: e.tensor_copy(dttb[:], dttf[:]))
    v_dtt = P.ctrs["vs"].n
    dlt = SA("m_dlt", [128, T], F32)
    wait("tensor", "vs", max(v_dbcb, v_dtt))
    for ec in range(8):
        for t0 in range(0, T, 512):
            op("ts", lambda e, ec=ec, t0=t0: e.matmul(
                PS[:, t0:t0 + 512], dttb[:, ec * 128:(ec + 1) * 128],
                dbcb[0:64, t0:t0 + 512], start=True, stop=True))
        wait("scalar", "ts")
        op("ss", lambda e, ec=ec: e.activation(
            dlt[:], PS[:, 0:T], Act.Exp, bias=dtb_sb[:, ec:ec + 1]))
        op("ss", lambda e: e.activation(dlt[:], dlt[:], Act.Ln, bias=1.0))
        wait("gpsimd", "ss")
        op("qg", lambda e, ec=ec: e.dma_start(out=dl_scr.ap()[ec], in_=dlt[:]))
        if "d_delta" in dbg_t:
            op("qg", lambda e, ec=ec: e.dma_start(out=dbg_t["d_delta"].ap()[ec], in_=dlt[:]))
        wait("tensor", "qg")
    SF("m_xpf", "m_xpb", "m_dbcp", "m_xcq0", "m_xcq1", "m_dttf", "m_dttb", "m_dlt", "m_dbcb")
    bar()

    for c in range(8):
        op("qg", lambda e, c=c: e.dma_start(out=xfm(c), in_=park_scr.ap()[c]))
    g_park = P.ctrs["qg"].n

    bcst = SA("m_bcst", [128, 32 * 1024], BF16)
    dq = [SA("m_dq0", [128, 1024], F32), SA("m_dq1", [128, 1024], F32)]
    xq = [SA("m_xq0", [128, 1024], BF16), SA("m_xq1", [128, 1024], BF16)]
    uq = SA("m_uq", [128, 1024], BF16)
    dA = [PS[:, 0:1024], PS[:, 1024:2048]]
    bxq = [SA("m_bxq", [128, 1024], BF16), SA("m_bxq1", [128, 1024], BF16)]
    hsq = SA("m_hsq", [128, 1024], BF16)
    ytq = [SA("m_ytq0", [128, 1024], BF16), SA("m_ytq1", [128, 1024], BF16)]
    zq = [SA("m_zq0", [128, 1024], BF16), SA("m_zq1", [128, 1024], BF16)]
    ymq = SA("m_ymq", [128, 1024], BF16)
    dxq = bxq[0]
    ld_snap = {}
    ymq_g = {}
    for tq in range(4):
        t0 = tq * 1024
        wait("sync", "vs")
        bc_half = None
        for n in range(NST):
            op("qs", lambda e, n=n, t0=t0: e.dma_start(
                out=bcst[:, 2 * n * 1024:(2 * n + 2) * 1024]
                .rearrange("p (r j) -> p r j", j=1024),
                in_=dbc_bf.ap()[n::16, t0:t0 + 1024].partition_broadcast(128)))
            if n == 7:
                bc_half = P.ctrs["qs"].n
        bc_q = P.ctrs["qs"].n
        def emit_loads(ec_):
            bi_ = ec_ % 2
            wait("sync", "vs")
            op("qs", lambda e, ec=ec_, t0=t0, bi=bi_: e.dma_start(
                out=dq[bi][:], in_=dl_scr.ap()[ec, :, t0:t0 + 1024]))
            op("qs", lambda e, ec=ec_, t0=t0, bi=bi_: e.dma_start(
                out=xq[bi][:], in_=xc_scr.ap()[ec, :, t0:t0 + 1024]))
            op("qs", lambda e, ec=ec_, t0=t0, bi=bi_: e.dma_start(
                out=zq[bi][:], in_=zs_scr.ap()[ec, :, t0:t0 + 1024]))
            ld_snap[(tq, ec_)] = P.ctrs["qs"].n
        for ec in range(8):
            bi = ec % 2
            yb = 2048 + (ec % 2) * 1024
            if ec == 0:
                emit_loads(0)
            wait("vector", "qs", ld_snap[(tq, ec)])
            wait("vector", "qs", bc_half)
            op("vs", lambda e, bi=bi: e.tensor_mul(uq[:], dq[bi][:], xq[bi][:]))
            if ec < 7:
                emit_loads(ec + 1)
            wait("scalar", "vs")
            acc_prev = None
            acc_prev2 = None
            scan_v = {}
            for n in range(NST):
                ba = n % 2
                col = ec * 16 + n
                if n >= 2:
                    wait("scalar", "vs", scan_v[n - 2])
                op("ss", lambda e, ba=ba, col=col, bi=bi: e.activation(
                    dA[ba], dq[bi][:], Act.Exp, scale=amat_sb[:, col:col + 1]),
                    sw=False)
                exp_s = P.ctrs["ss"].n
                if n == 8:
                    wait("vector", "qs", bc_q)
                op("vs", lambda e, n=n, ba=ba: e.tensor_mul(
                    bxq[ba][:], uq[:], bcst[:, 2 * n * 1024:(2 * n + 1) * 1024]),
                    sw=(n < 2))
                wait("vector", "ss", exp_s)
                scan_v[n] = op("vs", lambda e, ba=ba, col=col: e.tensor_tensor_scan(
                    hsq[:], dA[ba], bxq[ba][:], state_sb[:, col:col + 1],
                    Alu.mult, Alu.add))
                op("vs", lambda e, col=col: e.tensor_copy(
                    state_sb[:, col:col + 1], hsq[:, 1023:1024]))
                if acc_prev2 is not None:
                    wait("vector", "ts", acc_prev2)
                op("vs", lambda e, n=n, ba=ba: e.tensor_mul(
                    ytq[ba][:], hsq[:],
                    bcst[:, (2 * n + 1) * 1024:(2 * n + 2) * 1024]),
                    sw=(n < 2))
                wait("tensor", "vs")
                for hh in range(2):
                    op("ts", lambda e, hh=hh, n=n, yb=yb, ba=ba: e.matmul(
                        PS[:, yb + hh * 512:yb + (hh + 1) * 512],
                        eye_sb[:], ytq[ba][:, hh * 512:(hh + 1) * 512],
                        start=(n == 0), stop=False, skip_group_check=True))
                acc_prev2 = acc_prev
                acc_prev = P.ctrs["ts"].n
            if acc_prev is not None:
                wait("vector", "ts", acc_prev)
            op("vs", lambda e, bi=bi, ec=ec: e.tensor_scalar_mul(
                dxq[:], xq[bi][:], dpar_sb[:, ec:ec + 1]))
            wait("tensor", "vs")
            for hh in range(2):
                op("ts", lambda e, hh=hh, yb=yb: e.matmul(
                    PS[:, yb + hh * 512:yb + (hh + 1) * 512],
                    eye_sb[:], dxq[:, hh * 512:(hh + 1) * 512],
                    start=False, stop=(hh == 1), skip_group_check=True))
            t_acc = P.ctrs["ts"].n
            wait("vector", "ts", t_acc)
            ymq_g[(tq, ec)] = op("vs", lambda e, bi=bi, yb=yb: e.tensor_mul(
                ymq[:], PS[:, yb:yb + 1024], zq[bi][:]))
            wait("gpsimd", "vs", ymq_g[(tq, ec)])
            op("qg", lambda e, ec=ec, t0=t0: e.dma_start(
                out=ym_scr.ap()[ec, :, t0:t0 + 1024], in_=ymq[:]))
            if "d_ymul" in dbg_t:
                op("qg", lambda e, ec=ec, t0=t0: e.dma_start(
                    out=dbg_t["d_ymul"].ap()[ec, :, t0:t0 + 1024], in_=ymq[:]))
            wait("vector", "qg")
    SF("m_bcst", "m_dq0", "m_xq0", "m_xq1",
       "m_dq1", "m_uq", "m_bxq", "m_bxq1", "m_hsq", "m_ytq0", "m_ytq1",
       "m_zq0", "m_zq1", "m_ymq")
    bar()

    outf = [SA("c4_outf", [128, C], F32), SA("c4_outf1", [128, C], F32)]
    outt = SA("c4_outt", [128, 8 * C], BF16)
    ocv = []
    for ec in range(8):
        if ec >= 2:
            wait("sync", "vs", ocv[ec - 2])
        op("qs", lambda e, ec=ec: e.dma_start(out=outf[ec % 2][:], in_=outT.ap()[ec]))
        wait("vector", "qs")
        ocv.append(op("vs", lambda e, ec=ec: e.tensor_copy(
            outt[:, ec * C:(ec + 1) * C], outf[ec % 2][:])))
    v_outt = P.ctrs["vs"].n
    wait("tensor", "vs", v_outt)
    ymgs = SA("c4_ymgs", [128, 8 * 512], BF16)
    prt = [SA("c4_prt", [128, 512], F32), SA("c4_prt1", [128, 512], F32)]
    prt_g = {}
    for gs in range(8):
        wait("sync", "ts")
        op("qs", lambda e, gs=gs: e.dma_start(
            out=ymgs[:].rearrange("p (c j) -> p c j", j=512),
            in_=ym_scr.ap()[:, :, gs * 512:(gs + 1) * 512].transpose([1, 0, 2])))
        q_ym = P.ctrs["qs"].n
        wait("tensor", "qs", q_ym)
        for cc in range(8):
            bk = (gs * 8 + cc) % 2
            wait("tensor", "vs")
            for ec in range(8):
                op("ts", lambda e, cc=cc, ec=ec, bk=bk: e.matmul(
                    PS[:, bk * 512:(bk + 1) * 512],
                    outt[:, ec * C + cc * 128:ec * C + cc * 128 + 128],
                    ymgs[:, ec * 512:(ec + 1) * 512],
                    start=(ec == 0), stop=(ec == 7), skip_group_check=True))
            wait("vector", "ts")
            bp = (gs * 8 + cc) % 2
            wait("vector", "qg", prt_g.get(bp, 0))
            op("vs", lambda e, bk=bk, bp=bp: e.tensor_copy(
                prt[bp][:], PS[:, bk * 512:(bk + 1) * 512]))
            wait("gpsimd", "vs")
            for bb_ in range(2):
                op("qg", lambda e, cc=cc, gs=gs, bb_=bb_, bp=bp: e.dma_start(
                    out=rs_in.ap()[bb_, cc, :, gs * 256:gs * 256 + 256]
                    .rearrange("p (u j) -> p u j", j=128),
                    in_=prt[bp][:].rearrange("p (u v j) -> p u v j", u=2, v=2)[:, :, bb_, :]))
            prt_g[bp] = P.ctrs["qg"].n
    g_rs = P.ctrs["qg"].n
    wait("gpsimd", "qg", g_rs)
    op("gc", lambda e: e.collective_compute(
        "ReduceScatter", Alu.add, replica_groups=GROUPS,
        ins=[rs_in.ap().opt()], outs=[rs_out.ap().opt()]))
    SF("c4_outf", "c4_outf1", "c4_outt", "c4_ymgs", "c4_prt", "c4_prt1")
    bar()
    rsl = [SA("c4_rsl0", [128, T2], F32), SA("c4_rsl1", [128, T2], F32)]
    wait("vector", "qg", g_park)
    wait("sync", "gc")
    for c in range(8):
        bi = c % 2
        if c >= 2:
            wait("sync", "vs")
        op("qs", lambda e, c=c, bi=bi: e.dma_start(out=rsl[bi][:], in_=rs_out.ap()[c]))
        wait("vector", "qs")
        op("vs", lambda e, c=c, bi=bi: e.tensor_add(xfm(c), xfm(c), rsl[bi][:]))
    SF("c4_rsl0", "c4_rsl1")
    if "d_x3" in dbg_t:
        wait("gpsimd", "vs")
        for c in range(8):
            op("qg", lambda e, c=c: e.dma_start(out=dbg_t["d_x3"].ap()[c], in_=xfm(c)))
    bar()

    emit_ffn(4, "none", "b")
    for c in range(8):
        op("qg", lambda e, c=c: e.dma_start(out=y_own.ap()[c], in_=xfm(c)))
    g_out = P.ctrs["qg"].n
    wait("gpsimd", "qg", g_out)
    bar()

    block = nc.Block().__enter__()
    P.finalize(block)
    block.__exit__(None, None, None)
    return nc


def _prep_core(inputs, core):
    seq, par = core // 2, core % 2
    f32 = np.float32
    x = np.asarray(inputs["L0_attn_a_x"], dtype=f32)
    xT = np.ascontiguousarray(x[seq].T)
    own_cols = np.concatenate(
        [np.arange((2 * i + par) * 128, (2 * i + par) * 128 + 128) for i in range(16)])
    m = {}
    m["x_full"] = np.ascontiguousarray(xT.reshape(8, 128, T))
    m["x_own"] = np.ascontiguousarray(xT[:, own_cols].reshape(8, 128, T2))
    m["wq"] = np.ascontiguousarray(
        np.asarray(inputs["L0_attn_a_w_q"], f32).reshape(8, 128, C))
    m["wkvd"] = np.ascontiguousarray(
        np.asarray(inputs["L0_attn_a_w_kv_down"], f32).reshape(8, 128, KVR))
    wkvu = np.asarray(inputs["L0_attn_a_w_kv_up"], f32)
    m["wkvu_k"] = np.ascontiguousarray(wkvu[:, :C])
    m["wkvu_v"] = np.ascontiguousarray(wkvu[:, C:])
    m["wout"] = np.ascontiguousarray(
        np.asarray(inputs["L0_attn_a_w_out"], f32).reshape(8, 128, C))
    wg = np.asarray(inputs["L0_ffn_w_gate"], np.int32)
    wu = np.asarray(inputs["L0_ffn_w_up"], np.int32)
    wd = np.asarray(inputs["L0_ffn_w_down"], np.int32)
    m["wgT"] = np.ascontiguousarray(wg.T.reshape(8, 128, FH))
    m["wuT"] = np.ascontiguousarray(wu.T.reshape(8, 128, FH))
    m["wdT"] = np.ascontiguousarray(wd.T.reshape(24, 128, C))
    ipw = np.asarray(inputs["L0_attn_b_in_proj_w"], f32)
    sel = np.concatenate([ipw[par * EO:(par + 1) * EO],
                          ipw[EF + par * EO:EF + (par + 1) * EO]])
    m["inpT"] = np.ascontiguousarray(sel.T.reshape(8, 128, EF))
    xpw = np.asarray(inputs["L0_attn_b_x_proj_w"], f32)
    m["xpT"] = np.ascontiguousarray(
        xpw[:, par * EO:(par + 1) * EO].T.reshape(8, 128, 96))
    dtw = np.asarray(inputs["L0_attn_b_dt_w"], f32)
    m["dtT"] = np.ascontiguousarray(dtw[par * EO:(par + 1) * EO].T)
    opw = np.asarray(inputs["L0_attn_b_out_proj_w"], f32)
    m["outT"] = np.ascontiguousarray(
        opw[:, par * EO:(par + 1) * EO].T.reshape(8, 128, C))
    alog = np.asarray(inputs["L0_attn_b_A_log"], f32)[par * EO:(par + 1) * EO]
    m["a_log"] = np.ascontiguousarray(
        alog.reshape(8, 128, 16).transpose(1, 0, 2).reshape(128, 128))
    cw = np.asarray(inputs["L0_attn_b_conv_w"], f32)[par * EO:(par + 1) * EO]
    m["convw"] = np.ascontiguousarray(
        cw.reshape(8, 128, 4).transpose(1, 0, 2).reshape(128, 32))
    for name, key in [("convb", "L0_attn_b_conv_b"), ("dtb", "L0_attn_b_dt_b"),
                      ("dpar", "L0_attn_b_D_param")]:
        v = np.asarray(inputs[key], f32)[par * EO:(par + 1) * EO]
        m[name] = np.ascontiguousarray(v.reshape(8, 128).T)
    import ml_dtypes
    bf = ml_dtypes.bfloat16
    m["ones128"] = np.ones((128, 128), dtype=bf)
    m["eye128"] = np.eye(128, dtype=np.float32).astype(bf)
    s2 = np.zeros((2, 128), np.float32)
    s2[0, 0:64] = 1.0
    s2[1, 64:128] = 1.0
    m["sel2"] = s2.astype(bf)
    tril = np.triu(np.ones((128, 128), np.float32))
    m["mask_a"] = (tril if par == 0 else np.ones((128, 128), np.float32)).astype(bf)
    m["mask_b"] = (np.zeros((128, 128), np.float32) if par == 0 else tril).astype(bf)
    return m


_NC_CACHE = {}


def kernel(**inputs):
    key = "main"
    if key not in _NC_CACHE:
        _NC_CACHE[key] = build_nc()
    nc = _NC_CACHE[key]
    in_maps = [_prep_core(inputs, core) for core in range(8)]
    res = run_bass_kernel_spmd(nc, in_maps, core_ids=list(range(8)))
    out = np.empty((B, T, C), dtype=np.float32)
    for core in range(8):
        seq, par = core // 2, core % 2
        yo = np.asarray(res.results[core]["y_own"])
        yo = yo.reshape(C, T2)
        cols = np.concatenate(
            [np.arange((2 * i + par) * 128, (2 * i + par) * 128 + 128)
             for i in range(16)])
        out[seq][cols, :] = yo.T
    return out


# version 07
# speedup vs baseline: 1.3530x; 1.0269x over previous
import numpy as np
from contextlib import ExitStack

import concourse.bass as bass
import concourse.mybir as mybir
from concourse.bass_utils import run_bass_kernel_spmd

F32 = mybir.dt.float32
BF16 = mybir.dt.bfloat16
I32 = mybir.dt.int32
Alu = mybir.AluOpType
Act = mybir.ActivationFunctionType

B, T, C = 4, 4096, 1024
T2 = T // 2
NH, HD, KVR = 16, 64, 128
EF, EO, NST, DTR = 2048, 1024, 16, 64
FH = 3072
QSCALE = 0.125
FSC = 1.0 / 127.0
EPS = 1e-5
GROUPS = [[0, 1], [2, 3], [4, 5], [6, 7]]
ENGS = ("tensor", "vector", "scalar", "sync", "gpsimd")


class Ctr:
    __slots__ = ("name", "sem", "step", "n", "is_dma", "watermark", "settled")

    def __init__(self, name, sem, step):
        self.name, self.sem, self.step, self.n = name, sem, step, 0
        self.is_dma = step == 16
        self.watermark = 0
        self.settled = 0


class Prog:

    def __init__(self, nc, E):
        self.nc = nc
        self.parts = {e: [] for e in ENGS}
        self.ctrs = {}
        for name, eng, step in [("ts", "tensor", 1), ("vs", "vector", 1),
                                ("ss", "scalar", 1), ("gc", "gpsimd", 1),
                                ("qs", "sync", 16), ("qg", "gpsimd", 16)]:
            self.ctrs[name] = Ctr(name, E(nc.semaphore(name)), step)
        self.home = {"ts": "tensor", "vs": "vector", "ss": "scalar",
                     "gc": "gpsimd", "qs": "sync", "qg": "gpsimd"}
        self.waited = {e: {} for e in ENGS}

    def wait(self, eng, cname, val=None):
        c = self.ctrs[cname]
        val = c.n if val is None else val
        if val <= 0 or self.waited[eng].get(cname, -1) >= val:
            return
        if c.is_dma and val < c.n:
            val = c.n
            if self.waited[eng].get(cname, -1) >= val:
                return
        self.waited[eng][cname] = val
        if c.is_dma and eng != self.home[cname]:
            c.watermark = max(c.watermark, val)
        self.parts[eng].append(lambda e, s=c.sem, v=val: e.wait_ge(s, v))

    def op(self, cname, fn, sw=True):
        c = self.ctrs[cname]
        eng = self.home[cname]
        if sw and eng in ("vector", "scalar"):
            self.wait(eng, cname, c.n)
        if c.is_dma and c.watermark > c.settled and c.n > c.settled:
            self.waited[eng][cname] = max(self.waited[eng].get(cname, -1), c.n)
            self.parts[eng].append(lambda e, s=c.sem, v=c.n: e.wait_ge(s, v))
            c.settled = c.n
        c.n += c.step
        self.parts[eng].append(
            lambda e, fn=fn, s=c.sem, k=c.step: fn(e).then_inc(s, k))
        return c.n

    def raw(self, eng, fn):
        self.parts[eng].append(fn)

    def bar(self):
        vals = {k: c.n for k, c in self.ctrs.items()}
        for k, c in self.ctrs.items():
            if c.is_dma:
                self.wait(self.home[k], k, c.n)
                c.settled = c.n
        for e in ENGS:
            for k, v in vals.items():
                self.wait(e, k, v)

    def finalize(self, block):
        getattr(block, "tensor")(lambda e: [f(e) for f in self.parts["tensor"]])
        getattr(block, "vector")(lambda e: [f(e) for f in self.parts["vector"]])
        getattr(block, "scalar")(lambda e: [f(e) for f in self.parts["scalar"]])
        getattr(block, "sync")(lambda e: [f(e) for f in self.parts["sync"]])
        getattr(block, "gpsimd")(lambda e: [f(e) for f in self.parts["gpsimd"]])


def build_nc(dbg=(), groups=None):
    nc = bass.Bass("TRN2", target_bir_lowering=False)
    dbg = set(dbg)
    global GROUPS
    GROUPS = groups if groups is not None else [[0, 1], [2, 3], [4, 5], [6, 7]]
    ctx = ExitStack()
    E = ctx.enter_context

    def DI(name, shape, dtype=F32):
        return nc.dram_tensor(name, list(shape), dtype, kind="ExternalInput")

    x_own = DI("x_own", [8, 128, T2])
    x_full = DI("x_full", [8, 128, T])
    wq = DI("wq", [8, 128, C])
    wkvd = DI("wkvd", [8, 128, KVR])
    wkvu_k = DI("wkvu_k", [128, C])
    wkvu_v = DI("wkvu_v", [128, C])
    wout = DI("wout", [8, 128, C])
    wgT = DI("wgT", [8, 128, FH], I32)
    wuT = DI("wuT", [8, 128, FH], I32)
    wdT = DI("wdT", [24, 128, C], I32)
    inpT = DI("inpT", [8, 128, EF])
    xpT = DI("xpT", [8, 128, 96])
    dtT = DI("dtT", [64, EO])
    outT = DI("outT", [8, 128, C])
    a_log = DI("a_log", [128, 128])
    convw = DI("convw", [128, 32])
    convb = DI("convb", [128, 8])
    dtb = DI("dtb", [128, 8])
    dpar = DI("dpar", [128, 8])
    ones128 = DI("ones128", [128, 128], BF16)
    eye128 = DI("eye128", [128, 128], BF16)
    sel2 = DI("sel2", [2, 128], BF16)
    mask_a = DI("mask_a", [128, 128], BF16)
    mask_b = DI("mask_b", [128, 128], BF16)

    y_own = nc.dram_tensor("y_own", [8, 128, T2], F32, kind="ExternalOutput")

    dbg_t = {}
    for name, shape, dt in [
        ("d_xhat1", [8, 128, T], BF16), ("d_q", [8, 128, T2], BF16),
        ("d_k", [8, 128, T], BF16), ("d_x1", [8, 128, T2], F32),
        ("d_x2", [8, 128, T2], F32), ("d_dbc", [128, T], F32),
        ("d_delta", [8, 128, T], F32), ("d_xcs", [8, 128, T], BF16),
        ("d_x3", [8, 128, T2], F32), ("d_ymul", [8, 128, T], BF16),
    ]:
        if name in dbg:
            dbg_t[name] = nc.dram_tensor(name, shape, dt, kind="ExternalOutput")

    kT_scr = nc.dram_tensor("kT_scr", [8, 128, T], BF16)
    v_scr = nc.dram_tensor("v_scr", [32, 128, C], BF16)
    q_scr = nc.dram_tensor("q_scr", [8, 128, T2], BF16)
    wg_scr = nc.dram_tensor("wg_scr", [24, 8, 128, 128], BF16)
    wu_scr = nc.dram_tensor("wu_scr", [24, 8, 128, 128], BF16)
    wd_scr = nc.dram_tensor("wd_scr", [8, 24, 128, 128], BF16)
    xc_scr = nc.dram_tensor("xc_scr", [8, 128, T], BF16)
    zs_scr = nc.dram_tensor("zs_scr", [8, 128, T], BF16)
    dl_scr = nc.dram_tensor("dl_scr", [8, 128, T], F32)
    ym_scr = nc.dram_tensor("ym_scr", [8, 128, T], BF16)
    park_scr = nc.dram_tensor("park_scr", [8, 128, T2], F32)
    ln3_in = nc.dram_tensor("ln3_in", [1025, T2], BF16)
    ln3_out = nc.dram_tensor("ln3_out", [2, 1025, T2], BF16)
    dbc_in = nc.dram_tensor("dbc_in", [96, T], F32)
    dbc_out = nc.dram_tensor("dbc_out", [96, T], F32)
    dbc_bf = nc.dram_tensor("dbc_bf", [32, T], BF16)
    rs_in = nc.dram_tensor("rs_in", [2, 8, 128, T2], F32)
    rs_out = nc.dram_tensor("rs_out", [8, 128, T2], F32)

    P = Prog(nc, E)
    sb_frees = {}
    sb_order = []

    def SA(name, shape, dtype):
        cm = nc.sbuf_tensor(name, list(shape), dtype)
        t = cm.__enter__()
        sb_frees[name] = cm
        sb_order.append(name)
        return t

    def SF(*names):
        for n in sorted(names, key=lambda n: -sb_order.index(n)):
            assert n == sb_order[-1], f"non-LIFO free {n} vs stack {sb_order[-6:]}"
            sb_order.pop()
            sb_frees.pop(n).__exit__(None, None, None)

    PS = nc.psum_tensor("PS", [128, 4096], F32).__enter__()

    ones_sb = SA("ones_sb", [128, 128], BF16)
    eye_sb = SA("eye_sb", [128, 128], BF16)
    sel2_sb = SA("sel2_sb", [2, 128], BF16)
    maska_sb = SA("maska_sb", [128, 128], BF16)
    maskb_sb = SA("maskb_sb", [128, 128], BF16)
    class _RowMap:
        def __init__(self, raw, width, slots):
            self.raw, self.w, self.slots = raw, width, slots
        def __getitem__(self, key):
            rs, cs = key
            off = self.slots[rs.start]
            a = cs.start or 0
            b = cs.stop if cs.stop is not None else self.w
            return self.raw[0:1, off + a:off + b]

    rows_raw = SA("rows_raw", [1, T2 + 2 * T], BF16)
    rows_sb = _RowMap(rows_raw, T, {0: 0, 2: 0, 4: 0, 6: 0, 5: T2,
                                    1: T2 + T, 3: T2 + T})
    csum_raw = SA("csum_raw", [1, 8192], BF16)
    csum_sb = _RowMap(csum_raw, 8192, {0: 0, 1: 3072, 2: 6144, 3: 7168, 4: 6144})
    srow = SA("srow", [1, 2048], F32)
    scol = SA("scol", [128, 160], F32)
    sbcol = SA("sbcol", [128, 64], BF16)
    c0s = SA("c0s", [1, 8], F32)
    c0b = SA("c0b", [1, 2], BF16)
    cvec_sb = SA("cvec_sb", [128, 8], F32)
    amat_sb = SA("amat_sb", [128, 128], F32)
    convw_sb = SA("convw_sb", [128, 32], F32)
    convb_sb = SA("convb_sb", [128, 8], F32)
    dtb_sb = SA("dtb_sb", [128, 8], F32)
    dpar_sb = SA("dpar_sb", [128, 8], F32)
    state_sb = SA("state_sb", [128, 128], F32)
    xfm_h = [SA("x_fm", [128, 8 * T2], F32)]

    def xfm(c):
        return xfm_h[0][:, c * T2:(c + 1) * T2]

    op, wait, bar = P.op, P.wait, P.bar

    def lp(fn):
        def go(e):
            with nc.allow_low_precision(reason="intentional bf16 elementwise"):
                return fn(e)
        return go

    al_sb = SA("al_sb", [128, 128], F32)
    for dst, src in [(ones_sb, ones128), (eye_sb, eye128), (sel2_sb, sel2),
                     (maska_sb, mask_a), (maskb_sb, mask_b), (convw_sb, convw),
                     (convb_sb, convb), (dtb_sb, dtb), (dpar_sb, dpar),
                     (al_sb, a_log)]:
        op("qs", lambda e, d=dst, s=src: e.dma_start(out=d[:], in_=s.ap()))
    wait("scalar", "qs")
    op("ss", lambda e: e.activation(amat_sb[:], al_sb[:], Act.Exp))
    wait("vector", "ss")
    op("vs", lambda e: e.tensor_scalar_mul(amat_sb[:], amat_sb[:], -1.0))
    op("vs", lambda e: e.memset(state_sb[:], 0.0))
    op("vs", lambda e: e.memset(cvec_sb[:, 1:2], EPS))

    wi = [SA("p0_wi0", [128, FH], I32), SA("p0_wi1", [128, FH], I32)]
    wb = [SA("p0_wb0", [128, FH], BF16), SA("p0_wb1", [128, FH], BF16)]
    conv_sets = [(wgT, wg_scr, 0, 8, FH), (wuT, wu_scr, 1, 8, FH)]
    conv_v = {}
    store_g = {}
    colsum_t = {}
    it = 0
    for src, dst, crow, nchunk, w in conv_sets:
        for c in range(nchunk):
            b = it % 2
            if it >= 2:
                wait("sync", "vs", conv_v[it - 2])
            op("qs", lambda e, b=b, src=src, c=c, w=w:
               e.dma_start(out=wi[b][:, 0:w], in_=src.ap()[c]))
            wait("vector", "qs")
            if it >= 2:
                wait("vector", "qg", store_g[it - 2])
                if it - 2 in colsum_t:
                    wait("vector", "ts", colsum_t[it - 2])
            conv_v[it] = op("vs", lambda e, b=b, w=w:
                            e.tensor_copy(wb[b][:, 0:w], wi[b][:, 0:w]))
            wait("gpsimd", "vs", conv_v[it])
            if crow is None:
                store_g[it] = op("qg", lambda e, b=b, dst=dst, c=c:
                                 e.dma_start(out=dst.ap()[:, c].transpose([1, 0, 2]),
                                             in_=wb[b][:, 0:C].rearrange("p (f j) -> p f j", j=128)))
            else:
                store_g[it] = op("qg", lambda e, b=b, dst=dst, c=c:
                                 e.dma_start(out=dst.ap()[:, c].transpose([1, 0, 2]),
                                             in_=wb[b][:, 0:FH].rearrange("p (h j) -> p h j", j=128)))
            if crow is not None:
                wait("tensor", "vs", conv_v[it])
                for h in range(6):
                    op("ts", lambda e, b=b, h=h, c=c, nchunk=nchunk:
                       e.matmul(PS[0:1, h * 512:(h + 1) * 512], ones_sb[:, 0:1],
                                wb[b][:, h * 512:(h + 1) * 512],
                                start=(c == 0), stop=(c == nchunk - 1),
                                skip_group_check=True))
                colsum_t[it] = P.ctrs["ts"].n
            it += 1
        if crow is not None:
            wait("vector", "ts")
            op("vs", lambda e, crow=crow: e.tensor_copy(
                csum_sb[crow:crow + 1, 0:FH], PS[0:1, 0:FH]))
            wait("tensor", "vs")
    SF("al_sb", "p0_wi0", "p0_wi1", "p0_wb0", "p0_wb1")
    bar()
    WD_PENDING = True

    def ln_stats(xb_chunks, ntok, row_idx, nchunks=8, extra_sq=None):
        xsq = extra_sq
        nc_cols = ntok // 128
        nhalf = max(1, ntok // 2048)
        for half in range(nhalf):
            h0 = half * 2048
            hw = min(2048, ntok - h0)
            hc = hw // 128
            wait("tensor", "vs")
            for c in range(nchunks):
                wait("scalar", "vs")
                wait("scalar", "ts")
                op("ss", lambda e, c=c, h0=h0, hw=hw:
                   e.activation(xsq[:, 0:hw], xb_chunks(c)[:, h0:h0 + hw], Act.Square))
                wait("tensor", "ss")
                for t0 in range(0, hw, 512):
                    op("ts", lambda e, c=c, h0=h0, t0=t0:
                       e.matmul(PS[0:1, t0:t0 + 512], ones_sb[:, 0:1],
                                xb_chunks(c)[:, h0 + t0:h0 + t0 + 512],
                                start=(c == 0), stop=(c == nchunks - 1),
                                skip_group_check=True))
                    op("ts", lambda e, c=c, t0=t0:
                       e.matmul(PS[0:1, 2048 + t0:2048 + t0 + 512], ones_sb[:, 0:1],
                                xsq[:, t0:t0 + 512],
                                start=(c == 0), stop=(c == nchunks - 1),
                                skip_group_check=True))
            wait("vector", "ts")
            wait("vector", "qs")
            op("vs", lambda e, hw=hw: e.tensor_copy(srow[0:1, 0:hw], PS[0:1, 0:hw]))
            wait("sync", "vs")
            op("qs", lambda e, half=half, hc=hc, hw=hw: e.dma_start(
                out=scol[:, half * 16:half * 16 + hc], in_=srow[0:1, 0:hw]))
            qsx = P.ctrs["qs"].n
            wait("vector", "qs", qsx)
            op("vs", lambda e, hw=hw: e.tensor_copy(srow[0:1, 0:hw],
                                                    PS[0:1, 2048:2048 + hw]))
            wait("sync", "vs")
            op("qs", lambda e, half=half, hc=hc, hw=hw: e.dma_start(
                out=scol[:, 32 + half * 16:32 + half * 16 + hc], in_=srow[0:1, 0:hw]))
        wait("vector", "qs")
        ncc = nc_cols
        op("vs", lambda e: e.tensor_scalar_mul(
            scol[:, 64:64 + ncc], scol[:, 0:ncc], 1.0 / 1024.0))
        op("vs", lambda e: e.tensor_mul(
            scol[:, 96:96 + ncc], scol[:, 64:64 + ncc], scol[:, 64:64 + ncc]))
        op("vs", lambda e: e.scalar_tensor_tensor(
            scol[:, 96:96 + ncc], scol[:, 32:32 + ncc], 1.0 / 1024.0,
            scol[:, 96:96 + ncc], Alu.mult, Alu.subtract))
        wait("scalar", "vs")
        op("ss", lambda e: e.activation(
            scol[:, 96:96 + ncc], scol[:, 96:96 + ncc], Act.Ln,
            bias=cvec_sb[:, 1:2]))
        op("ss", lambda e: e.activation(
            scol[:, 96:96 + ncc], scol[:, 96:96 + ncc], Act.Exp, scale=-0.5))
        wait("vector", "ss")
        op("vs", lambda e: e.scalar_tensor_tensor(
            scol[:, 128:128 + ncc], scol[:, 64:64 + ncc], -1.0,
            scol[:, 96:96 + ncc], Alu.mult, Alu.mult))
        op("vs", lambda e: e.tensor_copy(sbcol[:, 0:ncc], scol[:, 96:96 + ncc]))
        op("vs", lambda e: e.tensor_copy(sbcol[:, 32:32 + ncc], scol[:, 128:128 + ncc]))
        wait("sync", "vs")
        for half in range(nhalf):
            h0 = half * 2048
            hw = min(2048, ntok - h0)
            hc = hw // 128
            op("qs", lambda e, h0=h0, hw=hw, half=half, hc=hc: e.dma_start(
                out=rows_sb[5:6, h0:h0 + hw],
                in_=sbcol[:, half * 16:half * 16 + hc]))
            op("qs", lambda e, h0=h0, hw=hw, half=half, hc=hc, r=row_idx: e.dma_start(
                out=rows_sb[r:r + 1, h0:h0 + hw],
                in_=sbcol[:, 32 + half * 16:32 + half * 16 + hc]))
        wait("tensor", "qs")
        wait("vector", "qs")

    def bcast_row(row_ap, ntok, dst_sb, pbank=0):
        for t0 in range(0, ntok, 512):
            bk = pbank + (t0 // 512) % 2
            wait("tensor", "vs")
            op("ts", lambda e, t0=t0, bk=bk: e.matmul(
                PS[:, bk * 512:(bk + 1) * 512], ones_sb[0:1, :],
                row_ap[:, t0:t0 + 512], start=True, stop=True))
            wait("vector", "ts")
            op("vs", lambda e, t0=t0, bk=bk: e.tensor_copy(
                dst_sb[:, t0:t0 + 512], PS[:, bk * 512:(bk + 1) * 512]))

    for c in range(8):
        op("qs", lambda e, c=c: e.dma_start(out=xfm(c), in_=x_own.ap()[c]))
    q_xown = P.ctrs["qs"].n

    xb1 = SA("xb1", [128, 8 * T2], BF16)
    xsq_a = SA("xsq_a", [128, T2], BF16)
    rbc_a = SA("rbc_a", [128, T], BF16)
    wait("vector", "qs", q_xown)
    for c in range(8):
        op("vs", lambda e, c=c: e.tensor_copy(xb1[:, c * T2:(c + 1) * T2], xfm(c)))
    ln_stats(lambda c: xb1[:, c * T2:(c + 1) * T2], T2, 0, extra_sq=xsq_a)
    bcast_row(rows_sb[5:6, :], T2, rbc_a)
    wait("vector", "vs")
    for c in range(8):
        op("vs", lambda e, c=c: e.tensor_mul(
            xb1[:, c * T2:(c + 1) * T2], xb1[:, c * T2:(c + 1) * T2], rbc_a[:, 0:T2]))
    v_xhat1 = P.ctrs["vs"].n

    wqf = [SA("wqf0", [128, C], F32), SA("wqf1", [128, C], F32)]
    wqb = SA("wqb", [128, 8 * 128], BF16)
    wqb8 = SA("wqb8", [8, 1], F32)
    qout = SA("qout", [128, T2], BF16)
    qsq = SA("qsq", [128, T2], BF16)
    wq_all = SA("wq_all", [128, 8 * C], BF16)
    conv_done = []
    for c in range(8):
        b = c % 2
        if c >= 2:
            wait("sync", "vs", conv_done[c - 2])
        op("qs", lambda e, c=c, b=b: e.dma_start(out=wqf[b][:], in_=wq.ap()[c]))
        wait("vector", "qs")
        conv_done.append(op("vs", lambda e, c=c, b=b: e.tensor_copy(
            wq_all[:, c * C:(c + 1) * C], wqf[b][:])))
        wait("tensor", "vs", conv_done[-1])
        for h in range(2):
            op("ts", lambda e, c=c, h=h: e.matmul(
                PS[0:1, h * 512:(h + 1) * 512], ones_sb[:, 0:1],
                wq_all[:, c * C + h * 512:c * C + (h + 1) * 512],
                start=(c == 0), stop=(c == 7), skip_group_check=True))
    wait("vector", "ts")
    op("vs", lambda e: e.tensor_copy(csum_sb[2:3, 0:C], PS[0:1, 0:C]))
    wait("tensor", "vs")
    wait("tensor", "vs", v_xhat1)
    qevs = []
    for fc in range(8):
        wait("scalar", "qg")
        for t0 in range(0, T2, 512):
            bk = 2 + ((fc * 4 + t0 // 512) % 2)
            if len(qevs) >= 2:
                wait("tensor", "ss", qevs[-2])
            for cc in range(8):
                op("ts", lambda e, fc=fc, t0=t0, cc=cc, bk=bk: e.matmul(
                    PS[:, bk * 512:(bk + 1) * 512],
                    wq_all[:, cc * C + fc * 128:cc * C + fc * 128 + 128],
                    xb1[:, cc * T2 + t0:cc * T2 + t0 + 512],
                    start=(cc == 0), stop=False, skip_group_check=True))
            op("ts", lambda e, fc=fc, t0=t0, bk=bk: e.matmul(
                PS[:, bk * 512:(bk + 1) * 512],
                csum_sb[2:3, fc * 128:fc * 128 + 128],
                rows_sb[0:1, t0:t0 + 512],
                start=False, stop=True, skip_group_check=True))
            wait("scalar", "ts")
            qevs.append(op("ss", lambda e, t0=t0, bk=bk: e.activation(
                qout[:, t0:t0 + 512], PS[:, bk * 512:(bk + 1) * 512], Act.Copy)))
        wait("gpsimd", "ss")
        op("qg", lambda e, fc=fc: e.dma_start(out=q_scr.ap()[fc], in_=qout[:]))
        wait("scalar", "ts")
        op("ss", lambda e: e.activation(qsq[:], qout[:], Act.Square))
        wait("tensor", "ss")
        for t0 in range(0, T2, 512):
            op("ts", lambda e, fc=fc, t0=t0: e.matmul(
                PS[0:1, 2048 + t0:2048 + t0 + 512], ones_sb[:, 0:1],
                qsq[:, t0:t0 + 512], start=(fc == 0), stop=(fc == 7),
                skip_group_check=True))
        if "d_q" in dbg_t:
            op("qg", lambda e, fc=fc: e.dma_start(out=dbg_t["d_q"].ap()[fc], in_=qout[:]))
    wait("vector", "ts")
    op("vs", lambda e: e.reduce_max(c0s[0:1, 0:1], PS[0:1, 2048:4096],
                                    axis=mybir.AxisListType.X))
    SF("xb1", "wqf0", "wqf1", "wqb", "wqb8", "qout", "qsq", "wq_all", "rbc_a", "xsq_a")
    bar()

    xbf = SA("xbf", [128, 8 * T], BF16)
    xfull_f = [SA("xfull_f0", [128, T], F32)]
    xsq_b = SA("xsq_b", [128, T2], BF16)
    rbc_b = SA("rbc_b", [128, T], BF16)

    cd = []
    for c in range(8):
        if c >= 1:
            wait("sync", "vs", cd[c - 1])
        op("qs", lambda e, c=c: e.dma_start(out=xfull_f[0][:], in_=x_full.ap()[c]))
        wait("vector", "qs")
        cd.append(op("vs", lambda e, c=c: e.tensor_copy(
            xbf[:, c * T:(c + 1) * T], xfull_f[0][:])))
    ln_stats(lambda c: xbf[:, c * T:(c + 1) * T], T, 1, extra_sq=xsq_b)
    bcast_row(rows_sb[5:6, :], T, rbc_b)
    for c in range(8):
        op("vs", lambda e, c=c: e.tensor_mul(
            xbf[:, c * T:(c + 1) * T], xbf[:, c * T:(c + 1) * T], rbc_b[:, 0:T]))
    v_xhatf = P.ctrs["vs"].n
    if "d_xhat1" in dbg_t:
        wait("gpsimd", "vs", v_xhatf)
        for c in range(8):
            op("qg", lambda e, c=c: e.dma_start(
                out=dbg_t["d_xhat1"].ap()[c], in_=xbf[:, c * T:(c + 1) * T]))
    SF("xfull_f0", "xsq_b", "rbc_b")

    wkvd_all = SA("wkvd_all", [128, 8 * KVR], BF16)
    wkvdf = SA("wkvdf", [128, KVR], F32)
    for c in range(8):
        wait("sync", "vs")
        op("qs", lambda e, c=c: e.dma_start(out=wkvdf[:], in_=wkvd.ap()[c]))
        wait("vector", "qs")
        op("vs", lambda e, c=c: e.tensor_copy(
            wkvd_all[:, c * KVR:(c + 1) * KVR], wkvdf[:]))
        wait("tensor", "vs")
        op("ts", lambda e, c=c: e.matmul(
            PS[0:1, 0:KVR], ones_sb[:, 0:1], wkvd_all[:, c * KVR:(c + 1) * KVR],
            start=(c == 0), stop=(c == 7), skip_group_check=True))
    wait("vector", "ts")
    op("vs", lambda e: e.tensor_copy(csum_sb[3:4, 0:KVR], PS[0:1, 0:KVR]))
    wait("tensor", "vs")
    lat_sb = SA("lat_sb", [128, T], BF16)
    wait("tensor", "vs", v_xhatf)
    ev = []
    for t0 in range(0, T, 512):
        bk = 2 + (t0 // 512) % 2
        if len(ev) >= 2:
            wait("tensor", "ss", ev[-2])
        for cc in range(8):
            op("ts", lambda e, t0=t0, cc=cc, bk=bk: e.matmul(
                PS[:, bk * 512:(bk + 1) * 512],
                wkvd_all[:, cc * KVR:(cc + 1) * KVR],
                xbf[:, cc * T + t0:cc * T + t0 + 512],
                start=(cc == 0), stop=False, skip_group_check=True))
        op("ts", lambda e, t0=t0, bk=bk: e.matmul(
            PS[:, bk * 512:(bk + 1) * 512], csum_sb[3:4, 0:KVR],
            rows_sb[1:2, t0:t0 + 512], start=False, stop=True,
            skip_group_check=True))
        wait("scalar", "ts")
        ev.append(op("ss", lambda e, t0=t0, bk=bk: e.activation(
            lat_sb[:, t0:t0 + 512], PS[:, bk * 512:(bk + 1) * 512], Act.Copy)))
    wkuk_sb = SA("wkuk_sb", [128, C], BF16)
    wkuv_sb = SA("wkuv_sb", [128, C], BF16)
    kchunk = SA("kchunk", [128, T], BF16)
    wkf = kchunk[:].bitcast(F32)[:, 0:C]
    op("qs", lambda e: e.dma_start(out=wkf, in_=wkvu_k.ap()))
    wait("vector", "qs")
    op("vs", lambda e: e.tensor_copy(wkuk_sb[:], wkf))
    wait("sync", "vs")
    op("qs", lambda e: e.dma_start(out=wkf, in_=wkvu_v.ap()))
    wait("vector", "qs")
    v_wkuv = op("vs", lambda e: e.tensor_copy(wkuv_sb[:], wkf))
    ksq = SA("ksq", [128, T2], BF16)
    wait("tensor", "vs", v_wkuv)
    wait("tensor", "ss")
    for fc in range(8):
        wait("scalar", "qg")
        ev = []
        for t0 in range(0, T, 512):
            bk = 2 + (t0 // 512) % 2
            if len(ev) >= 2:
                wait("tensor", "ss", ev[-2])
            op("ts", lambda e, fc=fc, t0=t0, bk=bk: e.matmul(
                PS[:, bk * 512:(bk + 1) * 512],
                wkuk_sb[:, fc * 128:(fc + 1) * 128],
                lat_sb[:, t0:t0 + 512], start=True, stop=True))
            wait("scalar", "ts")
            ev.append(op("ss", lambda e, t0=t0, bk=bk: e.activation(
                kchunk[:, t0:t0 + 512], PS[:, bk * 512:(bk + 1) * 512], Act.Copy)))
        wait("gpsimd", "ss")
        op("qg", lambda e, fc=fc: e.dma_start(out=kT_scr.ap()[fc], in_=kchunk[:]))
        if "d_k" in dbg_t:
            op("qg", lambda e, fc=fc: e.dma_start(out=dbg_t["d_k"].ap()[fc], in_=kchunk[:]))
        for half in range(2):
            wait("scalar", "ts")
            op("ss", lambda e, half=half: e.activation(
                ksq[:], kchunk[:, half * 2048:(half + 1) * 2048], Act.Square))
            wait("tensor", "ss")
            for t0 in range(0, 2048, 512):
                op("ts", lambda e, fc=fc, t0=t0, half=half: e.matmul(
                    PS[0:1, 2048 + t0:2048 + t0 + 512],
                    ones_sb[:, 0:1], ksq[:, t0:t0 + 512],
                    start=(fc == 0 and half == 0), stop=(fc == 7 and half == 1),
                    skip_group_check=True))
    wait("vector", "ts")
    op("vs", lambda e: e.reduce_max(c0s[0:1, 1:2], PS[0:1, 2048:4096],
                                    axis=mybir.AxisListType.X))
    vchunk = SA("vchunk", [128, C], BF16)
    wait("tensor", "vs")
    ev = []
    for tc in range(32):
        bk = 2 + (tc % 2) * 2
        wait("scalar", "qg")
        if len(ev) >= 2:
            wait("tensor", "ss", ev[-2])
        for d0 in range(0, C, 512):
            op("ts", lambda e, tc=tc, d0=d0, bk=bk: e.matmul(
                PS[:, (bk + d0 // 512) * 512:(bk + d0 // 512 + 1) * 512],
                lat_sb[:, tc * 128:(tc + 1) * 128],
                wkuv_sb[:, d0:d0 + 512], start=True, stop=True))
        wait("scalar", "ts")
        ev.append(op("ss", lambda e, bk=bk: e.activation(
            vchunk[:], PS[:, bk * 512:(bk + 2) * 512], Act.Copy)))
        wait("gpsimd", "ss")
        op("qg", lambda e, tc=tc: e.dma_start(out=v_scr.ap()[tc], in_=vchunk[:]))
    op("vs", lambda e: e.tensor_mul(c0s[0:1, 2:3], c0s[0:1, 0:1], c0s[0:1, 1:2]))
    wait("scalar", "vs")
    op("ss", lambda e: e.activation(c0s[0:1, 3:4], c0s[0:1, 2:3], Act.Sqrt))
    wait("vector", "ss")
    op("vs", lambda e: e.tensor_scalar_mul(c0s[0:1, 4:5], c0s[0:1, 3:4], -QSCALE))
    op("vs", lambda e: e.tensor_copy(c0b[0:1, 0:1], c0s[0:1, 4:5]))
    wait("tensor", "vs")
    op("ts", lambda e: e.matmul(PS[:, 0:1], ones_sb[0:1, :], c0b[0:1, 0:1],
                                start=True, stop=True))
    wait("vector", "ts")
    op("vs", lambda e: e.tensor_copy(cvec_sb[:, 0:1], PS[:, 0:1]))
    SF("wkvd_all", "wkvdf", "lat_sb", "wkuk_sb", "wkuv_sb",
       "kchunk", "ksq", "vchunk")
    bar()
    SF("xbf")

    dens_sb = SA("dens_sb", [16, T2], BF16)
    ysb = SA("ysb", [128, 8 * T2], BF16)
    rcp2 = SA("rcp2", [2, T2], BF16)
    khp = [SA("khp0", [128, T], BF16), SA("khp1", [128, T], BF16)]
    qhp = [SA("qhp0", [128, T2], BF16), SA("qhp1", [128, T2], BF16)]
    vstrip = [SA("vstrip0", [128, 32 * 130], BF16), SA("vstrip1", [128, 32 * 130], BF16)]
    ptile = [SA("ptile0", [128, 1024], BF16), SA("ptile1", [128, 1024], BF16)]
    for b in range(2):
        op("vs", lambda e, b=b: e.memset(
            vstrip[b][:].rearrange("p (c w) -> p c w", w=130)[:, :, 64:65], 1.0))
        op("vs", lambda e, b=b: e.memset(
            vstrip[b][:].rearrange("p (c w) -> p c w", w=130)[:, :, 129:130], 1.0))
    v_ones = P.ctrs["vs"].n
    wait("sync", "qg")

    wdi = [SA("wdi0", [128, C], I32), SA("wdi1", [128, C], I32)]
    wdb = [SA("wdb0", [128, C], BF16), SA("wdb1", [128, C], BF16)]
    wd_v = {}
    wd_g = {}
    for c24 in range(24):
        b = c24 % 2
        if c24 >= 2:
            wait("sync", "vs", wd_v[c24 - 2])
        op("qs", lambda e, c24=c24, b=b: e.dma_start(out=wdi[b][:], in_=wdT.ap()[c24]))
        wait("vector", "qs")
        if c24 >= 2:
            wait("vector", "qg", wd_g[c24 - 2])
        wd_v[c24] = op("vs", lambda e, b=b: e.tensor_copy(wdb[b][:], wdi[b][:]))
        wait("gpsimd", "vs", wd_v[c24])
        wd_g[c24] = op("qg", lambda e, c24=c24, b=b: e.dma_start(
            out=wd_scr.ap()[:, c24].transpose([1, 0, 2]),
            in_=wdb[b][:].rearrange("p (f j) -> p f j", j=128)))
    SF("wdi0", "wdi1", "wdb0", "wdb1")

    attn_iters = []
    for hp in range(8):
        for slot in range(4):
            for c in range(8 * slot + 8):
                attn_iters.append((hp, slot, c))

    qload_snap = {}
    vload_snap = {}
    pv_done_snap = {}
    exp_snap = {}
    maskv_snap = {}
    st_ready = {}
    for hp in range(8):
        bq = hp % 2
        if hp >= 2:
            wait("sync", "ts", pv_done_snap[(hp - 2, 3)])
        op("qs", lambda e, hp=hp, bq=bq: e.dma_start(out=qhp[bq][:],
                                                     in_=q_scr.ap()[hp]))
        op("qs", lambda e, hp=hp, bq=bq: e.dma_start(out=khp[bq][:],
                                                     in_=kT_scr.ap()[hp]))
        qload_snap[hp] = P.ctrs["qs"].n
        for slot in range(4):
            ext = 8 * slot + 8
            bv = (hp * 4 + slot) % 2
            if (hp * 4 + slot) >= 2:
                prev = (hp, slot - 2) if slot >= 2 else (hp - 1, slot + 2)
                wait("sync", "ts", pv_done_snap[prev])
            for h in range(2):
                op("qs", lambda e, hp=hp, ext=ext, bv=bv, h=h: e.dma_start(
                    out=vstrip[bv][:].rearrange("p (c w) -> p c w", w=130)
                    [:, 0:ext, h * 65:h * 65 + 64],
                    in_=v_scr.ap()[0:ext, :, hp * 128 + h * 64:hp * 128 + h * 64 + 64]
                    .transpose([1, 0, 2])))
            vload_snap[(hp, slot)] = P.ctrs["qs"].n
            obank = 4 + 2 * (slot % 2)
            pv_args = {}
            for c in range(ext):
                it = (hp, slot, c)
                jmin = max(0, (c - 8 * slot) // 2)
                W = 512 - jmin * 128
                ping = c % 2
                wait("tensor", "qs", qload_snap[hp])
                wait("tensor", "qs", vload_snap[(hp, slot)])
                wait("tensor", "vs", v_ones)
                if (hp, slot, c - 2) in exp_snap:
                    wait("tensor", "ss", exp_snap[(hp, slot, c - 2)])
                for h in range(2):
                    op("ts", lambda e, hp=hp, slot=slot, c=c, h=h, jmin=jmin,
                       ping=ping, bq=bq, W=W: e.matmul(
                           PS[:, (ping * 2 + h) * 512 + jmin * 128:
                              (ping * 2 + h) * 512 + 512],
                           khp[bq][h * 64:(h + 1) * 64,
                                   c * 128:c * 128 + 128],
                           qhp[bq][h * 64:(h + 1) * 64,
                                   slot * 512 + jmin * 128:slot * 512 + 512],
                           start=True, stop=True, tile_position=(h * 64, 0),
                           skip_group_check=True))
                st_ready[it] = P.ctrs["ts"].n

                def emit_pv(cc):
                    itp = (hp, slot, cc)
                    if itp in maskv_snap:
                        wait("tensor", "vs", maskv_snap[itp])
                    else:
                        wait("tensor", "ss", exp_snap[itp])
                    jm, bvv = pv_args[cc]
                    for h in range(2):
                        op("ts", lambda e, cc=cc, h=h, jm=jm, bvv=bvv, ob=obank,
                           ex=ext: e.matmul(
                            PS[0:65, (ob + h) * 512 + jm * 128:
                               (ob + h) * 512 + 512],
                            vstrip[bvv][:, cc * 130 + h * 65:cc * 130 + h * 65 + 65],
                            ptile[cc % 2][:, h * 512 + jm * 128:h * 512 + 512],
                            start=(cc == 0), stop=(cc == ex - 1),
                            skip_group_check=True))
                pv_args[c] = (jmin, bv)
                if c >= 1:
                    emit_pv(c - 1)
                    pv_done_snap[(hp, slot)] = P.ctrs["ts"].n

                wait("scalar", "ts", st_ready[it])
                exp_snap[it] = op("ss", lambda e, ping=ping, jmin=jmin, W=W, c=c:
                                  e.activation(
                    ptile[c % 2][:, 0:1024].rearrange("p (h w) -> p h w", h=2)
                    [:, :, jmin * 128:512],
                    PS[:, ping * 1024:ping * 1024 + 1024]
                    .rearrange("p (h w) -> p h w", h=2)[:, :, jmin * 128:512],
                    Act.Exp, bias=cvec_sb[:, 0:1], scale=QSCALE), sw=False)
                mj = None
                if c % 2 == 0 and 0 <= (c // 2) - 4 * slot <= 3:
                    mj = ((c // 2) - 4 * slot, maska_sb)
                if c % 2 == 1 and 0 <= ((c - 1) // 2) - 4 * slot <= 3:
                    mj = (((c - 1) // 2) - 4 * slot, maskb_sb)
                if mj is not None:
                    j, msk = mj
                    wait("vector", "ss", exp_snap[it])
                    for h in range(2):
                        op("vs", lambda e, h=h, j=j, msk=msk, c=c: e.tensor_mul(
                            ptile[c % 2][:, h * 512 + j * 128:h * 512 + (j + 1) * 128],
                            ptile[c % 2][:, h * 512 + j * 128:h * 512 + (j + 1) * 128],
                            msk[:]), sw=False)
                    maskv_snap[it] = P.ctrs["vs"].n
            emit_pv(ext - 1)
            pv_done_snap[(hp, slot)] = P.ctrs["ts"].n
            wait("vector", "ts", pv_done_snap[(hp, slot)])
            for h in range(2):
                op("vs", lambda e, hp=hp, slot=slot, h=h, obank=obank: e.tensor_copy(
                    ysb[h * 64:(h + 1) * 64, hp * T2 + slot * 512:hp * T2 + slot * 512 + 512],
                    PS[0:64, (obank + h) * 512:(obank + h) * 512 + 512]))
            op("vs", lambda e, obank=obank: e.tensor_copy(
                ptile[0][64:65, 0:1024],
                PS[64:65, obank * 512:obank * 512 + 1024]))
            wait("gpsimd", "vs")
            op("qg", lambda e, hp=hp, slot=slot: e.dma_start(
                out=dens_sb[2 * hp:2 * hp + 2, slot * 512:slot * 512 + 512],
                in_=ptile[0][64:65, 0:1024]))
            dens_g = P.ctrs["qg"].n
            P.wait("scalar", "qg", dens_g)
            wait("tensor", "vs")
    wait("vector", "qg")
    op("vs", lp(lambda e: e.reciprocal(dens_sb[:, :], dens_sb[:, :])))
    v_rcp = P.ctrs["vs"].n
    wait("tensor", "vs", v_rcp)
    wait("tensor", "ss")
    for hp in range(8):
        wait("sync", "vs")
        op("qs", lambda e, hp=hp: e.dma_start(
            out=rcp2[:], in_=dens_sb[2 * hp:2 * hp + 2, :]))
        q_r = P.ctrs["qs"].n
        wait("tensor", "qs", q_r)
        for t0 in range(0, T2, 512):
            bk = (t0 // 512) % 4
            wait("tensor", "vs")
            op("ts", lambda e, t0=t0, bk=bk: e.matmul(
                PS[:, bk * 512:(bk + 1) * 512], sel2_sb[:, :],
                rcp2[0:2, t0:t0 + 512], start=True, stop=True))
            wait("vector", "ts")
            op("vs", lambda e, hp=hp, t0=t0, bk=bk: e.tensor_mul(
                ysb[:, hp * T2 + t0:hp * T2 + t0 + 512],
                ysb[:, hp * T2 + t0:hp * T2 + t0 + 512],
                PS[:, bk * 512:(bk + 1) * 512]))
    SF("qhp0", "qhp1", "vstrip0", "vstrip1", "ptile0", "ptile1", "khp0", "khp1")
    woutf = SA("woutf", [128, C], F32)
    wout_all = SA("wout_all", [128, 8 * C], BF16)
    for hc in range(8):
        wait("sync", "vs")
        op("qs", lambda e, hc=hc: e.dma_start(out=woutf[:], in_=wout.ap()[hc]))
        wait("vector", "qs")
        op("vs", lambda e, hc=hc: e.tensor_copy(
            wout_all[:, hc * C:(hc + 1) * C], woutf[:]))
    v_wout = P.ctrs["vs"].n
    wait("tensor", "vs", v_wout)
    evs = []
    for cc in range(8):
        for slot in range(4):
            bk = 4 + (cc * 4 + slot) % 2
            if len(evs) >= 2:
                wait("tensor", "vs", evs[-2])
            for hc in range(8):
                op("ts", lambda e, cc=cc, slot=slot, hc=hc, bk=bk: e.matmul(
                    PS[:, bk * 512:(bk + 1) * 512],
                    wout_all[:, hc * C + cc * 128:hc * C + cc * 128 + 128],
                    ysb[:, hc * T2 + slot * 512:hc * T2 + slot * 512 + 512],
                    start=(hc == 0), stop=(hc == 7), skip_group_check=True))
            wait("vector", "ts")
            evs.append(op("vs", lambda e, cc=cc, slot=slot, bk=bk: e.tensor_add(
                xfm(cc)[:, slot * 512:slot * 512 + 512],
                xfm(cc)[:, slot * 512:slot * 512 + 512],
                PS[:, bk * 512:(bk + 1) * 512])))
    SF("woutf", "wout_all", "rcp2", "ysb", "dens_sb")
    if "d_x1" in dbg_t:
        wait("gpsimd", "vs")
        for c in range(8):
            op("qg", lambda e, c=c: e.dma_start(out=dbg_t["d_x1"].ap()[c], in_=xfm(c)))
    bar()

    def emit_ffn(ln_row, dbg_name, sfx=""):
        xb = SA("f_xb" + sfx, [128, 8 * T2], BF16)
        xsq = SA("f_xsq" + sfx, [128, T2], BF16)
        rbc = SA("f_rbc" + sfx, [128, T2], BF16)
        wait("vector", "qs")
        for c in range(8):
            op("vs", lambda e, c=c: e.tensor_copy(xb[:, c * T2:(c + 1) * T2], xfm(c)))
        ln_stats(lambda c: xb[:, c * T2:(c + 1) * T2], T2, ln_row, extra_sq=xsq)
        bcast_row(rows_sb[5:6, :], T2, rbc)
        for c in range(8):
            op("vs", lambda e, c=c: e.tensor_mul(
                xb[:, c * T2:(c + 1) * T2], xb[:, c * T2:(c + 1) * T2], rbc[:, 0:T2]))
        v_xh = P.ctrs["vs"].n
        wait("tensor", "vs", v_xh)
        wait("sync", "vs", v_xh)
        wait("gpsimd", "vs", v_xh)
        SF("f_xsq" + sfx, "f_rbc" + sfx)

        wgt = [SA("f_wg0" + sfx, [128, 1024], BF16), SA("f_wg1" + sfx, [128, 1024], BF16)]
        wdt = [SA("f_wd0" + sfx, [128, 3072], BF16), SA("f_wd1" + sfx, [128, 3072], BF16)]
        wut = [wdt[0][:, 0:1024], wdt[1][:, 0:1024]]
        hact = SA("f_hact" + sfx, [128, 24 * 1024], BF16)
        tdone = {}
        hdone = {}
        dcons = {}
        for tt in range(2):
            t0 = tt * 1024
            for h in range(24):
                bw = h % 2
                key = (tt, h)
                if h >= 2:
                    wait("sync", "ts", tdone[(tt, h - 2)])
                elif tt:
                    wait("sync", "ts")
                op("qs", lambda e, h=h, bw=bw: e.dma_start(
                    out=wgt[bw][:].rearrange("p (c j) -> p c j", j=128),
                    in_=wg_scr.ap()[h].transpose([1, 0, 2])))
                op("qs", lambda e, h=h, bw=bw: e.dma_start(
                    out=wut[bw].rearrange("p (c j) -> p c j", j=128),
                    in_=wu_scr.ap()[h].transpose([1, 0, 2])))
                wl = P.ctrs["qs"].n
                wait("tensor", "qs", wl)
                gb = (h % 2) * 2048
                if (tt, h - 2) in hdone:
                    wait("tensor", "vs", hdone[(tt, h - 2)])
                elif tt and h <= 1:
                    wait("tensor", "vs", dcons[(tt - 1, 7)])
                for half in range(2):
                    hh = t0 + half * 512
                    for cc in range(8):
                        op("ts", lambda e, cc=cc, hh=hh, bw=bw, half=half, gb=gb: e.matmul(
                            PS[:, gb + half * 512:gb + (half + 1) * 512],
                            wgt[bw][:, cc * 128:(cc + 1) * 128],
                            xb[:, cc * T2 + hh:cc * T2 + hh + 512],
                            start=(cc == 0), stop=False, skip_group_check=True))
                    op("ts", lambda e, h=h, hh=hh, half=half, ln_row=ln_row, gb=gb: e.matmul(
                        PS[:, gb + half * 512:gb + (half + 1) * 512],
                        csum_sb[0:1, h * 128:(h + 1) * 128],
                        rows_sb[ln_row:ln_row + 1, hh:hh + 512],
                        start=False, stop=True, skip_group_check=True))
                    for cc in range(8):
                        op("ts", lambda e, cc=cc, hh=hh, half=half, gb=gb, bw=bw: e.matmul(
                            PS[:, gb + 1024 + half * 512:gb + 1024 + (half + 1) * 512],
                            wut[bw][:, cc * 128:(cc + 1) * 128],
                            xb[:, cc * T2 + hh:cc * T2 + hh + 512],
                            start=(cc == 0), stop=False, skip_group_check=True))
                    t_up = op("ts", lambda e, h=h, hh=hh, half=half, ln_row=ln_row, gb=gb: e.matmul(
                        PS[:, gb + 1024 + half * 512:gb + 1024 + (half + 1) * 512],
                        csum_sb[1:2, h * 128:(h + 1) * 128],
                        rows_sb[ln_row:ln_row + 1, hh:hh + 512],
                        start=False, stop=True, skip_group_check=True))
                tdone[key] = t_up
                wait("scalar", "ts", t_up)
                s_g = op("ss", lambda e, h=h, gb=gb: e.activation(
                    hact[:, h * 1024:(h + 1) * 1024], PS[:, gb:gb + 1024],
                    Act.Silu, scale=FSC))
                wait("vector", "ss", s_g)
                hdone[key] = op("vs", lambda e, h=h, gb=gb: e.scalar_tensor_tensor(
                    hact[:, h * 1024:(h + 1) * 1024], PS[:, gb + 1024:gb + 2048],
                    FSC, hact[:, h * 1024:(h + 1) * 1024], Alu.mult, Alu.mult))
            wait("tensor", "vs", hdone[(tt, 23)])
            dmm = {}
            for cc in range(8):
                key = (tt, cc)
                db = (cc % 2) * 1024
                bwd = cc % 2
                if cc >= 2:
                    wait("sync", "ts", dmm[cc - 2])
                else:
                    wait("sync", "ts")
                op("qs", lambda e, cc=cc, bwd=bwd: e.dma_start(
                    out=wdt[bwd][:].rearrange("p (h j) -> p h j", j=128),
                    in_=wd_scr.ap()[cc].transpose([1, 0, 2])))
                dl = P.ctrs["qs"].n
                wait("tensor", "qs", dl)
                prev2 = (tt, cc - 2) if cc >= 2 else ((tt - 1, 6 + cc) if tt else None)
                if prev2 is not None and prev2 in dcons:
                    wait("tensor", "vs", dcons[prev2])
                for half in range(2):
                    for h in range(24):
                        op("ts", lambda e, h=h, half=half, db=db, bwd=bwd: e.matmul(
                            PS[:, db + half * 512:db + (half + 1) * 512],
                            wdt[bwd][:, h * 128:(h + 1) * 128],
                            hact[:, h * 1024 + half * 512:h * 1024 + half * 512 + 512],
                            start=(h == 0), stop=(h == 23), skip_group_check=True))
                dmm[cc] = P.ctrs["ts"].n
                wait("vector", "ts")
                dcons[key] = op("vs", lambda e, cc=cc, t0=t0, db=db: e.scalar_tensor_tensor(
                    xfm(cc)[:, t0:t0 + 1024], PS[:, db:db + 1024],
                    FSC, xfm(cc)[:, t0:t0 + 1024], Alu.mult, Alu.add))
        SF("f_xb" + sfx, "f_wg0" + sfx, "f_wg1" + sfx, "f_wd0" + sfx,
           "f_wd1" + sfx, "f_hact" + sfx)
        if dbg_name in dbg_t:
            wait("gpsimd", "vs")
            for c in range(8):
                op("qg", lambda e, c=c, n=dbg_name: e.dma_start(
                    out=dbg_t[n].ap()[c], in_=xfm(c)))
        bar()

    emit_ffn(2, "d_x2", "a")

    xb3 = SA("m_xb3", [128, 8 * T2], BF16)
    xsq3 = SA("m_xsq3", [128, T2], BF16)
    rbc3 = SA("m_rbc3", [128, T2], BF16)
    wait("vector", "qs")
    for c in range(8):
        op("vs", lambda e, c=c: e.tensor_copy(xb3[:, c * T2:(c + 1) * T2], xfm(c)))
    ln_stats(lambda c: xb3[:, c * T2:(c + 1) * T2], T2, 6, extra_sq=xsq3)
    bcast_row(rows_sb[5:6, :], T2, rbc3)
    for c in range(8):
        op("vs", lambda e, c=c: e.tensor_mul(
            xb3[:, c * T2:(c + 1) * T2], xb3[:, c * T2:(c + 1) * T2], rbc3[:, 0:T2]))
    v_x3h = P.ctrs["vs"].n
    wait("gpsimd", "vs", v_x3h)
    for c in range(8):
        op("qg", lambda e, c=c: e.dma_start(
            out=ln3_in.ap()[c * 128:(c + 1) * 128, :], in_=xb3[:, c * T2:(c + 1) * T2]))
    op("qg", lambda e: e.dma_start(out=ln3_in.ap()[1024:1025, :], in_=rows_sb[6:7, 0:T2]))
    for c in range(8):
        op("qg", lambda e, c=c: e.dma_start(out=park_scr.ap()[c], in_=xfm(c)))
    g_ln3 = P.ctrs["qg"].n
    wait("gpsimd", "qg", g_ln3)
    bar()
    op("gc", lambda e: e.collective_compute(
        "AllGather", Alu.bypass, replica_groups=GROUPS,
        ins=[ln3_in.ap().opt()], outs=[ln3_out.ap().opt()]))
    SF("m_xb3", "m_xsq3", "m_rbc3")
    bar()

    rhsln = xfm_h[0][:].bitcast(BF16)
    for c in range(8):
        for b in range(2):
            op("qs", lambda e, c=c, b=b: e.dma_start(
                out=rhsln[:, c * T:(c + 1) * T]
                .rearrange("p (g j) -> p g j", j=128)[:, b::2, :],
                in_=ln3_out.ap()[b, c * 128:(c + 1) * 128, :]
                .rearrange("p (g j) -> p g j", j=128).transpose([0, 1, 2])))
    for b in range(2):
        op("qs", lambda e, b=b: e.dma_start(
            out=rows_sb[3:4, 0:T].rearrange("o (g j) -> o g j", j=128)[:, b::2, :],
            in_=ln3_out.ap()[b, 1024:1025, :].rearrange("o (g j) -> o g j", j=128)))
    q_rhs = P.ctrs["qs"].n

    inpf = SA("m_inpf", [128, 1024], F32)
    inpb = [SA("m_inpb0", [128, 1024], BF16), SA("m_inpb1", [128, 1024], BF16)]
    xzc = [SA("m_xzc", [128, T], F32), SA("m_xzc1", [128, T], F32)]
    cacc = SA("m_cacc", [128, T], F32)
    xcsb = SA("m_xcsb", [128, T], BF16)
    wait("tensor", "qs", q_rhs)
    evx = []
    for ec in range(16):
        bw = ec % 2
        if ec > 0:
            wait("scalar", "qg")
        wait("sync", "vs")
        op("qs", lambda e, ec=ec: e.dma_start(
            out=inpf[:].rearrange("p (c j) -> p c j", j=128),
            in_=inpT.ap()[:, :, ec * 128:(ec + 1) * 128].transpose([1, 0, 2])))
        wait("vector", "qs")
        v_cv = op("vs", lambda e, bw=bw: e.tensor_copy(inpb[bw][:], inpf[:]))
        wait("tensor", "vs", v_cv)
        for cc in range(8):
            op("ts", lambda e, cc=cc, bw=bw: e.matmul(
                PS[0:1, 3584:3584 + 128],
                ones_sb[:, 0:1], inpb[bw][:, cc * 128:(cc + 1) * 128],
                start=(cc == 0), stop=(cc == 7), skip_group_check=True))
        wait("vector", "ts")
        op("vs", lambda e, ec=ec: e.tensor_copy(
            csum_sb[4:5, ec * 128:(ec + 1) * 128], PS[0:1, 3584:3584 + 128]))
        wait("tensor", "vs")
        for t0 in range(0, T, 512):
            bk = (t0 // 512) % 2
            if len(evx) >= 2:
                wait("tensor", "ss", evx[-2])
            for cc in range(8):
                op("ts", lambda e, cc=cc, t0=t0, bw=bw, bk=bk: e.matmul(
                    PS[:, bk * 512:(bk + 1) * 512],
                    inpb[bw][:, cc * 128:(cc + 1) * 128],
                    rhsln[:, cc * T + t0:cc * T + t0 + 512],
                    start=(cc == 0), stop=False, skip_group_check=True))
            op("ts", lambda e, ec=ec, t0=t0, bk=bk: e.matmul(
                PS[:, bk * 512:(bk + 1) * 512],
                csum_sb[4:5, ec * 128:ec * 128 + 128],
                rows_sb[3:4, t0:t0 + 512], start=False, stop=True,
                skip_group_check=True))
            if ec < 8 and evx:
                wait("scalar", "vs")
            wait("scalar", "ts")
            if ec < 8:
                evx.append(op("ss", lambda e, t0=t0, bk=bk, bz=ec % 2: e.activation(
                    xzc[bz][:, t0:t0 + 512], PS[:, bk * 512:(bk + 1) * 512], Act.Copy)))
            else:
                evx.append(op("ss", lambda e, t0=t0, bk=bk: e.activation(
                    xcsb[:, t0:t0 + 512], PS[:, bk * 512:(bk + 1) * 512], Act.Silu)))
        if ec >= 8:
            wait("gpsimd", "ss")
            op("qg", lambda e, ec=ec: e.dma_start(out=zs_scr.ap()[ec - 8], in_=xcsb[:]))
            continue
        wait("vector", "ss")
        op("vs", lambda e, ec=ec, bz=ec % 2: e.tensor_scalar_mul(
            cacc[:], xzc[bz][:], convw_sb[:, ec * 4 + 3:ec * 4 + 4]))
        for k in range(3):
            sh = 3 - k
            op("vs", lambda e, ec=ec, k=k, sh=sh, bz=ec % 2: e.scalar_tensor_tensor(
                cacc[:, sh:T], xzc[bz][:, 0:T - sh], convw_sb[:, ec * 4 + k:ec * 4 + k + 1],
                cacc[:, sh:T], Alu.mult, Alu.add))
        v_conv = P.ctrs["vs"].n
        wait("scalar", "vs", v_conv)
        op("ss", lambda e, ec=ec: e.activation(
            xcsb[:], cacc[:], Act.Silu, bias=convb_sb[:, ec:ec + 1]))
        wait("gpsimd", "ss")
        op("qg", lambda e, ec=ec: e.dma_start(out=xc_scr.ap()[ec], in_=xcsb[:]))
        if "d_xcs" in dbg_t:
            op("qg", lambda e, ec=ec: e.dma_start(out=dbg_t["d_xcs"].ap()[ec], in_=xcsb[:]))
    SF("m_inpf", "m_inpb0", "m_inpb1", "m_xzc", "m_xzc1", "m_cacc", "m_xcsb")
    bar()

    dbcb = SA("m_dbcb", [128, T], BF16)
    xpf = SA("m_xpf", [128, 768], F32)
    xpb = SA("m_xpb", [128, 768], BF16)
    op("qs", lambda e: e.dma_start(
        out=xpf[:].rearrange("p (c j) -> p c j", j=96), in_=xpT.ap().transpose([1, 0, 2])))
    wait("vector", "qs")
    op("vs", lambda e: e.tensor_copy(xpb[:], xpf[:]))
    dbcp = SA("m_dbcp", [128, T], F32)
    xcq = [SA("m_xcq0", [128, 512], BF16), SA("m_xcq1", [128, 512], BF16)]
    wait("tensor", "vs")
    for ts8 in range(8):
        t0 = ts8 * 512
        bk = ts8 % 2
        wait("tensor", "vs")
        for ec in range(8):
            bq = (ts8 * 8 + ec) % 2
            if ts8 * 8 + ec >= 2:
                wait("sync", "ts")
            op("qs", lambda e, ec=ec, t0=t0, bq=bq: e.dma_start(
                out=xcq[bq][:], in_=xc_scr.ap()[ec, :, t0:t0 + 512]))
            wait("tensor", "qs")
            op("ts", lambda e, ec=ec, t0=t0, bq=bq, bk=bk: e.matmul(
                PS[0:96, bk * 512:bk * 512 + 512],
                xpb[:, ec * 96:(ec + 1) * 96], xcq[bq][:],
                start=(ec == 0), stop=(ec == 7), skip_group_check=True))
        wait("vector", "ts")
        op("vs", lambda e, t0=t0, bk=bk: e.tensor_copy(
            dbcp[0:96, t0:t0 + 512], PS[0:96, bk * 512:bk * 512 + 512]))
    wait("gpsimd", "vs")
    op("qg", lambda e: e.dma_start(out=dbc_in.ap(), in_=dbcp[0:96, :]))
    wait("gpsimd", "qg")
    op("gc", lambda e: e.collective_compute(
        "AllReduce", Alu.add, replica_groups=GROUPS,
        ins=[dbc_in.ap().opt()], outs=[dbc_out.ap().opt()]))
    wait("sync", "gc")
    op("qs", lambda e: e.dma_start(out=dbcp[0:96, :], in_=dbc_out.ap()))
    q_dbc = P.ctrs["qs"].n
    if "d_dbc" in dbg_t:
        wait("gpsimd", "qs", q_dbc)
        op("qg", lambda e: e.dma_start(out=dbg_t["d_dbc"].ap()[0:96, :], in_=dbcp[0:96, :]))
    wait("vector", "qs", q_dbc)
    op("vs", lambda e: e.tensor_copy(dbcb[0:96, :], dbcp[0:96, :]))
    v_dbcb = P.ctrs["vs"].n
    wait("gpsimd", "vs", v_dbcb)
    op("qg", lambda e: e.dma_start(out=dbc_bf.ap(), in_=dbcb[64:96, :]))
    dttf = SA("m_dttf", [64, EO], F32)
    dttb = SA("m_dttb", [64, EO], BF16)
    op("qs", lambda e: e.dma_start(out=dttf[:], in_=dtT.ap()))
    wait("vector", "qs")
    op("vs", lambda e: e.tensor_copy(dttb[:], dttf[:]))
    v_dtt = P.ctrs["vs"].n
    dlt = [SA("m_dlt", [128, T], F32), SA("m_dlt1", [128, T], F32)]
    wait("tensor", "vs", max(v_dbcb, v_dtt))
    dexp = {}
    for ec in range(8):
        pb = (ec % 2) * 2048
        bd = ec % 2
        if ec >= 2:
            wait("tensor", "ss", dexp[ec - 2])
        for t0 in range(0, T, 512):
            op("ts", lambda e, ec=ec, t0=t0, pb=pb: e.matmul(
                PS[:, pb + t0 % 2048:pb + t0 % 2048 + 512],
                dttb[:, ec * 128:(ec + 1) * 128],
                dbcb[0:64, t0:t0 + 512], start=(t0 < 2048), stop=(t0 >= 2048),
                skip_group_check=True))
        wait("scalar", "ts")
        if ec >= 2:
            wait("scalar", "qg")
        for hh in range(2):
            dexp[ec] = op("ss", lambda e, ec=ec, pb=pb, hh=hh, bd=bd: e.activation(
                dlt[bd][:, hh * 2048:(hh + 1) * 2048],
                PS[:, pb:pb + 2048], Act.Exp, bias=dtb_sb[:, ec:ec + 1]))
        op("ss", lambda e, bd=bd: e.activation(dlt[bd][:], dlt[bd][:], Act.Ln, bias=1.0))
        wait("gpsimd", "ss")
        op("qg", lambda e, ec=ec, bd=bd: e.dma_start(out=dl_scr.ap()[ec], in_=dlt[bd][:]))
        if "d_delta" in dbg_t:
            op("qg", lambda e, ec=ec, bd=bd: e.dma_start(out=dbg_t["d_delta"].ap()[ec], in_=dlt[bd][:]))
    SF("m_xpf", "m_xpb", "m_dbcp", "m_xcq0", "m_xcq1", "m_dttf", "m_dttb", "m_dlt", "m_dlt1", "m_dbcb")
    bar()

    for c in range(8):
        op("qg", lambda e, c=c: e.dma_start(out=xfm(c), in_=park_scr.ap()[c]))
    g_park = P.ctrs["qg"].n

    bcst = SA("m_bcst", [128, 32 * 1024], BF16)
    dq = [SA("m_dq0", [128, 1024], F32), SA("m_dq1", [128, 1024], F32)]
    xq = [SA("m_xq0", [128, 1024], BF16), SA("m_xq1", [128, 1024], BF16)]
    uq = SA("m_uq", [128, 1024], BF16)
    dA = [PS[:, 0:1024], PS[:, 1024:2048]]
    bxq = [SA("m_bxq", [128, 1024], BF16), SA("m_bxq1", [128, 1024], BF16)]
    hsq = SA("m_hsq", [128, 1024], BF16)
    ytq = [SA("m_ytq0", [128, 1024], BF16), SA("m_ytq1", [128, 1024], BF16)]
    zq = [SA("m_zq0", [128, 1024], BF16), SA("m_zq1", [128, 1024], BF16)]
    ymq = SA("m_ymq", [128, 1024], BF16)
    dxq = bxq[0]
    ld_snap = {}
    ymq_g = {}
    for tq in range(4):
        t0 = tq * 1024
        wait("sync", "vs")
        bc_half = None
        for n in range(NST):
            op("qs", lambda e, n=n, t0=t0: e.dma_start(
                out=bcst[:, 2 * n * 1024:(2 * n + 2) * 1024]
                .rearrange("p (r j) -> p r j", j=1024),
                in_=dbc_bf.ap()[n::16, t0:t0 + 1024].partition_broadcast(128)))
            if n == 7:
                bc_half = P.ctrs["qs"].n
        bc_q = P.ctrs["qs"].n
        def emit_loads(ec_):
            bi_ = ec_ % 2
            wait("sync", "vs")
            op("qs", lambda e, ec=ec_, t0=t0, bi=bi_: e.dma_start(
                out=dq[bi][:], in_=dl_scr.ap()[ec, :, t0:t0 + 1024]))
            op("qs", lambda e, ec=ec_, t0=t0, bi=bi_: e.dma_start(
                out=xq[bi][:], in_=xc_scr.ap()[ec, :, t0:t0 + 1024]))
            op("qs", lambda e, ec=ec_, t0=t0, bi=bi_: e.dma_start(
                out=zq[bi][:], in_=zs_scr.ap()[ec, :, t0:t0 + 1024]))
            ld_snap[(tq, ec_)] = P.ctrs["qs"].n
        for ec in range(8):
            bi = ec % 2
            yb = 2048 + (ec % 2) * 1024
            if ec == 0:
                emit_loads(0)
            wait("vector", "qs", ld_snap[(tq, ec)])
            wait("vector", "qs", bc_half)
            op("vs", lambda e, bi=bi: e.tensor_mul(uq[:], dq[bi][:], xq[bi][:]))
            if ec < 7:
                emit_loads(ec + 1)
            wait("scalar", "vs")
            acc_prev = None
            acc_prev2 = None
            scan_v = {}
            for n in range(NST):
                ba = n % 2
                col = ec * 16 + n
                if n >= 2:
                    wait("scalar", "vs", scan_v[n - 2])
                op("ss", lambda e, ba=ba, col=col, bi=bi: e.activation(
                    dA[ba], dq[bi][:], Act.Exp, scale=amat_sb[:, col:col + 1]),
                    sw=False)
                exp_s = P.ctrs["ss"].n
                if n == 8:
                    wait("vector", "qs", bc_q)
                op("vs", lambda e, n=n, ba=ba: e.tensor_mul(
                    bxq[ba][:], uq[:], bcst[:, 2 * n * 1024:(2 * n + 1) * 1024]),
                    sw=(n < 2))
                wait("vector", "ss", exp_s)
                scan_v[n] = op("vs", lambda e, ba=ba, col=col: e.tensor_tensor_scan(
                    hsq[:], dA[ba], bxq[ba][:], state_sb[:, col:col + 1],
                    Alu.mult, Alu.add))
                op("vs", lambda e, col=col: e.tensor_copy(
                    state_sb[:, col:col + 1], hsq[:, 1023:1024]))
                if acc_prev2 is not None:
                    wait("vector", "ts", acc_prev2)
                op("vs", lambda e, n=n, ba=ba: e.tensor_mul(
                    ytq[ba][:], hsq[:],
                    bcst[:, (2 * n + 1) * 1024:(2 * n + 2) * 1024]),
                    sw=(n < 2))
                wait("tensor", "vs")
                for hh in range(2):
                    op("ts", lambda e, hh=hh, n=n, yb=yb, ba=ba: e.matmul(
                        PS[:, yb + hh * 512:yb + (hh + 1) * 512],
                        eye_sb[:], ytq[ba][:, hh * 512:(hh + 1) * 512],
                        start=(n == 0), stop=False, skip_group_check=True))
                acc_prev2 = acc_prev
                acc_prev = P.ctrs["ts"].n
            if acc_prev is not None:
                wait("vector", "ts", acc_prev)
            op("vs", lambda e, bi=bi, ec=ec: e.tensor_scalar_mul(
                dxq[:], xq[bi][:], dpar_sb[:, ec:ec + 1]))
            wait("tensor", "vs")
            for hh in range(2):
                op("ts", lambda e, hh=hh, yb=yb: e.matmul(
                    PS[:, yb + hh * 512:yb + (hh + 1) * 512],
                    eye_sb[:], dxq[:, hh * 512:(hh + 1) * 512],
                    start=False, stop=(hh == 1), skip_group_check=True))
            t_acc = P.ctrs["ts"].n
            wait("vector", "ts", t_acc)
            ymq_g[(tq, ec)] = op("vs", lambda e, bi=bi, yb=yb: e.tensor_mul(
                ymq[:], PS[:, yb:yb + 1024], zq[bi][:]))
            wait("gpsimd", "vs", ymq_g[(tq, ec)])
            op("qg", lambda e, ec=ec, t0=t0: e.dma_start(
                out=ym_scr.ap()[ec, :, t0:t0 + 1024], in_=ymq[:]))
            if "d_ymul" in dbg_t:
                op("qg", lambda e, ec=ec, t0=t0: e.dma_start(
                    out=dbg_t["d_ymul"].ap()[ec, :, t0:t0 + 1024], in_=ymq[:]))
            wait("vector", "qg")
    SF("m_bcst", "m_dq0", "m_xq0", "m_xq1",
       "m_dq1", "m_uq", "m_bxq", "m_bxq1", "m_hsq", "m_ytq0", "m_ytq1",
       "m_zq0", "m_zq1", "m_ymq")
    bar()

    outf = [SA("c4_outf", [128, C], F32), SA("c4_outf1", [128, C], F32)]
    outt = SA("c4_outt", [128, 8 * C], BF16)
    ocv = []
    for ec in range(8):
        if ec >= 2:
            wait("sync", "vs", ocv[ec - 2])
        op("qs", lambda e, ec=ec: e.dma_start(out=outf[ec % 2][:], in_=outT.ap()[ec]))
        wait("vector", "qs")
        ocv.append(op("vs", lambda e, ec=ec: e.tensor_copy(
            outt[:, ec * C:(ec + 1) * C], outf[ec % 2][:])))
    v_outt = P.ctrs["vs"].n
    wait("tensor", "vs", v_outt)
    ymgs = SA("c4_ymgs", [128, 8 * 512], BF16)
    prt = [SA("c4_prt", [128, 512], F32), SA("c4_prt1", [128, 512], F32)]
    prt_g = {}
    for gs in range(8):
        wait("sync", "ts")
        op("qs", lambda e, gs=gs: e.dma_start(
            out=ymgs[:].rearrange("p (c j) -> p c j", j=512),
            in_=ym_scr.ap()[:, :, gs * 512:(gs + 1) * 512].transpose([1, 0, 2])))
        q_ym = P.ctrs["qs"].n
        wait("tensor", "qs", q_ym)
        for cc in range(8):
            bk = (gs * 8 + cc) % 2
            wait("tensor", "vs")
            for ec in range(8):
                op("ts", lambda e, cc=cc, ec=ec, bk=bk: e.matmul(
                    PS[:, bk * 512:(bk + 1) * 512],
                    outt[:, ec * C + cc * 128:ec * C + cc * 128 + 128],
                    ymgs[:, ec * 512:(ec + 1) * 512],
                    start=(ec == 0), stop=(ec == 7), skip_group_check=True))
            wait("vector", "ts")
            bp = (gs * 8 + cc) % 2
            wait("vector", "qg", prt_g.get(bp, 0))
            op("vs", lambda e, bk=bk, bp=bp: e.tensor_copy(
                prt[bp][:], PS[:, bk * 512:(bk + 1) * 512]))
            wait("gpsimd", "vs")
            for bb_ in range(2):
                op("qg", lambda e, cc=cc, gs=gs, bb_=bb_, bp=bp: e.dma_start(
                    out=rs_in.ap()[bb_, cc, :, gs * 256:gs * 256 + 256]
                    .rearrange("p (u j) -> p u j", j=128),
                    in_=prt[bp][:].rearrange("p (u v j) -> p u v j", u=2, v=2)[:, :, bb_, :]))
            prt_g[bp] = P.ctrs["qg"].n
    g_rs = P.ctrs["qg"].n
    wait("gpsimd", "qg", g_rs)
    op("gc", lambda e: e.collective_compute(
        "ReduceScatter", Alu.add, replica_groups=GROUPS,
        ins=[rs_in.ap().opt()], outs=[rs_out.ap().opt()]))
    SF("c4_outf", "c4_outf1", "c4_outt", "c4_ymgs", "c4_prt", "c4_prt1")
    bar()
    rsl = [SA("c4_rsl0", [128, T2], F32), SA("c4_rsl1", [128, T2], F32)]
    wait("vector", "qg", g_park)
    wait("sync", "gc")
    for c in range(8):
        bi = c % 2
        if c >= 2:
            wait("sync", "vs")
        op("qs", lambda e, c=c, bi=bi: e.dma_start(out=rsl[bi][:], in_=rs_out.ap()[c]))
        wait("vector", "qs")
        op("vs", lambda e, c=c, bi=bi: e.tensor_add(xfm(c), xfm(c), rsl[bi][:]))
    SF("c4_rsl0", "c4_rsl1")
    if "d_x3" in dbg_t:
        wait("gpsimd", "vs")
        for c in range(8):
            op("qg", lambda e, c=c: e.dma_start(out=dbg_t["d_x3"].ap()[c], in_=xfm(c)))
    bar()

    emit_ffn(4, "none", "b")
    for c in range(8):
        op("qg", lambda e, c=c: e.dma_start(out=y_own.ap()[c], in_=xfm(c)))
    g_out = P.ctrs["qg"].n
    wait("gpsimd", "qg", g_out)
    bar()

    block = nc.Block().__enter__()
    P.finalize(block)
    block.__exit__(None, None, None)
    return nc


def _prep_core(inputs, core):
    seq, par = core // 2, core % 2
    f32 = np.float32
    x = np.asarray(inputs["L0_attn_a_x"], dtype=f32)
    xT = np.ascontiguousarray(x[seq].T)
    own_cols = np.concatenate(
        [np.arange((2 * i + par) * 128, (2 * i + par) * 128 + 128) for i in range(16)])
    m = {}
    m["x_full"] = np.ascontiguousarray(xT.reshape(8, 128, T))
    m["x_own"] = np.ascontiguousarray(xT[:, own_cols].reshape(8, 128, T2))
    m["wq"] = np.ascontiguousarray(
        np.asarray(inputs["L0_attn_a_w_q"], f32).reshape(8, 128, C))
    m["wkvd"] = np.ascontiguousarray(
        np.asarray(inputs["L0_attn_a_w_kv_down"], f32).reshape(8, 128, KVR))
    wkvu = np.asarray(inputs["L0_attn_a_w_kv_up"], f32)
    m["wkvu_k"] = np.ascontiguousarray(wkvu[:, :C])
    m["wkvu_v"] = np.ascontiguousarray(wkvu[:, C:])
    m["wout"] = np.ascontiguousarray(
        np.asarray(inputs["L0_attn_a_w_out"], f32).reshape(8, 128, C))
    wg = np.asarray(inputs["L0_ffn_w_gate"], np.int32)
    wu = np.asarray(inputs["L0_ffn_w_up"], np.int32)
    wd = np.asarray(inputs["L0_ffn_w_down"], np.int32)
    m["wgT"] = np.ascontiguousarray(wg.T.reshape(8, 128, FH))
    m["wuT"] = np.ascontiguousarray(wu.T.reshape(8, 128, FH))
    m["wdT"] = np.ascontiguousarray(wd.T.reshape(24, 128, C))
    ipw = np.asarray(inputs["L0_attn_b_in_proj_w"], f32)
    sel = np.concatenate([ipw[par * EO:(par + 1) * EO],
                          ipw[EF + par * EO:EF + (par + 1) * EO]])
    m["inpT"] = np.ascontiguousarray(sel.T.reshape(8, 128, EF))
    xpw = np.asarray(inputs["L0_attn_b_x_proj_w"], f32)
    m["xpT"] = np.ascontiguousarray(
        xpw[:, par * EO:(par + 1) * EO].T.reshape(8, 128, 96))
    dtw = np.asarray(inputs["L0_attn_b_dt_w"], f32)
    m["dtT"] = np.ascontiguousarray(dtw[par * EO:(par + 1) * EO].T)
    opw = np.asarray(inputs["L0_attn_b_out_proj_w"], f32)
    m["outT"] = np.ascontiguousarray(
        opw[:, par * EO:(par + 1) * EO].T.reshape(8, 128, C))
    alog = np.asarray(inputs["L0_attn_b_A_log"], f32)[par * EO:(par + 1) * EO]
    m["a_log"] = np.ascontiguousarray(
        alog.reshape(8, 128, 16).transpose(1, 0, 2).reshape(128, 128))
    cw = np.asarray(inputs["L0_attn_b_conv_w"], f32)[par * EO:(par + 1) * EO]
    m["convw"] = np.ascontiguousarray(
        cw.reshape(8, 128, 4).transpose(1, 0, 2).reshape(128, 32))
    for name, key in [("convb", "L0_attn_b_conv_b"), ("dtb", "L0_attn_b_dt_b"),
                      ("dpar", "L0_attn_b_D_param")]:
        v = np.asarray(inputs[key], f32)[par * EO:(par + 1) * EO]
        m[name] = np.ascontiguousarray(v.reshape(8, 128).T)
    import ml_dtypes
    bf = ml_dtypes.bfloat16
    m["ones128"] = np.ones((128, 128), dtype=bf)
    m["eye128"] = np.eye(128, dtype=np.float32).astype(bf)
    s2 = np.zeros((2, 128), np.float32)
    s2[0, 0:64] = 1.0
    s2[1, 64:128] = 1.0
    m["sel2"] = s2.astype(bf)
    tril = np.triu(np.ones((128, 128), np.float32))
    m["mask_a"] = (tril if par == 0 else np.ones((128, 128), np.float32)).astype(bf)
    m["mask_b"] = (np.zeros((128, 128), np.float32) if par == 0 else tril).astype(bf)
    return m


_NC_CACHE = {}


def kernel(**inputs):
    key = "main"
    if key not in _NC_CACHE:
        _NC_CACHE[key] = build_nc()
    nc = _NC_CACHE[key]
    in_maps = [_prep_core(inputs, core) for core in range(8)]
    res = run_bass_kernel_spmd(nc, in_maps, core_ids=list(range(8)))
    out = np.empty((B, T, C), dtype=np.float32)
    for core in range(8):
        seq, par = core // 2, core % 2
        yo = np.asarray(res.results[core]["y_own"])
        yo = yo.reshape(C, T2)
        cols = np.concatenate(
            [np.arange((2 * i + par) * 128, (2 * i + par) * 128 + 128)
             for i in range(16)])
        out[seq][cols, :] = yo.T
    return out
